# version 06
# baseline (speedup 1.0000x reference)
import numpy as np
import ml_dtypes
from contextlib import ExitStack
import concourse.bass as bass
import concourse.mybir as mybir
from concourse.bass_utils import run_bass_kernel_spmd

F32 = mybir.dt.float32
BF16 = mybir.dt.bfloat16
AF = mybir.ActivationFunctionType
ALU = mybir.AluOpType
AX = mybir.AxisListType
BF = ml_dtypes.bfloat16

NCORE = 8
B, S, D = 2, 8192, 1024
TC = 2048
NCHUNK = D // 128


class Prog:
    DMA_STREAMS = ("qsp", "qg", "qcc", "qact")
    STREAMS = ("pe", "act", "dve", "pool", "qsp", "qg", "qcc", "qact")

    PARANOID = False

    def __init__(self, name="prog"):
        self.nc = bass.Bass("TRN2", target_bir_lowering=False, debug=False)
        self.stack = ExitStack()
        self.ops = []
        self.last_w = {}
        self.readers = {}
        self.n_sb = 0
        self.n_ps = 0
        self._finalized = False
        self.tag = ""
        self._stage_stack = None
        self._fence = []

    def stage_begin(self, tag):
        assert self._stage_stack is None
        self.tag = tag + "_"
        self._stage_stack = ExitStack()

    def stage_end(self):
        self._stage_stack.close()
        self._stage_stack = None
        self.tag = ""
        last = {}
        for i, rec in enumerate(self.ops):
            last[rec["stream"]] = i
        self._fence = list(last.values())

    def fence(self):
        last = {}
        for i, rec in enumerate(self.ops):
            last[rec["stream"]] = i
        self._fence = list(last.values())

    def collective(self, ins, outs, groups, kind="AllGather", op=None):
        if op is None:
            op = ALU.bypass
        for a, o in zip(ins, outs):
            self.op("qcc", lambda e, a=a, o=o: e.collective_compute(
                kind, op, replica_groups=groups,
                ins=[a.tensor.ap().opt() if hasattr(a, "tensor") else a for a in [a]],
                outs=[o.tensor.ap().opt()]), rmw=("ccchain",))

    def inp(self, name, shape, dtype=F32):
        return self.nc.dram_tensor(name, list(shape), dtype, kind="ExternalInput").ap()

    def out(self, name, shape, dtype=F32):
        return self.nc.dram_tensor(name, list(shape), dtype, kind="ExternalOutput").ap()

    def dram(self, name, shape, dtype=F32):
        return self.nc.dram_tensor(self.tag + name, list(shape), dtype,
                                   kind="Internal").ap()

    def sb(self, name, shape, dtype=F32):
        if self._stage_stack is not None:
            return self._stage_stack.enter_context(
                self.nc.sbuf_tensor(self.tag + name, list(shape), dtype)).ap()
        return self.nc.alloc_sbuf_tensor(name, list(shape), dtype).ap()

    def ps(self, name, shape, dtype=F32):
        if self._stage_stack is not None:
            return self._stage_stack.enter_context(
                self.nc.psum_tensor(self.tag + name, list(shape), dtype)).ap()
        return self.nc.alloc_psum_tensor(name, list(shape), dtype).ap()

    def op(self, stream, emit, reads=(), writes=(), rmw=()):
        assert stream in self.STREAMS, stream
        deps = set()
        for k in tuple(reads) + tuple(rmw):
            if k in self.last_w:
                deps.add(self.last_w[k])
        for k in tuple(writes) + tuple(rmw):
            if k in self.last_w:
                deps.add(self.last_w[k])
            for r in self.readers.get(k, ()):
                deps.add(r)
        opid = len(self.ops)
        for k in tuple(reads) + tuple(rmw):
            self.readers.setdefault(k, []).append(opid)
        for k in tuple(writes) + tuple(rmw):
            self.last_w[k] = opid
            self.readers[k] = [opid] if k in rmw else []
        deps.discard(opid)
        for f in self._fence:
            if f < opid:
                deps.add(f)
        if self.PARANOID and opid > 0:
            deps.add(opid - 1)
        self.ops.append(dict(stream=stream, emit=emit, deps=deps))
        return opid

    DEBUG = False

    def dbg(self, name, ap, reads):
        if not self.DEBUG:
            return
        t = self.nc.dram_tensor("dbg_" + name, [ap.shape[0], ap.shape[1]],
                                ap.dtype, kind="ExternalOutput").ap()
        self.dma("qsp", t[:], ap, reads=tuple(reads), writes=(f"_dbg{name}",))

    def dma(self, queue, out, in_, reads=(), writes=(), **kw):
        return self.op(queue, lambda e: e.dma_start(out=out, in_=in_, **kw),
                       reads=reads, writes=writes)

    DMA_ROT = 8

    def finalize(self):
        assert not self._finalized
        self._finalized = True
        n = len(self.ops)
        import bisect
        dma_ord = {}
        counters = {s: 0 for s in self.DMA_STREAMS}
        for i, rec in enumerate(self.ops):
            s = rec["stream"]
            if s in self.DMA_STREAMS:
                dma_ord[i] = counters[s]
                counters[s] += 1
        marks = {s: [] for s in self.STREAMS}
        waits = [[] for _ in range(n)]
        pending = [sorted(rec["deps"]) for rec in self.ops]
        for i, rec in enumerate(self.ops):
            for d in pending[i]:
                ds = self.ops[d]["stream"]
                if ds in self.DMA_STREAMS:
                    waits[i].append(("dma", ds, d))
                    continue
                if ds == rec["stream"] == "pe":
                    continue
                m = marks[ds]
                j = bisect.bisect_left(m, d)
                if not (j < len(m) and m[j] < i):
                    bisect.insort(m, d)
        for i, rec in enumerate(self.ops):
            for d in pending[i]:
                ds = self.ops[d]["stream"]
                if ds in self.DMA_STREAMS or (ds == rec["stream"] == "pe"):
                    continue
                m = marks[ds]
                j = bisect.bisect_left(m, d)
                assert j < len(m) and m[j] >= d and m[j] < i
                waits[i].append(("cmp", ds, m[j]))
        for s in ("pe", "act", "dve", "pool"):
            sops = [i for i, r in enumerate(self.ops) if r["stream"] == s]
            if sops and sops[-1] not in marks[s]:
                marks[s].append(sops[-1])
        cval = {}
        for s in ("pe", "act", "dve", "pool"):
            for k, opid in enumerate(marks[s]):
                cval[(s, opid)] = k + 1
        markset = {(s, o) for s in ("pe", "act", "dve", "pool") for o in marks[s]}

        nc = self.nc
        R = self.DMA_ROT
        csem = {s: self.stack.enter_context(nc.semaphore("sem_" + s))
                for s in ("pe", "act", "dve", "pool")}
        dsem = {s: [self.stack.enter_context(nc.semaphore(f"sem_{s}{r}"))
                    for r in range(R)] for s in self.DMA_STREAMS}

        def dma_sem_val(opid):
            o = dma_ord[opid]
            s = self.ops[opid]["stream"]
            step = 1 if s == "qcc" else 16
            return dsem[s][o % R], step * (o // R + 1)

        block = self.stack.enter_context(nc.Block())
        bass_engines = dict(pe="tensor", act="scalar", dve="vector",
                            pool="gpsimd", qsp="sync", qg="gpsimd", qcc="gpsimd",
                            qact="scalar")
        groups = {}
        for i, rec in enumerate(self.ops):
            groups.setdefault(bass_engines[rec["stream"]], []).append(i)
        watermark = {}

        def emit_for(beng, eng):
            for i in groups.get(beng, ()):
                rec = self.ops[i]
                s = rec["stream"]
                if s in self.DMA_STREAMS and dma_ord[i] >= R:
                    o = dma_ord[i] - R
                    prev = [k for k, v in dma_ord.items()
                            if self.ops[k]["stream"] == s and v == o][0]
                    h, v = dma_sem_val(prev)
                    wkey = (beng, ("dma", s, o % R))
                    if watermark.get(wkey, 0) < v:
                        eng.wait_ge(h, v)
                        watermark[wkey] = v
                for w in waits[i]:
                    if w[0] == "dma":
                        _, ds, dop = w
                        h, v = dma_sem_val(dop)
                        wkey = (beng, ("dma", ds, dma_ord[dop] % R))
                    else:
                        _, ds, dop = w
                        h, v = csem[ds], cval[(ds, dop)]
                        wkey = (beng, ds)
                    if watermark.get(wkey, 0) >= v:
                        continue
                    eng.wait_ge(h, v)
                    watermark[wkey] = v
                ins = rec["emit"](eng)
                if s in self.DMA_STREAMS:
                    h, v = dma_sem_val(i)
                    ins.then_inc(h, 1 if s == "qcc" else 16)
                elif (s, i) in markset:
                    ins.then_inc(csem[s], 1)

        for beng_name, attr in (("tensor", "tensor"), ("scalar", "scalar"),
                                ("vector", "vector"), ("gpsimd", "gpsimd"),
                                ("sync", "sync")):
            deco = getattr(block, attr)
            deco(lambda eng, b=beng_name: emit_for(b, eng))

        @block.sync
        def _(sync):
            for s in ("pe", "act", "dve", "pool"):
                if marks[s]:
                    sync.wait_ge(csem[s], len(marks[s]))
            for s in self.DMA_STREAMS:
                cnt = counters[s]
                step = 1 if s == "qcc" else 16
                for r in range(R):
                    tot = (cnt - r + R - 1) // R
                    if tot > 0:
                        sync.wait_ge(dsem[s][r], step * tot)

        self.stack.close()
        return nc


def bcast_row(p, tag, row_ap, row_key, ncols, out_tile, out_key, nrows=128, dtype=F32):
    d = p.dram(f"bcd_{tag}", [1, ncols], dtype)
    p.dma("qsp", d[:], row_ap, reads=(row_key,), writes=(f"bcd_{tag}",))
    p.dma("qsp", out_tile, d[:].partition_broadcast(nrows),
          reads=(f"bcd_{tag}",), writes=(out_key,))


def np_split_rows(x, n):
    return [np.ascontiguousarray(v) for v in np.split(x, n, axis=0)]


def stage_sc(p, sc):
    sc2 = dict(sc)
    sc2["ps_stat"] = p.ps("ps_stat", [1, 512], F32)
    return sc2


def ln_slice(p, tag, xs, xs_key, W, xhat_w, xhat_key, sc, eps=1e-5, affine=None):
    assert W <= 512
    rows = sc["lnrows"]
    m, s2, rstd, mr = (rows[:, 0:W], rows[:, 512:512 + W],
                       rows[:, 1024:1024 + W], rows[:, 1536:1536 + W])
    nc8 = len(xs)
    for ec in range(nc8):
        p.op("pe", lambda e, ec=ec: e.matmul(
            sc["ps_stat"][:, 0:512], sc["ones_f32"][:], xs[ec][:, 0:W],
            start=(ec == 0), stop=(ec == nc8 - 1)),
            reads=(f"{xs_key}{ec}", "ones"), writes=("ps_stat",))
    p.op("dve", lambda e: e.tensor_scalar(
        out=m[0:1, :], in0=sc["ps_stat"][:, 0:512], scalar1=1.0 / (nc8 * 128),
        scalar2=None, op0=ALU.mult), reads=("ps_stat",), writes=("lnrows",))
    for ec in range(nc8):
        p.op("act", lambda e, ec=ec: e.activation(
            out=sc["sq"][:, 0:W], in_=xs[ec][:, 0:W], func=AF.Square),
            reads=(f"{xs_key}{ec}",), writes=("sq",))
        p.op("pe", lambda e, ec=ec: e.matmul(
            sc["ps_stat"][:, 0:W], sc["ones_f32"][:], sc["sq"][:, 0:W],
            start=(ec == 0), stop=(ec == nc8 - 1)),
            reads=("sq", "ones"), writes=("ps_stat",))
    p.op("dve", lambda e: e.tensor_scalar(
        out=s2[0:1, :], in0=sc["ps_stat"][:, 0:W], scalar1=1.0 / (nc8 * 128),
        scalar2=None, op0=ALU.mult), reads=("ps_stat",), writes=("lnrows",))
    p.op("dve", lambda e: e.tensor_tensor(out=rstd, in0=m, in1=m, op=ALU.mult),
         rmw=("lnrows",))
    p.op("dve", lambda e: e.tensor_tensor(out=rstd, in0=s2, in1=rstd, op=ALU.subtract),
         rmw=("lnrows",))
    p.op("act", lambda e: e.activation(out=rstd, in_=rstd, func=AF.Sqrt,
         bias=sc["eps"][0:1, :]), reads=("consts",), rmw=("lnrows",))
    p.op("dve", lambda e: e.reciprocal(rstd, rstd), rmw=("lnrows",))
    p.op("dve", lambda e: e.tensor_tensor(out=mr, in0=m, in1=rstd, op=ALU.mult),
         rmw=("lnrows",))
    bcast_row(p, f"{tag}r", rstd[0:1, :], "lnrows", W, sc["rb"][:, 0:W], "rb")
    bcast_row(p, f"{tag}m", mr[0:1, :], "lnrows", W, sc["mb"][:, 0:W], "mb")
    for ec in range(nc8):
        p.op("dve", lambda e, ec=ec: e.tensor_tensor(
            out=sc["sq"][:, 0:W], in0=xs[ec][:, 0:W], in1=sc["rb"][:, 0:W], op=ALU.mult),
            reads=(f"{xs_key}{ec}", "rb"), writes=("sq",))
        if affine is None:
            p.op("pool", lambda e, ec=ec: e.tensor_tensor(
                out=xhat_w(ec), in0=sc["sq"][:, 0:W], in1=sc["mb"][:, 0:W],
                op=ALU.subtract),
                reads=("sq", "mb"), writes=(f"{xhat_key}{ec}",))
        else:
            gcol, bcol, akey = affine
            p.op("pool", lambda e, ec=ec: e.tensor_tensor(
                out=sc["sq"][:, 0:W], in0=sc["sq"][:, 0:W], in1=sc["mb"][:, 0:W],
                op=ALU.subtract), reads=("mb",), rmw=("sq",))
            p.op("pool", lambda e, ec=ec, g=gcol(ec), b=bcol(ec): e.tensor_scalar(
                out=xhat_w(ec), in0=sc["sq"][:, 0:W], scalar1=g, scalar2=b,
                op0=ALU.mult, op1=ALU.add),
                reads=("sq", akey), writes=(f"{xhat_key}{ec}",))


def base_scratch(p):
    sc = {
        "ones_f32": p.sb("ones_f32", [128, 1], F32),
        "ones_bf": p.sb("ones_bf", [128, 1], BF16),
        "sq": p.sb("sq_s", [128, 512], F32),
        "lnrows": p.sb("lnrows", [1, 2048], F32),
        "rb": p.sb("rb_s", [128, 512], F32),
        "mb": p.sb("mb_s", [128, 512], F32),
        "eps": None,
    }
    sc["eps"] = p.sb("eps_c", [128, 1], F32)
    sc["cm24"] = p.sb("cm24_c", [128, 1], F32)
    p.op("dve", lambda e: e.memset(sc["ones_f32"][:], 1.0), writes=("ones",))
    p.op("dve", lambda e: e.memset(sc["ones_bf"][:], 1.0), writes=("ones",))
    p.op("dve", lambda e: e.memset(sc["eps"][:], 1e-5), writes=("consts",))
    p.op("dve", lambda e: e.memset(sc["cm24"][:], -24.0), writes=("consts",))
    return sc


def bld_a1(p=None, ext=None):
    standalone = p is None
    if standalone:
        p = Prog()
        ext = dict(xqT=p.inp("xqT", [D, TC]), wqT=p.inp("wqT", [D, D]),
                   wkT=p.inp("wkT", [D, D]), wvT=p.inp("wvT", [D, D]),
                   bq=p.inp("bq", [D, 1]), bk=p.inp("bk", [D, 1]),
                   bvr=p.inp("bvr", [1, D]), q_o=p.out("q_o", [D, TC], BF16),
                   k_o=p.out("k_o", [D, TC], BF16), v_o=p.out("v_o", [TC, D], BF16),
                   sc=base_scratch(p))
    xqT, wqT, wkT, wvT = ext["xqT"], ext["wqT"], ext["wkT"], ext["wvT"]
    bq, bk, bvr = ext["bq"], ext["bk"], ext["bvr"]
    q_o, k_o, v_o = ext["q_o"], ext["k_o"], ext["v_o"]
    sc = ext["sc"]
    p.stage_begin("a1")
    sc = stage_sc(p, sc)
    xs = [p.sb(f"xs{i}", [128, 512], F32) for i in range(NCHUNK)]
    xh = [p.sb(f"xh{i}", [128, 512], BF16) for i in range(NCHUNK)]
    wq = [p.sb(f"wq{i}", [128, D], BF16) for i in range(NCHUNK)]
    wk = [p.sb(f"wk{i}", [128, D], BF16) for i in range(NCHUNK)]
    wv = [p.sb(f"wv{i}", [128, D], BF16) for i in range(NCHUNK)]
    bq_sb = p.sb("bq_sb", [128, NCHUNK], F32)
    bk_sb = p.sb("bk_sb", [128, NCHUNK], F32)
    bv_bc = p.sb("bv_bc", [128, D], F32)
    stg = [p.sb(f"stg{j}", [128, 512], BF16) for j in range(4)]
    pp = [p.ps(f"pp{j}", [128, 512], F32) for j in range(4)]
    for i in range(NCHUNK):
        p.dma("qg", wq[i][:], wqT[i * 128:(i + 1) * 128, :], writes=("wq",))
        p.dma("qg", wk[i][:], wkT[i * 128:(i + 1) * 128, :], writes=("wk",))
        p.dma("qg", wv[i][:], wvT[i * 128:(i + 1) * 128, :], writes=("wv",))
        p.dma("qsp", bq_sb[:, i:i + 1], bq[i * 128:(i + 1) * 128, :], writes=("bias",))
        p.dma("qsp", bk_sb[:, i:i + 1], bk[i * 128:(i + 1) * 128, :], writes=("bias",))
    p.dma("qsp", bv_bc[:], bvr[:].partition_broadcast(128), writes=("bias",))

    for si in range(TC // 512):
        sl = slice(si * 512, (si + 1) * 512)
        for ec in range(NCHUNK):
            p.dma("qsp", xs[ec][:], xqT[ec * 128:(ec + 1) * 128, sl], writes=(f"xs{ec}",))
        ln_slice(p, f"s{si}", xs, "xs", 512, lambda ec: xh[ec][:], "xh", sc)
        for (wt, wkey, bsb, dst) in ((wq, "wq", bq_sb, q_o), (wk, "wk", bk_sb, k_o)):
            for eo in range(NCHUNK):
                pj = pp[eo % 2]
                for ec in range(NCHUNK):
                    p.op("pe", lambda e, pj=pj, wt=wt, ec=ec, eo=eo: e.matmul(
                        pj[:], wt[ec][:, eo * 128:(eo + 1) * 128], xh[ec][:],
                        start=(ec == 0), stop=(ec == NCHUNK - 1)),
                        reads=(f"xh{ec}", wkey), writes=(f"pp{eo % 2}",))
                st = stg[eo % 2]
                p.op("dve", lambda e, pj=pj, st=st, bsb=bsb, eo=eo: e.tensor_scalar(
                    out=st[:], in0=pj[:], scalar1=bsb[:, eo:eo + 1], scalar2=None,
                    op0=ALU.add), reads=(f"pp{eo % 2}", "bias"), writes=(f"stg{eo % 2}",))
                p.dma("qsp", dst[eo * 128:(eo + 1) * 128, sl], st[:],
                      reads=(f"stg{eo % 2}",), writes=("out",))
        for tt in range(4):
            for vg in range(2):
                j = 2 + (tt * 2 + vg) % 2
                pj = pp[j]
                for ec in range(NCHUNK):
                    p.op("pe", lambda e, pj=pj, ec=ec, tt=tt, vg=vg: e.matmul(
                        pj[:], xh[ec][:, tt * 128:(tt + 1) * 128],
                        wv[ec][:, vg * 512:(vg + 1) * 512],
                        start=(ec == 0), stop=(ec == NCHUNK - 1)),
                        reads=(f"xh{ec}", "wv"), writes=(f"pp{j}",))
                st = stg[j]
                p.op("dve", lambda e, pj=pj, st=st, vg=vg: e.tensor_tensor(
                    out=st[:], in0=pj[:], in1=bv_bc[:, vg * 512:(vg + 1) * 512],
                    op=ALU.add), reads=(f"pp{j}", "bias"), writes=(f"stg{j}",))
                p.dma("qsp", v_o[si * 512 + tt * 128: si * 512 + (tt + 1) * 128,
                              vg * 512:(vg + 1) * 512], st[:],
                      reads=(f"stg{j}",), writes=("out",))
    p.stage_end()
    return p.finalize() if standalone else None


def bld_a2(p=None, ext=None):
    standalone = p is None
    if standalone:
        p = Prog()
        ext = dict(qT=p.inp("qT", [D, TC], BF16), kT=p.inp("kT", [D, S], BF16),
                   vF=p.inp("vF", [S, D], BF16), xtm=p.inp("xtm", [TC, D]),
                   idenb=p.inp("idenb", [128, 128], BF16),
                   xo=p.out("xo", [TC, D]))
    qT, kT, vF, xtm = ext["qT"], ext["kT"], ext["vF"], ext["xtm"]
    idenb, xo = ext["idenb"], ext["xo"]
    fm_out = ext.get("xo_fm") is not None
    kT_gath = ext.get("kT_gathered", False)
    p.stage_begin("a2")
    q_fm = [p.sb(f"qf{i}", [128, TC], BF16) for i in range(NCHUNK)]
    iden = p.sb("iden", [128, 128], BF16)
    scores = p.sb("scores", [128, S], F32)
    prob = p.sb("prob", [128, S], BF16)
    pt = p.sb("pt", [128, S], BF16)
    kst = [p.sb(f"kst{j}", [128, NCHUNK * 512], BF16) for j in range(4)]
    vst = [p.sb(f"vst{j}", [128, D], BF16) for j in range(8)]
    mxc = p.sb("mxc", [128, 4], F32)
    out_tm = p.sb("out_tm", [128, D], F32)
    xr = p.sb("xr", [128, D], F32)
    ps_sc = [p.ps(f"ps_sc{j}", [128, 512], F32) for j in range(2)]
    ps_o = [p.ps(f"ps_o{j}", [128, 512], F32) for j in range(2)]
    ps_tr = [p.ps(f"ps_tr{j}", [128, 128], BF16) for j in range(2)]
    if fm_out:
        ps_trf = [p.ps("ps_trf0", [128, 128], F32)] * 2
        idenf_sb = p.sb("idenf_sb", [128, 128], F32)
        ofm = p.sb("ofm", [128, 128], F32)
        p.dma("qsp", idenf_sb[:], ext["idf"][:], writes=("idenf",))

    p.dma("qg", iden[:], idenb[:], writes=("iden",))
    for i in range(NCHUNK):
        p.dma("qg", q_fm[i][:], qT[i * 128:(i + 1) * 128, :], writes=("qf",))
    NKG = S // 512
    NKT = S // 128
    for qt in range(TC // 128):
        qsl = slice(qt * 128, (qt + 1) * 128)
        for kg in range(NKG):
            buf = kst[kg % 4]
            bk_ = f"kst{kg % 4}"
            if not kT_gath:
                p.dma("qsp", buf[:].rearrange("p (c w) -> p c w", w=512),
                      kT[:, kg * 512:(kg + 1) * 512].rearrange("(c p) w -> p c w", p=128),
                      reads=(), writes=(bk_,))
            else:
                jc, lc = kg // 4, (kg % 4) * 512
                p.dma("qsp", buf[:].rearrange("p (c w) -> p c w", w=512),
                      kT[jc * D:(jc + 1) * D, lc:lc + 512]
                      .rearrange("(c p) w -> p c w", p=128),
                      reads=(), writes=(bk_,))
            pj = ps_sc[kg % 2]
            for ec in range(NCHUNK):
                p.op("pe", lambda e, pj=pj, ec=ec, buf=buf, qsl=qsl: e.matmul(
                    pj[:], q_fm[ec][:, qsl], buf[:, ec * 512:(ec + 1) * 512],
                    start=(ec == 0), stop=(ec == NCHUNK - 1)),
                    reads=("qf", bk_), writes=(f"ps_sc{kg % 2}",))
            p.op("dve", lambda e, pj=pj, kg=kg: e.tensor_copy(
                scores[:, kg * 512:(kg + 1) * 512], pj[:]),
                reads=(f"ps_sc{kg % 2}",), writes=(f"sc{kg}",))
        p.op("dve", lambda e: e.reduce_max(mxc[:, 0:1], scores[:], axis=AX.X),
             reads=tuple(f"sc{kg}" for kg in range(NKG)), writes=("mxc",))
        p.op("dve", lambda e: e.tensor_scalar(
            out=mxc[:, 1:2], in0=mxc[:, 0:1], scalar1=-1.0 / 32.0, scalar2=None,
            op0=ALU.mult), rmw=("mxc",))
        p.op("act", lambda e: e.activation(
            out=prob[:], in_=scores[:], func=AF.Exp, bias=mxc[:, 1:2],
            scale=1.0 / 32.0, accum_out=mxc[:, 2:3]),
            reads=tuple(f"sc{kg}" for kg in range(NKG)), writes=("prob",), rmw=("mxc",))
        p.op("dve", lambda e: e.reciprocal(mxc[:, 3:4], mxc[:, 2:3]), rmw=("mxc",))
        for kt in range(NKT):
            pj = ps_tr[kt % 2]
            p.op("pe", lambda e, pj=pj, kt=kt: e.transpose(
                pj[:], prob[:, kt * 128:(kt + 1) * 128], iden[:]),
                reads=("prob", "iden"), writes=(f"ps_tr{kt % 2}",))
            eng = "dve" if kt % 2 == 0 else "act"
            if eng == "dve":
                p.op("dve", lambda e, pj=pj, kt=kt: e.tensor_copy(
                    pt[:, kt * 128:(kt + 1) * 128], pj[:]),
                    reads=(f"ps_tr{kt % 2}",), writes=(f"pt{kt}",))
            else:
                p.op("act", lambda e, pj=pj, kt=kt: e.activation(
                    out=pt[:, kt * 128:(kt + 1) * 128], in_=pj[:], func=AF.Copy),
                    reads=(f"ps_tr{kt % 2}",), writes=(f"pt{kt}",))
        for kt in range(NKT):
            vb = vst[kt % 8]
            vk = f"vst{kt % 8}"
            p.dma("qg", vb[:], vF[kt * 128:(kt + 1) * 128, :], writes=(vk,))
            for og in range(2):
                p.op("pe", lambda e, vb=vb, kt=kt, og=og: e.matmul(
                    ps_o[og][:], pt[:, kt * 128:(kt + 1) * 128],
                    vb[:, og * 512:(og + 1) * 512],
                    start=(kt == 0), stop=(kt == NKT - 1)),
                    reads=(f"pt{kt}", vk), writes=(f"ps_o{og}",))
        p.dma("qsp", xr[:], xtm[qsl, :], writes=("xr",))
        for og in range(2):
            p.op("dve", lambda e, og=og: e.tensor_scalar(
                out=out_tm[:, og * 512:(og + 1) * 512], in0=ps_o[og][:],
                scalar1=mxc[:, 3:4], scalar2=None, op0=ALU.mult),
                reads=(f"ps_o{og}", "mxc"), writes=("out_tm",))
            p.op("pool", lambda e, og=og: e.tensor_tensor(
                out=out_tm[:, og * 512:(og + 1) * 512],
                in0=out_tm[:, og * 512:(og + 1) * 512],
                in1=xr[:, og * 512:(og + 1) * 512], op=ALU.add),
                reads=("xr",), rmw=("out_tm",))
        if not fm_out:
            p.dma("qsp", xo[qsl, :], out_tm[:], reads=("out_tm",), writes=("xout",))
        else:
            for oc in range(NCHUNK):
                pj = ps_trf[oc % 2]
                p.op("pe", lambda e, pj=pj, oc=oc: e.transpose(
                    pj[:], out_tm[:, oc * 128:(oc + 1) * 128], idenf_sb[:]),
                    reads=("out_tm", "idenf"), writes=("ps_trf0",))
                p.op("dve", lambda e, pj=pj, oc=oc: e.tensor_copy(ofm[:], pj[:]),
                     reads=("ps_trf0",), writes=("ofm",))
                p.dma("qsp", ext["xo_fm"][oc * 128:(oc + 1) * 128, qsl], ofm[:],
                      reads=("ofm",), writes=("xout",))
    p.stage_end()
    return p.finalize() if standalone else None


_EYE_BF = np.eye(128, dtype=np.float32).astype(BF)


def _cores():
    return [(c, c // 4, (c % 4) * TC) for c in range(NCORE)]


def host_attn_a(x, Wq, bq, Wk, bk, Wv, bv):
    wqT = np.ascontiguousarray(Wq.T)
    wkT = np.ascontiguousarray(Wk.T)
    wvT = np.ascontiguousarray(Wv.T)
    bqc = np.ascontiguousarray(bq.reshape(D, 1))
    bkc = np.ascontiguousarray(bk.reshape(D, 1))
    bvr = np.ascontiguousarray(bv.reshape(1, D))
    nc1 = bld_a1()
    in1 = []
    for c, b, o in _cores():
        chunk = x[b, o:o + TC]
        in1.append(dict(xqT=np.ascontiguousarray(chunk.T), wqT=wqT, wkT=wkT,
                        wvT=wvT, bq=bqc, bk=bkc, bvr=bvr))
    r1 = run_bass_kernel_spmd(nc1, in1, core_ids=list(range(NCORE))).results
    kT = {b: np.concatenate([r1[c]["k_o"] for c, bb, o in _cores() if bb == b], axis=1)
          for b in range(B)}
    vF = {b: np.concatenate([r1[c]["v_o"] for c, bb, o in _cores() if bb == b], axis=0)
          for b in range(B)}
    nc2 = bld_a2()
    in2 = []
    for c, b, o in _cores():
        in2.append(dict(qT=r1[c]["q_o"], kT=np.ascontiguousarray(kT[b]),
                        vF=np.ascontiguousarray(vF[b]),
                        xtm=np.ascontiguousarray(x[b, o:o + TC]), idenb=_EYE_BF))
    r2 = run_bass_kernel_spmd(nc2, in2, core_ids=list(range(NCORE))).results
    out = np.empty((B, S, D), np.float32)
    for c, b, o in _cores():
        out[b, o:o + TC] = r2[c]["xo"]
    return out


def bld_ffn(pre_add=False, pre_proj=False, p=None, ext=None, stage="f"):
    standalone = p is None
    if standalone:
        p = Prog()
        ext = dict(xT=p.inp("xT", [D, TC]), rw1=p.inp("rw1", [D, D]),
                   rw2=p.inp("rw2", [D, 8]), rb1=p.inp("rb1", [D, 1]),
                   rb2=p.inp("rb2", [8, 1]), we1=p.inp("we1", [8, D, 4096]),
                   we2=p.inp("we2", [8, 4096, D]), be1c=p.inp("be1c", [128, 256]),
                   be2r=p.inp("be2r", [8, D]), idf=p.inp("idf", [128, 128]),
                   xoT=p.out("xoT", [D, TC]), sc=base_scratch(p))
        if pre_add:
            ext["oT"] = p.inp("oT", [D, TC])
        if pre_proj:
            ext["oT"] = p.inp("oT", [D, TC], BF16)
            ext["wo"] = p.inp("wo", [D, D])
    xT, rw1, rw2, rb1, rb2 = ext["xT"], ext["rw1"], ext["rw2"], ext["rb1"], ext["rb2"]
    we1, we2, be1c, be2r, idf = ext["we1"], ext["we2"], ext["be1c"], ext["be2r"], ext["idf"]
    xoT, sc = ext["xoT"], ext["sc"]
    if pre_add or pre_proj:
        oT = ext["oT"]
    if pre_proj:
        wo = ext["wo"]
    ocol_dyn = ext.get("ocol_dyn")
    wbf = ext.get("wbf", False)
    wq1 = "qsp" if wbf else "qg"
    wq2 = "qg"
    p.stage_begin(stage)
    sc = stage_sc(p, sc)
    TB = 1024
    x_tb = [p.sb(f"x{i}", [128, TB], F32) for i in range(NCHUNK)]
    xh = [p.sb(f"xh{i}", [128, TB], BF16) for i in range(NCHUNK)]
    r1sl = [p.sb(f"r1_{i}", [128, 512], BF16) for i in range(NCHUNK)]
    rw2_sb = [p.sb(f"rw2_{i}", [128, 8], BF16) for i in range(NCHUNK)]
    rb1_sb = p.sb("rb1_sb", [128, NCHUNK], F32)
    rb2_sb = p.sb("rb2_sb", [8, 1], F32)
    be1_sb = p.sb("be1_sb", [128, 256], F32)
    be2_sb = p.sb("be2_sb", [8, D], BF16)
    lgT = p.sb("lgT", [8, TB], F32)
    lg = [p.sb(f"lg{t}", [128, 8], F32) for t in range(8)]
    wtm = [p.sb(f"wtm{t}", [128, 8], F32) for t in range(8)]
    gt8 = [p.sb(f"gt{t}", [128, 8], F32) for t in range(8)]
    ex8 = [p.sb(f"ex{t}", [128, 8], F32) for t in range(8)]
    nm8 = [p.sb(f"nm{t}", [128, 2], F32) for t in range(8)]
    wT_bf = p.sb("wT_bf", [8, TB], BF16)
    tk = p.sb("tk", [128, 16], F32)
    iden_f = p.sb("iden_f", [128, 128], F32)
    osl = p.sb("osl", [128, 512], F32)
    if pre_proj:
        o_all = p.sb("o_all", [128, NCHUNK * TB], BF16)
        o_fm = [o_all[:, i * TB:(i + 1) * TB] for i in range(NCHUNK)]
    hbuf = [p.sb(f"hb{i}", [128, TB], BF16) for i in range(16)]
    w2buf = [p.sb(f"w2b{i}", [128, D], BF16) for i in range(16)]
    w1col = [p.sb(f"w1c{j}", [128, NCHUNK * 128], BF16) for j in range(4)]
    y_acc = [p.sb(f"ya{t}", [128, D], F32) for t in range(8)]
    ps_h = [p.ps(f"ps_h{j}", [128, 512], F32) for j in range(2)]
    ps_eo = [p.ps(f"ps_eo{j}", [128, 512], F32) for j in range(3)]
    ps_l = p.ps("ps_l", [8, 512], F32)
    ps_t = p.ps("ps_t", [128, 128], F32)

    p.dma("qsp", iden_f[:], idf[:], writes=("idenf",))
    for i in range(NCHUNK):
        p.dma("qg", rw2_sb[i][:], rw2[i * 128:(i + 1) * 128, :], writes=("rw2",))
        p.dma("qsp", rb1_sb[:, i:i + 1], rb1[i * 128:(i + 1) * 128, :], writes=("rbias",))
    p.dma("qsp", rb2_sb[:], rb2[:], writes=("rbias",))
    p.dma("qsp", be1_sb[:], be1c[:], writes=("rbias",))
    p.dma("qg", be2_sb[:], be2r[:], writes=("be2",))

    for tb in range(2):
        tb0 = tb * TB
        for ec in range(NCHUNK):
            p.dma("qsp", x_tb[ec][:], xT[ec * 128:(ec + 1) * 128, tb0:tb0 + TB],
                  writes=(f"x{ec}",))
        if pre_add:
            for ec in range(NCHUNK):
                for si in range(2):
                    sl = slice(si * 512, (si + 1) * 512)
                    p.dma("qsp", osl[:], oT[ec * 128:(ec + 1) * 128,
                                            tb0 + si * 512:tb0 + (si + 1) * 512],
                          writes=("osl",))
                    p.op("dve", lambda e, ec=ec, sl=sl: e.tensor_tensor(
                        out=x_tb[ec][:, sl], in0=x_tb[ec][:, sl], in1=osl[:],
                        op=ALU.add), reads=("osl",), rmw=(f"x{ec}",))
        if pre_proj:
            if ocol_dyn is None:
                for ec in range(NCHUNK):
                    p.dma("qg", o_fm[ec], oT[ec * 128:(ec + 1) * 128, tb0:tb0 + TB],
                          writes=(f"of{ec}",))
            else:
                def _emit(e, tb0=tb0):
                    with e.register() as r0:
                        e.reg_load(r0, ocol_dyn)
                        e.reg_add(r0, r0, tb0)
                        return e.dma_start(
                            out=o_all[:].rearrange("p (ec c) -> p ec c", c=TB),
                            in_=bass.AP(oT.tensor, r0,
                                        [[S, 128], [128 * S, NCHUNK], [1, TB]]))
                p.op("qsp", _emit, reads=("offs",),
                     writes=tuple(f"of{ec}" for ec in range(NCHUNK)))
            for oc in range(NCHUNK):
                wc = w1col[oc % 2]
                wck = f"w1c{oc % 2}"
                p.dma("qg", wc[:].rearrange("p (c w) -> p c w", w=128),
                      wo[:, oc * 128:(oc + 1) * 128].rearrange(
                          "(c p) w -> p c w", p=128), writes=(wck,))
                for qg in range(2):
                    pj = ps_h[qg]
                    pk = f"ps_h{qg}"
                    sl = slice(qg * 512, (qg + 1) * 512)
                    for ec in range(NCHUNK):
                        p.op("pe", lambda e, pj=pj, wc=wc, ec=ec, sl=sl: e.matmul(
                            pj[:], wc[:, ec * 128:(ec + 1) * 128], o_fm[ec][:, sl.start:sl.stop],
                            start=(ec == 0), stop=(ec == NCHUNK - 1)),
                            reads=(f"of{ec}", wck), writes=(pk,))
                    p.op("dve", lambda e, pj=pj, oc=oc, sl=sl: e.tensor_tensor(
                        out=x_tb[oc][:, sl], in0=x_tb[oc][:, sl], in1=pj[:],
                        op=ALU.add), reads=(pk,), rmw=(f"x{oc}",))
        for si in range(2):
            sl = slice(si * 512, (si + 1) * 512)
            xs = [x_tb[ec][:, sl] for ec in range(NCHUNK)]
            ln_slice(p, f"f{tb}{si}", xs, "x", 512,
                     lambda ec, sl=sl: xh[ec][:, sl], "xh", sc)
        for qg in range(2):
            sl = slice(qg * 512, (qg + 1) * 512)
            for eo in range(NCHUNK):
                pj = ps_h[eo % 2]
                pk = f"ps_h{eo % 2}"
                wc = w1col[eo % 2]
                wck = f"w1c{eo % 2}"
                p.dma("qg", wc[:].rearrange("p (c w) -> p c w", w=128),
                      rw1[:, eo * 128:(eo + 1) * 128].rearrange(
                          "(c p) w -> p c w", p=128), writes=(wck,))
                for ec in range(NCHUNK):
                    p.op("pe", lambda e, pj=pj, wc=wc, ec=ec, sl=sl: e.matmul(
                        pj[:], wc[:, ec * 128:(ec + 1) * 128], xh[ec][:, sl],
                        start=(ec == 0), stop=(ec == NCHUNK - 1)),
                        reads=(f"xh{ec}", wck), writes=(pk,))
                p.op("act", lambda e, pj=pj, eo=eo: e.activation(
                    out=r1sl[eo][:], in_=pj[:], func=AF.Silu,
                    bias=rb1_sb[:, eo:eo + 1]),
                    reads=(pk, "rbias"), writes=(f"r1_{eo}",))
            for ec in range(NCHUNK):
                p.op("pe", lambda e, ec=ec: e.matmul(
                    ps_l[:], rw2_sb[ec][:, :], r1sl[ec][:],
                    start=(ec == 0), stop=(ec == NCHUNK - 1)),
                    reads=(f"r1_{ec}", "rw2"), writes=("ps_l",))
            p.op("dve", lambda e, sl=sl: e.tensor_scalar(
                out=lgT[:, sl], in0=ps_l[:], scalar1=rb2_sb[:], scalar2=None,
                op0=ALU.add), reads=("ps_l", "rbias"), writes=("lgT",))
        for tt in range(8):
            p.op("pe", lambda e, tt=tt: e.transpose(
                ps_t[0:128, 0:8], lgT[0:8, tt * 128:(tt + 1) * 128], iden_f[0:8, 0:8]),
                reads=("lgT", "idenf"), writes=("ps_t",))
            p.op("dve", lambda e, tt=tt: e.tensor_copy(lg[tt][:], ps_t[0:128, 0:8]),
                 reads=("ps_t",), writes=(f"lg{tt}",))
        for tt in range(8):
            L, lk = lg[tt], f"lg{tt}"
            m1 = tk[:, tt:tt + 1]
            m2 = tk[:, 8 + tt:9 + tt]
            p.op("dve", lambda e, L=L, m1=m1: e.reduce_max(m1, L[:], axis=AX.X),
                 reads=(lk,), writes=(f"tk{tt}",))
            p.op("dve", lambda e, L=L, m1=m1, tt=tt: e.tensor_scalar(
                out=wtm[tt][:], in0=L[:], scalar1=m1, scalar2=-1e30,
                op0=ALU.is_ge, op1=ALU.mult), reads=(lk, f"tk{tt}"), writes=(f"wtm{tt}",))
            p.op("dve", lambda e, L=L, tt=tt: e.tensor_tensor(
                out=wtm[tt][:], in0=L[:], in1=wtm[tt][:], op=ALU.add),
                reads=(lk,), rmw=(f"wtm{tt}",))
            p.op("dve", lambda e, tt=tt, m2=m2: e.reduce_max(
                m2, wtm[tt][:], axis=AX.X),
                reads=(f"wtm{tt}",), writes=(f"tk2_{tt}",))
            gt, ex, nm = gt8[tt], ex8[tt], nm8[tt]
            p.op("dve", lambda e, L=L, m2=m2, gt=gt: e.tensor_single_scalar(
                out=gt[:], in_=L[:], scalar=m2, op=ALU.is_ge),
                reads=(lk, f"tk2_{tt}"), writes=(f"gt{tt}",))
            p.op("dve", lambda e, m1=m1, nm=nm: e.tensor_scalar(
                out=nm[:, 0:1], in0=m1, scalar1=-1.0, scalar2=None, op0=ALU.mult),
                reads=(f"tk{tt}",), writes=(f"nm{tt}",))
            p.op("act", lambda e, L=L, nm=nm, ex=ex: e.activation(
                out=ex[:], in_=L[:], func=AF.Exp, bias=nm[:, 0:1]),
                reads=(lk, f"nm{tt}"), writes=(f"ex{tt}",))
            p.op("dve", lambda e, m2=m2, nm=nm: e.tensor_tensor(
                out=nm[:, 1:2], in0=m2, in1=nm[:, 0:1], op=ALU.add),
                reads=(f"tk2_{tt}",), rmw=(f"nm{tt}",))
            p.op("act", lambda e, nm=nm: e.activation(
                out=nm[:, 1:2], in_=nm[:, 1:2], func=AF.Exp), rmw=(f"nm{tt}",))
            p.op("dve", lambda e, nm=nm: e.tensor_scalar(
                out=nm[:, 1:2], in0=nm[:, 1:2], scalar1=1.0, scalar2=None,
                op0=ALU.add), rmw=(f"nm{tt}",))
            p.op("dve", lambda e, nm=nm: e.reciprocal(nm[:, 1:2], nm[:, 1:2]),
                 rmw=(f"nm{tt}",))
            p.op("dve", lambda e, ex=ex, nm=nm, gt=gt, tt=tt: e.scalar_tensor_tensor(
                out=wtm[tt][:], in0=ex[:], scalar=nm[:, 1:2], in1=gt[:],
                op0=ALU.mult, op1=ALU.mult),
                reads=(f"ex{tt}", f"nm{tt}", f"gt{tt}"), writes=(f"wtm{tt}",))
            p.op("pe", lambda e, tt=tt: e.transpose(
                ps_t[0:8, :], wtm[tt][:], iden_f[:]),
                reads=(f"wtm{tt}", "idenf"), writes=("ps_t",))
            p.op("dve", lambda e, tt=tt: e.tensor_copy(
                wT_bf[0:8, tt * 128:(tt + 1) * 128], ps_t[0:8, 0:128]),
                reads=("ps_t",), writes=("wT",))
        p.dbg(f"lgT{tb}", lgT[:], ("lgT",))
        p.dbg(f"wT{tb}", wT_bf[:], ("wT",))
        p.dbg(f"xh0_{tb}", xh[0][:], ("xh0",))
        p.dbg(f"tk{tb}", tk[:], tuple(f"tk{t}" for t in range(8)) + tuple(f"tk2_{t}" for t in range(8)))
        p.dbg(f"gt0_{tb}", gt8[0][:], ("gt0",))
        p.dbg(f"ex0_{tb}", ex8[0][:], ("ex0",))
        p.dbg(f"nm0_{tb}", nm8[0][:], ("nm0",))
        p.dbg(f"wtm0_{tb}", wtm[0][:], ("wtm0",))
        for tl in range(8):
            for og in range(2):
                jj = (tl * 2 + og) % 3
                pj = ps_eo[jj]
                p.op("pe", lambda e, pj=pj, tl=tl, og=og: e.matmul(
                    pj[:], wT_bf[0:8, tl * 128:(tl + 1) * 128],
                    be2_sb[0:8, og * 512:(og + 1) * 512], start=True, stop=True),
                    reads=("wT", "be2"), writes=(f"ps_eo{jj}",))
                p.op("dve", lambda e, pj=pj, tl=tl, og=og: e.tensor_copy(
                    y_acc[tl][:, og * 512:(og + 1) * 512], pj[:]),
                    reads=(f"ps_eo{jj}",), writes=(f"ya{tl}",))
        p.dbg(f"yinit{tb}", y_acc[0][:], ("ya0",))
        for exi in range(8):
            for hh in range(2):
                for hc in range(16):
                    g = hh * 16 + hc
                    p.dma(wq2, w2buf[hc][:],
                          we2[exi, g * 128:(g + 1) * 128, :], writes=(f"w2b{hc}",))
                for hc in range(16):
                    g = hh * 16 + hc
                    wc = w1col[hc % 4]
                    wck = f"w1c{hc % 4}"
                    p.dma(wq1, wc[:].rearrange("p (c w) -> p c w", w=128),
                          we1[exi, :, g * 128:(g + 1) * 128].rearrange(
                              "(c p) w -> p c w", p=128), writes=(wck,))
                    for ec in range(NCHUNK):
                        for qg in range(2):
                            pj = ps_h[qg]
                            pk = f"ps_h{qg}"
                            sl = slice(qg * 512, (qg + 1) * 512)
                            p.op("pe", lambda e, pj=pj, wc=wc, ec=ec, sl=sl: e.matmul(
                                pj[:], wc[:, ec * 128:(ec + 1) * 128], xh[ec][:, sl],
                                start=(ec == 0), stop=(ec == NCHUNK - 1)),
                                reads=(f"xh{ec}", wck), writes=(pk,))
                    for qg in range(2):
                        pj = ps_h[qg]
                        pk = f"ps_h{qg}"
                        sl = slice(qg * 512, (qg + 1) * 512)
                        p.op("act", lambda e, pj=pj, hc=hc, sl=sl, exi=exi, g=g:
                             e.activation(
                                 out=hbuf[hc][:, sl], in_=pj[:], func=AF.Silu,
                                 bias=be1_sb[:, exi * 32 + g:exi * 32 + g + 1]),
                             reads=(pk, "rbias"), writes=(f"hb{hc}",))
                for tl in range(8):
                    for hc in range(16):
                        for og in range(2):
                            pj = ps_eo[og]
                            pk = f"ps_eo{og}"
                            p.op("pe", lambda e, pj=pj, hc=hc, tl=tl, og=og: e.matmul(
                                pj[:], hbuf[hc][:, tl * 128:(tl + 1) * 128],
                                w2buf[hc][:, og * 512:(og + 1) * 512],
                                start=(hc == 0), stop=(hc == 15)),
                                reads=(f"hb{hc}", f"w2b{hc}"), writes=(pk,))
                    for og in range(2):
                        pj = ps_eo[og]
                        pk = f"ps_eo{og}"
                        p.op("dve", lambda e, pj=pj, tl=tl, og=og, exi=exi:
                             e.scalar_tensor_tensor(
                                 out=y_acc[tl][:, og * 512:(og + 1) * 512], in0=pj[:],
                                 scalar=wtm[tl][:, exi:exi + 1],
                                 in1=y_acc[tl][:, og * 512:(og + 1) * 512],
                                 op0=ALU.mult, op1=ALU.add),
                             reads=(pk, f"wtm{tl}"), rmw=(f"ya{tl}",))
        p.dbg(f"yfin{tb}", y_acc[0][:], ("ya0",))
        p.dbg(f"hb0_{tb}", hbuf[0][:], ("hb0",))
        for tl in range(8):
            for oc in range(NCHUNK):
                p.op("pe", lambda e, tl=tl, oc=oc: e.transpose(
                    ps_t[:], y_acc[tl][:, oc * 128:(oc + 1) * 128], iden_f[:]),
                    reads=(f"ya{tl}", "idenf"), writes=("ps_t",))
                p.op("dve", lambda e, tl=tl, oc=oc: e.tensor_tensor(
                    out=x_tb[oc][:, tl * 128:(tl + 1) * 128],
                    in0=x_tb[oc][:, tl * 128:(tl + 1) * 128],
                    in1=ps_t[:], op=ALU.add),
                    reads=("ps_t",), rmw=(f"x{oc}",))
                p.op("pool", lambda e, oc=oc, tl=tl: e.tensor_tensor(
                    out=x_tb[oc][:, tl * 128:(tl + 1) * 128],
                    in0=x_tb[oc][:, tl * 128:(tl + 1) * 128],
                    in1=xh[oc][:, tl * 128:(tl + 1) * 128], op=ALU.add),
                    reads=(f"xh{oc}",), rmw=(f"x{oc}",))
        for ec in range(NCHUNK):
            p.dma("qsp", xoT[ec * 128:(ec + 1) * 128, tb0:tb0 + TB], x_tb[ec][:],
                  reads=(f"x{ec}",), writes=("xout",))
    p.stage_end()
    return p.finalize() if standalone else None


def host_ffn(x, rw1, rb1, rw2, rb2, we1, be1, we2, be2, o=None):
    nc = bld_ffn(pre_add=o is not None)
    be1c = np.ascontiguousarray(be1.reshape(8, 32, 128).transpose(2, 0, 1).reshape(128, 256))
    idf = np.eye(128, dtype=np.float32)
    ins = []
    for c, b, ofs in _cores():
        d = dict(xT=np.ascontiguousarray(x[b, ofs:ofs + TC].T),
                 rw1=rw1, rw2=rw2, rb1=np.ascontiguousarray(rb1.reshape(D, 1)),
                 rb2=np.ascontiguousarray(rb2.reshape(8, 1)),
                 we1=we1, we2=we2, be1c=be1c, be2r=be2, idf=idf)
        if o is not None:
            d["oT"] = np.ascontiguousarray(o[b, ofs:ofs + TC].T)
        ins.append(d)
    r = run_bass_kernel_spmd(nc, ins, core_ids=list(range(NCORE))).results
    out = np.empty((B, S, D), np.float32)
    for c, b, ofs in _cores():
        out[b, ofs:ofs + TC] = r[c]["xoT"].T
    return out


def bld_b1(p=None, ext=None):
    standalone = p is None
    if standalone:
        p = Prog()
        ext = dict(xT=p.inp("xT", [D, TC]), wqp=p.inp("wqp", [D, D]),
                   wkp=p.inp("wkp", [D, D]), wv=p.inp("wv", [D, D]),
                   cosd=p.inp("cosd", [TC, 512]), sind=p.inp("sind", [TC, 512]),
                   idb=p.inp("idb", [128, 128], BF16),
                   q_o=p.out("q_o", [D, TC], BF16), k_o=p.out("k_o", [D, TC], BF16),
                   v_o=p.out("v_o", [D, TC], BF16), sc=base_scratch(p))
    xT, wqp, wkp, wv = ext["xT"], ext["wqp"], ext["wkp"], ext["wv"]
    cosd, sind = ext["cosd"], ext["sind"]
    q_o, k_o, v_o, sc = ext["q_o"], ext["k_o"], ext["v_o"], ext["sc"]
    p.stage_begin("b1")
    sc = stage_sc(p, sc)
    idb_sb = p.sb("idb_sb", [128, 128], BF16)
    p.dma("qg", idb_sb[:], ext["idb"][:], writes=("idb",))
    ps_t2 = [p.ps(f"ps_t2{j}", [128, 128], BF16) for j in range(2)]
    tstg = p.sb("tstg", [128, 128], BF16)
    xs = [p.sb(f"xs{i}", [128, 512], F32) for i in range(NCHUNK)]
    xh = [p.sb(f"xh{i}", [128, 512], BF16) for i in range(NCHUNK)]
    wq_sb = [p.sb(f"wq{i}", [128, D], BF16) for i in range(NCHUNK)]
    wk_sb = [p.sb(f"wk{i}", [128, D], BF16) for i in range(NCHUNK)]
    wv_sb = [p.sb(f"wv{i}", [128, D], BF16) for i in range(NCHUNK)]
    qtm = p.sb("qtm", [128, D], F32)
    ktm = p.sb("ktm", [128, D], F32)
    rtm = p.sb("rtm", [128, D], F32)
    rtmb = p.sb("rtmb", [128, D], BF16)
    cs = p.sb("cs", [128, 512], F32)
    sn = p.sb("sn", [128, 512], F32)
    tmp = p.sb("tmp", [128, 512], F32)
    tmp2 = p.sb("tmp2", [128, 512], F32)
    stg = [p.sb(f"stg{j}", [128, 512], BF16) for j in range(2)]
    pp = [p.ps(f"pp{j}", [128, 512], F32) for j in range(4)]
    for i in range(NCHUNK):
        p.dma("qg", wq_sb[i][:], wqp[i * 128:(i + 1) * 128, :], writes=("wq",))
        p.dma("qg", wk_sb[i][:], wkp[i * 128:(i + 1) * 128, :], writes=("wk",))
        p.dma("qg", wv_sb[i][:], wv[i * 128:(i + 1) * 128, :], writes=("wv",))

    def rope(src_tile, src_key, tt):
        EV, OD = src_tile[:, 0:512], src_tile[:, 512:1024]
        RE, RO = rtm[:, 0:512], rtm[:, 512:1024]
        p.op("dve", lambda e: e.tensor_tensor(out=RE, in0=EV, in1=cs[:], op=ALU.mult),
             reads=(src_key, "cs"), writes=("rtm",))
        p.op("dve", lambda e: e.tensor_tensor(out=tmp[:], in0=OD, in1=sn[:], op=ALU.mult),
             reads=(src_key, "sn"), writes=("tmp",))
        p.op("dve", lambda e: e.tensor_tensor(out=RE, in0=RE, in1=tmp[:], op=ALU.subtract),
             reads=("tmp",), rmw=("rtm",))
        p.op("pool", lambda e: e.tensor_tensor(out=RO, in0=EV, in1=sn[:], op=ALU.mult),
             reads=(src_key, "sn"), writes=("rtm2",))
        p.op("pool", lambda e: e.tensor_tensor(out=tmp2[:], in0=OD, in1=cs[:], op=ALU.mult),
             reads=(src_key, "cs"), writes=("tmp2",))
        p.op("pool", lambda e: e.tensor_tensor(out=RO, in0=RO, in1=tmp2[:], op=ALU.add),
             reads=("tmp2",), rmw=("rtm2",))

    for si in range(4):
        sl = slice(si * 512, (si + 1) * 512)
        for ec in range(NCHUNK):
            p.dma("qsp", xs[ec][:], xT[ec * 128:(ec + 1) * 128, sl], writes=(f"xs{ec}",))
        ln_slice(p, f"s{si}", xs, "xs", 512, lambda ec: xh[ec][:], "xh", sc)
        for tt in range(4):
            trow = si * 512 + tt * 128
            for (wt, wkey, dst_t, dkey) in ((wq_sb, "wq", qtm, "qtm"),
                                            (wk_sb, "wk", ktm, "ktm")):
                for fg in range(2):
                    pj = pp[fg]
                    for ec in range(NCHUNK):
                        p.op("pe", lambda e, pj=pj, wt=wt, ec=ec, tt=tt, fg=fg: e.matmul(
                            pj[:], xh[ec][:, tt * 128:(tt + 1) * 128],
                            wt[ec][:, fg * 512:(fg + 1) * 512],
                            start=(ec == 0), stop=(ec == NCHUNK - 1)),
                            reads=(f"xh{ec}", wkey), writes=(f"pp{fg}",))
                    p.op("dve", lambda e, pj=pj, dst_t=dst_t, fg=fg: e.tensor_copy(
                        dst_t[:, fg * 512:(fg + 1) * 512], pj[:]),
                        reads=(f"pp{fg}",), writes=(dkey,))
            p.dma("qsp", cs[:], cosd[trow:trow + 128, :], writes=("cs",))
            p.dma("qsp", sn[:], sind[trow:trow + 128, :], writes=("sn",))
            for (t_t, tkey, tkey2, dst) in ((qtm, "qtm", "rtm", q_o),
                                            (ktm, "ktm", "rtm", k_o)):
                rope(t_t, tkey, tt)
                p.op("dve", lambda e: e.tensor_copy(rtmb[:, 0:512], rtm[:, 0:512]),
                     reads=("rtm",), writes=("rtmb",))
                p.op("pool", lambda e: e.tensor_copy(rtmb[:, 512:1024], rtm[:, 512:1024]),
                     reads=("rtm2",), rmw=("rtmb",))
                for f in range(8):
                    pj = ps_t2[f % 2]
                    p.op("pe", lambda e, pj=pj, f=f: e.transpose(
                        pj[:], rtmb[:, f * 128:(f + 1) * 128], idb_sb[:]),
                        reads=("rtmb", "idb"), writes=(f"ps_t2{f % 2}",))
                    p.op("dve", lambda e, pj=pj: e.tensor_copy(tstg[:], pj[:]),
                         reads=(f"ps_t2{f % 2}",), writes=("tstg",))
                    for hp in range(4):
                        head = (f % 4) * 4 + hp
                        roff = head * 64 + (0 if f < 4 else 32)
                        p.dma("qsp", dst[roff:roff + 32, trow:trow + 128],
                              tstg[hp * 32:(hp + 1) * 32, :],
                              reads=("tstg",), writes=("out",))
            for vg in range(2):
                pj = pp[2 + vg]
                for ec in range(NCHUNK):
                    p.op("pe", lambda e, pj=pj, ec=ec, tt=tt, vg=vg: e.matmul(
                        pj[:], xh[ec][:, tt * 128:(tt + 1) * 128],
                        wv_sb[ec][:, vg * 512:(vg + 1) * 512],
                        start=(ec == 0), stop=(ec == NCHUNK - 1)),
                        reads=(f"xh{ec}", "wv"), writes=(f"pp{2 + vg}",))
                p.op("dve", lambda e, pj=pj, vg=vg: e.tensor_copy(stg[vg][:], pj[:]),
                     reads=(f"pp{2 + vg}",), writes=(f"stg{vg}",))
                for fq in range(4):
                    f = vg * 4 + fq
                    pj2 = ps_t2[fq % 2]
                    p.op("pe", lambda e, pj2=pj2, vg=vg, fq=fq: e.transpose(
                        pj2[:], stg[vg][:, fq * 128:(fq + 1) * 128], idb_sb[:]),
                        reads=(f"stg{vg}", "idb"), writes=(f"ps_t2{fq % 2}",))
                    p.op("dve", lambda e, pj2=pj2: e.tensor_copy(tstg[:], pj2[:]),
                         reads=(f"ps_t2{fq % 2}",), writes=("tstg",))
                    p.dma("qsp", v_o[f * 128:(f + 1) * 128, trow:trow + 128], tstg[:],
                          reads=("tstg",), writes=("out",))
    p.stage_end()
    return p.finalize() if standalone else None


def bld_b2(p=None, ext=None):
    standalone = p is None
    if standalone:
        p = Prog()
        ext = dict(qb=p.inp("qb", [4, 64, S], BF16), kb=p.inp("kb", [4, 64, S], BF16),
                   vb=p.inp("vb", [4, 64, S], BF16), triA=p.inp("triA", [128, 256], BF16),
                   triB=p.inp("triB", [128, 256], BF16),
                   idb=p.inp("idb2", [128, 128], BF16),
                   ob=p.out("ob", [4, 64, S], BF16), sc=base_scratch(p))
    triA, triB, ob, sc = ext["triA"], ext["triB"], ext["ob"], ext["sc"]
    fused = "qkvg" in ext
    if not fused:
        qb, kb, vb = ext["qb"], ext["kb"], ext["vb"]
    else:
        qkvg, ho = ext["qkvg"], ext["ho"]
    p.stage_begin("b2")
    idb_sb = p.sb("idb_sb", [128, 128], BF16)
    p.dma("qg", idb_sb[:], ext["idb"][:], writes=("idb",))
    hosb = p.sb("hosb", [1, 4], mybir.dt.int32)
    if fused:
        p.dma("qsp", hosb[:], ho[:], writes=("offs",))
    qt = p.sb("qt", [64, S], BF16)
    kt = p.sb("kt", [64, S], BF16)
    vfm = p.sb("vfm", [64, S], BF16)
    ps_vt = [p.ps(f"ps_vt{j}", [128, 64], BF16) for j in range(2)]
    ps_bc = p.ps("ps_bc", [64, 256], F32)
    ones64 = p.sb("ones64", [128, 64], F32)
    vt = p.sb("vt", [128, 64 * 65], BF16)
    tA = p.sb("tA", [128, 256], BF16)
    tB = p.sb("tB", [128, 256], BF16)
    pt = [p.sb(f"pt{j}", [128, 256], BF16) for j in range(3)]
    rd = p.sb("rd", [128, 256], F32)
    rdb = p.sb("rdb", [64, 256], F32)
    o_sb = [p.sb(f"osb{j}", [64, 256], BF16) for j in range(2)]
    ps_sc = [p.ps(f"ps_sc{j}", [128, 256], F32) for j in range(3)]
    ps_o = [p.ps(f"ps_o{j}", [65, 256], F32) for j in range(2)]
    p.dma("qsp", tA[:], triA[:], writes=("tri",))
    p.dma("qsp", tB[:], triB[:], writes=("tri",))
    p.op("dve", lambda e: e.memset(ones64[:], 1.0), writes=("ones64",))
    NQG = S // 256
    for pair in range(4):
        p.op("dve", lambda e: e.memset(vt[:], 1.0), writes=("vt",))
        if not fused:
            p.dma("qg", qt[:], qb[pair], writes=("qt",))
            p.dma("qg", kt[:], kb[pair], writes=("kt",))
            p.dma("qg", vfm[:], vb[pair], writes=("vfm",))
        else:
            for (dst, bofs, dk) in ((qt, 0, "qt"), (kt, D * TC, "kt"),
                                    (vfm, 2 * D * TC, "vfm")):
                def _emq(e, dst=dst, bofs=bofs, pair=pair):
                    with e.register() as r0:
                        e.reg_load(r0, hosb[0:1, pair:pair + 1])
                        e.reg_add(r0, r0, bofs)
                        return e.dma_start(
                            out=dst[:].rearrange("p (j c) -> p j c", c=TC),
                            in_=bass.AP(qkvg.tensor, r0,
                                        [[TC, 64], [3 * D * TC, 4], [1, TC]]))
                p.op("qact", _emq, reads=("offs",), writes=(dk,))
        for kti in range(64):
            pj = ps_vt[kti % 2]
            p.op("pe", lambda e, pj=pj, kti=kti: e.transpose(
                pj[:], vfm[0:64, kti * 128:(kti + 1) * 128], idb_sb[0:64, 0:64]),
                reads=("vfm", "idb"), writes=(f"ps_vt{kti % 2}",))
            p.op("dve", lambda e, pj=pj, kti=kti: e.tensor_copy(
                vt[:, kti * 65:kti * 65 + 64], pj[:]),
                reads=(f"ps_vt{kti % 2}",), rmw=("vt",))
        for qg in range(NQG):
            nkt = 2 * qg + 2
            po = ps_o[qg % 2]
            pok = f"ps_o{qg % 2}"
            for kti in range(nkt):
                j = kti % 3
                pj = ps_sc[j]
                pk = f"ps_sc{j}"
                ptj = pt[j]
                ptk = f"pt{j}"
                p.op("pe", lambda e, pj=pj, kti=kti, qg=qg: e.matmul(
                    pj[:], kt[:, kti * 128:(kti + 1) * 128],
                    qt[:, qg * 256:(qg + 1) * 256], start=True, stop=True),
                    reads=("kt", "qt"), writes=(pk,))
                p.op("act", lambda e, pj=pj, ptj=ptj: e.activation(
                    out=ptj[:], in_=pj[:], func=AF.Exp, bias=sc["cm24"][:, 0:1],
                    scale=0.125), reads=(pk, "consts"), writes=(ptk,))
                if kti == nkt - 2:
                    p.op("dve", lambda e, ptj=ptj: e.tensor_tensor(
                        out=ptj[:], in0=ptj[:], in1=tA[:], op=ALU.mult),
                        reads=("tri",), rmw=(ptk,))
                elif kti == nkt - 1:
                    p.op("dve", lambda e, ptj=ptj: e.tensor_tensor(
                        out=ptj[:], in0=ptj[:], in1=tB[:], op=ALU.mult),
                        reads=("tri",), rmw=(ptk,))
                p.op("pe", lambda e, po=po, kti=kti, ptj=ptj: e.matmul(
                    po[:], vt[:, kti * 65:(kti + 1) * 65], ptj[:],
                    start=(kti == 0), stop=(kti == nkt - 1)),
                    reads=("vt", ptk), writes=(pok,))
            p.op("dve", lambda e, po=po: e.reciprocal(rd[64:65, :], po[64:65, :]),
                 reads=(pok,), writes=("rd",))
            p.op("pe", lambda e: e.matmul(
                ps_bc[0:64, :], ones64[64:65, 0:64], rd[64:65, :],
                start=True, stop=True), reads=("rd", "ones64"), writes=("ps_bc",))
            p.op("dve", lambda e: e.tensor_copy(rdb[0:64, :], ps_bc[0:64, :]),
                 reads=("ps_bc",), writes=("rdb",))
            osb = o_sb[qg % 2]
            p.op("dve", lambda e, po=po, osb=osb: e.tensor_tensor(
                out=osb[:], in0=po[0:64, :], in1=rdb[0:64, :], op=ALU.mult),
                reads=(pok, "rdb"), writes=(f"osb{qg % 2}",))
            p.dma("qsp", ob[pair][:, qg * 256:(qg + 1) * 256], osb[:],
                  reads=(f"osb{qg % 2}",), writes=("out",))
    p.stage_end()
    return p.finalize() if standalone else None


_ROPE_PERM = None


def _rope_perm():
    global _ROPE_PERM
    if _ROPE_PERM is None:
        perm = []
        for h in range(16):
            for i in range(32):
                perm.append(h * 64 + 2 * i)
        for h in range(16):
            for i in range(32):
                perm.append(h * 64 + 2 * i + 1)
        _ROPE_PERM = np.array(perm)
    return _ROPE_PERM


def _rope_tables(offset, n, theta=10000.0):
    inv = 1.0 / theta ** (np.arange(0, 64, 2, dtype=np.float64) / 64)
    pos = np.arange(offset, offset + n, dtype=np.float64)
    ang = pos[:, None] * inv[None, :]
    cos = np.cos(ang).astype(np.float32)
    sin = np.sin(ang).astype(np.float32)
    return np.tile(cos, (1, 16)), np.tile(sin, (1, 16))


def host_attn_b(x, wq, wk, wv, wo, rw1, rb1, rw2, rb2, we1, be1, we2, be2):
    perm = _rope_perm()
    wqp = np.ascontiguousarray(wq[:, perm])
    wkp = np.ascontiguousarray(wk[:, perm])
    nc1 = bld_b1()
    in1 = []
    for c, b, o in _cores():
        cos, sin = _rope_tables(o, TC)
        in1.append(dict(xT=np.ascontiguousarray(x[b, o:o + TC].T), wqp=wqp, wkp=wkp,
                        wv=wv, cosd=cos, sind=sin, idb=_EYE_BF))
    r1 = run_bass_kernel_spmd(nc1, in1, core_ids=list(range(NCORE))).results
    qf = {b: np.concatenate([r1[c]["q_o"] for c, bb, o in _cores() if bb == b], axis=1)
          for b in range(B)}
    kf = {b: np.concatenate([r1[c]["k_o"] for c, bb, o in _cores() if bb == b], axis=1)
          for b in range(B)}
    vf = {b: np.concatenate([r1[c]["v_o"] for c, bb, o in _cores() if bb == b], axis=1)
          for b in range(B)}
    nc2 = bld_b2()
    r_ = np.arange(128)[:, None]
    c_ = np.arange(256)[None, :]
    triA = (c_ >= r_).astype(BF)
    triB = (c_ >= r_ + 128).astype(BF)
    in2 = []
    for c in range(NCORE):
        qb = np.empty((4, 64, S), BF)
        kb = np.empty((4, 64, S), BF)
        vb = np.empty((4, 64, S), BF)
        for j in range(4):
            pair = c * 4 + j
            b, h = pair // 16, pair % 16
            qb[j] = qf[b][h * 64:(h + 1) * 64, :]
            kb[j] = kf[b][h * 64:(h + 1) * 64, :]
            vb[j] = vf[b][h * 64:(h + 1) * 64, :]
        in2.append(dict(qb=qb, kb=kb, vb=vb, triA=triA, triB=triB, idb2=_EYE_BF))
    r2 = run_bass_kernel_spmd(nc2, in2, core_ids=list(range(NCORE))).results
    of = {b: np.empty((S, D), BF) for b in range(B)}
    for c in range(NCORE):
        for j in range(4):
            pair = c * 4 + j
            b, h = pair // 16, pair % 16
            of[b][:, h * 64:(h + 1) * 64] = r2[c]["ob"][j].T
    nc3 = bld_ffn(pre_proj=True)
    be1c = np.ascontiguousarray(
        be1.reshape(8, 32, 128).transpose(2, 0, 1).reshape(128, 256))
    idf = np.eye(128, dtype=np.float32)
    in3 = []
    for c, b, o in _cores():
        in3.append(dict(xT=np.ascontiguousarray(x[b, o:o + TC].T),
                        oT=np.ascontiguousarray(of[b][o:o + TC].T), wo=wo,
                        rw1=rw1, rw2=rw2, rb1=np.ascontiguousarray(rb1.reshape(D, 1)),
                        rb2=np.ascontiguousarray(rb2.reshape(8, 1)),
                        we1=we1, we2=we2, be1c=be1c, be2r=be2, idf=idf))
    r3 = run_bass_kernel_spmd(nc3, in3, core_ids=list(range(NCORE))).results
    out = np.empty((B, S, D), np.float32)
    for c, b, o in _cores():
        out[b, o:o + TC] = r3[c]["xoT"].T
    return out


def bld_cffn(p=None, ext=None, stage="c"):
    standalone = p is None
    if standalone:
        p = Prog()
        ext = dict(xT=p.inp("xT", [D, TC]), G1R=p.inp("G1R", [32, 128]),
                   G2R=p.inp("G2R", [64, 16]), G1I=p.inp("G1I", [32, 128]),
                   G2I=p.inp("G2I", [64, 16]), brc=p.inp("brc", [512, 1]),
                   bic=p.inp("bic", [512, 1]), idf=p.inp("idf", [128, 128]),
                   xoT=p.out("xoT", [D, TC]), sc=base_scratch(p))
    xT, G1R, G2R, G1I, G2I = ext["xT"], ext["G1R"], ext["G2R"], ext["G1I"], ext["G2I"]
    brc, bic, idf, xoT, sc = ext["brc"], ext["bic"], ext["idf"], ext["xoT"], ext["sc"]
    p.stage_begin(stage)
    sc = stage_sc(p, sc)
    iden_f = p.sb("iden_f", [128, 128], F32)
    g1 = p.sb("g1", [32, 128], F32)
    g2 = p.sb("g2", [64, 16], F32)
    g1t = p.sb("g1t", [128, 32], F32)
    g2t = p.sb("g2t", [16, 64], F32)
    urow = p.sb("urow", [1, 1024], F32)
    vrow = p.sb("vrow", [1, 256], F32)
    ot = [p.sb(f"ot{j}", [128, 256], F32) for j in range(8)]
    wr_sb = [p.sb(f"wr{j}", [128, 512], BF16) for j in range(4)]
    wi_sb = [p.sb(f"wi{j}", [128, 512], BF16) for j in range(4)]
    wni_sb = [p.sb(f"wni{j}", [128, 512], BF16) for j in range(4)]
    b_sb = p.sb("b_sb", [128, NCHUNK], F32)
    x_fm = [p.sb(f"x{i}", [128, TC], F32) for i in range(NCHUNK)]
    xh = [p.sb(f"xh{i}", [128, TC], BF16) for i in range(NCHUNK)]
    ps_t = p.ps("ps_t", [128, 128], F32)
    ps_w = [p.ps(f"ps_w{j}", [128, 256], F32) for j in range(4)]
    ps_o = [p.ps(f"ps_o{j}", [128, 512], F32) for j in range(2)]
    g1d = p.dram("g1d", [128, 32])
    g2d = p.dram("g2d", [16, 64])
    outer_d = p.dram("outer_d", [1024, 256])

    p.dma("qsp", iden_f[:], idf[:], writes=("idenf",))
    for i in range(4):
        p.dma("qsp", b_sb[:, i:i + 1], brc[i * 128:(i + 1) * 128, :], writes=("bias",))
        p.dma("qsp", b_sb[:, 4 + i:5 + i], bic[i * 128:(i + 1) * 128, :], writes=("bias",))
    for ec in range(NCHUNK):
        p.dma("qsp", x_fm[ec][:], xT[ec * 128:(ec + 1) * 128, :], writes=(f"x{ec}",))
    def build_w(G1, G2, dst, dkey, negdst=None, negkey=None):
        p.dma("qsp", g1[:], G1[:], writes=("g1",))
        p.dma("qsp", g2[:], G2[:], writes=("g2",))
        p.op("pe", lambda e: e.transpose(ps_t[0:128, 0:32], g1[:], iden_f[0:32, 0:32]),
             reads=("g1", "idenf"), writes=("ps_t",))
        p.op("dve", lambda e: e.tensor_copy(g1t[:], ps_t[0:128, 0:32]),
             reads=("ps_t",), writes=("g1t",))
        p.op("pe", lambda e: e.transpose(ps_t[0:16, 0:64], g2[:], iden_f[0:64, 0:64]),
             reads=("g2", "idenf"), writes=("ps_t",))
        p.op("dve", lambda e: e.tensor_copy(g2t[:], ps_t[0:16, 0:64]),
             reads=("ps_t",), writes=("g2t",))
        p.dma("qsp", g1d[:], g1t[:], reads=("g1t",), writes=("g1d",))
        p.dma("qsp", g2d[:], g2t[:], reads=("g2t",), writes=("g2d",))
        for wave in range(2):
            for c in range(4):
                p.dma("qsp", urow[:],
                      bass.AP(g1d.tensor, c * 32 * 32, [[1, 1], [1, 1024]]),
                      reads=("g1d",), writes=("urow",))
                p.dma("qsp", vrow[:],
                      bass.AP(g2d.tensor, c * 16, [[1, 1], [64, 16], [1, 16]]),
                      reads=("g2d",), writes=("vrow",))
                p.dbg(f"urow{dkey}{c}", urow[:], ("urow",))
                p.dbg(f"vrow{dkey}{c}", vrow[:], ("vrow",))
                for j in range(4):
                    u8 = wave * 4 + j
                    pj = ps_w[j]
                    p.op("pe", lambda e, pj=pj, u8=u8, c=c: e.matmul(
                        pj[:], urow[0:1, u8 * 128:(u8 + 1) * 128], vrow[0:1, :],
                        start=(c == 0), stop=(c == 3)),
                        reads=("urow", "vrow"), writes=(f"ps_w{j}",))
                    if c == 3:
                        p.op("dve", lambda e, pj=pj, u8=u8, j=j: e.tensor_copy(
                            ot[u8][:], pj[:]),
                            reads=(f"ps_w{j}",), writes=(f"ot{u8}",))
        for u8 in range(8):
            p.dma("qsp", outer_d[u8 * 128:(u8 + 1) * 128, :], ot[u8][:],
                  reads=(f"ot{u8}",), writes=("outer_d",))
        for qc in range(4):
            for dd in range(8):
                row0 = qc * 128 + dd * 16
                d0 = row0 // 16
                p.dma("qg", dst[qc][dd * 16:(dd + 1) * 16, :].rearrange(
                    "p (b e) -> p b e", e=16),
                    bass.AP(outer_d.tensor, d0 * 32 * 256,
                            [[16, 16], [256, 32], [1, 16]]),
                    reads=("outer_d",), writes=(dkey,))
        if negdst is not None:
            for qc in range(4):
                p.op("dve", lambda e, qc=qc: e.tensor_scalar(
                    out=negdst[qc][:], in0=dst[qc][:], scalar1=-1.0, scalar2=None,
                    op0=ALU.mult), reads=(dkey,), writes=(negkey,))

    for si in range(4):
        sl = slice(si * 512, (si + 1) * 512)
        xs = [x_fm[ec][:, sl] for ec in range(NCHUNK)]
        ln_slice(p, f"c{si}", xs, "x", 512, lambda ec, sl=sl: xh[ec][:, sl], "xh", sc)
    build_w(G1R, G2R, wr_sb, "wr")
    build_w(G1I, G2I, wi_sb, "wi", wni_sb, "wni")
    for qc in range(4):
        p.dbg(f"wr{qc}", wr_sb[qc][:], ("wr",))
        p.dbg(f"wi{qc}", wi_sb[qc][:], ("wi",))

    for half, (wa, wak, wb, wbk) in enumerate(
            ((wr_sb, "wr", wni_sb, "wni"), (wi_sb, "wi", wr_sb, "wr"))):
        for rc in range(4):
            oc = half * 4 + rc
            for qg in range(4):
                sl = slice(qg * 512, (qg + 1) * 512)
                pj = ps_o[qg % 2]
                pk = f"ps_o{qg % 2}"
                for qc in range(4):
                    p.op("pe", lambda e, pj=pj, wa=wa, qc=qc, rc=rc, sl=sl: e.matmul(
                        pj[:], wa[qc][:, rc * 128:(rc + 1) * 128], xh[qc][:, sl],
                        start=(qc == 0), stop=False),
                        reads=(f"xh{qc}", wak), writes=(pk,))
                for qc in range(4):
                    p.op("pe", lambda e, pj=pj, wb=wb, qc=qc, rc=rc, sl=sl: e.matmul(
                        pj[:], wb[qc][:, rc * 128:(rc + 1) * 128], xh[4 + qc][:, sl],
                        start=False, stop=(qc == 3)),
                        reads=(f"xh{4 + qc}", wbk), writes=(pk,))
                p.op("dve", lambda e, pj=pj, oc=oc, sl=sl: e.scalar_tensor_tensor(
                    out=x_fm[oc][:, sl], in0=pj[:], scalar=b_sb[:, oc:oc + 1],
                    in1=x_fm[oc][:, sl], op0=ALU.add, op1=ALU.add),
                    reads=(pk, "bias"), rmw=(f"x{oc}",))
    for ec in range(NCHUNK):
        p.dma("qsp", xoT[ec * 128:(ec + 1) * 128, :], x_fm[ec][:],
              reads=(f"x{ec}",), writes=("xout",))
    p.stage_end()
    return p.finalize() if standalone else None


def host_cffn(x, G1R, G2R, G1I, G2I, br, bi):
    nc = bld_cffn()
    idf = np.eye(128, dtype=np.float32)
    g1r = np.ascontiguousarray(G1R.reshape(32, 128))
    g2r = np.ascontiguousarray(G2R.reshape(64, 16))
    g1i = np.ascontiguousarray(G1I.reshape(32, 128))
    g2i = np.ascontiguousarray(G2I.reshape(64, 16))
    ins = []
    for c, b, o in _cores():
        ins.append(dict(xT=np.ascontiguousarray(x[b, o:o + TC].T),
                        G1R=g1r, G2R=g2r, G1I=g1i, G2I=g2i,
                        brc=np.ascontiguousarray(br.reshape(512, 1)),
                        bic=np.ascontiguousarray(bi.reshape(512, 1)), idf=idf))
    r = run_bass_kernel_spmd(nc, ins, core_ids=list(range(NCORE))).results
    out = np.empty((B, S, D), np.float32)
    for c, b, o in _cores():
        out[b, o:o + TC] = r[c]["xoT"].T
    return out


def bld_n1(p=None, ext=None):
    standalone = p is None
    if standalone:
        p = Prog()
        ext = dict(xT=p.inp("xT", [D, TC]), Wh=p.inp("Wh", [D, 4096]),
                   bhr=p.inp("bhr", [1, 4096]), Wqk=p.inp("Wqk", [D, 128]),
                   bqk=p.inp("bqk", [128, 1]), lng=p.inp("lng", [D, 1]),
                   lnb=p.inp("lnb", [D, 1]), gb=p.inp("gb", [128, 4]),
                   dnr=p.inp("dnr", [1, TC]), v_o=p.out("v_o", [TC, 2048], BF16),
                   g_o=p.out("g_o", [TC, 2048], BF16), q_o=p.out("q_o", [128, TC], BF16),
                   k_o=p.out("k_o", [128, TC], BF16),
                   xh1_o=p.out("xh1_o", [D, TC], BF16), sc=base_scratch(p))
    xT, Wh, bhr, Wqk, bqk = ext["xT"], ext["Wh"], ext["bhr"], ext["Wqk"], ext["bqk"]
    lng, lnb, gb, dnr = ext["lng"], ext["lnb"], ext["gb"], ext["dnr"]
    v_o, g_o, q_o, k_o, xh1_o, sc = (ext["v_o"], ext["g_o"], ext["q_o"],
                                     ext["k_o"], ext["xh1_o"], ext["sc"])
    p.stage_begin("n1")
    sc = stage_sc(p, sc)
    xs = [p.sb(f"xs{i}", [128, 512], F32) for i in range(NCHUNK)]
    xh1 = [p.sb(f"xh1_{i}", [128, 512], F32) for i in range(NCHUNK)]
    xh = [p.sb(f"xh{i}", [128, 512], BF16) for i in range(NCHUNK)]
    xh1b = [p.sb(f"xh1b{i}", [128, 512], BF16) for i in range(NCHUNK)]
    wh_sb = [p.sb(f"wh{i}", [128, 4096], BF16) for i in range(NCHUNK)]
    wqk_sb = [p.sb(f"wqk{i}", [128, 128], BF16) for i in range(NCHUNK)]
    bh_bc = p.sb("bh_bc", [128, 4096], F32)
    bqk_sb = p.sb("bqk_sb", [128, 1], F32)
    lng_sb = p.sb("lng_sb", [128, NCHUNK], F32)
    lnb_sb = p.sb("lnb_sb", [128, NCHUNK], F32)
    gb_sb = p.sb("gb_sb", [128, 4], F32)
    dn_bc = p.sb("dn_bc", [128, 512], F32)
    qkf = p.sb("qkf", [128, 512], F32)
    stg = [p.sb(f"stg{j}", [128, 512], BF16) for j in range(4)]
    pp = [p.ps(f"pp{j}", [128, 512], F32) for j in range(4)]
    for i in range(NCHUNK):
        p.dma("qg", wh_sb[i][:], Wh[i * 128:(i + 1) * 128, :], writes=("wh",))
        p.dma("qg", wqk_sb[i][:], Wqk[i * 128:(i + 1) * 128, :], writes=("wqk",))
        p.dma("qsp", lng_sb[:, i:i + 1], lng[i * 128:(i + 1) * 128, :], writes=("aff",))
        p.dma("qsp", lnb_sb[:, i:i + 1], lnb[i * 128:(i + 1) * 128, :], writes=("aff",))
    p.dma("qsp", bh_bc[:], bhr[:].partition_broadcast(128), writes=("bias",))
    p.dma("qsp", bqk_sb[:], bqk[:], writes=("bias",))
    p.dma("qsp", gb_sb[:], gb[:], writes=("bias",))
    for si in range(4):
        sl = slice(si * 512, (si + 1) * 512)
        for ec in range(NCHUNK):
            p.dma("qsp", xs[ec][:], xT[ec * 128:(ec + 1) * 128, sl], writes=(f"xs{ec}",))
        ln_slice(p, f"a{si}", xs, "xs", 512, lambda ec: xh1[ec][:], "xh1", sc)
        ln_slice(p, f"b{si}", xh1, "xh1", 512, lambda ec: xh[ec][:], "xh", sc,
                 affine=(lambda ec: lng_sb[:, ec:ec + 1],
                         lambda ec: lnb_sb[:, ec:ec + 1], "aff"))
        for ec in range(NCHUNK):
            p.op("dve", lambda e, ec=ec, sl=sl: e.tensor_copy(
                xh1b[ec][:], xh1[ec][:]), reads=(f"xh1{ec}",), writes=("x1b",))
            p.dma("qsp", xh1_o[ec * 128:(ec + 1) * 128, sl], xh1b[ec][:],
                  reads=("x1b",), writes=("out",))
        for tt in range(4):
            trow = si * 512 + tt * 128
            for vg in range(8):
                j = vg % 4
                pj = pp[j]
                pk = f"pp{j}"
                for ec in range(NCHUNK):
                    p.op("pe", lambda e, pj=pj, ec=ec, tt=tt, vg=vg: e.matmul(
                        pj[:], xh[ec][:, tt * 128:(tt + 1) * 128],
                        wh_sb[ec][:, vg * 512:(vg + 1) * 512],
                        start=(ec == 0), stop=(ec == NCHUNK - 1)),
                        reads=(f"xh{ec}", "wh"), writes=(pk,))
                p.op("dve", lambda e, pj=pj, j=j, vg=vg: e.tensor_tensor(
                    out=pj[:], in0=pj[:], in1=bh_bc[:, vg * 512:(vg + 1) * 512],
                    op=ALU.add), reads=("bias",), rmw=(pk,))
                p.op("act", lambda e, pj=pj, j=j: e.activation(
                    out=stg[j][:], in_=pj[:], func=AF.Silu),
                    reads=(pk,), writes=(f"stg{j}",))
                dst = v_o if vg < 4 else g_o
                col = (vg % 4) * 512
                p.dma("qsp", dst[trow:trow + 128, col:col + 512], stg[j][:],
                      reads=(f"stg{j}",), writes=("out",))
    for qg in range(4):
        sl = slice(qg * 512, (qg + 1) * 512)
        for ec in range(NCHUNK):
            p.dma("qsp", xs[ec][:], xT[ec * 128:(ec + 1) * 128, sl], writes=(f"xs{ec}",))
        ln_slice(p, f"c{qg}", xs, "xs", 512, lambda ec: xh1[ec][:], "xh1", sc)
        ln_slice(p, f"d{qg}", xh1, "xh1", 512, lambda ec: xh[ec][:], "xh", sc,
                 affine=(lambda ec: lng_sb[:, ec:ec + 1],
                         lambda ec: lnb_sb[:, ec:ec + 1], "aff"))
        pj = pp[0]
        for ec in range(NCHUNK):
            p.op("pe", lambda e, pj=pj, ec=ec, sl=sl: e.matmul(
                pj[:], wqk_sb[ec][:], xh[ec][:],
                start=(ec == 0), stop=(ec == NCHUNK - 1)),
                reads=(f"xh{ec}", "wqk"), writes=("pp0",))
        p.op("act", lambda e, pj=pj: e.activation(
            out=qkf[:], in_=pj[:], func=AF.Silu, bias=bqk_sb[:]),
            reads=("pp0", "bias"), writes=("qkf",))
        bcast_row(p, f"dn{qg}", dnr[0:1, sl], "_inp", 512, dn_bc[:], "dnbc")
        p.op("dve", lambda e: e.tensor_scalar(
            out=stg[0][:], in0=qkf[:], scalar1=gb_sb[:, 0:1], scalar2=gb_sb[:, 1:2],
            op0=ALU.mult, op1=ALU.add), reads=("qkf", "bias"), writes=("stg0",))
        p.op("dve", lambda e: e.tensor_tensor(
            out=stg[0][:], in0=stg[0][:], in1=dn_bc[:], op=ALU.mult),
            reads=("dnbc",), rmw=("stg0",))
        p.op("pool", lambda e: e.tensor_scalar(
            out=stg[1][:], in0=qkf[:], scalar1=gb_sb[:, 2:3], scalar2=gb_sb[:, 3:4],
            op0=ALU.mult, op1=ALU.add), reads=("qkf", "bias"), writes=("stg1",))
        p.dma("qsp", q_o[:, sl], stg[0][:], reads=("stg0",), writes=("out",))
        p.dma("qsp", k_o[:, sl], stg[1][:], reads=("stg1",), writes=("out",))
    p.stage_end()
    return p.finalize() if standalone else None


def bld_n2(p=None, ext=None):
    standalone = p is None
    if standalone:
        p = Prog()
        ext = dict(q=p.inp("q", [128, TC], BF16), kF=p.inp("kF", [128, S], BF16),
                   vF=p.inp("vF", [S, 2048], BF16), gate=p.inp("gate", [TC, 2048], BF16),
                   Mm=p.inp("Mm", [S, TC], BF16), xT=p.inp("xT", [D, TC]),
                   xh1T=p.inp("xh1T", [D, TC], BF16), Wo=p.inp("Wo", [2048, D]),
                   bo=p.inp("bo", [D, 1]), idb=p.inp("idb", [128, 128], BF16),
                   xoT=p.out("xoT", [D, TC]))
    q, kF, vF, gate = ext["q"], ext["kF"], ext.get("vF"), ext["gate"]
    Mm, xT, xh1T, Wo, bo, idb, xoT = (ext["Mm"], ext["xT"], ext["xh1T"], ext["Wo"],
                                      ext["bo"], ext["idb"], ext["xoT"])
    kvgath = ext.get("kvgath")
    p.stage_begin("n2")
    og_d = p.dram("og_d", [2048, TC], BF16)
    pt_d = p.dram("pt_d", [16, 128, 64 * 128], BF16)
    q_sb = p.sb("q_sb", [128, TC], BF16)
    k_sb = p.sb("k_sb", [128, S], BF16)
    vv = p.sb("vv", [128, 64 * 512], BF16)
    ptb = p.sb("ptb", [128, 64 * 128], BF16)
    mcol = p.sb("mcol", [128, 64 * 128], BF16)
    gt = p.sb("gt", [128, 512], BF16)
    ogs = p.sb("ogs", [128, 512], BF16)
    iden = p.sb("iden", [128, 128], BF16)
    relu_t = p.sb("relu_t", [128, 128], F32)
    xsl = [p.sb(f"xsl{i}", [128, 512], F32) for i in range(NCHUNK)]
    ocol0 = p.sb("ocol0", [128, 512], BF16)
    oco = p.sb("oco", [128, 16 * 512], BF16)
    wcol = [p.sb(f"wcol{j}", [128, 16 * 128], BF16) for j in range(2)]
    bo_sb = p.sb("bo_sb", [128, NCHUNK], F32)
    x1sl = p.sb("x1sl", [128, 512], BF16)
    ps_s = [p.ps(f"ps_s{j}", [128, 128], F32) for j in range(2)]
    ps_pv = [p.ps(f"ps_pv{j}", [128, 512], F32) for j in range(2)]
    ps_tr = [p.ps(f"ps_tr{j}", [128, 128], BF16) for j in range(2)]
    ps_pj = [p.ps("ps_pj0", [128, 512], F32)] * 2
    p.dma("qg", iden[:], idb[:], writes=("iden",))
    p.dma("qg", q_sb[:], q[:], writes=("q",))
    if kvgath is None:
        p.dma("qg", k_sb[:], kF[:], writes=("k",))
    else:
        CH5 = 128 + TC
        for j in range(4):
            p.dma("qg", k_sb[:, j * TC:(j + 1) * TC],
                  kvgath[j * CH5:j * CH5 + 128, :], writes=("k",))
    for i in range(NCHUNK):
        p.dma("qsp", bo_sb[:, i:i + 1], bo[i * 128:(i + 1) * 128, :], writes=("bias",))
    for vq in range(4):
        for h in range(4):
            vsrc = (vF[h * 2048:(h + 1) * 2048, vq * 512:(vq + 1) * 512]
                    if kvgath is None else
                    kvgath[h * (128 + TC) + 128:(h + 1) * (128 + TC),
                           vq * 512:(vq + 1) * 512])
            p.dma("qsp", vv[:].rearrange("p (k w) -> p k w", w=512)[:, h * 16:(h + 1) * 16, :],
                  vsrc.rearrange("(k p) w -> p k w", p=128), writes=("vv",))
        for qt in range(16):
            if vq == 0:
                p.dma("qsp", mcol[:].rearrange("p (k w) -> p k w", w=128),
                      Mm[:, qt * 128:(qt + 1) * 128].rearrange("(k p) w -> p k w", p=128),
                      writes=("mcol",))
                for kt in range(64):
                    j = kt % 2
                    pj = ps_s[j]
                    pk = f"ps_s{j}"
                    p.op("pe", lambda e, pj=pj, kt=kt, qt=qt: e.matmul(
                        pj[:], k_sb[:, kt * 128:(kt + 1) * 128],
                        q_sb[:, qt * 128:(qt + 1) * 128], start=True, stop=True),
                        reads=("k", "q"), writes=(pk,))
                    p.op("dve", lambda e, pj=pj, kt=kt: e.scalar_tensor_tensor(
                        out=relu_t[:], in0=pj[:], scalar=0.0,
                        in1=mcol[:, kt * 128:(kt + 1) * 128],
                        op0=ALU.max, op1=ALU.mult),
                        reads=(pk, "mcol"), writes=("relu_t",))
                    p.op("act", lambda e, kt=kt: e.activation(
                        out=ptb[:, kt * 128:(kt + 1) * 128], in_=relu_t[:],
                        func=AF.Square), reads=("relu_t",), writes=(f"ptb{kt}",))
                p.dma("qsp", pt_d[qt], ptb[:],
                      reads=tuple(f"ptb{kt}" for kt in range(64)), writes=("pt_d",))
                if qt == 3:
                    p.dbg("ptb3", ptb[:], tuple(f"ptb{kt}" for kt in range(64)))
            else:
                p.dma("qsp", ptb[:], pt_d[qt],
                      reads=("pt_d",), writes=tuple(f"ptb{kt}" for kt in range(64)))
            pj = ps_pv[qt % 2]
            pk = f"ps_pv{qt % 2}"
            for kt in range(64):
                p.op("pe", lambda e, pj=pj, kt=kt: e.matmul(
                    pj[:], ptb[:, kt * 128:(kt + 1) * 128],
                    vv[:, kt * 512:(kt + 1) * 512],
                    start=(kt == 0), stop=(kt == 63)),
                    reads=(f"ptb{kt}", "vv"), writes=(pk,))
            p.dma("qg", gt[:], gate[qt * 128:(qt + 1) * 128, vq * 512:(vq + 1) * 512],
                  writes=("gt",))
            p.op("dve", lambda e, pj=pj: e.tensor_tensor(
                out=ogs[:], in0=pj[:], in1=gt[:], op=ALU.mult),
                reads=(pk, "gt"), writes=("ogs",))
            for j4 in range(4):
                tj = ps_tr[j4 % 2]
                p.op("pe", lambda e, tj=tj, j4=j4: e.transpose(
                    tj[:], ogs[:, j4 * 128:(j4 + 1) * 128], iden[:]),
                    reads=("ogs", "iden"), writes=(f"ps_tr{j4 % 2}",))
                p.op("dve", lambda e, tj=tj, j4=j4: e.tensor_copy(
                    ocol0[:, j4 * 128:(j4 + 1) * 128], tj[:]),
                    reads=(f"ps_tr{j4 % 2}",), writes=("oc0",))
            p.dma("qsp", og_d[vq * 512:(vq + 1) * 512, qt * 128:(qt + 1) * 128]
                  .rearrange("(a p) w -> p a w", p=128),
                  ocol0[:].rearrange("p (a w) -> p a w", w=128),
                  reads=("oc0",), writes=("og_d",))
    p.dbg("ogd", og_d[:, 0:512], ("og_d",))
    for oc in range(NCHUNK):
        wc = wcol[oc % 2]
        wck = f"wcol{oc % 2}"
        p.dma("qg", wc[:].rearrange("p (c w) -> p c w", w=128),
              Wo[:, oc * 128:(oc + 1) * 128].rearrange("(c p) w -> p c w", p=128),
              writes=(wck,))
        for qg in range(4):
            sl = slice(qg * 512, (qg + 1) * 512)
            p.dma("qsp", oco[:].rearrange("p (c w) -> p c w", w=512),
                  og_d[:, sl].rearrange("(c p) w -> p c w", p=128),
                  reads=("og_d",), writes=("oc1",))
            p.dma("qsp", xsl[oc][:], xT[oc * 128:(oc + 1) * 128, sl],
                  writes=(f"xsl{oc}",))
            p.dma("qg", x1sl[:], xh1T[oc * 128:(oc + 1) * 128, sl], writes=("x1sl",))
            p.op("pool", lambda e, oc=oc: e.tensor_tensor(
                out=xsl[oc][:], in0=xsl[oc][:], in1=x1sl[:], op=ALU.add),
                reads=("x1sl",), rmw=(f"xsl{oc}",))
            pj = ps_pj[0]
            pk = "ps_pj0"
            for vc in range(16):
                p.op("pe", lambda e, pj=pj, wc=wc, vc=vc: e.matmul(
                    pj[:], wc[:, vc * 128:(vc + 1) * 128],
                    oco[:, vc * 512:(vc + 1) * 512],
                    start=(vc == 0), stop=(vc == 15)),
                    reads=("oc1", wck), writes=(pk,))
            p.op("dve", lambda e, pj=pj, oc=oc: e.scalar_tensor_tensor(
                out=xsl[oc][:], in0=pj[:], scalar=bo_sb[:, oc:oc + 1],
                in1=xsl[oc][:], op0=ALU.add, op1=ALU.add),
                reads=(pk, "bias"), rmw=(f"xsl{oc}",))
            p.dma("qsp", xoT[oc * 128:(oc + 1) * 128, sl], xsl[oc][:],
                  reads=(f"xsl{oc}",), writes=("xout",))
    p.stage_end()
    return p.finalize() if standalone else None


def host_nblock(x, ln_g, ln_b, W_h, b_h, W_qk, b_qk, osg, osb, W_out, b_out):
    nc1 = bld_n1()
    gb = np.stack([osg[0], osb[0], osg[1], osb[1]], axis=1).astype(np.float32)
    gb = np.ascontiguousarray(gb)
    in1 = []
    for c, b, o in _cores():
        dnr = (1.0 / (np.arange(o, o + TC, dtype=np.float64) + 1.0)).astype(np.float32)
        in1.append(dict(xT=np.ascontiguousarray(x[b, o:o + TC].T), Wh=W_h,
                        bhr=np.ascontiguousarray(b_h.reshape(1, 4096)), Wqk=W_qk,
                        bqk=np.ascontiguousarray(b_qk.reshape(128, 1)),
                        lng=np.ascontiguousarray(ln_g.reshape(D, 1)),
                        lnb=np.ascontiguousarray(ln_b.reshape(D, 1)),
                        gb=gb, dnr=dnr.reshape(1, TC)))
    r1 = run_bass_kernel_spmd(nc1, in1, core_ids=list(range(NCORE))).results
    vF = {b: np.concatenate([r1[c]["v_o"] for c, bb, o in _cores() if bb == b], axis=0)
          for b in range(B)}
    kB = {b: np.concatenate([r1[c]["k_o"] for c, bb, o in _cores() if bb == b], axis=1)
          for b in range(B)}
    nc2 = bld_n2()
    in2 = []
    for c, b, o in _cores():
        kidx = np.arange(S)[:, None]
        qidx = np.arange(o, o + TC)[None, :]
        Mm = (kidx <= qidx).astype(BF)
        in2.append(dict(q=r1[c]["q_o"], kF=np.ascontiguousarray(kB[b]),
                        vF=np.ascontiguousarray(vF[b]), gate=r1[c]["g_o"], Mm=Mm,
                        xT=np.ascontiguousarray(x[b, o:o + TC].T),
                        xh1T=r1[c]["xh1_o"], Wo=W_out,
                        bo=np.ascontiguousarray(b_out.reshape(D, 1)), idb=_EYE_BF))
    r2 = run_bass_kernel_spmd(nc2, in2, core_ids=list(range(NCORE))).results
    out = np.empty((B, S, D), np.float32)
    for c, b, o in _cores():
        out[b, o:o + TC] = r2[c]["xoT"].T
    return out


def bld_m1(p=None, ext=None):
    standalone = p is None
    TE = 2560
    if standalone:
        p = Prog()
        ext = dict(xTe=p.inp("xTe", [D, TE]), nw=p.inp("nw", [D, 1]),
                   ipT=p.inp("ipT", [D, 4096]), cw=p.inp("cw", [128, 64]),
                   cb=p.inp("cb", [128, 16]), xpT=p.inp("xpT", [2048, 96]),
                   dpT=p.inp("dpT", [64, 2048]), dtb=p.inp("dtb", [128, 16]),
                   uc_o=p.out("uc_o", [2048, TC], BF16),
                   dt_o=p.out("dt_o", [2048, TC], BF16), bc_o=p.out("bc_o", [32, TC]),
                   z_o=p.out("z_o", [2048, TC], BF16),
                   xh_o=p.out("xh_o", [D, TC], BF16), sc=base_scratch(p))
    nw, ipT, cw, cb, xpT, dpT, dtb = (ext["nw"], ext["ipT"], ext["cw"], ext["cb"],
                                      ext["xpT"], ext["dpT"], ext["dtb"])
    uc_o, dt_o, bc_o, z_o, xh_o, sc = (ext["uc_o"], ext["dt_o"], ext["bc_o"],
                                       ext["z_o"], ext["xh_o"], ext["sc"])
    fused = "xcur" in ext
    if not fused:
        xTe = ext["xTe"]
    else:
        xcur, halog, hoffi, hmaski = (ext["xcur"], ext["halog"], ext["hoff"],
                                      ext["hmask"])
    p.stage_begin("m1")
    sc = stage_sc(p, sc)
    if fused:
        hoff_sb = p.sb("hoff_sb", [1, 2], mybir.dt.int32)
        hmask_sb = p.sb("hmask_sb", [128, 1], F32)
        hal_all = p.sb("hal_all", [128, NCHUNK * 3], F32)
        p.dma("qsp", hoff_sb[:], hoffi[:], writes=("offs",))
        p.dma("qsp", hmask_sb[:], hmaski[:], writes=("par",))
    epsm = p.sb("epsm", [128, 1], F32)
    c1 = p.sb("c1", [128, 1], F32)
    p.op("dve", lambda e: e.memset(epsm[:], 1.1920928955078125e-07), writes=("consts",))
    p.op("dve", lambda e: e.memset(c1[:], 1.0), writes=("consts",))
    xs = [p.sb(f"xs{i}", [128, 512], F32) for i in range(NCHUNK)]
    xh = [p.sb(f"xh{i}", [128, 512], BF16) for i in range(NCHUNK)]
    hh = [p.sb(f"hh{i}", [128, 512], BF16) for i in range(NCHUNK)]
    nw_sb = p.sb("nw_sb", [128, NCHUNK], F32)
    cw_sb = p.sb("cw_sb", [128, 64], F32)
    cb_sb = p.sb("cb_sb", [128, 16], F32)
    dtb_sb = p.sb("dtb_sb", [128, 16], F32)
    uext = [p.sb(f"ue{i}", [128, TE], BF16) for i in range(16)]
    ipcol = [p.sb(f"ipc{j}", [128, NCHUNK * 128], BF16) for j in range(2)]
    xp_sb = [p.sb(f"xp{i}", [128, 96], BF16) for i in range(16)]
    dp_sb = p.sb("dp_sb", [64, 2048], BF16)
    xdb = p.sb("xdb", [96, 2048], F32)
    dti = p.sb("dti", [64, 2048], BF16)
    cvb = [p.sb(f"cvb{j}", [128, 2048], F32) for j in range(2)]
    ucb = p.sb("ucb", [128, 2048], BF16)
    spb = [p.sb(f"spb{j}", [128, 512], F32) for j in range(3)]
    stg = [p.sb(f"stg{j}", [128, 512], BF16) for j in range(2)]
    pp = [p.ps(f"pp{j}", [128, 512], F32) for j in range(2)]
    ps_xd = p.ps("ps_xd", [96, 2048], F32)
    for i in range(NCHUNK):
        p.dma("qsp", nw_sb[:, i:i + 1], nw[i * 128:(i + 1) * 128, :], writes=("par",))
    p.dma("qsp", cw_sb[:], cw[:], writes=("par",))
    p.dma("qsp", cb_sb[:], cb[:], writes=("par",))
    p.dma("qsp", dtb_sb[:], dtb[:], writes=("par",))
    for i in range(16):
        p.dma("qg", xp_sb[i][:], xpT[i * 128:(i + 1) * 128, :], writes=("xp",))
    p.dma("qg", dp_sb[:], dpT[:], writes=("dp",))

    for si in range(5):
        sl = slice(si * 512, (si + 1) * 512)
        for ec in range(NCHUNK):
            if not fused:
                p.dma("qsp", xs[ec][:], xTe[ec * 128:(ec + 1) * 128, sl],
                      writes=(f"xs{ec}",))
            elif si == 0:
                p.op("pool", lambda e, ec=ec: e.memset(xs[ec][:], 0.0),
                     writes=(f"xs{ec}",))
                if ec == 0:
                    def _emh(e):
                        with e.register() as r0:
                            e.reg_load(r0, hoff_sb[0:1, 0:1])
                            return e.dma_start(
                                out=hal_all[:].rearrange("p (ec c) -> p ec c", c=3),
                                in_=bass.AP(halog.tensor, r0,
                                            [[3, 128], [128 * 3, NCHUNK], [1, 3]]))
                    p.op("qsp", _emh, reads=("offs",), writes=("hal_all",))
                p.op("dve", lambda e, ec=ec: e.tensor_scalar(
                    out=xs[ec][:, 509:512], in0=hal_all[:, ec * 3:(ec + 1) * 3],
                    scalar1=hmask_sb[:, 0:1], scalar2=None, op0=ALU.mult),
                    reads=("par", "hal_all"), rmw=(f"xs{ec}",))
            else:
                p.dma("qsp", xs[ec][:], xcur[ec * 128:(ec + 1) * 128,
                                             (si - 1) * 512: si * 512],
                      writes=(f"xs{ec}",))
        ln_slice(p, f"m{si}", xs, "xs", 512, lambda ec: xh[ec][:], "xh", sc)
        if si > 0:
            csl = slice((si - 1) * 512, si * 512)
            for ec in range(NCHUNK):
                p.op("dve", lambda e, ec=ec: e.tensor_copy(stg[0][:], xh[ec][:]),
                     reads=(f"xh{ec}",), writes=("stg0",))
                p.dma("qsp", xh_o[ec * 128:(ec + 1) * 128, csl], stg[0][:],
                      reads=("stg0",), writes=("out",))
        rows = sc["lnrows"]
        rr = rows[:, 1024:1536]
        for ec in range(NCHUNK):
            p.op("act", lambda e, ec=ec: e.activation(
                out=sc["sq"][:], in_=xh[ec][:], func=AF.Square),
                reads=(f"xh{ec}",), writes=("sq",))
            p.op("pe", lambda e, ec=ec: e.matmul(
                sc["ps_stat"][:, 0:512], sc["ones_f32"][:], sc["sq"][:],
                start=(ec == 0), stop=(ec == NCHUNK - 1)),
                reads=("sq", "ones"), writes=("ps_stat",))
        p.op("dve", lambda e: e.tensor_scalar(
            out=rr[0:1, :], in0=sc["ps_stat"][:, 0:512], scalar1=1.0 / D,
            scalar2=None, op0=ALU.mult), reads=("ps_stat",), writes=("lnrows",))
        p.op("act", lambda e: e.activation(out=rr[0:1, :], in_=rr[0:1, :],
             func=AF.Sqrt, bias=epsm[0:1, :]), reads=("consts",), rmw=("lnrows",))
        p.op("dve", lambda e: e.reciprocal(rr[0:1, :], rr[0:1, :]), rmw=("lnrows",))
        bcast_row(p, f"rm{si}", rr[0:1, :], "lnrows", 512, sc["rb"][:], "rb")
        for ec in range(NCHUNK):
            p.op("dve", lambda e, ec=ec: e.scalar_tensor_tensor(
                out=hh[ec][:], in0=xh[ec][:], scalar=nw_sb[:, ec:ec + 1],
                in1=sc["rb"][:], op0=ALU.mult, op1=ALU.mult),
                reads=(f"xh{ec}", "par", "rb"), writes=(f"hh{ec}",))
        for uc in range(32):
            wc = ipcol[uc % 2]
            wck = f"ipc{uc % 2}"
            p.dma("qg", wc[:].rearrange("p (c w) -> p c w", w=128),
                  ipT[:, uc * 128:(uc + 1) * 128].rearrange("(c p) w -> p c w", p=128),
                  writes=(wck,))
            if si == 0 and uc >= 16:
                continue
            pj = pp[uc % 2]
            pk = f"pp{uc % 2}"
            for ec in range(NCHUNK):
                p.op("pe", lambda e, pj=pj, wc=wc, ec=ec: e.matmul(
                    pj[:], wc[:, ec * 128:(ec + 1) * 128], hh[ec][:],
                    start=(ec == 0), stop=(ec == NCHUNK - 1)),
                    reads=(f"hh{ec}", wck), writes=(pk,))
            if uc < 16:
                p.op("dve", lambda e, pj=pj, uc=uc, sl=sl: e.tensor_copy(
                    uext[uc][:, sl], pj[:]), reads=(pk,), writes=(f"ue{uc}",))
            else:
                p.op("dve", lambda e, pj=pj, uc=uc: e.tensor_copy(
                    stg[uc % 2][:], pj[:]), reads=(pk,), writes=(f"stg{uc % 2}",))
                p.dma("qsp", z_o[(uc - 16) * 128:(uc - 15) * 128,
                               (si - 1) * 512: si * 512], stg[uc % 2][:],
                      reads=(f"stg{uc % 2}",), writes=("out",))
    for dc in range(16):
        p.op("dve", lambda e, dc=dc: e.tensor_scalar(
            out=cvb[0][:], in0=uext[dc][:, 509:509 + 2048],
            scalar1=cw_sb[:, dc * 4:dc * 4 + 1], scalar2=cb_sb[:, dc:dc + 1],
            op0=ALU.mult, op1=ALU.add),
            reads=(f"ue{dc}", "par"), writes=("cvb0",))
        for k in range(1, 4):
            src_b, dst_b = cvb[(k + 1) % 2], cvb[k % 2]
            p.op("dve", lambda e, dc=dc, k=k, src_b=src_b, dst_b=dst_b:
                 e.scalar_tensor_tensor(
                     out=dst_b[:], in0=uext[dc][:, 509 + k:509 + k + 2048],
                     scalar=cw_sb[:, dc * 4 + k:dc * 4 + k + 1], in1=src_b[:],
                     op0=ALU.mult, op1=ALU.add),
                 reads=(f"ue{dc}", "par", f"cvb{(k + 1) % 2}"), writes=(f"cvb{k % 2}",))
        p.op("act", lambda e: e.activation(out=ucb[:], in_=cvb[1][:], func=AF.Silu),
             reads=("cvb1",), writes=("ucb",))
        p.dma("qsp", uc_o[dc * 128:(dc + 1) * 128, :], ucb[:],
              reads=("ucb",), writes=("out",))
        for qg in range(4):
            p.op("pe", lambda e, dc=dc, qg=qg: e.matmul(
                ps_xd[:, qg * 512:(qg + 1) * 512], xp_sb[dc][:],
                ucb[:, qg * 512:(qg + 1) * 512],
                start=(dc == 0), stop=(dc == 15)),
                reads=("ucb", "xp"), writes=("ps_xd",))
    for qg in range(4):
        p.op("dve", lambda e, qg=qg: e.tensor_copy(
            xdb[:, qg * 512:(qg + 1) * 512], ps_xd[:, qg * 512:(qg + 1) * 512]),
            reads=("ps_xd",), writes=("xdb",))
    p.dma("qsp", bc_o[:], xdb[64:96, :], reads=("xdb",), writes=("out",))
    p.op("dve", lambda e: e.tensor_copy(dti[:], xdb[0:64, :]),
         reads=("xdb",), writes=("dti",))
    for dc in range(16):
        for qg in range(4):
            pj = pp[qg % 2]
            pk = f"pp{qg % 2}"
            p.op("pe", lambda e, pj=pj, dc=dc, qg=qg: e.matmul(
                pj[:], dp_sb[:, dc * 128:(dc + 1) * 128],
                dti[0:64, qg * 512:(qg + 1) * 512], start=True, stop=True),
                reads=("dti", "dp"), writes=(pk,))
            a, bq, cq = spb[0], spb[1], spb[2]
            p.op("dve", lambda e, pj=pj, dc=dc: e.tensor_scalar(
                out=a[:], in0=pj[:], scalar1=dtb_sb[:, dc:dc + 1], scalar2=None,
                op0=ALU.add), reads=(pk, "par"), writes=("spb0",))
            p.op("dve", lambda e: e.tensor_scalar(
                out=bq[:], in0=a[:], scalar1=-1.0, scalar2=None, op0=ALU.mult),
                reads=("spb0",), writes=("spb1",))
            p.op("dve", lambda e: e.tensor_tensor(
                out=bq[:], in0=a[:], in1=bq[:], op=ALU.max),
                reads=("spb0",), rmw=("spb1",))
            p.op("act", lambda e: e.activation(out=bq[:], in_=bq[:], func=AF.Exp,
                 scale=-1.0), rmw=("spb1",))
            p.op("act", lambda e: e.activation(out=bq[:], in_=bq[:], func=AF.Ln,
                 bias=c1[:, 0:1]), reads=("consts",), rmw=("spb1",))
            p.op("dve", lambda e: e.tensor_scalar(
                out=cq[:], in0=a[:], scalar1=0.0, scalar2=None, op0=ALU.max),
                reads=("spb0",), writes=("spb2",))
            p.op("pool", lambda e, qg=qg: e.tensor_tensor(
                out=stg[qg % 2][:], in0=bq[:], in1=cq[:], op=ALU.add),
                reads=("spb1", "spb2"), writes=(f"stg{qg % 2}",))
            p.dma("qsp", dt_o[dc * 128:(dc + 1) * 128, qg * 512:(qg + 1) * 512],
                  stg[qg % 2][:], reads=(f"stg{qg % 2}",), writes=("out",))
    p.stage_end()
    return p.finalize() if standalone else None


def bld_m2(p=None, ext=None):
    standalone = p is None
    if standalone:
        p = Prog()
        ext = dict(dts=p.inp("dts", [512, S], BF16), ucs=p.inp("ucs", [512, S], BF16),
                   bcf=p.inp("bcf", [32, S]), alog_i=p.inp("alog_i", [64, 128]),
                   dpar=p.inp("dpar", [128, 4]),
                   selc_i=p.inp("selc_i", [128, 2048], BF16),
                   sely_i=p.inp("sely_i", [128, 2048], BF16),
                   y_o=p.out("y_o", [512, S], BF16))
    alog_i, dpar, selc_i, sely_i, y_o = (ext["alog_i"], ext["dpar"], ext["selc_i"],
                                         ext["sely_i"], ext["y_o"])
    fused = "ucdtg" in ext
    if not fused:
        dts, ucs, bcf = ext["dts"], ext["ucs"], ext["bcf"]
    else:
        ucdtg, bcg, dsoffi = ext["ucdtg"], ext["bcg"], ext["dsoff"]
    p.stage_begin("m2")
    if fused:
        dsoff_sb = p.sb("dsoff_sb", [1, 2], mybir.dt.int32)
        p.dma("qsp", dsoff_sb[:], dsoffi[:], writes=("offs",))
    FS = 1024
    NFS = S // FS
    dt_dc = p.sb("dt_dc", [128, S], BF16)
    uc_dc = p.sb("uc_dc", [128, S], BF16)
    dtu = p.sb("dtu", [128, S], BF16)
    brep = p.sb("brep", [128, S], F32)
    crep = p.sb("crep", [128, S], F32)
    alog_sb = p.sb("alog_sb", [64, 128], F32)
    selc = p.sb("selc", [128, 2048], BF16)
    sely = p.sb("sely", [128, 2048], BF16)
    selp = [p.sb(f"selp{g}", [128, 128], BF16) for g in range(16)]
    abc = p.sb("abc", [128, 128], F32)
    dA = [p.sb(f"dA{j}", [128, FS], F32) for j in range(2)]
    dBu = [p.sb(f"dBu{j}", [128, FS], F32) for j in range(2)]
    hbf = [p.sb(f"hbf{j}", [128, FS], F32) for j in range(2)]
    hc = [p.sb(f"hc{j}", [128, FS], BF16) for j in range(2)]
    ysb = [p.sb(f"ysb{j}", [128, FS], BF16) for j in range(2)]
    carr = p.sb("carr", [128, 16], F32)
    dp_sb = p.sb("dp_sb", [128, 4], F32)
    ps_a = p.ps("ps_a", [128, FS], F32)
    ps_b = p.ps("ps_b", [128, FS], F32)
    ps_a2, ps_b2 = ps_a, ps_b
    ps_y = p.ps("ps_y", [128, FS], F32)
    p.dma("qsp", selc[:], selc_i[:], writes=("selc",))
    p.dma("qsp", dp_sb[:], dpar[:], writes=("dp",))
    p.dma("qsp", alog_sb[:], alog_i[:], writes=("alog",))
    p.op("act", lambda e: e.activation(out=alog_sb[:], in_=alog_sb[:], func=AF.Exp),
         rmw=("alog",))
    p.op("dve", lambda e: e.tensor_scalar(
        out=alog_sb[:], in0=alog_sb[:], scalar1=-1.0, scalar2=None, op0=ALU.mult),
        rmw=("alog",))
    p.dma("qsp", sely[:], sely_i[:], writes=("sely",))
    for g in range(8):
        if not fused:
            p.dma("qsp", brep[g * 16:(g + 1) * 16, :], bcf[0:16, :], writes=("brep",))
            p.dma("qsp", crep[g * 16:(g + 1) * 16, :], bcf[16:32, :], writes=("crep",))
        else:
            for j in range(4):
                p.dma("qsp", brep[g * 16:(g + 1) * 16, j * TC:(j + 1) * TC],
                      bcg[j * 32:j * 32 + 16, :], writes=("brep",))
                p.dma("qsp", crep[g * 16:(g + 1) * 16, j * TC:(j + 1) * TC],
                      bcg[j * 32 + 16:j * 32 + 32, :], writes=("crep",))
    for dc in range(4):
        if not fused:
            p.dma("qg", dt_dc[:], dts[dc * 128:(dc + 1) * 128, :], writes=("dtdc",))
            p.dma("qg", uc_dc[:], ucs[dc * 128:(dc + 1) * 128, :], writes=("ucdc",))
        else:
            for (dstt, bofs, dk) in ((uc_dc, 0, "ucdc"), (dt_dc, 2048 * TC, "dtdc")):
                def _emd(e, dstt=dstt, bofs=bofs, dc=dc):
                    with e.register() as r0:
                        e.reg_load(r0, dsoff_sb[0:1, 0:1])
                        e.reg_add(r0, r0, bofs + dc * 128 * TC)
                        return e.dma_start(
                            out=dstt[:].rearrange("p (j c) -> p j c", c=TC),
                            in_=bass.AP(ucdtg.tensor, r0,
                                        [[TC, 128], [4096 * TC, 4], [1, TC]]))
                p.op("qsp", _emd, reads=("offs",), writes=(dk,))
        for h in range(4):
            p.op("dve", lambda e, h=h: e.tensor_tensor(
                out=dtu[:, h * 2048:(h + 1) * 2048], in0=dt_dc[:, h * 2048:(h + 1) * 2048],
                in1=uc_dc[:, h * 2048:(h + 1) * 2048], op=ALU.mult),
                reads=("dtdc", "ucdc"), writes=(f"dtu{h}",))
        for g in range(16):
            bcast_row(p, f"a{dc}_{g}", alog_sb[dc * 16 + g:dc * 16 + g + 1, :],
                      "alog", 128, abc[:], "abc")
            p.op("dve", lambda e, g=g: e.tensor_tensor(
                out=selp[g][:], in0=selc[:, g * 128:(g + 1) * 128], in1=abc[:],
                op=ALU.mult), reads=("selc", "abc"), writes=(f"selp{g}",))
        p.op("dve", lambda e: e.memset(carr[:], 0.0),
             writes=tuple(f"carr{g}" for g in range(16)))
        for fs in range(NFS):
            fsl = slice(fs * FS, (fs + 1) * FS)
            for g in range(16):
                w = g % 2
                dAw, dBw, hbw, hcw = dA[w], dBu[w], hbf[w], hc[w]
                psaw, psbw = ps_a, ps_b
                pka, pkb = ("ps_a", "ps_b")
                for j in range(FS // 512):
                    p.op("pe", lambda e, g=g, j=j, fsl=fsl, psaw=psaw: e.matmul(
                        psaw[:, j * 512:(j + 1) * 512], selp[g][:],
                        dt_dc[:, fsl.start + j * 512: fsl.start + (j + 1) * 512],
                        start=True, stop=True),
                        reads=(f"selp{g}", "dtdc"), writes=(pka,))
                    p.op("pe", lambda e, g=g, j=j, fsl=fsl, psbw=psbw: e.matmul(
                        psbw[:, j * 512:(j + 1) * 512], selc[:, g * 128:(g + 1) * 128],
                        dtu[:, fsl.start + j * 512: fsl.start + (j + 1) * 512],
                        start=True, stop=True),
                        reads=("selc", f"dtu{fsl.start // 2048}"), writes=(pkb,))
                p.op("act", lambda e, dAw=dAw, psaw=psaw: e.activation(
                    out=dAw[:], in_=psaw[:], func=AF.Exp),
                    reads=(pka,), writes=(f"dA{w}",))
                p.op("dve", lambda e, fsl=fsl, dBw=dBw, psbw=psbw: e.tensor_tensor(
                    out=dBw[:], in0=psbw[:], in1=brep[:, fsl], op=ALU.mult),
                    reads=(pkb, "brep"), writes=(f"dBu{w}",))
                p.op("dve", lambda e, g=g, dAw=dAw, dBw=dBw, hbw=hbw: e.tensor_tensor_scan(
                    out=hbw[:], data0=dAw[:], data1=dBw[:], initial=carr[:, g:g + 1],
                    op0=ALU.mult, op1=ALU.add),
                    reads=(f"dA{w}", f"dBu{w}", f"carr{g}"), writes=(f"hbf{w}",))
                p.op("pool", lambda e, g=g, hbw=hbw: e.tensor_copy(
                    carr[:, g:g + 1], hbw[:, FS - 1:FS]),
                    reads=(f"hbf{w}",), writes=(f"carr{g}",))
                p.op("pool", lambda e, fsl=fsl, hbw=hbw, hcw=hcw: e.tensor_tensor(
                    out=hcw[:], in0=hbw[:], in1=crep[:, fsl], op=ALU.mult),
                    reads=(f"hbf{w}", "crep"), writes=(f"hc{w}",))
                for j in range(FS // 512):
                    p.op("pe", lambda e, g=g, j=j, hcw=hcw: e.matmul(
                        ps_y[:, j * 512:(j + 1) * 512], sely[:, g * 128:(g + 1) * 128],
                        hcw[:, j * 512:(j + 1) * 512],
                        start=(g == 0), stop=(g == 15)),
                        reads=("sely", f"hc{w}"), writes=("ps_y",))
            yw = ysb[fs % 2]
            for j in range(FS // 512):
                p.op("dve", lambda e, j=j, fsl=fsl, dc=dc, yw=yw: e.scalar_tensor_tensor(
                    out=yw[:, j * 512:(j + 1) * 512],
                    in0=uc_dc[:, fsl.start + j * 512: fsl.start + (j + 1) * 512],
                    scalar=dp_sb[:, dc:dc + 1],
                    in1=ps_y[:, j * 512:(j + 1) * 512], op0=ALU.mult, op1=ALU.add),
                    reads=("ucdc", "dp", "ps_y"), writes=(f"ysb{fs % 2}",))
            p.dma("qsp", y_o[dc * 128:(dc + 1) * 128, fsl], yw[:],
                  reads=(f"ysb{fs % 2}",), writes=("yout",))
    p.stage_end()
    return p.finalize() if standalone else None


def bld_m3(p=None, ext=None):
    standalone = p is None
    if standalone:
        p = Prog()
        ext = dict(yT=p.inp("yT", [2048, TC], BF16), zT=p.inp("zT", [2048, TC], BF16),
                   xhT=p.inp("xhT", [D, TC], BF16), xT=p.inp("xT", [D, TC]),
                   opT=p.inp("opT", [2048, D]), xoT=p.out("xoT", [D, TC]))
    yT, zT, xhT, xT, opT, xoT = (ext["yT"], ext["zT"], ext["xhT"], ext["xT"],
                                 ext["opT"], ext["xoT"])
    ycol_dyn = ext.get("ycol_dyn")
    p.stage_begin("m3")
    op_sb = [p.sb(f"op{i}", [128, D], BF16) for i in range(16)]
    yall = p.sb("yall", [128, 16 * 512], BF16)
    ysl = [yall[:, i * 512:(i + 1) * 512] for i in range(16)]
    gy = [p.sb(f"gy{i}", [128, 512], BF16) for i in range(16)]
    zsl = p.sb("zsl", [128, 512], BF16)
    zss = p.sb("zss", [128, 512], BF16)
    xsl = p.sb("xsl", [128, 512], F32)
    x1l = p.sb("x1l", [128, 512], BF16)
    ost = p.sb("ost", [128, 512], F32)
    pp = [p.ps(f"pp{j}", [128, 512], F32) for j in range(2)]
    for i in range(16):
        p.dma("qg", op_sb[i][:], opT[i * 128:(i + 1) * 128, :], writes=("op",))
    for qg in range(4):
        sl = slice(qg * 512, (qg + 1) * 512)
        if ycol_dyn is None:
            for dc in range(16):
                p.dma("qg", ysl[dc], yT[dc * 128:(dc + 1) * 128, sl],
                      writes=(f"ysl{dc}",))
        else:
            def _emity(e, sl=sl):
                with e.register() as r0:
                    e.reg_load(r0, ycol_dyn)
                    e.reg_add(r0, r0, sl.start)
                    return e.dma_start(
                        out=yall[:].rearrange("p (dc c) -> p dc c", c=512),
                        in_=bass.AP(yT.tensor, r0, [[S, 128], [128 * S, 16], [1, 512]]))
            p.op("qsp", _emity, reads=("offs",),
                 writes=tuple(f"ysl{dc}" for dc in range(16)))
        for dc in range(16):
            p.dma("qg", zsl[:], zT[dc * 128:(dc + 1) * 128, sl], writes=("zsl",))
            p.op("act", lambda e: e.activation(out=zss[:], in_=zsl[:], func=AF.Silu),
                 reads=("zsl",), writes=("zss",))
            p.op("dve", lambda e, dc=dc: e.tensor_tensor(
                out=gy[dc][:], in0=ysl[dc], in1=zss[:], op=ALU.mult),
                reads=(f"ysl{dc}", "zss"), writes=(f"gy{dc}",))
        for oc in range(NCHUNK):
            pj = pp[oc % 2]
            pk = f"pp{oc % 2}"
            for dc in range(16):
                p.op("pe", lambda e, pj=pj, dc=dc, oc=oc: e.matmul(
                    pj[:], op_sb[dc][:, oc * 128:(oc + 1) * 128], gy[dc][:],
                    start=(dc == 0), stop=(dc == 15)),
                    reads=(f"gy{dc}", "op"), writes=(pk,))
            p.dma("qsp", xsl[:], xT[oc * 128:(oc + 1) * 128, sl], writes=("xsl",))
            p.dma("qg", x1l[:], xhT[oc * 128:(oc + 1) * 128, sl], writes=("x1l",))
            p.op("dve", lambda e, pj=pj: e.tensor_tensor(
                out=ost[:], in0=pj[:], in1=xsl[:], op=ALU.add),
                reads=(pk, "xsl"), writes=("ost",))
            p.op("pool", lambda e: e.tensor_tensor(
                out=ost[:], in0=ost[:], in1=x1l[:], op=ALU.add),
                reads=("x1l",), rmw=("ost",))
            p.dma("qsp", xoT[oc * 128:(oc + 1) * 128, sl], ost[:],
                  reads=("ost",), writes=("xout",))
    p.stage_end()
    return p.finalize() if standalone else None


def host_mamba(x, norm_w, in_proj_w, conv_w, conv_b, x_proj_w, dt_proj_w, dt_proj_b,
               A_log, Dp, out_proj_w):
    nc1 = bld_m1()
    cwm = np.ascontiguousarray(conv_w.reshape(16, 128, 4).transpose(1, 0, 2).reshape(128, 64))
    cbm = np.ascontiguousarray(conv_b.reshape(16, 128).T)
    dtbm = np.ascontiguousarray(dt_proj_b.reshape(16, 128).T)
    ipT = np.ascontiguousarray(in_proj_w.T)
    xpT = np.ascontiguousarray(x_proj_w.T)
    dpT = np.ascontiguousarray(dt_proj_w.T)
    in1 = []
    for c, b, o in _cores():
        xTe = np.zeros((D, 2560), np.float32)
        xTe[:, 512:] = x[b, o:o + TC].T
        if o > 0:
            xTe[:, 509:512] = x[b, o - 3:o].T
        in1.append(dict(xTe=xTe, nw=np.ascontiguousarray(norm_w.reshape(D, 1)),
                        ipT=ipT, cw=cwm, cb=cbm, xpT=xpT, dpT=dpT, dtb=dtbm))
    r1 = run_bass_kernel_spmd(nc1, in1, core_ids=list(range(NCORE))).results
    ucf = {b: np.concatenate([r1[c]["uc_o"] for c, bb, o in _cores() if bb == b], axis=1)
           for b in range(B)}
    dtf = {b: np.concatenate([r1[c]["dt_o"] for c, bb, o in _cores() if bb == b], axis=1)
           for b in range(B)}
    bcf = {b: np.concatenate([r1[c]["bc_o"] for c, bb, o in _cores() if bb == b], axis=1)
           for b in range(B)}
    selc = np.zeros((128, 16, 128), np.float32)
    for g in range(16):
        for pp_ in range(128):
            selc[g * 8 + pp_ // 16, g, pp_] = 1.0
    selc = selc.reshape(128, 2048).astype(BF)
    sely = np.zeros((128, 16, 128), np.float32)
    for g in range(16):
        for pp_ in range(128):
            sely[pp_, g, g * 8 + pp_ // 16] = 1.0
    sely = sely.reshape(128, 2048).astype(BF)
    nc2 = bld_m2()
    in2 = []
    for c in range(NCORE):
        b, ds = c // 4, (c % 4) * 512
        alog = np.ascontiguousarray(A_log[ds:ds + 512].reshape(64, 128))
        dpar = np.ascontiguousarray(Dp[ds:ds + 512].reshape(4, 128).T)
        in2.append(dict(dts=np.ascontiguousarray(dtf[b][ds:ds + 512]),
                        ucs=np.ascontiguousarray(ucf[b][ds:ds + 512]),
                        bcf=np.ascontiguousarray(bcf[b]), alog_i=alog, dpar=dpar,
                        selc_i=np.ascontiguousarray(selc),
                        sely_i=np.ascontiguousarray(sely)))
    r2 = run_bass_kernel_spmd(nc2, in2, core_ids=list(range(NCORE))).results
    yf = {b: np.concatenate([r2[b * 4 + j]["y_o"] for j in range(4)], axis=0)
          for b in range(B)}
    nc3 = bld_m3()
    opT = np.ascontiguousarray(out_proj_w.T)
    in3 = []
    for c, b, o in _cores():
        in3.append(dict(yT=np.ascontiguousarray(yf[b][:, o:o + TC]), zT=r1[c]["z_o"],
                        xhT=r1[c]["xh_o"], xT=np.ascontiguousarray(x[b, o:o + TC].T),
                        opT=opT))
    r3 = run_bass_kernel_spmd(nc3, in3, core_ids=list(range(NCORE))).results
    out = np.empty((B, S, D), np.float32)
    for c, b, o in _cores():
        out[b, o:o + TC] = r3[c]["xoT"].T
    return out


def kernel(**inp):
    return kernel_fused(**inp)


def kernel_unfused(**inp):
    g = lambda k: np.ascontiguousarray(np.asarray(inp[k], np.float32))
    x = g("L0_attn_a_strat")
    x = host_attn_a(x, g("L0_attn_a_Wq"), g("L0_attn_a_bq"), g("L0_attn_a_Wk"),
                    g("L0_attn_a_bk"), g("L0_attn_a_Wv"), g("L0_attn_a_bv"))
    x = host_ffn(x, g("L0_ffn_rw1"), g("L0_ffn_rb1"), g("L0_ffn_rw2"), g("L0_ffn_rb2"),
                 g("L0_ffn_we1"), g("L0_ffn_be1"), g("L0_ffn_we2"), g("L0_ffn_be2"))
    x = host_attn_b(x, g("L0_attn_b_wq"), g("L0_attn_b_wk"), g("L0_attn_b_wv"),
                    g("L0_attn_b_wo"), g("L0_ffn_rw1"), g("L0_ffn_rb1"),
                    g("L0_ffn_rw2"), g("L0_ffn_rb2"), g("L0_ffn_we1"),
                    g("L0_ffn_be1"), g("L0_ffn_we2"), g("L0_ffn_be2"))
    x = host_mamba(x, g("L1_attn_a_norm_w"), g("L1_attn_a_in_proj_w"),
                   g("L1_attn_a_conv_w"), g("L1_attn_a_conv_b"), g("L1_attn_a_x_proj_w"),
                   g("L1_attn_a_dt_proj_w"), g("L1_attn_a_dt_proj_b"),
                   g("L1_attn_a_A_log"), g("L1_attn_a_D"), g("L1_attn_a_out_proj_w"))
    x = host_cffn(x, g("L1_ffn_G1_R"), g("L1_ffn_G2_R"), g("L1_ffn_G1_I"),
                  g("L1_ffn_G2_I"), g("L1_ffn_bias_real"), g("L1_ffn_bias_imag"))
    x = host_nblock(x, g("L1_attn_b_ln_g"), g("L1_attn_b_ln_b"), g("L1_attn_b_W_hidden"),
                    g("L1_attn_b_b_hidden"), g("L1_attn_b_W_qk"), g("L1_attn_b_b_qk"),
                    g("L1_attn_b_os_gamma"), g("L1_attn_b_os_beta"),
                    g("L1_attn_b_W_out"), g("L1_attn_b_b_out"))
    x = host_cffn(x, g("L1_ffn_G1_R"), g("L1_ffn_G2_R"), g("L1_ffn_G1_I"),
                  g("L1_ffn_G2_I"), g("L1_ffn_bias_real"), g("L1_ffn_bias_imag"))
    return x


GROUPS = [[0, 1, 2, 3], [4, 5, 6, 7]]


def bld_fused(upto=99):
    p = Prog()
    I = p.inp
    ext_in = dict(
        xqT=I("xqT", [D, TC]), xtm=I("xtm", [TC, D]),
        wqT=I("wqT", [D, D]), wkT=I("wkT", [D, D]), wvT=I("wvT", [D, D]),
        bq=I("bq", [D, 1]), bk=I("bk", [D, 1]), bvr=I("bvr", [1, D]),
        idenb=I("idenb", [128, 128], BF16), idf=I("idf", [128, 128]),
        rw1=I("rw1", [D, D]), rw2=I("rw2", [D, 8]), rb1=I("rb1", [D, 1]),
        rb2=I("rb2", [8, 1]), we1=I("we1", [8, D, 4096]), we2=I("we2", [8, 4096, D]),
        be1c=I("be1c", [128, 256]), be2r=I("be2r", [8, D]),
        wqp=I("wqp", [D, D]), wkp=I("wkp", [D, D]), wv=I("wv", [D, D]),
        cosd=I("cosd", [TC, 512]), sind=I("sind", [TC, 512]),
        triA=I("triA", [128, 256], BF16), triB=I("triB", [128, 256], BF16),
        ho=I("ho", [1, 4], mybir.dt.int32), wo=I("wo", [D, D]),
        nw=I("nw", [D, 1]), ipT=I("ipT", [D, 4096]), cw=I("cw", [128, 64]),
        cb=I("cb", [128, 16]), xpT=I("xpT", [2048, 96]), dpT=I("dpT", [64, 2048]),
        dtb=I("dtb", [128, 16]), hoff=I("hoff", [1, 2], mybir.dt.int32),
        hmask=I("hmask", [128, 1]),
        alog_i=I("alog_i", [64, 128]), dpar=I("dpar", [128, 4]),
        selc_i=I("selc_i", [128, 2048], BF16), sely_i=I("sely_i", [128, 2048], BF16),
        dsoff=I("dsoff", [1, 2], mybir.dt.int32), opT=I("opT", [2048, D]),
        G1R=I("G1R", [32, 128]), G2R=I("G2R", [64, 16]),
        G1I=I("G1I", [32, 128]), G2I=I("G2I", [64, 16]),
        brc=I("brc", [512, 1]), bic=I("bic", [512, 1]),
        Wh=I("Wh", [D, 4096]), bhr=I("bhr", [1, 4096]), Wqk=I("Wqk", [D, 128]),
        bqk=I("bqk", [128, 1]), lng=I("lng", [D, 1]), lnb=I("lnb", [D, 1]),
        gb=I("gb", [128, 4]), dnr=I("dnr", [1, TC]), Mm=I("Mm", [S, TC], BF16),
        Wo=I("Wo", [2048, D]), bo=I("bo", [D, 1]),
        offs=I("offs", [1, 8], mybir.dt.int32),
    )
    xout = p.out("xout", [D, TC])
    sc = base_scratch(p)
    offs_sb = p.sb("offs_sb", [1, 8], mybir.dt.int32)
    p.dma("qsp", offs_sb[:], ext_in["offs"][:], writes=("offs",))
    we1b = p.dram("we1b", [8, D, 4096], BF16)
    we2b = p.dram("we2b", [8, 4096, D], BF16)
    for e8 in range(8):
        p.dma("qg", we1b[e8], ext_in["we1"][e8], writes=("we1b",))
        p.dma("qg", we2b[e8], ext_in["we2"][e8], writes=("we2b",))

    E = dict(ext_in)
    E["sc"] = sc
    dram = p.dram
    cur = None

    def tap(x_ap, stage_no):
        if upto == stage_no:
            for ec in range(NCHUNK):
                p.dma("qsp", xout[ec * 128:(ec + 1) * 128, :],
                      x_ap[ec * 128:(ec + 1) * 128, :], writes=("tapout",))
            return True
        return False

    q1 = dram("q1", [D, TC], BF16)
    k1c = dram("k1c", [D, TC], BF16)
    v1c = dram("v1c", [TC, D], BF16)
    bld_a1(p, dict(E, q_o=q1, k_o=k1c, v_o=v1c))
    kg1 = dram("kg1", [4 * D, TC], BF16)
    vg1 = dram("vg1", [S, D], BF16)
    p.collective([k1c, v1c], [kg1, vg1], GROUPS)
    p.fence()
    xc1 = dram("xc1", [D, TC])
    bld_a2(p, dict(E, qT=q1, kT=kg1, vF=vg1, xo=None, xo_fm=xc1, kT_gathered=True))
    if tap(xc1, 2):
        return p.finalize()
    xc2 = dram("xc2", [D, TC])
    bld_ffn(p=p, ext=dict(E, xT=xc1, xoT=xc2, we1=we1b, we2=we2b, wbf=True),
            stage="f1")
    if tap(xc2, 3):
        return p.finalize()
    qkv2 = dram("qkv2", [3 * D, TC], BF16)
    bld_b1(p, dict(E, xT=xc2, idb=E["idenb"], q_o=qkv2[0:D, :],
                   k_o=qkv2[D:2 * D, :], v_o=qkv2[2 * D:3 * D, :]))
    qkvg2 = dram("qkvg2", [4 * 3 * D, TC], BF16)
    p.collective([qkv2], [qkvg2], GROUPS)
    p.fence()
    ob2 = dram("ob2", [4, 64, S], BF16)
    bld_b2(p, dict(E, qkvg=qkvg2, idb=E["idenb"], ob=ob2))
    og2 = dram("og2", [D, S], BF16)
    p.collective([ob2.rearrange("a b c -> (a b) c")], [og2], GROUPS)
    p.fence()
    xc3 = dram("xc3", [D, TC])
    bld_ffn(pre_proj=True, p=p,
            ext=dict(E, xT=xc2, xoT=xc3, oT=og2, ocol_dyn=offs_sb[0:1, 0:1],
                     we1=we1b, we2=we2b, wbf=True),
            stage="f2")
    if tap(xc3, 6):
        return p.finalize()
    h3c = dram("h3c", [D, 3])
    p.dma("qsp", h3c[:], xc3[:, TC - 3:TC], writes=("h3c",))
    hg3 = dram("hg3", [4 * D, 3])
    p.collective([h3c], [hg3], GROUPS)
    p.fence()
    ucdt1 = dram("ucdt1", [4096, TC], BF16)
    uc1 = ucdt1[0:2048, :]
    dt1 = ucdt1[2048:4096, :]
    bc1 = dram("bc1", [32, TC])
    z1 = dram("z1", [2048, TC], BF16)
    xh1m = dram("xh1m", [D, TC], BF16)
    bld_m1(p, dict(E, xcur=xc3, halog=hg3, uc_o=uc1, dt_o=dt1, bc_o=bc1,
                   z_o=z1, xh_o=xh1m))
    ucdtg = dram("ucdtg", [4 * 4096, TC], BF16)
    bcg = dram("bcg", [128, TC])
    p.collective([ucdt1, bc1], [ucdtg, bcg], GROUPS)
    p.fence()
    yc = dram("yc", [512, S], BF16)
    bld_m2(p, dict(E, ucdtg=ucdtg, bcg=bcg, y_o=yc))
    yg = dram("yg", [2048, S], BF16)
    p.collective([yc], [yg], GROUPS)
    p.fence()
    xc4 = dram("xc4", [D, TC])
    bld_m3(p, dict(E, yT=yg, ycol_dyn=offs_sb[0:1, 1:2], zT=z1, xhT=xh1m,
                   xT=xc3, xoT=xc4))
    if tap(xc4, 9):
        return p.finalize()
    xc5 = dram("xc5", [D, TC])
    bld_cffn(p=p, ext=dict(E, xT=xc4, xoT=xc5), stage="c1")
    if tap(xc5, 10):
        return p.finalize()
    kv5 = dram("kv5", [128 + TC, 2048], BF16)
    g5 = dram("g5", [TC, 2048], BF16)
    q5 = dram("q5", [128, TC], BF16)
    xh15 = dram("xh15", [D, TC], BF16)
    bld_n1(p, dict(E, xT=xc5, v_o=kv5[128:128 + TC, :], g_o=g5, q_o=q5,
                   k_o=kv5[0:128, :], xh1_o=xh15))
    kvg5 = dram("kvg5", [4 * (128 + TC), 2048], BF16)
    p.collective([kv5], [kvg5], GROUPS)
    p.fence()
    xc6 = dram("xc6", [D, TC])
    bld_n2(p, dict(E, q=q5, kF=None, kvgath=kvg5, gate=g5, xT=xc5,
                   xh1T=xh15, idb=E["idenb"], xoT=xc6))
    if tap(xc6, 12):
        return p.finalize()
    bld_cffn(p=p, ext=dict(E, xT=xc6, xoT=xout), stage="c2")
    return p.finalize()


def _fused_inmaps(inp):
    g = lambda k: np.ascontiguousarray(np.asarray(inp[k], np.float32))
    x = g("L0_attn_a_strat")
    perm = _rope_perm()
    wqp = np.ascontiguousarray(g("L0_attn_b_wq")[:, perm])
    wkp = np.ascontiguousarray(g("L0_attn_b_wk")[:, perm])
    be1c = np.ascontiguousarray(
        g("L0_ffn_be1").reshape(8, 32, 128).transpose(2, 0, 1).reshape(128, 256))
    idf = np.eye(128, dtype=np.float32)
    r_ = np.arange(128)[:, None]
    c_ = np.arange(256)[None, :]
    triA = (c_ >= r_).astype(BF)
    triB = (c_ >= r_ + 128).astype(BF)
    cwm = np.ascontiguousarray(
        g("L1_attn_a_conv_w").reshape(16, 128, 4).transpose(1, 0, 2).reshape(128, 64))
    cbm = np.ascontiguousarray(g("L1_attn_a_conv_b").reshape(16, 128).T)
    dtbm = np.ascontiguousarray(g("L1_attn_a_dt_proj_b").reshape(16, 128).T)
    selc = np.zeros((128, 16, 128), np.float32)
    sely = np.zeros((128, 16, 128), np.float32)
    for gg in range(16):
        for pp_ in range(128):
            selc[gg * 8 + pp_ // 16, gg, pp_] = 1.0
            sely[pp_, gg, gg * 8 + pp_ // 16] = 1.0
    selc = np.ascontiguousarray(selc.reshape(128, 2048).astype(BF))
    sely = np.ascontiguousarray(sely.reshape(128, 2048).astype(BF))
    gb = np.ascontiguousarray(np.stack(
        [g("L1_attn_b_os_gamma")[0], g("L1_attn_b_os_beta")[0],
         g("L1_attn_b_os_gamma")[1], g("L1_attn_b_os_beta")[1]], axis=1).astype(np.float32))
    com = dict(
        wqT=np.ascontiguousarray(g("L0_attn_a_Wq").T),
        wkT=np.ascontiguousarray(g("L0_attn_a_Wk").T),
        wvT=np.ascontiguousarray(g("L0_attn_a_Wv").T),
        bq=np.ascontiguousarray(g("L0_attn_a_bq").reshape(D, 1)),
        bk=np.ascontiguousarray(g("L0_attn_a_bk").reshape(D, 1)),
        bvr=np.ascontiguousarray(g("L0_attn_a_bv").reshape(1, D)),
        idenb=_EYE_BF, idf=idf,
        rw1=g("L0_ffn_rw1"), rw2=g("L0_ffn_rw2"),
        rb1=np.ascontiguousarray(g("L0_ffn_rb1").reshape(D, 1)),
        rb2=np.ascontiguousarray(g("L0_ffn_rb2").reshape(8, 1)),
        we1=g("L0_ffn_we1"), we2=g("L0_ffn_we2"), be1c=be1c, be2r=g("L0_ffn_be2"),
        wqp=wqp, wkp=wkp, wv=g("L0_attn_b_wv"), triA=triA, triB=triB,
        wo=g("L0_attn_b_wo"),
        nw=np.ascontiguousarray(g("L1_attn_a_norm_w").reshape(D, 1)),
        ipT=np.ascontiguousarray(g("L1_attn_a_in_proj_w").T), cw=cwm, cb=cbm,
        xpT=np.ascontiguousarray(g("L1_attn_a_x_proj_w").T),
        dpT=np.ascontiguousarray(g("L1_attn_a_dt_proj_w").T), dtb=dtbm,
        dpar=None, selc_i=selc, sely_i=sely,
        opT=np.ascontiguousarray(g("L1_attn_a_out_proj_w").T),
        G1R=np.ascontiguousarray(g("L1_ffn_G1_R").reshape(32, 128)),
        G2R=np.ascontiguousarray(g("L1_ffn_G2_R").reshape(64, 16)),
        G1I=np.ascontiguousarray(g("L1_ffn_G1_I").reshape(32, 128)),
        G2I=np.ascontiguousarray(g("L1_ffn_G2_I").reshape(64, 16)),
        brc=np.ascontiguousarray(g("L1_ffn_bias_real").reshape(512, 1)),
        bic=np.ascontiguousarray(g("L1_ffn_bias_imag").reshape(512, 1)),
        Wh=g("L1_attn_b_W_hidden"),
        bhr=np.ascontiguousarray(g("L1_attn_b_b_hidden").reshape(1, 4096)),
        Wqk=g("L1_attn_b_W_qk"),
        bqk=np.ascontiguousarray(g("L1_attn_b_b_qk").reshape(128, 1)),
        lng=np.ascontiguousarray(g("L1_attn_b_ln_g").reshape(D, 1)),
        lnb=np.ascontiguousarray(g("L1_attn_b_ln_b").reshape(D, 1)),
        gb=gb, Wo=g("L1_attn_b_W_out"),
        bo=np.ascontiguousarray(g("L1_attn_b_b_out").reshape(D, 1)),
    )
    alog = g("L1_attn_a_A_log")
    Dp = g("L1_attn_a_D")
    ins = []
    for c, b, o in _cores():
        j = c % 4
        ds = j * 512
        cos, sin = _rope_tables(o, TC)
        kidx = np.arange(S)[:, None]
        qidx = np.arange(o, o + TC)[None, :]
        d = dict(com)
        d.update(
            xqT=np.ascontiguousarray(x[b, o:o + TC].T),
            xtm=np.ascontiguousarray(x[b, o:o + TC]),
            cosd=cos, sind=sin,
            ho=np.array([[((c * 4 + jj) % 16) * 64 * TC for jj in range(4)]], np.int32),
            hoff=np.array([[((j - 1) % 4) * 1024 * 3, 0]], np.int32),
            hmask=np.full((128, 1), 0.0 if j == 0 else 1.0, np.float32),
            alog_i=np.ascontiguousarray(alog[ds:ds + 512].reshape(64, 128)),
            dpar=np.ascontiguousarray(Dp[ds:ds + 512].reshape(4, 128).T),
            dsoff=np.array([[ds * TC, 0]], np.int32),
            dnr=(1.0 / (np.arange(o, o + TC, dtype=np.float64) + 1.0))
                .astype(np.float32).reshape(1, TC),
            Mm=(kidx <= qidx).astype(BF),
            offs=np.array([[o, o, 0, 0, 0, 0, 0, 0]], np.int32),
        )
        ins.append(d)
    return ins


def kernel_fused(upto=99, **inp):
    nc = bld_fused(upto=upto)
    ins = _fused_inmaps(inp)
    r = run_bass_kernel_spmd(nc, ins, core_ids=list(range(NCORE))).results
    out = np.empty((B, S, D), np.float32)
    for c, b, o in _cores():
        out[b, o:o + TC] = r[c]["xout"].T
    return out


# version 07
# speedup vs baseline: 1.0482x; 1.0482x over previous
import numpy as np
import ml_dtypes
from contextlib import ExitStack
import concourse.bass as bass
import concourse.mybir as mybir
from concourse.bass_utils import run_bass_kernel_spmd

F32 = mybir.dt.float32
BF16 = mybir.dt.bfloat16
AF = mybir.ActivationFunctionType
ALU = mybir.AluOpType
AX = mybir.AxisListType
BF = ml_dtypes.bfloat16

NCORE = 8
B, S, D = 2, 8192, 1024
TC = 2048
NCHUNK = D // 128


class Prog:
    DMA_STREAMS = ("qsp", "qg", "qcc", "qact")
    STREAMS = ("pe", "act", "dve", "pool", "qsp", "qg", "qcc", "qact")

    PARANOID = False

    def __init__(self, name="prog"):
        self.nc = bass.Bass("TRN2", target_bir_lowering=False, debug=False)
        self.stack = ExitStack()
        self.ops = []
        self.last_w = {}
        self.readers = {}
        self.n_sb = 0
        self.n_ps = 0
        self._finalized = False
        self.tag = ""
        self._stage_stack = None
        self._fence = []

    def stage_begin(self, tag):
        assert self._stage_stack is None
        self.tag = tag + "_"
        self._stage_stack = ExitStack()

    def stage_end(self):
        self._stage_stack.close()
        self._stage_stack = None
        self.tag = ""
        last = {}
        for i, rec in enumerate(self.ops):
            last[rec["stream"]] = i
        self._fence = list(last.values())

    def fence(self):
        last = {}
        for i, rec in enumerate(self.ops):
            last[rec["stream"]] = i
        self._fence = list(last.values())

    def collective(self, ins, outs, groups, kind="AllGather", op=None):
        if op is None:
            op = ALU.bypass
        for a, o in zip(ins, outs):
            self.op("qcc", lambda e, a=a, o=o: e.collective_compute(
                kind, op, replica_groups=groups,
                ins=[a.tensor.ap().opt() if hasattr(a, "tensor") else a for a in [a]],
                outs=[o.tensor.ap().opt()]), rmw=("ccchain",))

    def inp(self, name, shape, dtype=F32):
        return self.nc.dram_tensor(name, list(shape), dtype, kind="ExternalInput").ap()

    def out(self, name, shape, dtype=F32):
        return self.nc.dram_tensor(name, list(shape), dtype, kind="ExternalOutput").ap()

    def dram(self, name, shape, dtype=F32):
        return self.nc.dram_tensor(self.tag + name, list(shape), dtype,
                                   kind="Internal").ap()

    def sb(self, name, shape, dtype=F32):
        if self._stage_stack is not None:
            return self._stage_stack.enter_context(
                self.nc.sbuf_tensor(self.tag + name, list(shape), dtype)).ap()
        return self.nc.alloc_sbuf_tensor(name, list(shape), dtype).ap()

    def ps(self, name, shape, dtype=F32):
        if self._stage_stack is not None:
            return self._stage_stack.enter_context(
                self.nc.psum_tensor(self.tag + name, list(shape), dtype)).ap()
        return self.nc.alloc_psum_tensor(name, list(shape), dtype).ap()

    def op(self, stream, emit, reads=(), writes=(), rmw=()):
        assert stream in self.STREAMS, stream
        deps = set()
        for k in tuple(reads) + tuple(rmw):
            if k in self.last_w:
                deps.add(self.last_w[k])
        for k in tuple(writes) + tuple(rmw):
            if k in self.last_w:
                deps.add(self.last_w[k])
            for r in self.readers.get(k, ()):
                deps.add(r)
        opid = len(self.ops)
        for k in tuple(reads) + tuple(rmw):
            self.readers.setdefault(k, []).append(opid)
        for k in tuple(writes) + tuple(rmw):
            self.last_w[k] = opid
            self.readers[k] = [opid] if k in rmw else []
        deps.discard(opid)
        for f in self._fence:
            if f < opid:
                deps.add(f)
        if self.PARANOID and opid > 0:
            deps.add(opid - 1)
        self.ops.append(dict(stream=stream, emit=emit, deps=deps))
        return opid

    DEBUG = False

    def dbg(self, name, ap, reads):
        if not self.DEBUG:
            return
        t = self.nc.dram_tensor("dbg_" + name, [ap.shape[0], ap.shape[1]],
                                ap.dtype, kind="ExternalOutput").ap()
        self.dma("qsp", t[:], ap, reads=tuple(reads), writes=(f"_dbg{name}",))

    def dma(self, queue, out, in_, reads=(), writes=(), **kw):
        return self.op(queue, lambda e: e.dma_start(out=out, in_=in_, **kw),
                       reads=reads, writes=writes)

    DMA_ROT = 8

    def finalize(self):
        assert not self._finalized
        self._finalized = True
        n = len(self.ops)
        import bisect
        dma_ord = {}
        counters = {s: 0 for s in self.DMA_STREAMS}
        for i, rec in enumerate(self.ops):
            s = rec["stream"]
            if s in self.DMA_STREAMS:
                dma_ord[i] = counters[s]
                counters[s] += 1
        marks = {s: [] for s in self.STREAMS}
        waits = [[] for _ in range(n)]
        pending = [sorted(rec["deps"]) for rec in self.ops]
        for i, rec in enumerate(self.ops):
            for d in pending[i]:
                ds = self.ops[d]["stream"]
                if ds in self.DMA_STREAMS:
                    waits[i].append(("dma", ds, d))
                    continue
                if ds == rec["stream"] == "pe":
                    continue
                m = marks[ds]
                j = bisect.bisect_left(m, d)
                if not (j < len(m) and m[j] < i):
                    bisect.insort(m, d)
        for i, rec in enumerate(self.ops):
            for d in pending[i]:
                ds = self.ops[d]["stream"]
                if ds in self.DMA_STREAMS or (ds == rec["stream"] == "pe"):
                    continue
                m = marks[ds]
                j = bisect.bisect_left(m, d)
                assert j < len(m) and m[j] >= d and m[j] < i
                waits[i].append(("cmp", ds, m[j]))
        for s in ("pe", "act", "dve", "pool"):
            sops = [i for i, r in enumerate(self.ops) if r["stream"] == s]
            if sops and sops[-1] not in marks[s]:
                marks[s].append(sops[-1])
        cval = {}
        for s in ("pe", "act", "dve", "pool"):
            for k, opid in enumerate(marks[s]):
                cval[(s, opid)] = k + 1
        markset = {(s, o) for s in ("pe", "act", "dve", "pool") for o in marks[s]}

        nc = self.nc
        R = self.DMA_ROT
        csem = {s: self.stack.enter_context(nc.semaphore("sem_" + s))
                for s in ("pe", "act", "dve", "pool")}
        dsem = {s: [self.stack.enter_context(nc.semaphore(f"sem_{s}{r}"))
                    for r in range(R)] for s in self.DMA_STREAMS}

        def dma_sem_val(opid):
            o = dma_ord[opid]
            s = self.ops[opid]["stream"]
            step = 1 if s == "qcc" else 16
            return dsem[s][o % R], step * (o // R + 1)

        block = self.stack.enter_context(nc.Block())
        bass_engines = dict(pe="tensor", act="scalar", dve="vector",
                            pool="gpsimd", qsp="sync", qg="gpsimd", qcc="gpsimd",
                            qact="scalar")
        groups = {}
        for i, rec in enumerate(self.ops):
            groups.setdefault(bass_engines[rec["stream"]], []).append(i)
        watermark = {}

        def emit_for(beng, eng):
            for i in groups.get(beng, ()):
                rec = self.ops[i]
                s = rec["stream"]
                if s in self.DMA_STREAMS and dma_ord[i] >= R:
                    o = dma_ord[i] - R
                    prev = [k for k, v in dma_ord.items()
                            if self.ops[k]["stream"] == s and v == o][0]
                    h, v = dma_sem_val(prev)
                    wkey = (beng, ("dma", s, o % R))
                    if watermark.get(wkey, 0) < v:
                        eng.wait_ge(h, v)
                        watermark[wkey] = v
                for w in waits[i]:
                    if w[0] == "dma":
                        _, ds, dop = w
                        h, v = dma_sem_val(dop)
                        wkey = (beng, ("dma", ds, dma_ord[dop] % R))
                    else:
                        _, ds, dop = w
                        h, v = csem[ds], cval[(ds, dop)]
                        wkey = (beng, ds)
                    if watermark.get(wkey, 0) >= v:
                        continue
                    eng.wait_ge(h, v)
                    watermark[wkey] = v
                ins = rec["emit"](eng)
                if s in self.DMA_STREAMS:
                    h, v = dma_sem_val(i)
                    ins.then_inc(h, 1 if s == "qcc" else 16)
                elif (s, i) in markset:
                    ins.then_inc(csem[s], 1)

        for beng_name, attr in (("tensor", "tensor"), ("scalar", "scalar"),
                                ("vector", "vector"), ("gpsimd", "gpsimd"),
                                ("sync", "sync")):
            deco = getattr(block, attr)
            deco(lambda eng, b=beng_name: emit_for(b, eng))

        @block.sync
        def _(sync):
            for s in ("pe", "act", "dve", "pool"):
                if marks[s]:
                    sync.wait_ge(csem[s], len(marks[s]))
            for s in self.DMA_STREAMS:
                cnt = counters[s]
                step = 1 if s == "qcc" else 16
                for r in range(R):
                    tot = (cnt - r + R - 1) // R
                    if tot > 0:
                        sync.wait_ge(dsem[s][r], step * tot)

        self.stack.close()
        return nc


def bcast_row(p, tag, row_ap, row_key, ncols, out_tile, out_key, nrows=128, dtype=F32):
    d = p.dram(f"bcd_{tag}", [1, ncols], dtype)
    p.dma("qsp", d[:], row_ap, reads=(row_key,), writes=(f"bcd_{tag}",))
    p.dma("qsp", out_tile, d[:].partition_broadcast(nrows),
          reads=(f"bcd_{tag}",), writes=(out_key,))


def np_split_rows(x, n):
    return [np.ascontiguousarray(v) for v in np.split(x, n, axis=0)]


def stage_sc(p, sc):
    sc2 = dict(sc)
    sc2["ps_stat"] = p.ps("ps_stat", [1, 512], F32)
    return sc2


def ln_slice(p, tag, xs, xs_key, W, xhat_w, xhat_key, sc, eps=1e-5, affine=None):
    assert W <= 512
    rows = sc["lnrows"]
    m, s2, rstd, mr = (rows[:, 0:W], rows[:, 512:512 + W],
                       rows[:, 1024:1024 + W], rows[:, 1536:1536 + W])
    nc8 = len(xs)
    for ec in range(nc8):
        p.op("pe", lambda e, ec=ec: e.matmul(
            sc["ps_stat"][:, 0:512], sc["ones_f32"][:], xs[ec][:, 0:W],
            start=(ec == 0), stop=(ec == nc8 - 1)),
            reads=(f"{xs_key}{ec}", "ones"), writes=("ps_stat",))
    p.op("dve", lambda e: e.tensor_scalar(
        out=m[0:1, :], in0=sc["ps_stat"][:, 0:512], scalar1=1.0 / (nc8 * 128),
        scalar2=None, op0=ALU.mult), reads=("ps_stat",), writes=("lnrows",))
    for ec in range(nc8):
        p.op("act", lambda e, ec=ec: e.activation(
            out=sc["sq"][:, 0:W], in_=xs[ec][:, 0:W], func=AF.Square),
            reads=(f"{xs_key}{ec}",), writes=("sq",))
        p.op("pe", lambda e, ec=ec: e.matmul(
            sc["ps_stat"][:, 0:W], sc["ones_f32"][:], sc["sq"][:, 0:W],
            start=(ec == 0), stop=(ec == nc8 - 1)),
            reads=("sq", "ones"), writes=("ps_stat",))
    p.op("dve", lambda e: e.tensor_scalar(
        out=s2[0:1, :], in0=sc["ps_stat"][:, 0:W], scalar1=1.0 / (nc8 * 128),
        scalar2=None, op0=ALU.mult), reads=("ps_stat",), writes=("lnrows",))
    p.op("dve", lambda e: e.tensor_tensor(out=rstd, in0=m, in1=m, op=ALU.mult),
         rmw=("lnrows",))
    p.op("dve", lambda e: e.tensor_tensor(out=rstd, in0=s2, in1=rstd, op=ALU.subtract),
         rmw=("lnrows",))
    p.op("act", lambda e: e.activation(out=rstd, in_=rstd, func=AF.Sqrt,
         bias=sc["eps"][0:1, :]), reads=("consts",), rmw=("lnrows",))
    p.op("dve", lambda e: e.reciprocal(rstd, rstd), rmw=("lnrows",))
    p.op("dve", lambda e: e.tensor_tensor(out=mr, in0=m, in1=rstd, op=ALU.mult),
         rmw=("lnrows",))
    bcast_row(p, f"{tag}r", rstd[0:1, :], "lnrows", W, sc["rb"][:, 0:W], "rb")
    bcast_row(p, f"{tag}m", mr[0:1, :], "lnrows", W, sc["mb"][:, 0:W], "mb")
    for ec in range(nc8):
        p.op("dve", lambda e, ec=ec: e.tensor_tensor(
            out=sc["sq"][:, 0:W], in0=xs[ec][:, 0:W], in1=sc["rb"][:, 0:W], op=ALU.mult),
            reads=(f"{xs_key}{ec}", "rb"), writes=("sq",))
        if affine is None:
            p.op("pool", lambda e, ec=ec: e.tensor_tensor(
                out=xhat_w(ec), in0=sc["sq"][:, 0:W], in1=sc["mb"][:, 0:W],
                op=ALU.subtract),
                reads=("sq", "mb"), writes=(f"{xhat_key}{ec}",))
        else:
            gcol, bcol, akey = affine
            p.op("pool", lambda e, ec=ec: e.tensor_tensor(
                out=sc["sq"][:, 0:W], in0=sc["sq"][:, 0:W], in1=sc["mb"][:, 0:W],
                op=ALU.subtract), reads=("mb",), rmw=("sq",))
            p.op("pool", lambda e, ec=ec, g=gcol(ec), b=bcol(ec): e.tensor_scalar(
                out=xhat_w(ec), in0=sc["sq"][:, 0:W], scalar1=g, scalar2=b,
                op0=ALU.mult, op1=ALU.add),
                reads=("sq", akey), writes=(f"{xhat_key}{ec}",))


def base_scratch(p):
    sc = {
        "ones_f32": p.sb("ones_f32", [128, 1], F32),
        "ones_bf": p.sb("ones_bf", [128, 1], BF16),
        "sq": p.sb("sq_s", [128, 512], F32),
        "lnrows": p.sb("lnrows", [1, 2048], F32),
        "rb": p.sb("rb_s", [128, 512], F32),
        "mb": p.sb("mb_s", [128, 512], F32),
        "eps": None,
    }
    sc["eps"] = p.sb("eps_c", [128, 1], F32)
    sc["cm24"] = p.sb("cm24_c", [128, 1], F32)
    p.op("dve", lambda e: e.memset(sc["ones_f32"][:], 1.0), writes=("ones",))
    p.op("dve", lambda e: e.memset(sc["ones_bf"][:], 1.0), writes=("ones",))
    p.op("dve", lambda e: e.memset(sc["eps"][:], 1e-5), writes=("consts",))
    p.op("dve", lambda e: e.memset(sc["cm24"][:], -24.0), writes=("consts",))
    return sc


def bld_a1(p=None, ext=None):
    standalone = p is None
    if standalone:
        p = Prog()
        ext = dict(xqT=p.inp("xqT", [D, TC]), wqT=p.inp("wqT", [D, D]),
                   wkT=p.inp("wkT", [D, D]), wvT=p.inp("wvT", [D, D]),
                   bq=p.inp("bq", [D, 1]), bk=p.inp("bk", [D, 1]),
                   bvr=p.inp("bvr", [1, D]), q_o=p.out("q_o", [D, TC], BF16),
                   k_o=p.out("k_o", [D, TC], BF16), v_o=p.out("v_o", [TC, D], BF16),
                   sc=base_scratch(p))
    xqT, wqT, wkT, wvT = ext["xqT"], ext["wqT"], ext["wkT"], ext["wvT"]
    bq, bk, bvr = ext["bq"], ext["bk"], ext["bvr"]
    q_o, k_o, v_o = ext["q_o"], ext["k_o"], ext["v_o"]
    sc = ext["sc"]
    p.stage_begin("a1")
    sc = stage_sc(p, sc)
    xs = [p.sb(f"xs{i}", [128, 512], F32) for i in range(NCHUNK)]
    xh = [p.sb(f"xh{i}", [128, 512], BF16) for i in range(NCHUNK)]
    wq = [p.sb(f"wq{i}", [128, D], BF16) for i in range(NCHUNK)]
    wk = [p.sb(f"wk{i}", [128, D], BF16) for i in range(NCHUNK)]
    wv = [p.sb(f"wv{i}", [128, D], BF16) for i in range(NCHUNK)]
    bq_sb = p.sb("bq_sb", [128, NCHUNK], F32)
    bk_sb = p.sb("bk_sb", [128, NCHUNK], F32)
    bv_bc = p.sb("bv_bc", [128, D], F32)
    stg = [p.sb(f"stg{j}", [128, 512], BF16) for j in range(4)]
    pp = [p.ps(f"pp{j}", [128, 512], F32) for j in range(4)]
    for i in range(NCHUNK):
        p.dma("qg", wq[i][:], wqT[i * 128:(i + 1) * 128, :], writes=("wq",))
        p.dma("qg", wk[i][:], wkT[i * 128:(i + 1) * 128, :], writes=("wk",))
        p.dma("qg", wv[i][:], wvT[i * 128:(i + 1) * 128, :], writes=("wv",))
        p.dma("qsp", bq_sb[:, i:i + 1], bq[i * 128:(i + 1) * 128, :], writes=("bias",))
        p.dma("qsp", bk_sb[:, i:i + 1], bk[i * 128:(i + 1) * 128, :], writes=("bias",))
    p.dma("qsp", bv_bc[:], bvr[:].partition_broadcast(128), writes=("bias",))

    for si in range(TC // 512):
        sl = slice(si * 512, (si + 1) * 512)
        for ec in range(NCHUNK):
            p.dma("qsp", xs[ec][:], xqT[ec * 128:(ec + 1) * 128, sl], writes=(f"xs{ec}",))
        ln_slice(p, f"s{si}", xs, "xs", 512, lambda ec: xh[ec][:], "xh", sc)
        for (wt, wkey, bsb, dst) in ((wq, "wq", bq_sb, q_o), (wk, "wk", bk_sb, k_o)):
            for eo in range(NCHUNK):
                pj = pp[eo % 2]
                for ec in range(NCHUNK):
                    p.op("pe", lambda e, pj=pj, wt=wt, ec=ec, eo=eo: e.matmul(
                        pj[:], wt[ec][:, eo * 128:(eo + 1) * 128], xh[ec][:],
                        start=(ec == 0), stop=(ec == NCHUNK - 1)),
                        reads=(f"xh{ec}", wkey), writes=(f"pp{eo % 2}",))
                st = stg[eo % 2]
                p.op("dve", lambda e, pj=pj, st=st, bsb=bsb, eo=eo: e.tensor_scalar(
                    out=st[:], in0=pj[:], scalar1=bsb[:, eo:eo + 1], scalar2=None,
                    op0=ALU.add), reads=(f"pp{eo % 2}", "bias"), writes=(f"stg{eo % 2}",))
                p.dma("qsp", dst[eo * 128:(eo + 1) * 128, sl], st[:],
                      reads=(f"stg{eo % 2}",), writes=("out",))
        for tt in range(4):
            for vg in range(2):
                j = 2 + (tt * 2 + vg) % 2
                pj = pp[j]
                for ec in range(NCHUNK):
                    p.op("pe", lambda e, pj=pj, ec=ec, tt=tt, vg=vg: e.matmul(
                        pj[:], xh[ec][:, tt * 128:(tt + 1) * 128],
                        wv[ec][:, vg * 512:(vg + 1) * 512],
                        start=(ec == 0), stop=(ec == NCHUNK - 1)),
                        reads=(f"xh{ec}", "wv"), writes=(f"pp{j}",))
                st = stg[j]
                p.op("dve", lambda e, pj=pj, st=st, vg=vg: e.tensor_tensor(
                    out=st[:], in0=pj[:], in1=bv_bc[:, vg * 512:(vg + 1) * 512],
                    op=ALU.add), reads=(f"pp{j}", "bias"), writes=(f"stg{j}",))
                p.dma("qsp", v_o[si * 512 + tt * 128: si * 512 + (tt + 1) * 128,
                              vg * 512:(vg + 1) * 512], st[:],
                      reads=(f"stg{j}",), writes=("out",))
    p.stage_end()
    return p.finalize() if standalone else None


def bld_a2(p=None, ext=None):
    standalone = p is None
    if standalone:
        p = Prog()
        ext = dict(qT=p.inp("qT", [D, TC], BF16), kT=p.inp("kT", [D, S], BF16),
                   vF=p.inp("vF", [S, D], BF16), xtm=p.inp("xtm", [TC, D]),
                   idenb=p.inp("idenb", [128, 128], BF16),
                   xo=p.out("xo", [TC, D]))
    qT, kT, vF, xtm = ext["qT"], ext["kT"], ext["vF"], ext["xtm"]
    idenb, xo = ext["idenb"], ext["xo"]
    fm_out = ext.get("xo_fm") is not None
    kT_gath = ext.get("kT_gathered", False)
    p.stage_begin("a2")
    q_fm = [p.sb(f"qf{i}", [128, TC], BF16) for i in range(NCHUNK)]
    iden = p.sb("iden", [128, 128], BF16)
    scores = p.sb("scores", [128, S], F32)
    prob = p.sb("prob", [128, S], BF16)
    pt = p.sb("pt", [128, S], BF16)
    kst = [p.sb(f"kst{j}", [128, NCHUNK * 512], BF16) for j in range(4)]
    vst = [p.sb(f"vst{j}", [128, D], BF16) for j in range(8)]
    mxc = p.sb("mxc", [128, 4], F32)
    out_tm = p.sb("out_tm", [128, D], F32)
    xr = p.sb("xr", [128, D], F32)
    ps_sc = [p.ps(f"ps_sc{j}", [128, 512], F32) for j in range(2)]
    ps_o = [p.ps(f"ps_o{j}", [128, 512], F32) for j in range(2)]
    ps_tr = [p.ps(f"ps_tr{j}", [128, 128], BF16) for j in range(2)]
    if fm_out:
        ps_trf = [p.ps("ps_trf0", [128, 128], F32)] * 2
        idenf_sb = p.sb("idenf_sb", [128, 128], F32)
        ofm = p.sb("ofm", [128, 128], F32)
        p.dma("qsp", idenf_sb[:], ext["idf"][:], writes=("idenf",))

    p.dma("qg", iden[:], idenb[:], writes=("iden",))
    for i in range(NCHUNK):
        p.dma("qg", q_fm[i][:], qT[i * 128:(i + 1) * 128, :], writes=("qf",))
    NKG = S // 512
    NKT = S // 128
    for qt in range(TC // 128):
        qsl = slice(qt * 128, (qt + 1) * 128)
        for kg in range(NKG):
            buf = kst[kg % 4]
            bk_ = f"kst{kg % 4}"
            if not kT_gath:
                p.dma("qsp", buf[:].rearrange("p (c w) -> p c w", w=512),
                      kT[:, kg * 512:(kg + 1) * 512].rearrange("(c p) w -> p c w", p=128),
                      reads=(), writes=(bk_,))
            else:
                jc, lc = kg // 4, (kg % 4) * 512
                p.dma("qsp", buf[:].rearrange("p (c w) -> p c w", w=512),
                      kT[jc * D:(jc + 1) * D, lc:lc + 512]
                      .rearrange("(c p) w -> p c w", p=128),
                      reads=(), writes=(bk_,))
            pj = ps_sc[kg % 2]
            for ec in range(NCHUNK):
                p.op("pe", lambda e, pj=pj, ec=ec, buf=buf, qsl=qsl: e.matmul(
                    pj[:], q_fm[ec][:, qsl], buf[:, ec * 512:(ec + 1) * 512],
                    start=(ec == 0), stop=(ec == NCHUNK - 1)),
                    reads=("qf", bk_), writes=(f"ps_sc{kg % 2}",))
            p.op("dve", lambda e, pj=pj, kg=kg: e.tensor_copy(
                scores[:, kg * 512:(kg + 1) * 512], pj[:]),
                reads=(f"ps_sc{kg % 2}",), writes=(f"sc{kg}",))
        p.op("dve", lambda e: e.reduce_max(mxc[:, 0:1], scores[:], axis=AX.X),
             reads=tuple(f"sc{kg}" for kg in range(NKG)), writes=("mxc",))
        p.op("dve", lambda e: e.tensor_scalar(
            out=mxc[:, 1:2], in0=mxc[:, 0:1], scalar1=-1.0 / 32.0, scalar2=None,
            op0=ALU.mult), rmw=("mxc",))
        p.op("act", lambda e: e.activation(
            out=prob[:], in_=scores[:], func=AF.Exp, bias=mxc[:, 1:2],
            scale=1.0 / 32.0, accum_out=mxc[:, 2:3]),
            reads=tuple(f"sc{kg}" for kg in range(NKG)), writes=("prob",), rmw=("mxc",))
        p.op("dve", lambda e: e.reciprocal(mxc[:, 3:4], mxc[:, 2:3]), rmw=("mxc",))
        for kt in range(NKT):
            pj = ps_tr[kt % 2]
            p.op("pe", lambda e, pj=pj, kt=kt: e.transpose(
                pj[:], prob[:, kt * 128:(kt + 1) * 128], iden[:]),
                reads=("prob", "iden"), writes=(f"ps_tr{kt % 2}",))
            eng = "dve" if kt % 2 == 0 else "act"
            if eng == "dve":
                p.op("dve", lambda e, pj=pj, kt=kt: e.tensor_copy(
                    pt[:, kt * 128:(kt + 1) * 128], pj[:]),
                    reads=(f"ps_tr{kt % 2}",), writes=(f"pt{kt}",))
            else:
                p.op("act", lambda e, pj=pj, kt=kt: e.activation(
                    out=pt[:, kt * 128:(kt + 1) * 128], in_=pj[:], func=AF.Copy),
                    reads=(f"ps_tr{kt % 2}",), writes=(f"pt{kt}",))
        for kt in range(NKT):
            vb = vst[kt % 8]
            vk = f"vst{kt % 8}"
            p.dma("qg", vb[:], vF[kt * 128:(kt + 1) * 128, :], writes=(vk,))
            for og in range(2):
                p.op("pe", lambda e, vb=vb, kt=kt, og=og: e.matmul(
                    ps_o[og][:], pt[:, kt * 128:(kt + 1) * 128],
                    vb[:, og * 512:(og + 1) * 512],
                    start=(kt == 0), stop=(kt == NKT - 1)),
                    reads=(f"pt{kt}", vk), writes=(f"ps_o{og}",))
        p.dma("qsp", xr[:], xtm[qsl, :], writes=("xr",))
        for og in range(2):
            p.op("dve", lambda e, og=og: e.tensor_scalar(
                out=out_tm[:, og * 512:(og + 1) * 512], in0=ps_o[og][:],
                scalar1=mxc[:, 3:4], scalar2=None, op0=ALU.mult),
                reads=(f"ps_o{og}", "mxc"), writes=("out_tm",))
            p.op("pool", lambda e, og=og: e.tensor_tensor(
                out=out_tm[:, og * 512:(og + 1) * 512],
                in0=out_tm[:, og * 512:(og + 1) * 512],
                in1=xr[:, og * 512:(og + 1) * 512], op=ALU.add),
                reads=("xr",), rmw=("out_tm",))
        if not fm_out:
            p.dma("qsp", xo[qsl, :], out_tm[:], reads=("out_tm",), writes=("xout",))
        else:
            for oc in range(NCHUNK):
                pj = ps_trf[oc % 2]
                p.op("pe", lambda e, pj=pj, oc=oc: e.transpose(
                    pj[:], out_tm[:, oc * 128:(oc + 1) * 128], idenf_sb[:]),
                    reads=("out_tm", "idenf"), writes=("ps_trf0",))
                p.op("dve", lambda e, pj=pj, oc=oc: e.tensor_copy(ofm[:], pj[:]),
                     reads=("ps_trf0",), writes=("ofm",))
                p.dma("qsp", ext["xo_fm"][oc * 128:(oc + 1) * 128, qsl], ofm[:],
                      reads=("ofm",), writes=("xout",))
    p.stage_end()
    return p.finalize() if standalone else None


_EYE_BF = np.eye(128, dtype=np.float32).astype(BF)


def _cores():
    return [(c, c // 4, (c % 4) * TC) for c in range(NCORE)]


def host_attn_a(x, Wq, bq, Wk, bk, Wv, bv):
    wqT = np.ascontiguousarray(Wq.T)
    wkT = np.ascontiguousarray(Wk.T)
    wvT = np.ascontiguousarray(Wv.T)
    bqc = np.ascontiguousarray(bq.reshape(D, 1))
    bkc = np.ascontiguousarray(bk.reshape(D, 1))
    bvr = np.ascontiguousarray(bv.reshape(1, D))
    nc1 = bld_a1()
    in1 = []
    for c, b, o in _cores():
        chunk = x[b, o:o + TC]
        in1.append(dict(xqT=np.ascontiguousarray(chunk.T), wqT=wqT, wkT=wkT,
                        wvT=wvT, bq=bqc, bk=bkc, bvr=bvr))
    r1 = run_bass_kernel_spmd(nc1, in1, core_ids=list(range(NCORE))).results
    kT = {b: np.concatenate([r1[c]["k_o"] for c, bb, o in _cores() if bb == b], axis=1)
          for b in range(B)}
    vF = {b: np.concatenate([r1[c]["v_o"] for c, bb, o in _cores() if bb == b], axis=0)
          for b in range(B)}
    nc2 = bld_a2()
    in2 = []
    for c, b, o in _cores():
        in2.append(dict(qT=r1[c]["q_o"], kT=np.ascontiguousarray(kT[b]),
                        vF=np.ascontiguousarray(vF[b]),
                        xtm=np.ascontiguousarray(x[b, o:o + TC]), idenb=_EYE_BF))
    r2 = run_bass_kernel_spmd(nc2, in2, core_ids=list(range(NCORE))).results
    out = np.empty((B, S, D), np.float32)
    for c, b, o in _cores():
        out[b, o:o + TC] = r2[c]["xo"]
    return out


def bld_ffn(pre_add=False, pre_proj=False, p=None, ext=None, stage="f"):
    standalone = p is None
    if standalone:
        p = Prog()
        ext = dict(xT=p.inp("xT", [D, TC]), rw1=p.inp("rw1", [D, D]),
                   rw2=p.inp("rw2", [D, 8]), rb1=p.inp("rb1", [D, 1]),
                   rb2=p.inp("rb2", [8, 1]), we1=p.inp("we1", [8, D, 4096]),
                   we2=p.inp("we2", [8, 4096, D]), be1c=p.inp("be1c", [128, 256]),
                   be2r=p.inp("be2r", [8, D]), idf=p.inp("idf", [128, 128]),
                   xoT=p.out("xoT", [D, TC]), sc=base_scratch(p))
        if pre_add:
            ext["oT"] = p.inp("oT", [D, TC])
        if pre_proj:
            ext["oT"] = p.inp("oT", [D, TC], BF16)
            ext["wo"] = p.inp("wo", [D, D])
    xT, rw1, rw2, rb1, rb2 = ext["xT"], ext["rw1"], ext["rw2"], ext["rb1"], ext["rb2"]
    we1, we2, be1c, be2r, idf = ext["we1"], ext["we2"], ext["be1c"], ext["be2r"], ext["idf"]
    xoT, sc = ext["xoT"], ext["sc"]
    if pre_add or pre_proj:
        oT = ext["oT"]
    if pre_proj:
        wo = ext["wo"]
    ocol_dyn = ext.get("ocol_dyn")
    wbf = ext.get("wbf", False)
    wq1 = "qsp" if wbf else "qg"
    wq2 = "qg"
    p.stage_begin(stage)
    sc = stage_sc(p, sc)
    TB = 1024
    x_tb = [p.sb(f"x{i}", [128, TB], F32) for i in range(NCHUNK)]
    xh = [p.sb(f"xh{i}", [128, TB], BF16) for i in range(NCHUNK)]
    r1sl = [p.sb(f"r1_{i}", [128, 512], BF16) for i in range(NCHUNK)]
    rw2_sb = [p.sb(f"rw2_{i}", [128, 8], BF16) for i in range(NCHUNK)]
    rb1_sb = p.sb("rb1_sb", [128, NCHUNK], F32)
    rb2_sb = p.sb("rb2_sb", [8, 1], F32)
    be1_sb = p.sb("be1_sb", [128, 256], F32)
    be2_sb = p.sb("be2_sb", [8, D], BF16)
    lgT = p.sb("lgT", [8, TB], F32)
    lg = [p.sb(f"lg{t}", [128, 8], F32) for t in range(8)]
    wtm = [p.sb(f"wtm{t}", [128, 8], F32) for t in range(8)]
    gt8 = [p.sb(f"gt{t}", [128, 8], F32) for t in range(8)]
    ex8 = [p.sb(f"ex{t}", [128, 8], F32) for t in range(8)]
    nm8 = [p.sb(f"nm{t}", [128, 2], F32) for t in range(8)]
    wT_bf = p.sb("wT_bf", [8, TB], BF16)
    tk = p.sb("tk", [128, 16], F32)
    iden_f = p.sb("iden_f", [128, 128], F32)
    osl = p.sb("osl", [128, 512], F32)
    if pre_proj:
        o_all = p.sb("o_all", [128, NCHUNK * TB], BF16)
        o_fm = [o_all[:, i * TB:(i + 1) * TB] for i in range(NCHUNK)]
    hbuf = [p.sb(f"hb{i}", [128, TB], BF16) for i in range(16)]
    w2buf = [p.sb(f"w2b{i}", [128, D], BF16) for i in range(16)]
    w1col = [p.sb(f"w1c{j}", [128, NCHUNK * 128], BF16) for j in range(4)]
    y_acc = [p.sb(f"ya{t}", [128, D], F32) for t in range(8)]
    ps_h = [p.ps(f"ps_h{j}", [128, 512], F32) for j in range(2)]
    ps_eo = [p.ps(f"ps_eo{j}", [128, 512], F32) for j in range(3)]
    ps_l = p.ps("ps_l", [8, 512], F32)
    ps_t = p.ps("ps_t", [128, 128], F32)

    p.dma("qsp", iden_f[:], idf[:], writes=("idenf",))
    for i in range(NCHUNK):
        p.dma("qg", rw2_sb[i][:], rw2[i * 128:(i + 1) * 128, :], writes=("rw2",))
        p.dma("qsp", rb1_sb[:, i:i + 1], rb1[i * 128:(i + 1) * 128, :], writes=("rbias",))
    p.dma("qsp", rb2_sb[:], rb2[:], writes=("rbias",))
    p.dma("qsp", be1_sb[:], be1c[:], writes=("rbias",))
    p.dma("qg", be2_sb[:], be2r[:], writes=("be2",))

    for tb in range(2):
        tb0 = tb * TB
        for ec in range(NCHUNK):
            p.dma("qsp", x_tb[ec][:], xT[ec * 128:(ec + 1) * 128, tb0:tb0 + TB],
                  writes=(f"x{ec}",))
        if pre_add:
            for ec in range(NCHUNK):
                for si in range(2):
                    sl = slice(si * 512, (si + 1) * 512)
                    p.dma("qsp", osl[:], oT[ec * 128:(ec + 1) * 128,
                                            tb0 + si * 512:tb0 + (si + 1) * 512],
                          writes=("osl",))
                    p.op("dve", lambda e, ec=ec, sl=sl: e.tensor_tensor(
                        out=x_tb[ec][:, sl], in0=x_tb[ec][:, sl], in1=osl[:],
                        op=ALU.add), reads=("osl",), rmw=(f"x{ec}",))
        if pre_proj:
            if ocol_dyn is None:
                for ec in range(NCHUNK):
                    p.dma("qg", o_fm[ec], oT[ec * 128:(ec + 1) * 128, tb0:tb0 + TB],
                          writes=(f"of{ec}",))
            else:
                def _emit(e, tb0=tb0):
                    with e.register() as r0:
                        e.reg_load(r0, ocol_dyn)
                        e.reg_add(r0, r0, tb0)
                        return e.dma_start(
                            out=o_all[:].rearrange("p (ec c) -> p ec c", c=TB),
                            in_=bass.AP(oT.tensor, r0,
                                        [[S, 128], [128 * S, NCHUNK], [1, TB]]))
                p.op("qsp", _emit, reads=("offs",),
                     writes=tuple(f"of{ec}" for ec in range(NCHUNK)))
            for oc in range(NCHUNK):
                wc = w1col[oc % 2]
                wck = f"w1c{oc % 2}"
                p.dma("qg", wc[:].rearrange("p (c w) -> p c w", w=128),
                      wo[:, oc * 128:(oc + 1) * 128].rearrange(
                          "(c p) w -> p c w", p=128), writes=(wck,))
                for qg in range(2):
                    pj = ps_h[qg]
                    pk = f"ps_h{qg}"
                    sl = slice(qg * 512, (qg + 1) * 512)
                    for ec in range(NCHUNK):
                        p.op("pe", lambda e, pj=pj, wc=wc, ec=ec, sl=sl: e.matmul(
                            pj[:], wc[:, ec * 128:(ec + 1) * 128], o_fm[ec][:, sl.start:sl.stop],
                            start=(ec == 0), stop=(ec == NCHUNK - 1)),
                            reads=(f"of{ec}", wck), writes=(pk,))
                    p.op("dve", lambda e, pj=pj, oc=oc, sl=sl: e.tensor_tensor(
                        out=x_tb[oc][:, sl], in0=x_tb[oc][:, sl], in1=pj[:],
                        op=ALU.add), reads=(pk,), rmw=(f"x{oc}",))
        for si in range(2):
            sl = slice(si * 512, (si + 1) * 512)
            xs = [x_tb[ec][:, sl] for ec in range(NCHUNK)]
            ln_slice(p, f"f{tb}{si}", xs, "x", 512,
                     lambda ec, sl=sl: xh[ec][:, sl], "xh", sc)
        for qg in range(2):
            sl = slice(qg * 512, (qg + 1) * 512)
            for eo in range(NCHUNK):
                pj = ps_h[eo % 2]
                pk = f"ps_h{eo % 2}"
                wc = w1col[eo % 2]
                wck = f"w1c{eo % 2}"
                p.dma("qg", wc[:].rearrange("p (c w) -> p c w", w=128),
                      rw1[:, eo * 128:(eo + 1) * 128].rearrange(
                          "(c p) w -> p c w", p=128), writes=(wck,))
                for ec in range(NCHUNK):
                    p.op("pe", lambda e, pj=pj, wc=wc, ec=ec, sl=sl: e.matmul(
                        pj[:], wc[:, ec * 128:(ec + 1) * 128], xh[ec][:, sl],
                        start=(ec == 0), stop=(ec == NCHUNK - 1)),
                        reads=(f"xh{ec}", wck), writes=(pk,))
                p.op("act", lambda e, pj=pj, eo=eo: e.activation(
                    out=r1sl[eo][:], in_=pj[:], func=AF.Silu,
                    bias=rb1_sb[:, eo:eo + 1]),
                    reads=(pk, "rbias"), writes=(f"r1_{eo}",))
            for ec in range(NCHUNK):
                p.op("pe", lambda e, ec=ec: e.matmul(
                    ps_l[:], rw2_sb[ec][:, :], r1sl[ec][:],
                    start=(ec == 0), stop=(ec == NCHUNK - 1)),
                    reads=(f"r1_{ec}", "rw2"), writes=("ps_l",))
            p.op("dve", lambda e, sl=sl: e.tensor_scalar(
                out=lgT[:, sl], in0=ps_l[:], scalar1=rb2_sb[:], scalar2=None,
                op0=ALU.add), reads=("ps_l", "rbias"), writes=("lgT",))
        for tt in range(8):
            p.op("pe", lambda e, tt=tt: e.transpose(
                ps_t[0:128, 0:8], lgT[0:8, tt * 128:(tt + 1) * 128], iden_f[0:8, 0:8]),
                reads=("lgT", "idenf"), writes=("ps_t",))
            p.op("dve", lambda e, tt=tt: e.tensor_copy(lg[tt][:], ps_t[0:128, 0:8]),
                 reads=("ps_t",), writes=(f"lg{tt}",))
        for tt in range(8):
            L, lk = lg[tt], f"lg{tt}"
            m1 = tk[:, tt:tt + 1]
            m2 = tk[:, 8 + tt:9 + tt]
            p.op("dve", lambda e, L=L, m1=m1: e.reduce_max(m1, L[:], axis=AX.X),
                 reads=(lk,), writes=(f"tk{tt}",))
            p.op("dve", lambda e, L=L, m1=m1, tt=tt: e.tensor_scalar(
                out=wtm[tt][:], in0=L[:], scalar1=m1, scalar2=-1e30,
                op0=ALU.is_ge, op1=ALU.mult), reads=(lk, f"tk{tt}"), writes=(f"wtm{tt}",))
            p.op("dve", lambda e, L=L, tt=tt: e.tensor_tensor(
                out=wtm[tt][:], in0=L[:], in1=wtm[tt][:], op=ALU.add),
                reads=(lk,), rmw=(f"wtm{tt}",))
            p.op("dve", lambda e, tt=tt, m2=m2: e.reduce_max(
                m2, wtm[tt][:], axis=AX.X),
                reads=(f"wtm{tt}",), writes=(f"tk2_{tt}",))
            gt, ex, nm = gt8[tt], ex8[tt], nm8[tt]
            p.op("dve", lambda e, L=L, m2=m2, gt=gt: e.tensor_single_scalar(
                out=gt[:], in_=L[:], scalar=m2, op=ALU.is_ge),
                reads=(lk, f"tk2_{tt}"), writes=(f"gt{tt}",))
            p.op("dve", lambda e, m1=m1, nm=nm: e.tensor_scalar(
                out=nm[:, 0:1], in0=m1, scalar1=-1.0, scalar2=None, op0=ALU.mult),
                reads=(f"tk{tt}",), writes=(f"nm{tt}",))
            p.op("act", lambda e, L=L, nm=nm, ex=ex: e.activation(
                out=ex[:], in_=L[:], func=AF.Exp, bias=nm[:, 0:1]),
                reads=(lk, f"nm{tt}"), writes=(f"ex{tt}",))
            p.op("dve", lambda e, m2=m2, nm=nm: e.tensor_tensor(
                out=nm[:, 1:2], in0=m2, in1=nm[:, 0:1], op=ALU.add),
                reads=(f"tk2_{tt}",), rmw=(f"nm{tt}",))
            p.op("act", lambda e, nm=nm: e.activation(
                out=nm[:, 1:2], in_=nm[:, 1:2], func=AF.Exp), rmw=(f"nm{tt}",))
            p.op("dve", lambda e, nm=nm: e.tensor_scalar(
                out=nm[:, 1:2], in0=nm[:, 1:2], scalar1=1.0, scalar2=None,
                op0=ALU.add), rmw=(f"nm{tt}",))
            p.op("dve", lambda e, nm=nm: e.reciprocal(nm[:, 1:2], nm[:, 1:2]),
                 rmw=(f"nm{tt}",))
            p.op("dve", lambda e, ex=ex, nm=nm, gt=gt, tt=tt: e.scalar_tensor_tensor(
                out=wtm[tt][:], in0=ex[:], scalar=nm[:, 1:2], in1=gt[:],
                op0=ALU.mult, op1=ALU.mult),
                reads=(f"ex{tt}", f"nm{tt}", f"gt{tt}"), writes=(f"wtm{tt}",))
            p.op("pe", lambda e, tt=tt: e.transpose(
                ps_t[0:8, :], wtm[tt][:], iden_f[:]),
                reads=(f"wtm{tt}", "idenf"), writes=("ps_t",))
            p.op("dve", lambda e, tt=tt: e.tensor_copy(
                wT_bf[0:8, tt * 128:(tt + 1) * 128], ps_t[0:8, 0:128]),
                reads=("ps_t",), writes=("wT",))
        p.dbg(f"lgT{tb}", lgT[:], ("lgT",))
        p.dbg(f"wT{tb}", wT_bf[:], ("wT",))
        p.dbg(f"xh0_{tb}", xh[0][:], ("xh0",))
        p.dbg(f"tk{tb}", tk[:], tuple(f"tk{t}" for t in range(8)) + tuple(f"tk2_{t}" for t in range(8)))
        p.dbg(f"gt0_{tb}", gt8[0][:], ("gt0",))
        p.dbg(f"ex0_{tb}", ex8[0][:], ("ex0",))
        p.dbg(f"nm0_{tb}", nm8[0][:], ("nm0",))
        p.dbg(f"wtm0_{tb}", wtm[0][:], ("wtm0",))
        for tl in range(8):
            for og in range(2):
                jj = (tl * 2 + og) % 3
                pj = ps_eo[jj]
                p.op("pe", lambda e, pj=pj, tl=tl, og=og: e.matmul(
                    pj[:], wT_bf[0:8, tl * 128:(tl + 1) * 128],
                    be2_sb[0:8, og * 512:(og + 1) * 512], start=True, stop=True),
                    reads=("wT", "be2"), writes=(f"ps_eo{jj}",))
                p.op("dve", lambda e, pj=pj, tl=tl, og=og: e.tensor_copy(
                    y_acc[tl][:, og * 512:(og + 1) * 512], pj[:]),
                    reads=(f"ps_eo{jj}",), writes=(f"ya{tl}",))
        p.dbg(f"yinit{tb}", y_acc[0][:], ("ya0",))
        for exi in range(8):
            for hh in range(2):
                for hc in range(16):
                    g = hh * 16 + hc
                    p.dma(wq2, w2buf[hc][:],
                          we2[exi, g * 128:(g + 1) * 128, :], writes=(f"w2b{hc}",))
                for hc in range(16):
                    g = hh * 16 + hc
                    wc = w1col[hc % 4]
                    wck = f"w1c{hc % 4}"
                    p.dma(wq1, wc[:].rearrange("p (c w) -> p c w", w=128),
                          we1[exi, :, g * 128:(g + 1) * 128].rearrange(
                              "(c p) w -> p c w", p=128), writes=(wck,))
                    for qg in range(2):
                        pj = ps_h[qg]
                        pk = f"ps_h{qg}"
                        sl = slice(qg * 512, (qg + 1) * 512)
                        for ec in range(NCHUNK):
                            p.op("pe", lambda e, pj=pj, wc=wc, ec=ec, sl=sl: e.matmul(
                                pj[:], wc[:, ec * 128:(ec + 1) * 128], xh[ec][:, sl],
                                start=(ec == 0), stop=(ec == NCHUNK - 1)),
                                reads=(f"xh{ec}", wck), writes=(pk,))
                        p.op("act", lambda e, pj=pj, hc=hc, sl=sl, exi=exi, g=g:
                             e.activation(
                                 out=hbuf[hc][:, sl], in_=pj[:], func=AF.Silu,
                                 bias=be1_sb[:, exi * 32 + g:exi * 32 + g + 1]),
                             reads=(pk, "rbias"), writes=(f"hb{hc}",))
                for tl in range(8):
                    for og in range(2):
                        jj = (tl * 2 + og) % 3
                        pj = ps_eo[jj]
                        pk = f"ps_eo{jj}"
                        for hc in range(16):
                            p.op("pe", lambda e, pj=pj, hc=hc, tl=tl, og=og: e.matmul(
                                pj[:], hbuf[hc][:, tl * 128:(tl + 1) * 128],
                                w2buf[hc][:, og * 512:(og + 1) * 512],
                                start=(hc == 0), stop=(hc == 15)),
                                reads=(f"hb{hc}", f"w2b{hc}"), writes=(pk,))
                        p.op("dve", lambda e, pj=pj, pk=pk, tl=tl, og=og, exi=exi:
                             e.scalar_tensor_tensor(
                                 out=y_acc[tl][:, og * 512:(og + 1) * 512], in0=pj[:],
                                 scalar=wtm[tl][:, exi:exi + 1],
                                 in1=y_acc[tl][:, og * 512:(og + 1) * 512],
                                 op0=ALU.mult, op1=ALU.add),
                             reads=(pk, f"wtm{tl}"), rmw=(f"ya{tl}",))
        p.dbg(f"yfin{tb}", y_acc[0][:], ("ya0",))
        p.dbg(f"hb0_{tb}", hbuf[0][:], ("hb0",))
        for tl in range(8):
            for oc in range(NCHUNK):
                p.op("pe", lambda e, tl=tl, oc=oc: e.transpose(
                    ps_t[:], y_acc[tl][:, oc * 128:(oc + 1) * 128], iden_f[:]),
                    reads=(f"ya{tl}", "idenf"), writes=("ps_t",))
                p.op("dve", lambda e, tl=tl, oc=oc: e.tensor_tensor(
                    out=x_tb[oc][:, tl * 128:(tl + 1) * 128],
                    in0=x_tb[oc][:, tl * 128:(tl + 1) * 128],
                    in1=ps_t[:], op=ALU.add),
                    reads=("ps_t",), rmw=(f"x{oc}",))
                p.op("pool", lambda e, oc=oc, tl=tl: e.tensor_tensor(
                    out=x_tb[oc][:, tl * 128:(tl + 1) * 128],
                    in0=x_tb[oc][:, tl * 128:(tl + 1) * 128],
                    in1=xh[oc][:, tl * 128:(tl + 1) * 128], op=ALU.add),
                    reads=(f"xh{oc}",), rmw=(f"x{oc}",))
        for ec in range(NCHUNK):
            p.dma("qsp", xoT[ec * 128:(ec + 1) * 128, tb0:tb0 + TB], x_tb[ec][:],
                  reads=(f"x{ec}",), writes=("xout",))
    p.stage_end()
    return p.finalize() if standalone else None


def host_ffn(x, rw1, rb1, rw2, rb2, we1, be1, we2, be2, o=None):
    nc = bld_ffn(pre_add=o is not None)
    be1c = np.ascontiguousarray(be1.reshape(8, 32, 128).transpose(2, 0, 1).reshape(128, 256))
    idf = np.eye(128, dtype=np.float32)
    ins = []
    for c, b, ofs in _cores():
        d = dict(xT=np.ascontiguousarray(x[b, ofs:ofs + TC].T),
                 rw1=rw1, rw2=rw2, rb1=np.ascontiguousarray(rb1.reshape(D, 1)),
                 rb2=np.ascontiguousarray(rb2.reshape(8, 1)),
                 we1=we1, we2=we2, be1c=be1c, be2r=be2, idf=idf)
        if o is not None:
            d["oT"] = np.ascontiguousarray(o[b, ofs:ofs + TC].T)
        ins.append(d)
    r = run_bass_kernel_spmd(nc, ins, core_ids=list(range(NCORE))).results
    out = np.empty((B, S, D), np.float32)
    for c, b, ofs in _cores():
        out[b, ofs:ofs + TC] = r[c]["xoT"].T
    return out


def bld_b1(p=None, ext=None):
    standalone = p is None
    if standalone:
        p = Prog()
        ext = dict(xT=p.inp("xT", [D, TC]), wqp=p.inp("wqp", [D, D]),
                   wkp=p.inp("wkp", [D, D]), wv=p.inp("wv", [D, D]),
                   cosd=p.inp("cosd", [TC, 512]), sind=p.inp("sind", [TC, 512]),
                   idb=p.inp("idb", [128, 128], BF16),
                   q_o=p.out("q_o", [D, TC], BF16), k_o=p.out("k_o", [D, TC], BF16),
                   v_o=p.out("v_o", [D, TC], BF16), sc=base_scratch(p))
    xT, wqp, wkp, wv = ext["xT"], ext["wqp"], ext["wkp"], ext["wv"]
    cosd, sind = ext["cosd"], ext["sind"]
    q_o, k_o, v_o, sc = ext["q_o"], ext["k_o"], ext["v_o"], ext["sc"]
    p.stage_begin("b1")
    sc = stage_sc(p, sc)
    idb_sb = p.sb("idb_sb", [128, 128], BF16)
    p.dma("qg", idb_sb[:], ext["idb"][:], writes=("idb",))
    ps_t2 = [p.ps(f"ps_t2{j}", [128, 128], BF16) for j in range(2)]
    tstg = p.sb("tstg", [128, 128], BF16)
    xs = [p.sb(f"xs{i}", [128, 512], F32) for i in range(NCHUNK)]
    xh = [p.sb(f"xh{i}", [128, 512], BF16) for i in range(NCHUNK)]
    wq_sb = [p.sb(f"wq{i}", [128, D], BF16) for i in range(NCHUNK)]
    wk_sb = [p.sb(f"wk{i}", [128, D], BF16) for i in range(NCHUNK)]
    wv_sb = [p.sb(f"wv{i}", [128, D], BF16) for i in range(NCHUNK)]
    qtm = p.sb("qtm", [128, D], F32)
    ktm = p.sb("ktm", [128, D], F32)
    rtm = p.sb("rtm", [128, D], F32)
    rtmb = p.sb("rtmb", [128, D], BF16)
    cs = p.sb("cs", [128, 512], F32)
    sn = p.sb("sn", [128, 512], F32)
    tmp = p.sb("tmp", [128, 512], F32)
    tmp2 = p.sb("tmp2", [128, 512], F32)
    stg = [p.sb(f"stg{j}", [128, 512], BF16) for j in range(2)]
    pp = [p.ps(f"pp{j}", [128, 512], F32) for j in range(4)]
    for i in range(NCHUNK):
        p.dma("qg", wq_sb[i][:], wqp[i * 128:(i + 1) * 128, :], writes=("wq",))
        p.dma("qg", wk_sb[i][:], wkp[i * 128:(i + 1) * 128, :], writes=("wk",))
        p.dma("qg", wv_sb[i][:], wv[i * 128:(i + 1) * 128, :], writes=("wv",))

    def rope(src_tile, src_key, tt):
        EV, OD = src_tile[:, 0:512], src_tile[:, 512:1024]
        RE, RO = rtm[:, 0:512], rtm[:, 512:1024]
        p.op("dve", lambda e: e.tensor_tensor(out=RE, in0=EV, in1=cs[:], op=ALU.mult),
             reads=(src_key, "cs"), writes=("rtm",))
        p.op("dve", lambda e: e.tensor_tensor(out=tmp[:], in0=OD, in1=sn[:], op=ALU.mult),
             reads=(src_key, "sn"), writes=("tmp",))
        p.op("dve", lambda e: e.tensor_tensor(out=RE, in0=RE, in1=tmp[:], op=ALU.subtract),
             reads=("tmp",), rmw=("rtm",))
        p.op("pool", lambda e: e.tensor_tensor(out=RO, in0=EV, in1=sn[:], op=ALU.mult),
             reads=(src_key, "sn"), writes=("rtm2",))
        p.op("pool", lambda e: e.tensor_tensor(out=tmp2[:], in0=OD, in1=cs[:], op=ALU.mult),
             reads=(src_key, "cs"), writes=("tmp2",))
        p.op("pool", lambda e: e.tensor_tensor(out=RO, in0=RO, in1=tmp2[:], op=ALU.add),
             reads=("tmp2",), rmw=("rtm2",))

    for si in range(4):
        sl = slice(si * 512, (si + 1) * 512)
        for ec in range(NCHUNK):
            p.dma("qsp", xs[ec][:], xT[ec * 128:(ec + 1) * 128, sl], writes=(f"xs{ec}",))
        ln_slice(p, f"s{si}", xs, "xs", 512, lambda ec: xh[ec][:], "xh", sc)
        for tt in range(4):
            trow = si * 512 + tt * 128
            for (wt, wkey, dst_t, dkey) in ((wq_sb, "wq", qtm, "qtm"),
                                            (wk_sb, "wk", ktm, "ktm")):
                for fg in range(2):
                    pj = pp[fg]
                    for ec in range(NCHUNK):
                        p.op("pe", lambda e, pj=pj, wt=wt, ec=ec, tt=tt, fg=fg: e.matmul(
                            pj[:], xh[ec][:, tt * 128:(tt + 1) * 128],
                            wt[ec][:, fg * 512:(fg + 1) * 512],
                            start=(ec == 0), stop=(ec == NCHUNK - 1)),
                            reads=(f"xh{ec}", wkey), writes=(f"pp{fg}",))
                    p.op("dve", lambda e, pj=pj, dst_t=dst_t, fg=fg: e.tensor_copy(
                        dst_t[:, fg * 512:(fg + 1) * 512], pj[:]),
                        reads=(f"pp{fg}",), writes=(dkey,))
            p.dma("qsp", cs[:], cosd[trow:trow + 128, :], writes=("cs",))
            p.dma("qsp", sn[:], sind[trow:trow + 128, :], writes=("sn",))
            for (t_t, tkey, tkey2, dst) in ((qtm, "qtm", "rtm", q_o),
                                            (ktm, "ktm", "rtm", k_o)):
                rope(t_t, tkey, tt)
                p.op("dve", lambda e: e.tensor_copy(rtmb[:, 0:512], rtm[:, 0:512]),
                     reads=("rtm",), writes=("rtmb",))
                p.op("pool", lambda e: e.tensor_copy(rtmb[:, 512:1024], rtm[:, 512:1024]),
                     reads=("rtm2",), rmw=("rtmb",))
                for f in range(8):
                    pj = ps_t2[f % 2]
                    p.op("pe", lambda e, pj=pj, f=f: e.transpose(
                        pj[:], rtmb[:, f * 128:(f + 1) * 128], idb_sb[:]),
                        reads=("rtmb", "idb"), writes=(f"ps_t2{f % 2}",))
                    p.op("dve", lambda e, pj=pj: e.tensor_copy(tstg[:], pj[:]),
                         reads=(f"ps_t2{f % 2}",), writes=("tstg",))
                    for hp in range(4):
                        head = (f % 4) * 4 + hp
                        roff = head * 64 + (0 if f < 4 else 32)
                        p.dma("qsp", dst[roff:roff + 32, trow:trow + 128],
                              tstg[hp * 32:(hp + 1) * 32, :],
                              reads=("tstg",), writes=("out",))
            for vg in range(2):
                pj = pp[2 + vg]
                for ec in range(NCHUNK):
                    p.op("pe", lambda e, pj=pj, ec=ec, tt=tt, vg=vg: e.matmul(
                        pj[:], xh[ec][:, tt * 128:(tt + 1) * 128],
                        wv_sb[ec][:, vg * 512:(vg + 1) * 512],
                        start=(ec == 0), stop=(ec == NCHUNK - 1)),
                        reads=(f"xh{ec}", "wv"), writes=(f"pp{2 + vg}",))
                p.op("dve", lambda e, pj=pj, vg=vg: e.tensor_copy(stg[vg][:], pj[:]),
                     reads=(f"pp{2 + vg}",), writes=(f"stg{vg}",))
                for fq in range(4):
                    f = vg * 4 + fq
                    pj2 = ps_t2[fq % 2]
                    p.op("pe", lambda e, pj2=pj2, vg=vg, fq=fq: e.transpose(
                        pj2[:], stg[vg][:, fq * 128:(fq + 1) * 128], idb_sb[:]),
                        reads=(f"stg{vg}", "idb"), writes=(f"ps_t2{fq % 2}",))
                    p.op("dve", lambda e, pj2=pj2: e.tensor_copy(tstg[:], pj2[:]),
                         reads=(f"ps_t2{fq % 2}",), writes=("tstg",))
                    p.dma("qsp", v_o[f * 128:(f + 1) * 128, trow:trow + 128], tstg[:],
                          reads=("tstg",), writes=("out",))
    p.stage_end()
    return p.finalize() if standalone else None


def bld_b2(p=None, ext=None):
    standalone = p is None
    if standalone:
        p = Prog()
        ext = dict(qb=p.inp("qb", [4, 64, S], BF16), kb=p.inp("kb", [4, 64, S], BF16),
                   vb=p.inp("vb", [4, 64, S], BF16), triA=p.inp("triA", [128, 256], BF16),
                   triB=p.inp("triB", [128, 256], BF16),
                   idb=p.inp("idb2", [128, 128], BF16),
                   ob=p.out("ob", [4, 64, S], BF16), sc=base_scratch(p))
    triA, triB, ob, sc = ext["triA"], ext["triB"], ext["ob"], ext["sc"]
    fused = "qkvg" in ext
    if not fused:
        qb, kb, vb = ext["qb"], ext["kb"], ext["vb"]
    else:
        qkvg, ho = ext["qkvg"], ext["ho"]
    p.stage_begin("b2")
    idb_sb = p.sb("idb_sb", [128, 128], BF16)
    p.dma("qg", idb_sb[:], ext["idb"][:], writes=("idb",))
    hosb = p.sb("hosb", [1, 4], mybir.dt.int32)
    if fused:
        p.dma("qsp", hosb[:], ho[:], writes=("offs",))
    qt = p.sb("qt", [64, S], BF16)
    kt = p.sb("kt", [64, S], BF16)
    vfm = p.sb("vfm", [64, S], BF16)
    ps_vt = [p.ps("ps_vt0", [128, 64], BF16)] * 2
    ps_bc = p.ps("ps_bc", [64, 256], F32)
    ones64 = p.sb("ones64", [128, 64], F32)
    vt = p.sb("vt", [128, 64 * 65], BF16)
    tA = p.sb("tA", [128, 256], BF16)
    tB = p.sb("tB", [128, 256], BF16)
    pt = [p.sb(f"pt{j}", [128, 256], BF16) for j in range(3)]
    rd = p.sb("rd", [128, 256], F32)
    rdb = p.sb("rdb", [64, 256], F32)
    o_sb = [p.sb(f"osb{j}", [64, 256], BF16) for j in range(3)]
    ps_sc = [p.ps(f"ps_sc{j}", [128, 256], F32) for j in range(3)]
    ps_o = [p.ps(f"ps_o{j}", [65, 256], F32) for j in range(3)]
    p.dma("qsp", tA[:], triA[:], writes=("tri",))
    p.dma("qsp", tB[:], triB[:], writes=("tri",))
    p.op("dve", lambda e: e.memset(ones64[:], 1.0), writes=("ones64",))
    NQG = S // 256
    for pair in range(4):
        p.op("dve", lambda e: e.memset(vt[:], 1.0), writes=("vt",))
        if not fused:
            p.dma("qg", qt[:], qb[pair], writes=("qt",))
            p.dma("qg", kt[:], kb[pair], writes=("kt",))
            p.dma("qg", vfm[:], vb[pair], writes=("vfm",))
        else:
            for (dst, bofs, dk) in ((qt, 0, "qt"), (kt, D * TC, "kt"),
                                    (vfm, 2 * D * TC, "vfm")):
                def _emq(e, dst=dst, bofs=bofs, pair=pair):
                    with e.register() as r0:
                        e.reg_load(r0, hosb[0:1, pair:pair + 1])
                        e.reg_add(r0, r0, bofs)
                        return e.dma_start(
                            out=dst[:].rearrange("p (j c) -> p j c", c=TC),
                            in_=bass.AP(qkvg.tensor, r0,
                                        [[TC, 64], [3 * D * TC, 4], [1, TC]]))
                p.op("qact", _emq, reads=("offs",), writes=(dk,))
        for kti in range(64):
            pj = ps_vt[0]
            p.op("pe", lambda e, pj=pj, kti=kti: e.transpose(
                pj[:], vfm[0:64, kti * 128:(kti + 1) * 128], idb_sb[0:64, 0:64]),
                reads=("vfm", "idb"), writes=("ps_vt0",))
            p.op("dve", lambda e, pj=pj, kti=kti: e.tensor_copy(
                vt[:, kti * 65:kti * 65 + 64], pj[:]),
                reads=("ps_vt0",), rmw=("vt",))
        for qg in range(NQG):
            nkt = 2 * qg + 2
            po = ps_o[qg % 3]
            pok = f"ps_o{qg % 3}"
            for kti in range(nkt):
                j = kti % 3
                pj = ps_sc[j]
                pk = f"ps_sc{j}"
                ptj = pt[j]
                ptk = f"pt{j}"
                p.op("pe", lambda e, pj=pj, kti=kti, qg=qg: e.matmul(
                    pj[:], kt[:, kti * 128:(kti + 1) * 128],
                    qt[:, qg * 256:(qg + 1) * 256], start=True, stop=True),
                    reads=("kt", "qt"), writes=(pk,))
                p.op("act", lambda e, pj=pj, ptj=ptj: e.activation(
                    out=ptj[:], in_=pj[:], func=AF.Exp, bias=sc["cm24"][:, 0:1],
                    scale=0.125), reads=(pk, "consts"), writes=(ptk,))
                if kti == nkt - 2:
                    p.op("dve", lambda e, ptj=ptj: e.tensor_tensor(
                        out=ptj[:], in0=ptj[:], in1=tA[:], op=ALU.mult),
                        reads=("tri",), rmw=(ptk,))
                elif kti == nkt - 1:
                    p.op("dve", lambda e, ptj=ptj: e.tensor_tensor(
                        out=ptj[:], in0=ptj[:], in1=tB[:], op=ALU.mult),
                        reads=("tri",), rmw=(ptk,))
                p.op("pe", lambda e, po=po, kti=kti, ptj=ptj: e.matmul(
                    po[:], vt[:, kti * 65:(kti + 1) * 65], ptj[:],
                    start=(kti == 0), stop=(kti == nkt - 1)),
                    reads=("vt", ptk), writes=(pok,))
            p.op("dve", lambda e, po=po: e.reciprocal(rd[64:65, :], po[64:65, :]),
                 reads=(pok,), writes=("rd",))
            p.op("pe", lambda e: e.matmul(
                ps_bc[0:64, :], ones64[64:65, 0:64], rd[64:65, :],
                start=True, stop=True), reads=("rd", "ones64"), writes=("ps_bc",))
            p.op("dve", lambda e: e.tensor_copy(rdb[0:64, :], ps_bc[0:64, :]),
                 reads=("ps_bc",), writes=("rdb",))
            osb = o_sb[qg % 3]
            p.op("dve", lambda e, po=po, osb=osb: e.tensor_tensor(
                out=osb[:], in0=po[0:64, :], in1=rdb[0:64, :], op=ALU.mult),
                reads=(pok, "rdb"), writes=(f"osb{qg % 3}",))
            p.dma("qsp", ob[pair][:, qg * 256:(qg + 1) * 256], osb[:],
                  reads=(f"osb{qg % 3}",), writes=("out",))
    p.stage_end()
    return p.finalize() if standalone else None


_ROPE_PERM = None


def _rope_perm():
    global _ROPE_PERM
    if _ROPE_PERM is None:
        perm = []
        for h in range(16):
            for i in range(32):
                perm.append(h * 64 + 2 * i)
        for h in range(16):
            for i in range(32):
                perm.append(h * 64 + 2 * i + 1)
        _ROPE_PERM = np.array(perm)
    return _ROPE_PERM


def _rope_tables(offset, n, theta=10000.0):
    inv = 1.0 / theta ** (np.arange(0, 64, 2, dtype=np.float64) / 64)
    pos = np.arange(offset, offset + n, dtype=np.float64)
    ang = pos[:, None] * inv[None, :]
    cos = np.cos(ang).astype(np.float32)
    sin = np.sin(ang).astype(np.float32)
    return np.tile(cos, (1, 16)), np.tile(sin, (1, 16))


def host_attn_b(x, wq, wk, wv, wo, rw1, rb1, rw2, rb2, we1, be1, we2, be2):
    perm = _rope_perm()
    wqp = np.ascontiguousarray(wq[:, perm])
    wkp = np.ascontiguousarray(wk[:, perm])
    nc1 = bld_b1()
    in1 = []
    for c, b, o in _cores():
        cos, sin = _rope_tables(o, TC)
        in1.append(dict(xT=np.ascontiguousarray(x[b, o:o + TC].T), wqp=wqp, wkp=wkp,
                        wv=wv, cosd=cos, sind=sin, idb=_EYE_BF))
    r1 = run_bass_kernel_spmd(nc1, in1, core_ids=list(range(NCORE))).results
    qf = {b: np.concatenate([r1[c]["q_o"] for c, bb, o in _cores() if bb == b], axis=1)
          for b in range(B)}
    kf = {b: np.concatenate([r1[c]["k_o"] for c, bb, o in _cores() if bb == b], axis=1)
          for b in range(B)}
    vf = {b: np.concatenate([r1[c]["v_o"] for c, bb, o in _cores() if bb == b], axis=1)
          for b in range(B)}
    nc2 = bld_b2()
    r_ = np.arange(128)[:, None]
    c_ = np.arange(256)[None, :]
    triA = (c_ >= r_).astype(BF)
    triB = (c_ >= r_ + 128).astype(BF)
    in2 = []
    for c in range(NCORE):
        qb = np.empty((4, 64, S), BF)
        kb = np.empty((4, 64, S), BF)
        vb = np.empty((4, 64, S), BF)
        for j in range(4):
            pair = c * 4 + j
            b, h = pair // 16, pair % 16
            qb[j] = qf[b][h * 64:(h + 1) * 64, :]
            kb[j] = kf[b][h * 64:(h + 1) * 64, :]
            vb[j] = vf[b][h * 64:(h + 1) * 64, :]
        in2.append(dict(qb=qb, kb=kb, vb=vb, triA=triA, triB=triB, idb2=_EYE_BF))
    r2 = run_bass_kernel_spmd(nc2, in2, core_ids=list(range(NCORE))).results
    of = {b: np.empty((S, D), BF) for b in range(B)}
    for c in range(NCORE):
        for j in range(4):
            pair = c * 4 + j
            b, h = pair // 16, pair % 16
            of[b][:, h * 64:(h + 1) * 64] = r2[c]["ob"][j].T
    nc3 = bld_ffn(pre_proj=True)
    be1c = np.ascontiguousarray(
        be1.reshape(8, 32, 128).transpose(2, 0, 1).reshape(128, 256))
    idf = np.eye(128, dtype=np.float32)
    in3 = []
    for c, b, o in _cores():
        in3.append(dict(xT=np.ascontiguousarray(x[b, o:o + TC].T),
                        oT=np.ascontiguousarray(of[b][o:o + TC].T), wo=wo,
                        rw1=rw1, rw2=rw2, rb1=np.ascontiguousarray(rb1.reshape(D, 1)),
                        rb2=np.ascontiguousarray(rb2.reshape(8, 1)),
                        we1=we1, we2=we2, be1c=be1c, be2r=be2, idf=idf))
    r3 = run_bass_kernel_spmd(nc3, in3, core_ids=list(range(NCORE))).results
    out = np.empty((B, S, D), np.float32)
    for c, b, o in _cores():
        out[b, o:o + TC] = r3[c]["xoT"].T
    return out


def bld_cffn(p=None, ext=None, stage="c"):
    standalone = p is None
    if standalone:
        p = Prog()
        ext = dict(xT=p.inp("xT", [D, TC]), G1R=p.inp("G1R", [32, 128]),
                   G2R=p.inp("G2R", [64, 16]), G1I=p.inp("G1I", [32, 128]),
                   G2I=p.inp("G2I", [64, 16]), brc=p.inp("brc", [512, 1]),
                   bic=p.inp("bic", [512, 1]), idf=p.inp("idf", [128, 128]),
                   xoT=p.out("xoT", [D, TC]), sc=base_scratch(p))
    xT, G1R, G2R, G1I, G2I = ext["xT"], ext["G1R"], ext["G2R"], ext["G1I"], ext["G2I"]
    brc, bic, idf, xoT, sc = ext["brc"], ext["bic"], ext["idf"], ext["xoT"], ext["sc"]
    p.stage_begin(stage)
    sc = stage_sc(p, sc)
    iden_f = p.sb("iden_f", [128, 128], F32)
    g1 = p.sb("g1", [32, 128], F32)
    g2 = p.sb("g2", [64, 16], F32)
    g1t = p.sb("g1t", [128, 32], F32)
    g2t = p.sb("g2t", [16, 64], F32)
    urow = p.sb("urow", [1, 1024], F32)
    vrow = p.sb("vrow", [1, 256], F32)
    ot = [p.sb(f"ot{j}", [128, 256], F32) for j in range(8)]
    wr_sb = [p.sb(f"wr{j}", [128, 512], BF16) for j in range(4)]
    wi_sb = [p.sb(f"wi{j}", [128, 512], BF16) for j in range(4)]
    wni_sb = [p.sb(f"wni{j}", [128, 512], BF16) for j in range(4)]
    b_sb = p.sb("b_sb", [128, NCHUNK], F32)
    x_fm = [p.sb(f"x{i}", [128, TC], F32) for i in range(NCHUNK)]
    xh = [p.sb(f"xh{i}", [128, TC], BF16) for i in range(NCHUNK)]
    ps_t = p.ps("ps_t", [128, 128], F32)
    ps_w = [p.ps(f"ps_w{j}", [128, 256], F32) for j in range(4)]
    ps_o = [p.ps(f"ps_o{j}", [128, 512], F32) for j in range(2)]
    g1d = p.dram("g1d", [128, 32])
    g2d = p.dram("g2d", [16, 64])
    outer_d = p.dram("outer_d", [1024, 256])

    p.dma("qsp", iden_f[:], idf[:], writes=("idenf",))
    for i in range(4):
        p.dma("qsp", b_sb[:, i:i + 1], brc[i * 128:(i + 1) * 128, :], writes=("bias",))
        p.dma("qsp", b_sb[:, 4 + i:5 + i], bic[i * 128:(i + 1) * 128, :], writes=("bias",))
    for ec in range(NCHUNK):
        p.dma("qsp", x_fm[ec][:], xT[ec * 128:(ec + 1) * 128, :], writes=(f"x{ec}",))
    def build_w(G1, G2, dst, dkey, negdst=None, negkey=None):
        p.dma("qsp", g1[:], G1[:], writes=("g1",))
        p.dma("qsp", g2[:], G2[:], writes=("g2",))
        p.op("pe", lambda e: e.transpose(ps_t[0:128, 0:32], g1[:], iden_f[0:32, 0:32]),
             reads=("g1", "idenf"), writes=("ps_t",))
        p.op("dve", lambda e: e.tensor_copy(g1t[:], ps_t[0:128, 0:32]),
             reads=("ps_t",), writes=("g1t",))
        p.op("pe", lambda e: e.transpose(ps_t[0:16, 0:64], g2[:], iden_f[0:64, 0:64]),
             reads=("g2", "idenf"), writes=("ps_t",))
        p.op("dve", lambda e: e.tensor_copy(g2t[:], ps_t[0:16, 0:64]),
             reads=("ps_t",), writes=("g2t",))
        p.dma("qsp", g1d[:], g1t[:], reads=("g1t",), writes=("g1d",))
        p.dma("qsp", g2d[:], g2t[:], reads=("g2t",), writes=("g2d",))
        for wave in range(2):
            for c in range(4):
                p.dma("qsp", urow[:],
                      bass.AP(g1d.tensor, c * 32 * 32, [[1, 1], [1, 1024]]),
                      reads=("g1d",), writes=("urow",))
                p.dma("qsp", vrow[:],
                      bass.AP(g2d.tensor, c * 16, [[1, 1], [64, 16], [1, 16]]),
                      reads=("g2d",), writes=("vrow",))
                p.dbg(f"urow{dkey}{c}", urow[:], ("urow",))
                p.dbg(f"vrow{dkey}{c}", vrow[:], ("vrow",))
                for j in range(4):
                    u8 = wave * 4 + j
                    pj = ps_w[j]
                    p.op("pe", lambda e, pj=pj, u8=u8, c=c: e.matmul(
                        pj[:], urow[0:1, u8 * 128:(u8 + 1) * 128], vrow[0:1, :],
                        start=(c == 0), stop=(c == 3)),
                        reads=("urow", "vrow"), writes=(f"ps_w{j}",))
                    if c == 3:
                        p.op("dve", lambda e, pj=pj, u8=u8, j=j: e.tensor_copy(
                            ot[u8][:], pj[:]),
                            reads=(f"ps_w{j}",), writes=(f"ot{u8}",))
        for u8 in range(8):
            p.dma("qsp", outer_d[u8 * 128:(u8 + 1) * 128, :], ot[u8][:],
                  reads=(f"ot{u8}",), writes=("outer_d",))
        for qc in range(4):
            for dd in range(8):
                row0 = qc * 128 + dd * 16
                d0 = row0 // 16
                p.dma("qg", dst[qc][dd * 16:(dd + 1) * 16, :].rearrange(
                    "p (b e) -> p b e", e=16),
                    bass.AP(outer_d.tensor, d0 * 32 * 256,
                            [[16, 16], [256, 32], [1, 16]]),
                    reads=("outer_d",), writes=(dkey,))
        if negdst is not None:
            for qc in range(4):
                p.op("dve", lambda e, qc=qc: e.tensor_scalar(
                    out=negdst[qc][:], in0=dst[qc][:], scalar1=-1.0, scalar2=None,
                    op0=ALU.mult), reads=(dkey,), writes=(negkey,))

    for si in range(4):
        sl = slice(si * 512, (si + 1) * 512)
        xs = [x_fm[ec][:, sl] for ec in range(NCHUNK)]
        ln_slice(p, f"c{si}", xs, "x", 512, lambda ec, sl=sl: xh[ec][:, sl], "xh", sc)
    build_w(G1R, G2R, wr_sb, "wr")
    build_w(G1I, G2I, wi_sb, "wi", wni_sb, "wni")
    for qc in range(4):
        p.dbg(f"wr{qc}", wr_sb[qc][:], ("wr",))
        p.dbg(f"wi{qc}", wi_sb[qc][:], ("wi",))

    for half, (wa, wak, wb, wbk) in enumerate(
            ((wr_sb, "wr", wni_sb, "wni"), (wi_sb, "wi", wr_sb, "wr"))):
        for rc in range(4):
            oc = half * 4 + rc
            for qg in range(4):
                sl = slice(qg * 512, (qg + 1) * 512)
                pj = ps_o[qg % 2]
                pk = f"ps_o{qg % 2}"
                for qc in range(4):
                    p.op("pe", lambda e, pj=pj, wa=wa, qc=qc, rc=rc, sl=sl: e.matmul(
                        pj[:], wa[qc][:, rc * 128:(rc + 1) * 128], xh[qc][:, sl],
                        start=(qc == 0), stop=False),
                        reads=(f"xh{qc}", wak), writes=(pk,))
                for qc in range(4):
                    p.op("pe", lambda e, pj=pj, wb=wb, qc=qc, rc=rc, sl=sl: e.matmul(
                        pj[:], wb[qc][:, rc * 128:(rc + 1) * 128], xh[4 + qc][:, sl],
                        start=False, stop=(qc == 3)),
                        reads=(f"xh{4 + qc}", wbk), writes=(pk,))
                p.op("dve", lambda e, pj=pj, oc=oc, sl=sl: e.scalar_tensor_tensor(
                    out=x_fm[oc][:, sl], in0=pj[:], scalar=b_sb[:, oc:oc + 1],
                    in1=x_fm[oc][:, sl], op0=ALU.add, op1=ALU.add),
                    reads=(pk, "bias"), rmw=(f"x{oc}",))
    for ec in range(NCHUNK):
        p.dma("qsp", xoT[ec * 128:(ec + 1) * 128, :], x_fm[ec][:],
              reads=(f"x{ec}",), writes=("xout",))
    p.stage_end()
    return p.finalize() if standalone else None


def host_cffn(x, G1R, G2R, G1I, G2I, br, bi):
    nc = bld_cffn()
    idf = np.eye(128, dtype=np.float32)
    g1r = np.ascontiguousarray(G1R.reshape(32, 128))
    g2r = np.ascontiguousarray(G2R.reshape(64, 16))
    g1i = np.ascontiguousarray(G1I.reshape(32, 128))
    g2i = np.ascontiguousarray(G2I.reshape(64, 16))
    ins = []
    for c, b, o in _cores():
        ins.append(dict(xT=np.ascontiguousarray(x[b, o:o + TC].T),
                        G1R=g1r, G2R=g2r, G1I=g1i, G2I=g2i,
                        brc=np.ascontiguousarray(br.reshape(512, 1)),
                        bic=np.ascontiguousarray(bi.reshape(512, 1)), idf=idf))
    r = run_bass_kernel_spmd(nc, ins, core_ids=list(range(NCORE))).results
    out = np.empty((B, S, D), np.float32)
    for c, b, o in _cores():
        out[b, o:o + TC] = r[c]["xoT"].T
    return out


def bld_n1(p=None, ext=None):
    standalone = p is None
    if standalone:
        p = Prog()
        ext = dict(xT=p.inp("xT", [D, TC]), Wh=p.inp("Wh", [D, 4096]),
                   bhr=p.inp("bhr", [1, 4096]), Wqk=p.inp("Wqk", [D, 128]),
                   bqk=p.inp("bqk", [128, 1]), lng=p.inp("lng", [D, 1]),
                   lnb=p.inp("lnb", [D, 1]), gb=p.inp("gb", [128, 4]),
                   dnr=p.inp("dnr", [1, TC]), v_o=p.out("v_o", [TC, 2048], BF16),
                   g_o=p.out("g_o", [TC, 2048], BF16), q_o=p.out("q_o", [128, TC], BF16),
                   k_o=p.out("k_o", [128, TC], BF16),
                   xh1_o=p.out("xh1_o", [D, TC], BF16), sc=base_scratch(p))
    xT, Wh, bhr, Wqk, bqk = ext["xT"], ext["Wh"], ext["bhr"], ext["Wqk"], ext["bqk"]
    lng, lnb, gb, dnr = ext["lng"], ext["lnb"], ext["gb"], ext["dnr"]
    v_o, g_o, q_o, k_o, xh1_o, sc = (ext["v_o"], ext["g_o"], ext["q_o"],
                                     ext["k_o"], ext["xh1_o"], ext["sc"])
    p.stage_begin("n1")
    sc = stage_sc(p, sc)
    xs = [p.sb(f"xs{i}", [128, 512], F32) for i in range(NCHUNK)]
    xh1 = [p.sb(f"xh1_{i}", [128, 512], F32) for i in range(NCHUNK)]
    xh = [p.sb(f"xh{i}", [128, 512], BF16) for i in range(NCHUNK)]
    xh1b = [p.sb(f"xh1b{i}", [128, 512], BF16) for i in range(NCHUNK)]
    wh_sb = [p.sb(f"wh{i}", [128, 4096], BF16) for i in range(NCHUNK)]
    wqk_sb = [p.sb(f"wqk{i}", [128, 128], BF16) for i in range(NCHUNK)]
    bh_bc = p.sb("bh_bc", [128, 4096], F32)
    bqk_sb = p.sb("bqk_sb", [128, 1], F32)
    lng_sb = p.sb("lng_sb", [128, NCHUNK], F32)
    lnb_sb = p.sb("lnb_sb", [128, NCHUNK], F32)
    gb_sb = p.sb("gb_sb", [128, 4], F32)
    dn_bc = p.sb("dn_bc", [128, 512], F32)
    qkf = p.sb("qkf", [128, 512], F32)
    stg = [p.sb(f"stg{j}", [128, 512], BF16) for j in range(4)]
    pp = [p.ps(f"pp{j}", [128, 512], F32) for j in range(4)]
    for i in range(NCHUNK):
        p.dma("qg", wh_sb[i][:], Wh[i * 128:(i + 1) * 128, :], writes=("wh",))
        p.dma("qg", wqk_sb[i][:], Wqk[i * 128:(i + 1) * 128, :], writes=("wqk",))
        p.dma("qsp", lng_sb[:, i:i + 1], lng[i * 128:(i + 1) * 128, :], writes=("aff",))
        p.dma("qsp", lnb_sb[:, i:i + 1], lnb[i * 128:(i + 1) * 128, :], writes=("aff",))
    p.dma("qsp", bh_bc[:], bhr[:].partition_broadcast(128), writes=("bias",))
    p.dma("qsp", bqk_sb[:], bqk[:], writes=("bias",))
    p.dma("qsp", gb_sb[:], gb[:], writes=("bias",))
    for si in range(4):
        sl = slice(si * 512, (si + 1) * 512)
        for ec in range(NCHUNK):
            p.dma("qsp", xs[ec][:], xT[ec * 128:(ec + 1) * 128, sl], writes=(f"xs{ec}",))
        ln_slice(p, f"a{si}", xs, "xs", 512, lambda ec: xh1[ec][:], "xh1", sc)
        ln_slice(p, f"b{si}", xh1, "xh1", 512, lambda ec: xh[ec][:], "xh", sc,
                 affine=(lambda ec: lng_sb[:, ec:ec + 1],
                         lambda ec: lnb_sb[:, ec:ec + 1], "aff"))
        for ec in range(NCHUNK):
            p.op("dve", lambda e, ec=ec, sl=sl: e.tensor_copy(
                xh1b[ec][:], xh1[ec][:]), reads=(f"xh1{ec}",), writes=("x1b",))
            p.dma("qsp", xh1_o[ec * 128:(ec + 1) * 128, sl], xh1b[ec][:],
                  reads=("x1b",), writes=("out",))
        for tt in range(4):
            trow = si * 512 + tt * 128
            for vg in range(8):
                j = vg % 4
                pj = pp[j]
                pk = f"pp{j}"
                for ec in range(NCHUNK):
                    p.op("pe", lambda e, pj=pj, ec=ec, tt=tt, vg=vg: e.matmul(
                        pj[:], xh[ec][:, tt * 128:(tt + 1) * 128],
                        wh_sb[ec][:, vg * 512:(vg + 1) * 512],
                        start=(ec == 0), stop=(ec == NCHUNK - 1)),
                        reads=(f"xh{ec}", "wh"), writes=(pk,))
                p.op("dve", lambda e, pj=pj, j=j, vg=vg: e.tensor_tensor(
                    out=pj[:], in0=pj[:], in1=bh_bc[:, vg * 512:(vg + 1) * 512],
                    op=ALU.add), reads=("bias",), rmw=(pk,))
                p.op("act", lambda e, pj=pj, j=j: e.activation(
                    out=stg[j][:], in_=pj[:], func=AF.Silu),
                    reads=(pk,), writes=(f"stg{j}",))
                dst = v_o if vg < 4 else g_o
                col = (vg % 4) * 512
                p.dma("qsp", dst[trow:trow + 128, col:col + 512], stg[j][:],
                      reads=(f"stg{j}",), writes=("out",))
    for qg in range(4):
        sl = slice(qg * 512, (qg + 1) * 512)
        for ec in range(NCHUNK):
            p.dma("qsp", xs[ec][:], xT[ec * 128:(ec + 1) * 128, sl], writes=(f"xs{ec}",))
        ln_slice(p, f"c{qg}", xs, "xs", 512, lambda ec: xh1[ec][:], "xh1", sc)
        ln_slice(p, f"d{qg}", xh1, "xh1", 512, lambda ec: xh[ec][:], "xh", sc,
                 affine=(lambda ec: lng_sb[:, ec:ec + 1],
                         lambda ec: lnb_sb[:, ec:ec + 1], "aff"))
        pj = pp[0]
        for ec in range(NCHUNK):
            p.op("pe", lambda e, pj=pj, ec=ec, sl=sl: e.matmul(
                pj[:], wqk_sb[ec][:], xh[ec][:],
                start=(ec == 0), stop=(ec == NCHUNK - 1)),
                reads=(f"xh{ec}", "wqk"), writes=("pp0",))
        p.op("act", lambda e, pj=pj: e.activation(
            out=qkf[:], in_=pj[:], func=AF.Silu, bias=bqk_sb[:]),
            reads=("pp0", "bias"), writes=("qkf",))
        bcast_row(p, f"dn{qg}", dnr[0:1, sl], "_inp", 512, dn_bc[:], "dnbc")
        p.op("dve", lambda e: e.tensor_scalar(
            out=stg[0][:], in0=qkf[:], scalar1=gb_sb[:, 0:1], scalar2=gb_sb[:, 1:2],
            op0=ALU.mult, op1=ALU.add), reads=("qkf", "bias"), writes=("stg0",))
        p.op("dve", lambda e: e.tensor_tensor(
            out=stg[0][:], in0=stg[0][:], in1=dn_bc[:], op=ALU.mult),
            reads=("dnbc",), rmw=("stg0",))
        p.op("pool", lambda e: e.tensor_scalar(
            out=stg[1][:], in0=qkf[:], scalar1=gb_sb[:, 2:3], scalar2=gb_sb[:, 3:4],
            op0=ALU.mult, op1=ALU.add), reads=("qkf", "bias"), writes=("stg1",))
        p.dma("qsp", q_o[:, sl], stg[0][:], reads=("stg0",), writes=("out",))
        p.dma("qsp", k_o[:, sl], stg[1][:], reads=("stg1",), writes=("out",))
    p.stage_end()
    return p.finalize() if standalone else None


def bld_n2(p=None, ext=None):
    standalone = p is None
    if standalone:
        p = Prog()
        ext = dict(q=p.inp("q", [128, TC], BF16), kF=p.inp("kF", [128, S], BF16),
                   vF=p.inp("vF", [S, 2048], BF16), gate=p.inp("gate", [TC, 2048], BF16),
                   Mm=p.inp("Mm", [S, TC], BF16), xT=p.inp("xT", [D, TC]),
                   xh1T=p.inp("xh1T", [D, TC], BF16), Wo=p.inp("Wo", [2048, D]),
                   bo=p.inp("bo", [D, 1]), idb=p.inp("idb", [128, 128], BF16),
                   xoT=p.out("xoT", [D, TC]))
    q, kF, vF, gate = ext["q"], ext["kF"], ext.get("vF"), ext["gate"]
    Mm, xT, xh1T, Wo, bo, idb, xoT = (ext["Mm"], ext["xT"], ext["xh1T"], ext["Wo"],
                                      ext["bo"], ext["idb"], ext["xoT"])
    kvgath = ext.get("kvgath")
    p.stage_begin("n2")
    og_d = p.dram("og_d", [2048, TC], BF16)
    pt_d = p.dram("pt_d", [16, 128, 64 * 128], BF16)
    q_sb = p.sb("q_sb", [128, TC], BF16)
    k_sb = p.sb("k_sb", [128, S], BF16)
    vv = p.sb("vv", [128, 64 * 512], BF16)
    ptb = p.sb("ptb", [128, 64 * 128], BF16)
    mcol = p.sb("mcol", [128, 64 * 128], BF16)
    gt = p.sb("gt", [128, 512], BF16)
    ogs = p.sb("ogs", [128, 512], BF16)
    iden = p.sb("iden", [128, 128], BF16)
    relu_t = p.sb("relu_t", [128, 128], F32)
    xsl = [p.sb(f"xsl{i}", [128, 512], F32) for i in range(NCHUNK)]
    ocol0 = p.sb("ocol0", [128, 512], BF16)
    oco = p.sb("oco", [128, 16 * 512], BF16)
    wcol = [p.sb(f"wcol{j}", [128, 16 * 128], BF16) for j in range(2)]
    bo_sb = p.sb("bo_sb", [128, NCHUNK], F32)
    x1sl = p.sb("x1sl", [128, 512], BF16)
    ps_s = [p.ps(f"ps_s{j}", [128, 128], F32) for j in range(2)]
    ps_pv = [p.ps(f"ps_pv{j}", [128, 512], F32) for j in range(2)]
    ps_tr = [p.ps(f"ps_tr{j}", [128, 128], BF16) for j in range(2)]
    ps_pj = [p.ps("ps_pj0", [128, 512], F32)] * 2
    p.dma("qg", iden[:], idb[:], writes=("iden",))
    p.dma("qg", q_sb[:], q[:], writes=("q",))
    if kvgath is None:
        p.dma("qg", k_sb[:], kF[:], writes=("k",))
    else:
        CH5 = 128 + TC
        for j in range(4):
            p.dma("qg", k_sb[:, j * TC:(j + 1) * TC],
                  kvgath[j * CH5:j * CH5 + 128, :], writes=("k",))
    for i in range(NCHUNK):
        p.dma("qsp", bo_sb[:, i:i + 1], bo[i * 128:(i + 1) * 128, :], writes=("bias",))
    for vq in range(4):
        for h in range(4):
            vsrc = (vF[h * 2048:(h + 1) * 2048, vq * 512:(vq + 1) * 512]
                    if kvgath is None else
                    kvgath[h * (128 + TC) + 128:(h + 1) * (128 + TC),
                           vq * 512:(vq + 1) * 512])
            p.dma("qsp", vv[:].rearrange("p (k w) -> p k w", w=512)[:, h * 16:(h + 1) * 16, :],
                  vsrc.rearrange("(k p) w -> p k w", p=128), writes=("vv",))
        for qt in range(16):
            if vq == 0:
                p.dma("qsp", mcol[:].rearrange("p (k w) -> p k w", w=128),
                      Mm[:, qt * 128:(qt + 1) * 128].rearrange("(k p) w -> p k w", p=128),
                      writes=("mcol",))
                for kt in range(64):
                    j = kt % 2
                    pj = ps_s[j]
                    pk = f"ps_s{j}"
                    p.op("pe", lambda e, pj=pj, kt=kt, qt=qt: e.matmul(
                        pj[:], k_sb[:, kt * 128:(kt + 1) * 128],
                        q_sb[:, qt * 128:(qt + 1) * 128], start=True, stop=True),
                        reads=("k", "q"), writes=(pk,))
                    p.op("dve", lambda e, pj=pj, kt=kt: e.scalar_tensor_tensor(
                        out=relu_t[:], in0=pj[:], scalar=0.0,
                        in1=mcol[:, kt * 128:(kt + 1) * 128],
                        op0=ALU.max, op1=ALU.mult),
                        reads=(pk, "mcol"), writes=("relu_t",))
                    p.op("act", lambda e, kt=kt: e.activation(
                        out=ptb[:, kt * 128:(kt + 1) * 128], in_=relu_t[:],
                        func=AF.Square), reads=("relu_t",), writes=(f"ptb{kt}",))
                p.dma("qsp", pt_d[qt], ptb[:],
                      reads=tuple(f"ptb{kt}" for kt in range(64)), writes=("pt_d",))
                if qt == 3:
                    p.dbg("ptb3", ptb[:], tuple(f"ptb{kt}" for kt in range(64)))
            else:
                p.dma("qsp", ptb[:], pt_d[qt],
                      reads=("pt_d",), writes=tuple(f"ptb{kt}" for kt in range(64)))
            pj = ps_pv[qt % 2]
            pk = f"ps_pv{qt % 2}"
            for kt in range(64):
                p.op("pe", lambda e, pj=pj, kt=kt: e.matmul(
                    pj[:], ptb[:, kt * 128:(kt + 1) * 128],
                    vv[:, kt * 512:(kt + 1) * 512],
                    start=(kt == 0), stop=(kt == 63)),
                    reads=(f"ptb{kt}", "vv"), writes=(pk,))
            p.dma("qg", gt[:], gate[qt * 128:(qt + 1) * 128, vq * 512:(vq + 1) * 512],
                  writes=("gt",))
            p.op("dve", lambda e, pj=pj: e.tensor_tensor(
                out=ogs[:], in0=pj[:], in1=gt[:], op=ALU.mult),
                reads=(pk, "gt"), writes=("ogs",))
            for j4 in range(4):
                tj = ps_tr[j4 % 2]
                p.op("pe", lambda e, tj=tj, j4=j4: e.transpose(
                    tj[:], ogs[:, j4 * 128:(j4 + 1) * 128], iden[:]),
                    reads=("ogs", "iden"), writes=(f"ps_tr{j4 % 2}",))
                p.op("dve", lambda e, tj=tj, j4=j4: e.tensor_copy(
                    ocol0[:, j4 * 128:(j4 + 1) * 128], tj[:]),
                    reads=(f"ps_tr{j4 % 2}",), writes=("oc0",))
            p.dma("qsp", og_d[vq * 512:(vq + 1) * 512, qt * 128:(qt + 1) * 128]
                  .rearrange("(a p) w -> p a w", p=128),
                  ocol0[:].rearrange("p (a w) -> p a w", w=128),
                  reads=("oc0",), writes=("og_d",))
    p.dbg("ogd", og_d[:, 0:512], ("og_d",))
    for oc in range(NCHUNK):
        wc = wcol[oc % 2]
        wck = f"wcol{oc % 2}"
        p.dma("qg", wc[:].rearrange("p (c w) -> p c w", w=128),
              Wo[:, oc * 128:(oc + 1) * 128].rearrange("(c p) w -> p c w", p=128),
              writes=(wck,))
        for qg in range(4):
            sl = slice(qg * 512, (qg + 1) * 512)
            p.dma("qsp", oco[:].rearrange("p (c w) -> p c w", w=512),
                  og_d[:, sl].rearrange("(c p) w -> p c w", p=128),
                  reads=("og_d",), writes=("oc1",))
            p.dma("qsp", xsl[oc][:], xT[oc * 128:(oc + 1) * 128, sl],
                  writes=(f"xsl{oc}",))
            p.dma("qg", x1sl[:], xh1T[oc * 128:(oc + 1) * 128, sl], writes=("x1sl",))
            p.op("pool", lambda e, oc=oc: e.tensor_tensor(
                out=xsl[oc][:], in0=xsl[oc][:], in1=x1sl[:], op=ALU.add),
                reads=("x1sl",), rmw=(f"xsl{oc}",))
            pj = ps_pj[0]
            pk = "ps_pj0"
            for vc in range(16):
                p.op("pe", lambda e, pj=pj, wc=wc, vc=vc: e.matmul(
                    pj[:], wc[:, vc * 128:(vc + 1) * 128],
                    oco[:, vc * 512:(vc + 1) * 512],
                    start=(vc == 0), stop=(vc == 15)),
                    reads=("oc1", wck), writes=(pk,))
            p.op("dve", lambda e, pj=pj, oc=oc: e.scalar_tensor_tensor(
                out=xsl[oc][:], in0=pj[:], scalar=bo_sb[:, oc:oc + 1],
                in1=xsl[oc][:], op0=ALU.add, op1=ALU.add),
                reads=(pk, "bias"), rmw=(f"xsl{oc}",))
            p.dma("qsp", xoT[oc * 128:(oc + 1) * 128, sl], xsl[oc][:],
                  reads=(f"xsl{oc}",), writes=("xout",))
    p.stage_end()
    return p.finalize() if standalone else None


def host_nblock(x, ln_g, ln_b, W_h, b_h, W_qk, b_qk, osg, osb, W_out, b_out):
    nc1 = bld_n1()
    gb = np.stack([osg[0], osb[0], osg[1], osb[1]], axis=1).astype(np.float32)
    gb = np.ascontiguousarray(gb)
    in1 = []
    for c, b, o in _cores():
        dnr = (1.0 / (np.arange(o, o + TC, dtype=np.float64) + 1.0)).astype(np.float32)
        in1.append(dict(xT=np.ascontiguousarray(x[b, o:o + TC].T), Wh=W_h,
                        bhr=np.ascontiguousarray(b_h.reshape(1, 4096)), Wqk=W_qk,
                        bqk=np.ascontiguousarray(b_qk.reshape(128, 1)),
                        lng=np.ascontiguousarray(ln_g.reshape(D, 1)),
                        lnb=np.ascontiguousarray(ln_b.reshape(D, 1)),
                        gb=gb, dnr=dnr.reshape(1, TC)))
    r1 = run_bass_kernel_spmd(nc1, in1, core_ids=list(range(NCORE))).results
    vF = {b: np.concatenate([r1[c]["v_o"] for c, bb, o in _cores() if bb == b], axis=0)
          for b in range(B)}
    kB = {b: np.concatenate([r1[c]["k_o"] for c, bb, o in _cores() if bb == b], axis=1)
          for b in range(B)}
    nc2 = bld_n2()
    in2 = []
    for c, b, o in _cores():
        kidx = np.arange(S)[:, None]
        qidx = np.arange(o, o + TC)[None, :]
        Mm = (kidx <= qidx).astype(BF)
        in2.append(dict(q=r1[c]["q_o"], kF=np.ascontiguousarray(kB[b]),
                        vF=np.ascontiguousarray(vF[b]), gate=r1[c]["g_o"], Mm=Mm,
                        xT=np.ascontiguousarray(x[b, o:o + TC].T),
                        xh1T=r1[c]["xh1_o"], Wo=W_out,
                        bo=np.ascontiguousarray(b_out.reshape(D, 1)), idb=_EYE_BF))
    r2 = run_bass_kernel_spmd(nc2, in2, core_ids=list(range(NCORE))).results
    out = np.empty((B, S, D), np.float32)
    for c, b, o in _cores():
        out[b, o:o + TC] = r2[c]["xoT"].T
    return out


def bld_m1(p=None, ext=None):
    standalone = p is None
    TE = 2560
    if standalone:
        p = Prog()
        ext = dict(xTe=p.inp("xTe", [D, TE]), nw=p.inp("nw", [D, 1]),
                   ipT=p.inp("ipT", [D, 4096]), cw=p.inp("cw", [128, 64]),
                   cb=p.inp("cb", [128, 16]), xpT=p.inp("xpT", [2048, 96]),
                   dpT=p.inp("dpT", [64, 2048]), dtb=p.inp("dtb", [128, 16]),
                   uc_o=p.out("uc_o", [2048, TC], BF16),
                   dt_o=p.out("dt_o", [2048, TC], BF16), bc_o=p.out("bc_o", [32, TC]),
                   z_o=p.out("z_o", [2048, TC], BF16),
                   xh_o=p.out("xh_o", [D, TC], BF16), sc=base_scratch(p))
    nw, ipT, cw, cb, xpT, dpT, dtb = (ext["nw"], ext["ipT"], ext["cw"], ext["cb"],
                                      ext["xpT"], ext["dpT"], ext["dtb"])
    uc_o, dt_o, bc_o, z_o, xh_o, sc = (ext["uc_o"], ext["dt_o"], ext["bc_o"],
                                       ext["z_o"], ext["xh_o"], ext["sc"])
    fused = "xcur" in ext
    if not fused:
        xTe = ext["xTe"]
    else:
        xcur, halog, hoffi, hmaski = (ext["xcur"], ext["halog"], ext["hoff"],
                                      ext["hmask"])
    p.stage_begin("m1")
    sc = stage_sc(p, sc)
    if fused:
        hoff_sb = p.sb("hoff_sb", [1, 2], mybir.dt.int32)
        hmask_sb = p.sb("hmask_sb", [128, 1], F32)
        hal_all = p.sb("hal_all", [128, NCHUNK * 3], F32)
        p.dma("qsp", hoff_sb[:], hoffi[:], writes=("offs",))
        p.dma("qsp", hmask_sb[:], hmaski[:], writes=("par",))
    epsm = p.sb("epsm", [128, 1], F32)
    c1 = p.sb("c1", [128, 1], F32)
    p.op("dve", lambda e: e.memset(epsm[:], 1.1920928955078125e-07), writes=("consts",))
    p.op("dve", lambda e: e.memset(c1[:], 1.0), writes=("consts",))
    xs = [p.sb(f"xs{i}", [128, 512], F32) for i in range(NCHUNK)]
    xh = [p.sb(f"xh{i}", [128, 512], BF16) for i in range(NCHUNK)]
    hh = [p.sb(f"hh{i}", [128, 512], BF16) for i in range(NCHUNK)]
    nw_sb = p.sb("nw_sb", [128, NCHUNK], F32)
    cw_sb = p.sb("cw_sb", [128, 64], F32)
    cb_sb = p.sb("cb_sb", [128, 16], F32)
    dtb_sb = p.sb("dtb_sb", [128, 16], F32)
    uext = [p.sb(f"ue{i}", [128, TE], BF16) for i in range(16)]
    ipcol = [p.sb(f"ipc{j}", [128, NCHUNK * 128], BF16) for j in range(2)]
    xp_sb = [p.sb(f"xp{i}", [128, 96], BF16) for i in range(16)]
    dp_sb = p.sb("dp_sb", [64, 2048], BF16)
    xdb = p.sb("xdb", [96, 2048], F32)
    dti = p.sb("dti", [64, 2048], BF16)
    cvb = [p.sb(f"cvb{j}", [128, 2048], F32) for j in range(2)]
    ucb = p.sb("ucb", [128, 2048], BF16)
    spb = [p.sb(f"spb{j}", [128, 512], F32) for j in range(3)]
    stg = [p.sb(f"stg{j}", [128, 512], BF16) for j in range(2)]
    pp = [p.ps(f"pp{j}", [128, 512], F32) for j in range(2)]
    ps_xd = p.ps("ps_xd", [96, 2048], F32)
    for i in range(NCHUNK):
        p.dma("qsp", nw_sb[:, i:i + 1], nw[i * 128:(i + 1) * 128, :], writes=("par",))
    p.dma("qsp", cw_sb[:], cw[:], writes=("par",))
    p.dma("qsp", cb_sb[:], cb[:], writes=("par",))
    p.dma("qsp", dtb_sb[:], dtb[:], writes=("par",))
    for i in range(16):
        p.dma("qg", xp_sb[i][:], xpT[i * 128:(i + 1) * 128, :], writes=("xp",))
    p.dma("qg", dp_sb[:], dpT[:], writes=("dp",))

    for si in range(5):
        sl = slice(si * 512, (si + 1) * 512)
        for ec in range(NCHUNK):
            if not fused:
                p.dma("qsp", xs[ec][:], xTe[ec * 128:(ec + 1) * 128, sl],
                      writes=(f"xs{ec}",))
            elif si == 0:
                p.op("pool", lambda e, ec=ec: e.memset(xs[ec][:], 0.0),
                     writes=(f"xs{ec}",))
                if ec == 0:
                    def _emh(e):
                        with e.register() as r0:
                            e.reg_load(r0, hoff_sb[0:1, 0:1])
                            return e.dma_start(
                                out=hal_all[:].rearrange("p (ec c) -> p ec c", c=3),
                                in_=bass.AP(halog.tensor, r0,
                                            [[3, 128], [128 * 3, NCHUNK], [1, 3]]))
                    p.op("qsp", _emh, reads=("offs",), writes=("hal_all",))
                p.op("dve", lambda e, ec=ec: e.tensor_scalar(
                    out=xs[ec][:, 509:512], in0=hal_all[:, ec * 3:(ec + 1) * 3],
                    scalar1=hmask_sb[:, 0:1], scalar2=None, op0=ALU.mult),
                    reads=("par", "hal_all"), rmw=(f"xs{ec}",))
            else:
                p.dma("qsp", xs[ec][:], xcur[ec * 128:(ec + 1) * 128,
                                             (si - 1) * 512: si * 512],
                      writes=(f"xs{ec}",))
        ln_slice(p, f"m{si}", xs, "xs", 512, lambda ec: xh[ec][:], "xh", sc)
        if si > 0:
            csl = slice((si - 1) * 512, si * 512)
            for ec in range(NCHUNK):
                p.op("dve", lambda e, ec=ec: e.tensor_copy(stg[0][:], xh[ec][:]),
                     reads=(f"xh{ec}",), writes=("stg0",))
                p.dma("qsp", xh_o[ec * 128:(ec + 1) * 128, csl], stg[0][:],
                      reads=("stg0",), writes=("out",))
        rows = sc["lnrows"]
        rr = rows[:, 1024:1536]
        for ec in range(NCHUNK):
            p.op("act", lambda e, ec=ec: e.activation(
                out=sc["sq"][:], in_=xh[ec][:], func=AF.Square),
                reads=(f"xh{ec}",), writes=("sq",))
            p.op("pe", lambda e, ec=ec: e.matmul(
                sc["ps_stat"][:, 0:512], sc["ones_f32"][:], sc["sq"][:],
                start=(ec == 0), stop=(ec == NCHUNK - 1)),
                reads=("sq", "ones"), writes=("ps_stat",))
        p.op("dve", lambda e: e.tensor_scalar(
            out=rr[0:1, :], in0=sc["ps_stat"][:, 0:512], scalar1=1.0 / D,
            scalar2=None, op0=ALU.mult), reads=("ps_stat",), writes=("lnrows",))
        p.op("act", lambda e: e.activation(out=rr[0:1, :], in_=rr[0:1, :],
             func=AF.Sqrt, bias=epsm[0:1, :]), reads=("consts",), rmw=("lnrows",))
        p.op("dve", lambda e: e.reciprocal(rr[0:1, :], rr[0:1, :]), rmw=("lnrows",))
        bcast_row(p, f"rm{si}", rr[0:1, :], "lnrows", 512, sc["rb"][:], "rb")
        for ec in range(NCHUNK):
            p.op("dve", lambda e, ec=ec: e.scalar_tensor_tensor(
                out=hh[ec][:], in0=xh[ec][:], scalar=nw_sb[:, ec:ec + 1],
                in1=sc["rb"][:], op0=ALU.mult, op1=ALU.mult),
                reads=(f"xh{ec}", "par", "rb"), writes=(f"hh{ec}",))
        for uc in range(32):
            wc = ipcol[uc % 2]
            wck = f"ipc{uc % 2}"
            p.dma("qg", wc[:].rearrange("p (c w) -> p c w", w=128),
                  ipT[:, uc * 128:(uc + 1) * 128].rearrange("(c p) w -> p c w", p=128),
                  writes=(wck,))
            if si == 0 and uc >= 16:
                continue
            pj = pp[uc % 2]
            pk = f"pp{uc % 2}"
            for ec in range(NCHUNK):
                p.op("pe", lambda e, pj=pj, wc=wc, ec=ec: e.matmul(
                    pj[:], wc[:, ec * 128:(ec + 1) * 128], hh[ec][:],
                    start=(ec == 0), stop=(ec == NCHUNK - 1)),
                    reads=(f"hh{ec}", wck), writes=(pk,))
            if uc < 16:
                p.op("dve", lambda e, pj=pj, uc=uc, sl=sl: e.tensor_copy(
                    uext[uc][:, sl], pj[:]), reads=(pk,), writes=(f"ue{uc}",))
            else:
                p.op("dve", lambda e, pj=pj, uc=uc: e.tensor_copy(
                    stg[uc % 2][:], pj[:]), reads=(pk,), writes=(f"stg{uc % 2}",))
                p.dma("qsp", z_o[(uc - 16) * 128:(uc - 15) * 128,
                               (si - 1) * 512: si * 512], stg[uc % 2][:],
                      reads=(f"stg{uc % 2}",), writes=("out",))
    for dc in range(16):
        p.op("dve", lambda e, dc=dc: e.tensor_scalar(
            out=cvb[0][:], in0=uext[dc][:, 509:509 + 2048],
            scalar1=cw_sb[:, dc * 4:dc * 4 + 1], scalar2=cb_sb[:, dc:dc + 1],
            op0=ALU.mult, op1=ALU.add),
            reads=(f"ue{dc}", "par"), writes=("cvb0",))
        for k in range(1, 4):
            src_b, dst_b = cvb[(k + 1) % 2], cvb[k % 2]
            p.op("dve", lambda e, dc=dc, k=k, src_b=src_b, dst_b=dst_b:
                 e.scalar_tensor_tensor(
                     out=dst_b[:], in0=uext[dc][:, 509 + k:509 + k + 2048],
                     scalar=cw_sb[:, dc * 4 + k:dc * 4 + k + 1], in1=src_b[:],
                     op0=ALU.mult, op1=ALU.add),
                 reads=(f"ue{dc}", "par", f"cvb{(k + 1) % 2}"), writes=(f"cvb{k % 2}",))
        p.op("act", lambda e: e.activation(out=ucb[:], in_=cvb[1][:], func=AF.Silu),
             reads=("cvb1",), writes=("ucb",))
        p.dma("qsp", uc_o[dc * 128:(dc + 1) * 128, :], ucb[:],
              reads=("ucb",), writes=("out",))
        for qg in range(4):
            p.op("pe", lambda e, dc=dc, qg=qg: e.matmul(
                ps_xd[:, qg * 512:(qg + 1) * 512], xp_sb[dc][:],
                ucb[:, qg * 512:(qg + 1) * 512],
                start=(dc == 0), stop=(dc == 15)),
                reads=("ucb", "xp"), writes=("ps_xd",))
    for qg in range(4):
        p.op("dve", lambda e, qg=qg: e.tensor_copy(
            xdb[:, qg * 512:(qg + 1) * 512], ps_xd[:, qg * 512:(qg + 1) * 512]),
            reads=("ps_xd",), writes=("xdb",))
    p.dma("qsp", bc_o[:], xdb[64:96, :], reads=("xdb",), writes=("out",))
    p.op("dve", lambda e: e.tensor_copy(dti[:], xdb[0:64, :]),
         reads=("xdb",), writes=("dti",))
    for dc in range(16):
        for qg in range(4):
            pj = pp[qg % 2]
            pk = f"pp{qg % 2}"
            p.op("pe", lambda e, pj=pj, dc=dc, qg=qg: e.matmul(
                pj[:], dp_sb[:, dc * 128:(dc + 1) * 128],
                dti[0:64, qg * 512:(qg + 1) * 512], start=True, stop=True),
                reads=("dti", "dp"), writes=(pk,))
            a, bq, cq = spb[0], spb[1], spb[2]
            p.op("dve", lambda e, pj=pj, dc=dc: e.tensor_scalar(
                out=a[:], in0=pj[:], scalar1=dtb_sb[:, dc:dc + 1], scalar2=None,
                op0=ALU.add), reads=(pk, "par"), writes=("spb0",))
            p.op("dve", lambda e: e.tensor_scalar(
                out=bq[:], in0=a[:], scalar1=-1.0, scalar2=None, op0=ALU.mult),
                reads=("spb0",), writes=("spb1",))
            p.op("dve", lambda e: e.tensor_tensor(
                out=bq[:], in0=a[:], in1=bq[:], op=ALU.max),
                reads=("spb0",), rmw=("spb1",))
            p.op("act", lambda e: e.activation(out=bq[:], in_=bq[:], func=AF.Exp,
                 scale=-1.0), rmw=("spb1",))
            p.op("act", lambda e: e.activation(out=bq[:], in_=bq[:], func=AF.Ln,
                 bias=c1[:, 0:1]), reads=("consts",), rmw=("spb1",))
            p.op("dve", lambda e: e.tensor_scalar(
                out=cq[:], in0=a[:], scalar1=0.0, scalar2=None, op0=ALU.max),
                reads=("spb0",), writes=("spb2",))
            p.op("pool", lambda e, qg=qg: e.tensor_tensor(
                out=stg[qg % 2][:], in0=bq[:], in1=cq[:], op=ALU.add),
                reads=("spb1", "spb2"), writes=(f"stg{qg % 2}",))
            p.dma("qsp", dt_o[dc * 128:(dc + 1) * 128, qg * 512:(qg + 1) * 512],
                  stg[qg % 2][:], reads=(f"stg{qg % 2}",), writes=("out",))
    p.stage_end()
    return p.finalize() if standalone else None


def bld_m2(p=None, ext=None):
    standalone = p is None
    if standalone:
        p = Prog()
        ext = dict(dts=p.inp("dts", [512, S], BF16), ucs=p.inp("ucs", [512, S], BF16),
                   bcf=p.inp("bcf", [32, S]), alog_i=p.inp("alog_i", [64, 128]),
                   dpar=p.inp("dpar", [128, 4]),
                   selc_i=p.inp("selc_i", [128, 2048], BF16),
                   sely_i=p.inp("sely_i", [128, 2048], BF16),
                   y_o=p.out("y_o", [512, S], BF16))
    alog_i, dpar, selc_i, sely_i, y_o = (ext["alog_i"], ext["dpar"], ext["selc_i"],
                                         ext["sely_i"], ext["y_o"])
    fused = "ucdtg" in ext
    if not fused:
        dts, ucs, bcf = ext["dts"], ext["ucs"], ext["bcf"]
    else:
        ucdtg, bcg, dsoffi = ext["ucdtg"], ext["bcg"], ext["dsoff"]
    p.stage_begin("m2")
    if fused:
        dsoff_sb = p.sb("dsoff_sb", [1, 2], mybir.dt.int32)
        p.dma("qsp", dsoff_sb[:], dsoffi[:], writes=("offs",))
    FS = 1024
    NFS = S // FS
    dt_dc = p.sb("dt_dc", [128, S], BF16)
    uc_dc = p.sb("uc_dc", [128, S], BF16)
    dtu = p.sb("dtu", [128, S], BF16)
    brep = p.sb("brep", [128, S], F32)
    crep = p.sb("crep", [128, S], F32)
    alog_sb = p.sb("alog_sb", [64, 128], F32)
    selc = p.sb("selc", [128, 2048], BF16)
    sely = p.sb("sely", [128, 2048], BF16)
    selp = [p.sb(f"selp{g}", [128, 128], BF16) for g in range(16)]
    abc = p.sb("abc", [128, 128], F32)
    dA = [p.sb(f"dA{j}", [128, FS], F32) for j in range(2)]
    dBu = [p.sb(f"dBu{j}", [128, FS], F32) for j in range(2)]
    hbf = [p.sb(f"hbf{j}", [128, FS], F32) for j in range(2)]
    hc = [p.sb(f"hc{j}", [128, FS], BF16) for j in range(2)]
    ysb = [p.sb(f"ysb{j}", [128, FS], BF16) for j in range(2)]
    carr = p.sb("carr", [128, 16], F32)
    dp_sb = p.sb("dp_sb", [128, 4], F32)
    ps_a = p.ps("ps_a", [128, FS], F32)
    ps_b = p.ps("ps_b", [128, FS], F32)
    ps_a2, ps_b2 = ps_a, ps_b
    ps_y = p.ps("ps_y", [128, FS], F32)
    p.dma("qsp", selc[:], selc_i[:], writes=("selc",))
    p.dma("qsp", dp_sb[:], dpar[:], writes=("dp",))
    p.dma("qsp", alog_sb[:], alog_i[:], writes=("alog",))
    p.op("act", lambda e: e.activation(out=alog_sb[:], in_=alog_sb[:], func=AF.Exp),
         rmw=("alog",))
    p.op("dve", lambda e: e.tensor_scalar(
        out=alog_sb[:], in0=alog_sb[:], scalar1=-1.0, scalar2=None, op0=ALU.mult),
        rmw=("alog",))
    p.dma("qsp", sely[:], sely_i[:], writes=("sely",))
    for g in range(8):
        if not fused:
            p.dma("qsp", brep[g * 16:(g + 1) * 16, :], bcf[0:16, :], writes=("brep",))
            p.dma("qsp", crep[g * 16:(g + 1) * 16, :], bcf[16:32, :], writes=("crep",))
        else:
            for j in range(4):
                p.dma("qsp", brep[g * 16:(g + 1) * 16, j * TC:(j + 1) * TC],
                      bcg[j * 32:j * 32 + 16, :], writes=("brep",))
                p.dma("qsp", crep[g * 16:(g + 1) * 16, j * TC:(j + 1) * TC],
                      bcg[j * 32 + 16:j * 32 + 32, :], writes=("crep",))
    for dc in range(4):
        if not fused:
            p.dma("qg", dt_dc[:], dts[dc * 128:(dc + 1) * 128, :], writes=("dtdc",))
            p.dma("qg", uc_dc[:], ucs[dc * 128:(dc + 1) * 128, :], writes=("ucdc",))
        else:
            for (dstt, bofs, dk) in ((uc_dc, 0, "ucdc"), (dt_dc, 2048 * TC, "dtdc")):
                def _emd(e, dstt=dstt, bofs=bofs, dc=dc):
                    with e.register() as r0:
                        e.reg_load(r0, dsoff_sb[0:1, 0:1])
                        e.reg_add(r0, r0, bofs + dc * 128 * TC)
                        return e.dma_start(
                            out=dstt[:].rearrange("p (j c) -> p j c", c=TC),
                            in_=bass.AP(ucdtg.tensor, r0,
                                        [[TC, 128], [4096 * TC, 4], [1, TC]]))
                p.op("qsp", _emd, reads=("offs",), writes=(dk,))
        for h in range(4):
            p.op("dve", lambda e, h=h: e.tensor_tensor(
                out=dtu[:, h * 2048:(h + 1) * 2048], in0=dt_dc[:, h * 2048:(h + 1) * 2048],
                in1=uc_dc[:, h * 2048:(h + 1) * 2048], op=ALU.mult),
                reads=("dtdc", "ucdc"), writes=(f"dtu{h}",))
        for g in range(16):
            bcast_row(p, f"a{dc}_{g}", alog_sb[dc * 16 + g:dc * 16 + g + 1, :],
                      "alog", 128, abc[:], "abc")
            p.op("dve", lambda e, g=g: e.tensor_tensor(
                out=selp[g][:], in0=selc[:, g * 128:(g + 1) * 128], in1=abc[:],
                op=ALU.mult), reads=("selc", "abc"), writes=(f"selp{g}",))
        p.op("dve", lambda e: e.memset(carr[:], 0.0),
             writes=tuple(f"carr{g}" for g in range(16)))
        for fs in range(NFS):
            fsl = slice(fs * FS, (fs + 1) * FS)
            for g in range(16):
                w = g % 2
                dAw, dBw, hbw, hcw = dA[w], dBu[w], hbf[w], hc[w]
                psaw, psbw = ps_a, ps_b
                pka, pkb = ("ps_a", "ps_b")
                for j in range(FS // 512):
                    p.op("pe", lambda e, g=g, j=j, fsl=fsl, psaw=psaw: e.matmul(
                        psaw[:, j * 512:(j + 1) * 512], selp[g][:],
                        dt_dc[:, fsl.start + j * 512: fsl.start + (j + 1) * 512],
                        start=True, stop=True),
                        reads=(f"selp{g}", "dtdc"), writes=(pka,))
                    p.op("pe", lambda e, g=g, j=j, fsl=fsl, psbw=psbw: e.matmul(
                        psbw[:, j * 512:(j + 1) * 512], selc[:, g * 128:(g + 1) * 128],
                        dtu[:, fsl.start + j * 512: fsl.start + (j + 1) * 512],
                        start=True, stop=True),
                        reads=("selc", f"dtu{fsl.start // 2048}"), writes=(pkb,))
                p.op("act", lambda e, dAw=dAw, psaw=psaw: e.activation(
                    out=dAw[:], in_=psaw[:], func=AF.Exp),
                    reads=(pka,), writes=(f"dA{w}",))
                p.op("dve", lambda e, fsl=fsl, dBw=dBw, psbw=psbw: e.tensor_tensor(
                    out=dBw[:], in0=psbw[:], in1=brep[:, fsl], op=ALU.mult),
                    reads=(pkb, "brep"), writes=(f"dBu{w}",))
                p.op("dve", lambda e, g=g, dAw=dAw, dBw=dBw, hbw=hbw: e.tensor_tensor_scan(
                    out=hbw[:], data0=dAw[:], data1=dBw[:], initial=carr[:, g:g + 1],
                    op0=ALU.mult, op1=ALU.add),
                    reads=(f"dA{w}", f"dBu{w}", f"carr{g}"), writes=(f"hbf{w}",))
                p.op("pool", lambda e, g=g, hbw=hbw: e.tensor_copy(
                    carr[:, g:g + 1], hbw[:, FS - 1:FS]),
                    reads=(f"hbf{w}",), writes=(f"carr{g}",))
                p.op("pool", lambda e, fsl=fsl, hbw=hbw, hcw=hcw: e.tensor_tensor(
                    out=hcw[:], in0=hbw[:], in1=crep[:, fsl], op=ALU.mult),
                    reads=(f"hbf{w}", "crep"), writes=(f"hc{w}",))
                for j in range(FS // 512):
                    p.op("pe", lambda e, g=g, j=j, hcw=hcw: e.matmul(
                        ps_y[:, j * 512:(j + 1) * 512], sely[:, g * 128:(g + 1) * 128],
                        hcw[:, j * 512:(j + 1) * 512],
                        start=(g == 0), stop=(g == 15)),
                        reads=("sely", f"hc{w}"), writes=("ps_y",))
            yw = ysb[fs % 2]
            for j in range(FS // 512):
                p.op("dve", lambda e, j=j, fsl=fsl, dc=dc, yw=yw: e.scalar_tensor_tensor(
                    out=yw[:, j * 512:(j + 1) * 512],
                    in0=uc_dc[:, fsl.start + j * 512: fsl.start + (j + 1) * 512],
                    scalar=dp_sb[:, dc:dc + 1],
                    in1=ps_y[:, j * 512:(j + 1) * 512], op0=ALU.mult, op1=ALU.add),
                    reads=("ucdc", "dp", "ps_y"), writes=(f"ysb{fs % 2}",))
            p.dma("qsp", y_o[dc * 128:(dc + 1) * 128, fsl], yw[:],
                  reads=(f"ysb{fs % 2}",), writes=("yout",))
    p.stage_end()
    return p.finalize() if standalone else None


def bld_m3(p=None, ext=None):
    standalone = p is None
    if standalone:
        p = Prog()
        ext = dict(yT=p.inp("yT", [2048, TC], BF16), zT=p.inp("zT", [2048, TC], BF16),
                   xhT=p.inp("xhT", [D, TC], BF16), xT=p.inp("xT", [D, TC]),
                   opT=p.inp("opT", [2048, D]), xoT=p.out("xoT", [D, TC]))
    yT, zT, xhT, xT, opT, xoT = (ext["yT"], ext["zT"], ext["xhT"], ext["xT"],
                                 ext["opT"], ext["xoT"])
    ycol_dyn = ext.get("ycol_dyn")
    p.stage_begin("m3")
    op_sb = [p.sb(f"op{i}", [128, D], BF16) for i in range(16)]
    yall = p.sb("yall", [128, 16 * 512], BF16)
    ysl = [yall[:, i * 512:(i + 1) * 512] for i in range(16)]
    gy = [p.sb(f"gy{i}", [128, 512], BF16) for i in range(16)]
    zsl = p.sb("zsl", [128, 512], BF16)
    zss = p.sb("zss", [128, 512], BF16)
    xsl = p.sb("xsl", [128, 512], F32)
    x1l = p.sb("x1l", [128, 512], BF16)
    ost = p.sb("ost", [128, 512], F32)
    pp = [p.ps(f"pp{j}", [128, 512], F32) for j in range(2)]
    for i in range(16):
        p.dma("qg", op_sb[i][:], opT[i * 128:(i + 1) * 128, :], writes=("op",))
    for qg in range(4):
        sl = slice(qg * 512, (qg + 1) * 512)
        if ycol_dyn is None:
            for dc in range(16):
                p.dma("qg", ysl[dc], yT[dc * 128:(dc + 1) * 128, sl],
                      writes=(f"ysl{dc}",))
        else:
            def _emity(e, sl=sl):
                with e.register() as r0:
                    e.reg_load(r0, ycol_dyn)
                    e.reg_add(r0, r0, sl.start)
                    return e.dma_start(
                        out=yall[:].rearrange("p (dc c) -> p dc c", c=512),
                        in_=bass.AP(yT.tensor, r0, [[S, 128], [128 * S, 16], [1, 512]]))
            p.op("qsp", _emity, reads=("offs",),
                 writes=tuple(f"ysl{dc}" for dc in range(16)))
        for dc in range(16):
            p.dma("qg", zsl[:], zT[dc * 128:(dc + 1) * 128, sl], writes=("zsl",))
            p.op("act", lambda e: e.activation(out=zss[:], in_=zsl[:], func=AF.Silu),
                 reads=("zsl",), writes=("zss",))
            p.op("dve", lambda e, dc=dc: e.tensor_tensor(
                out=gy[dc][:], in0=ysl[dc], in1=zss[:], op=ALU.mult),
                reads=(f"ysl{dc}", "zss"), writes=(f"gy{dc}",))
        for oc in range(NCHUNK):
            pj = pp[oc % 2]
            pk = f"pp{oc % 2}"
            for dc in range(16):
                p.op("pe", lambda e, pj=pj, dc=dc, oc=oc: e.matmul(
                    pj[:], op_sb[dc][:, oc * 128:(oc + 1) * 128], gy[dc][:],
                    start=(dc == 0), stop=(dc == 15)),
                    reads=(f"gy{dc}", "op"), writes=(pk,))
            p.dma("qsp", xsl[:], xT[oc * 128:(oc + 1) * 128, sl], writes=("xsl",))
            p.dma("qg", x1l[:], xhT[oc * 128:(oc + 1) * 128, sl], writes=("x1l",))
            p.op("dve", lambda e, pj=pj: e.tensor_tensor(
                out=ost[:], in0=pj[:], in1=xsl[:], op=ALU.add),
                reads=(pk, "xsl"), writes=("ost",))
            p.op("pool", lambda e: e.tensor_tensor(
                out=ost[:], in0=ost[:], in1=x1l[:], op=ALU.add),
                reads=("x1l",), rmw=("ost",))
            p.dma("qsp", xoT[oc * 128:(oc + 1) * 128, sl], ost[:],
                  reads=("ost",), writes=("xout",))
    p.stage_end()
    return p.finalize() if standalone else None


def host_mamba(x, norm_w, in_proj_w, conv_w, conv_b, x_proj_w, dt_proj_w, dt_proj_b,
               A_log, Dp, out_proj_w):
    nc1 = bld_m1()
    cwm = np.ascontiguousarray(conv_w.reshape(16, 128, 4).transpose(1, 0, 2).reshape(128, 64))
    cbm = np.ascontiguousarray(conv_b.reshape(16, 128).T)
    dtbm = np.ascontiguousarray(dt_proj_b.reshape(16, 128).T)
    ipT = np.ascontiguousarray(in_proj_w.T)
    xpT = np.ascontiguousarray(x_proj_w.T)
    dpT = np.ascontiguousarray(dt_proj_w.T)
    in1 = []
    for c, b, o in _cores():
        xTe = np.zeros((D, 2560), np.float32)
        xTe[:, 512:] = x[b, o:o + TC].T
        if o > 0:
            xTe[:, 509:512] = x[b, o - 3:o].T
        in1.append(dict(xTe=xTe, nw=np.ascontiguousarray(norm_w.reshape(D, 1)),
                        ipT=ipT, cw=cwm, cb=cbm, xpT=xpT, dpT=dpT, dtb=dtbm))
    r1 = run_bass_kernel_spmd(nc1, in1, core_ids=list(range(NCORE))).results
    ucf = {b: np.concatenate([r1[c]["uc_o"] for c, bb, o in _cores() if bb == b], axis=1)
           for b in range(B)}
    dtf = {b: np.concatenate([r1[c]["dt_o"] for c, bb, o in _cores() if bb == b], axis=1)
           for b in range(B)}
    bcf = {b: np.concatenate([r1[c]["bc_o"] for c, bb, o in _cores() if bb == b], axis=1)
           for b in range(B)}
    selc = np.zeros((128, 16, 128), np.float32)
    for g in range(16):
        for pp_ in range(128):
            selc[g * 8 + pp_ // 16, g, pp_] = 1.0
    selc = selc.reshape(128, 2048).astype(BF)
    sely = np.zeros((128, 16, 128), np.float32)
    for g in range(16):
        for pp_ in range(128):
            sely[pp_, g, g * 8 + pp_ // 16] = 1.0
    sely = sely.reshape(128, 2048).astype(BF)
    nc2 = bld_m2()
    in2 = []
    for c in range(NCORE):
        b, ds = c // 4, (c % 4) * 512
        alog = np.ascontiguousarray(A_log[ds:ds + 512].reshape(64, 128))
        dpar = np.ascontiguousarray(Dp[ds:ds + 512].reshape(4, 128).T)
        in2.append(dict(dts=np.ascontiguousarray(dtf[b][ds:ds + 512]),
                        ucs=np.ascontiguousarray(ucf[b][ds:ds + 512]),
                        bcf=np.ascontiguousarray(bcf[b]), alog_i=alog, dpar=dpar,
                        selc_i=np.ascontiguousarray(selc),
                        sely_i=np.ascontiguousarray(sely)))
    r2 = run_bass_kernel_spmd(nc2, in2, core_ids=list(range(NCORE))).results
    yf = {b: np.concatenate([r2[b * 4 + j]["y_o"] for j in range(4)], axis=0)
          for b in range(B)}
    nc3 = bld_m3()
    opT = np.ascontiguousarray(out_proj_w.T)
    in3 = []
    for c, b, o in _cores():
        in3.append(dict(yT=np.ascontiguousarray(yf[b][:, o:o + TC]), zT=r1[c]["z_o"],
                        xhT=r1[c]["xh_o"], xT=np.ascontiguousarray(x[b, o:o + TC].T),
                        opT=opT))
    r3 = run_bass_kernel_spmd(nc3, in3, core_ids=list(range(NCORE))).results
    out = np.empty((B, S, D), np.float32)
    for c, b, o in _cores():
        out[b, o:o + TC] = r3[c]["xoT"].T
    return out


def kernel(**inp):
    return kernel_fused(**inp)


def kernel_unfused(**inp):
    g = lambda k: np.ascontiguousarray(np.asarray(inp[k], np.float32))
    x = g("L0_attn_a_strat")
    x = host_attn_a(x, g("L0_attn_a_Wq"), g("L0_attn_a_bq"), g("L0_attn_a_Wk"),
                    g("L0_attn_a_bk"), g("L0_attn_a_Wv"), g("L0_attn_a_bv"))
    x = host_ffn(x, g("L0_ffn_rw1"), g("L0_ffn_rb1"), g("L0_ffn_rw2"), g("L0_ffn_rb2"),
                 g("L0_ffn_we1"), g("L0_ffn_be1"), g("L0_ffn_we2"), g("L0_ffn_be2"))
    x = host_attn_b(x, g("L0_attn_b_wq"), g("L0_attn_b_wk"), g("L0_attn_b_wv"),
                    g("L0_attn_b_wo"), g("L0_ffn_rw1"), g("L0_ffn_rb1"),
                    g("L0_ffn_rw2"), g("L0_ffn_rb2"), g("L0_ffn_we1"),
                    g("L0_ffn_be1"), g("L0_ffn_we2"), g("L0_ffn_be2"))
    x = host_mamba(x, g("L1_attn_a_norm_w"), g("L1_attn_a_in_proj_w"),
                   g("L1_attn_a_conv_w"), g("L1_attn_a_conv_b"), g("L1_attn_a_x_proj_w"),
                   g("L1_attn_a_dt_proj_w"), g("L1_attn_a_dt_proj_b"),
                   g("L1_attn_a_A_log"), g("L1_attn_a_D"), g("L1_attn_a_out_proj_w"))
    x = host_cffn(x, g("L1_ffn_G1_R"), g("L1_ffn_G2_R"), g("L1_ffn_G1_I"),
                  g("L1_ffn_G2_I"), g("L1_ffn_bias_real"), g("L1_ffn_bias_imag"))
    x = host_nblock(x, g("L1_attn_b_ln_g"), g("L1_attn_b_ln_b"), g("L1_attn_b_W_hidden"),
                    g("L1_attn_b_b_hidden"), g("L1_attn_b_W_qk"), g("L1_attn_b_b_qk"),
                    g("L1_attn_b_os_gamma"), g("L1_attn_b_os_beta"),
                    g("L1_attn_b_W_out"), g("L1_attn_b_b_out"))
    x = host_cffn(x, g("L1_ffn_G1_R"), g("L1_ffn_G2_R"), g("L1_ffn_G1_I"),
                  g("L1_ffn_G2_I"), g("L1_ffn_bias_real"), g("L1_ffn_bias_imag"))
    return x


GROUPS = [[0, 1, 2, 3], [4, 5, 6, 7]]


def bld_fused(upto=99):
    p = Prog()
    I = p.inp
    ext_in = dict(
        xqT=I("xqT", [D, TC]), xtm=I("xtm", [TC, D]),
        wqT=I("wqT", [D, D]), wkT=I("wkT", [D, D]), wvT=I("wvT", [D, D]),
        bq=I("bq", [D, 1]), bk=I("bk", [D, 1]), bvr=I("bvr", [1, D]),
        idenb=I("idenb", [128, 128], BF16), idf=I("idf", [128, 128]),
        rw1=I("rw1", [D, D]), rw2=I("rw2", [D, 8]), rb1=I("rb1", [D, 1]),
        rb2=I("rb2", [8, 1]), we1=I("we1", [8, D, 4096]), we2=I("we2", [8, 4096, D]),
        be1c=I("be1c", [128, 256]), be2r=I("be2r", [8, D]),
        wqp=I("wqp", [D, D]), wkp=I("wkp", [D, D]), wv=I("wv", [D, D]),
        cosd=I("cosd", [TC, 512]), sind=I("sind", [TC, 512]),
        triA=I("triA", [128, 256], BF16), triB=I("triB", [128, 256], BF16),
        ho=I("ho", [1, 4], mybir.dt.int32), wo=I("wo", [D, D]),
        nw=I("nw", [D, 1]), ipT=I("ipT", [D, 4096]), cw=I("cw", [128, 64]),
        cb=I("cb", [128, 16]), xpT=I("xpT", [2048, 96]), dpT=I("dpT", [64, 2048]),
        dtb=I("dtb", [128, 16]), hoff=I("hoff", [1, 2], mybir.dt.int32),
        hmask=I("hmask", [128, 1]),
        alog_i=I("alog_i", [64, 128]), dpar=I("dpar", [128, 4]),
        selc_i=I("selc_i", [128, 2048], BF16), sely_i=I("sely_i", [128, 2048], BF16),
        dsoff=I("dsoff", [1, 2], mybir.dt.int32), opT=I("opT", [2048, D]),
        G1R=I("G1R", [32, 128]), G2R=I("G2R", [64, 16]),
        G1I=I("G1I", [32, 128]), G2I=I("G2I", [64, 16]),
        brc=I("brc", [512, 1]), bic=I("bic", [512, 1]),
        Wh=I("Wh", [D, 4096]), bhr=I("bhr", [1, 4096]), Wqk=I("Wqk", [D, 128]),
        bqk=I("bqk", [128, 1]), lng=I("lng", [D, 1]), lnb=I("lnb", [D, 1]),
        gb=I("gb", [128, 4]), dnr=I("dnr", [1, TC]), Mm=I("Mm", [S, TC], BF16),
        Wo=I("Wo", [2048, D]), bo=I("bo", [D, 1]),
        offs=I("offs", [1, 8], mybir.dt.int32),
    )
    xout = p.out("xout", [D, TC])
    sc = base_scratch(p)
    offs_sb = p.sb("offs_sb", [1, 8], mybir.dt.int32)
    p.dma("qsp", offs_sb[:], ext_in["offs"][:], writes=("offs",))
    we1b = p.dram("we1b", [8, D, 4096], BF16)
    we2b = p.dram("we2b", [8, 4096, D], BF16)
    for e8 in range(8):
        p.dma("qg", we1b[e8], ext_in["we1"][e8], writes=("we1b",))
        p.dma("qg", we2b[e8], ext_in["we2"][e8], writes=("we2b",))

    E = dict(ext_in)
    E["sc"] = sc
    dram = p.dram
    cur = None

    def tap(x_ap, stage_no):
        if upto == stage_no:
            for ec in range(NCHUNK):
                p.dma("qsp", xout[ec * 128:(ec + 1) * 128, :],
                      x_ap[ec * 128:(ec + 1) * 128, :], writes=("tapout",))
            return True
        return False

    q1 = dram("q1", [D, TC], BF16)
    k1c = dram("k1c", [D, TC], BF16)
    v1c = dram("v1c", [TC, D], BF16)
    bld_a1(p, dict(E, q_o=q1, k_o=k1c, v_o=v1c))
    kg1 = dram("kg1", [4 * D, TC], BF16)
    vg1 = dram("vg1", [S, D], BF16)
    p.collective([k1c, v1c], [kg1, vg1], GROUPS)
    p.fence()
    xc1 = dram("xc1", [D, TC])
    bld_a2(p, dict(E, qT=q1, kT=kg1, vF=vg1, xo=None, xo_fm=xc1, kT_gathered=True))
    if tap(xc1, 2):
        return p.finalize()
    xc2 = dram("xc2", [D, TC])
    bld_ffn(p=p, ext=dict(E, xT=xc1, xoT=xc2, we1=we1b, we2=we2b, wbf=True),
            stage="f1")
    if tap(xc2, 3):
        return p.finalize()
    qkv2 = dram("qkv2", [3 * D, TC], BF16)
    bld_b1(p, dict(E, xT=xc2, idb=E["idenb"], q_o=qkv2[0:D, :],
                   k_o=qkv2[D:2 * D, :], v_o=qkv2[2 * D:3 * D, :]))
    qkvg2 = dram("qkvg2", [4 * 3 * D, TC], BF16)
    p.collective([qkv2], [qkvg2], GROUPS)
    p.fence()
    ob2 = dram("ob2", [4, 64, S], BF16)
    bld_b2(p, dict(E, qkvg=qkvg2, idb=E["idenb"], ob=ob2))
    og2 = dram("og2", [D, S], BF16)
    p.collective([ob2.rearrange("a b c -> (a b) c")], [og2], GROUPS)
    p.fence()
    xc3 = dram("xc3", [D, TC])
    bld_ffn(pre_proj=True, p=p,
            ext=dict(E, xT=xc2, xoT=xc3, oT=og2, ocol_dyn=offs_sb[0:1, 0:1],
                     we1=we1b, we2=we2b, wbf=True),
            stage="f2")
    if tap(xc3, 6):
        return p.finalize()
    h3c = dram("h3c", [D, 3])
    p.dma("qsp", h3c[:], xc3[:, TC - 3:TC], writes=("h3c",))
    hg3 = dram("hg3", [4 * D, 3])
    p.collective([h3c], [hg3], GROUPS)
    p.fence()
    ucdt1 = dram("ucdt1", [4096, TC], BF16)
    uc1 = ucdt1[0:2048, :]
    dt1 = ucdt1[2048:4096, :]
    bc1 = dram("bc1", [32, TC])
    z1 = dram("z1", [2048, TC], BF16)
    xh1m = dram("xh1m", [D, TC], BF16)
    bld_m1(p, dict(E, xcur=xc3, halog=hg3, uc_o=uc1, dt_o=dt1, bc_o=bc1,
                   z_o=z1, xh_o=xh1m))
    ucdtg = dram("ucdtg", [4 * 4096, TC], BF16)
    bcg = dram("bcg", [128, TC])
    p.collective([ucdt1, bc1], [ucdtg, bcg], GROUPS)
    p.fence()
    yc = dram("yc", [512, S], BF16)
    bld_m2(p, dict(E, ucdtg=ucdtg, bcg=bcg, y_o=yc))
    yg = dram("yg", [2048, S], BF16)
    p.collective([yc], [yg], GROUPS)
    p.fence()
    xc4 = dram("xc4", [D, TC])
    bld_m3(p, dict(E, yT=yg, ycol_dyn=offs_sb[0:1, 1:2], zT=z1, xhT=xh1m,
                   xT=xc3, xoT=xc4))
    if tap(xc4, 9):
        return p.finalize()
    xc5 = dram("xc5", [D, TC])
    bld_cffn(p=p, ext=dict(E, xT=xc4, xoT=xc5), stage="c1")
    if tap(xc5, 10):
        return p.finalize()
    kv5 = dram("kv5", [128 + TC, 2048], BF16)
    g5 = dram("g5", [TC, 2048], BF16)
    q5 = dram("q5", [128, TC], BF16)
    xh15 = dram("xh15", [D, TC], BF16)
    bld_n1(p, dict(E, xT=xc5, v_o=kv5[128:128 + TC, :], g_o=g5, q_o=q5,
                   k_o=kv5[0:128, :], xh1_o=xh15))
    kvg5 = dram("kvg5", [4 * (128 + TC), 2048], BF16)
    p.collective([kv5], [kvg5], GROUPS)
    p.fence()
    xc6 = dram("xc6", [D, TC])
    bld_n2(p, dict(E, q=q5, kF=None, kvgath=kvg5, gate=g5, xT=xc5,
                   xh1T=xh15, idb=E["idenb"], xoT=xc6))
    if tap(xc6, 12):
        return p.finalize()
    bld_cffn(p=p, ext=dict(E, xT=xc6, xoT=xout), stage="c2")
    return p.finalize()


def _fused_inmaps(inp):
    g = lambda k: np.ascontiguousarray(np.asarray(inp[k], np.float32))
    x = g("L0_attn_a_strat")
    perm = _rope_perm()
    wqp = np.ascontiguousarray(g("L0_attn_b_wq")[:, perm])
    wkp = np.ascontiguousarray(g("L0_attn_b_wk")[:, perm])
    be1c = np.ascontiguousarray(
        g("L0_ffn_be1").reshape(8, 32, 128).transpose(2, 0, 1).reshape(128, 256))
    idf = np.eye(128, dtype=np.float32)
    r_ = np.arange(128)[:, None]
    c_ = np.arange(256)[None, :]
    triA = (c_ >= r_).astype(BF)
    triB = (c_ >= r_ + 128).astype(BF)
    cwm = np.ascontiguousarray(
        g("L1_attn_a_conv_w").reshape(16, 128, 4).transpose(1, 0, 2).reshape(128, 64))
    cbm = np.ascontiguousarray(g("L1_attn_a_conv_b").reshape(16, 128).T)
    dtbm = np.ascontiguousarray(g("L1_attn_a_dt_proj_b").reshape(16, 128).T)
    selc = np.zeros((128, 16, 128), np.float32)
    sely = np.zeros((128, 16, 128), np.float32)
    for gg in range(16):
        for pp_ in range(128):
            selc[gg * 8 + pp_ // 16, gg, pp_] = 1.0
            sely[pp_, gg, gg * 8 + pp_ // 16] = 1.0
    selc = np.ascontiguousarray(selc.reshape(128, 2048).astype(BF))
    sely = np.ascontiguousarray(sely.reshape(128, 2048).astype(BF))
    gb = np.ascontiguousarray(np.stack(
        [g("L1_attn_b_os_gamma")[0], g("L1_attn_b_os_beta")[0],
         g("L1_attn_b_os_gamma")[1], g("L1_attn_b_os_beta")[1]], axis=1).astype(np.float32))
    com = dict(
        wqT=np.ascontiguousarray(g("L0_attn_a_Wq").T),
        wkT=np.ascontiguousarray(g("L0_attn_a_Wk").T),
        wvT=np.ascontiguousarray(g("L0_attn_a_Wv").T),
        bq=np.ascontiguousarray(g("L0_attn_a_bq").reshape(D, 1)),
        bk=np.ascontiguousarray(g("L0_attn_a_bk").reshape(D, 1)),
        bvr=np.ascontiguousarray(g("L0_attn_a_bv").reshape(1, D)),
        idenb=_EYE_BF, idf=idf,
        rw1=g("L0_ffn_rw1"), rw2=g("L0_ffn_rw2"),
        rb1=np.ascontiguousarray(g("L0_ffn_rb1").reshape(D, 1)),
        rb2=np.ascontiguousarray(g("L0_ffn_rb2").reshape(8, 1)),
        we1=g("L0_ffn_we1"), we2=g("L0_ffn_we2"), be1c=be1c, be2r=g("L0_ffn_be2"),
        wqp=wqp, wkp=wkp, wv=g("L0_attn_b_wv"), triA=triA, triB=triB,
        wo=g("L0_attn_b_wo"),
        nw=np.ascontiguousarray(g("L1_attn_a_norm_w").reshape(D, 1)),
        ipT=np.ascontiguousarray(g("L1_attn_a_in_proj_w").T), cw=cwm, cb=cbm,
        xpT=np.ascontiguousarray(g("L1_attn_a_x_proj_w").T),
        dpT=np.ascontiguousarray(g("L1_attn_a_dt_proj_w").T), dtb=dtbm,
        dpar=None, selc_i=selc, sely_i=sely,
        opT=np.ascontiguousarray(g("L1_attn_a_out_proj_w").T),
        G1R=np.ascontiguousarray(g("L1_ffn_G1_R").reshape(32, 128)),
        G2R=np.ascontiguousarray(g("L1_ffn_G2_R").reshape(64, 16)),
        G1I=np.ascontiguousarray(g("L1_ffn_G1_I").reshape(32, 128)),
        G2I=np.ascontiguousarray(g("L1_ffn_G2_I").reshape(64, 16)),
        brc=np.ascontiguousarray(g("L1_ffn_bias_real").reshape(512, 1)),
        bic=np.ascontiguousarray(g("L1_ffn_bias_imag").reshape(512, 1)),
        Wh=g("L1_attn_b_W_hidden"),
        bhr=np.ascontiguousarray(g("L1_attn_b_b_hidden").reshape(1, 4096)),
        Wqk=g("L1_attn_b_W_qk"),
        bqk=np.ascontiguousarray(g("L1_attn_b_b_qk").reshape(128, 1)),
        lng=np.ascontiguousarray(g("L1_attn_b_ln_g").reshape(D, 1)),
        lnb=np.ascontiguousarray(g("L1_attn_b_ln_b").reshape(D, 1)),
        gb=gb, Wo=g("L1_attn_b_W_out"),
        bo=np.ascontiguousarray(g("L1_attn_b_b_out").reshape(D, 1)),
    )
    alog = g("L1_attn_a_A_log")
    Dp = g("L1_attn_a_D")
    ins = []
    for c, b, o in _cores():
        j = c % 4
        ds = j * 512
        cos, sin = _rope_tables(o, TC)
        kidx = np.arange(S)[:, None]
        qidx = np.arange(o, o + TC)[None, :]
        d = dict(com)
        d.update(
            xqT=np.ascontiguousarray(x[b, o:o + TC].T),
            xtm=np.ascontiguousarray(x[b, o:o + TC]),
            cosd=cos, sind=sin,
            ho=np.array([[((c * 4 + jj) % 16) * 64 * TC for jj in range(4)]], np.int32),
            hoff=np.array([[((j - 1) % 4) * 1024 * 3, 0]], np.int32),
            hmask=np.full((128, 1), 0.0 if j == 0 else 1.0, np.float32),
            alog_i=np.ascontiguousarray(alog[ds:ds + 512].reshape(64, 128)),
            dpar=np.ascontiguousarray(Dp[ds:ds + 512].reshape(4, 128).T),
            dsoff=np.array([[ds * TC, 0]], np.int32),
            dnr=(1.0 / (np.arange(o, o + TC, dtype=np.float64) + 1.0))
                .astype(np.float32).reshape(1, TC),
            Mm=(kidx <= qidx).astype(BF),
            offs=np.array([[o, o, 0, 0, 0, 0, 0, 0]], np.int32),
        )
        ins.append(d)
    return ins


def kernel_fused(upto=99, **inp):
    nc = bld_fused(upto=upto)
    ins = _fused_inmaps(inp)
    r = run_bass_kernel_spmd(nc, ins, core_ids=list(range(NCORE))).results
    out = np.empty((B, S, D), np.float32)
    for c, b, o in _cores():
        out[b, o:o + TC] = r[c]["xout"].T
    return out


# version 09 (final)
# speedup vs baseline: 1.0628x; 1.0140x over previous
import numpy as np
import ml_dtypes
from contextlib import ExitStack
import concourse.bass as bass
import concourse.mybir as mybir
from concourse.bass_utils import run_bass_kernel_spmd

F32 = mybir.dt.float32
BF16 = mybir.dt.bfloat16
AF = mybir.ActivationFunctionType
ALU = mybir.AluOpType
AX = mybir.AxisListType
BF = ml_dtypes.bfloat16

NCORE = 8
B, S, D = 2, 8192, 1024
TC = 2048
NCHUNK = D // 128


class Prog:
    DMA_STREAMS = ("qsp", "qg", "qcc", "qact")
    STREAMS = ("pe", "act", "dve", "pool", "qsp", "qg", "qcc", "qact")

    PARANOID = False

    def __init__(self, name="prog"):
        self.nc = bass.Bass("TRN2", target_bir_lowering=False, debug=False)
        self.stack = ExitStack()
        self.ops = []
        self.last_w = {}
        self.readers = {}
        self.n_sb = 0
        self.n_ps = 0
        self._finalized = False
        self.tag = ""
        self._stage_stack = None
        self._fence = []

    def stage_begin(self, tag):
        assert self._stage_stack is None
        self.tag = tag + "_"
        self._stage_stack = ExitStack()

    def stage_end(self):
        self._stage_stack.close()
        self._stage_stack = None
        self.tag = ""
        last = {}
        for i, rec in enumerate(self.ops):
            last[rec["stream"]] = i
        self._fence = list(last.values())

    def fence(self):
        last = {}
        for i, rec in enumerate(self.ops):
            last[rec["stream"]] = i
        self._fence = list(last.values())

    def collective(self, ins, outs, groups, kind="AllGather", op=None, key=None,
                   reads=()):
        if op is None:
            op = ALU.bypass
        wk = (key,) if key else ()
        for a, o in zip(ins, outs):
            self.op("qcc", lambda e, a=a, o=o: e.collective_compute(
                kind, op, replica_groups=groups,
                ins=[a.tensor.ap().opt() if hasattr(a, "tensor") else a for a in [a]],
                outs=[o.tensor.ap().opt()]), reads=reads, writes=wk, rmw=("ccchain",))

    def inp(self, name, shape, dtype=F32):
        return self.nc.dram_tensor(name, list(shape), dtype, kind="ExternalInput").ap()

    def out(self, name, shape, dtype=F32):
        return self.nc.dram_tensor(name, list(shape), dtype, kind="ExternalOutput").ap()

    def dram(self, name, shape, dtype=F32):
        return self.nc.dram_tensor(self.tag + name, list(shape), dtype,
                                   kind="Internal").ap()

    def sb(self, name, shape, dtype=F32):
        if self._stage_stack is not None:
            return self._stage_stack.enter_context(
                self.nc.sbuf_tensor(self.tag + name, list(shape), dtype)).ap()
        return self.nc.alloc_sbuf_tensor(name, list(shape), dtype).ap()

    def ps(self, name, shape, dtype=F32):
        if self._stage_stack is not None:
            return self._stage_stack.enter_context(
                self.nc.psum_tensor(self.tag + name, list(shape), dtype)).ap()
        return self.nc.alloc_psum_tensor(name, list(shape), dtype).ap()

    def op(self, stream, emit, reads=(), writes=(), rmw=()):
        assert stream in self.STREAMS, stream
        deps = set()
        for k in tuple(reads) + tuple(rmw):
            if k in self.last_w:
                deps.add(self.last_w[k])
        for k in tuple(writes) + tuple(rmw):
            if k in self.last_w:
                deps.add(self.last_w[k])
            for r in self.readers.get(k, ()):
                deps.add(r)
        opid = len(self.ops)
        for k in tuple(reads) + tuple(rmw):
            self.readers.setdefault(k, []).append(opid)
        for k in tuple(writes) + tuple(rmw):
            self.last_w[k] = opid
            self.readers[k] = [opid] if k in rmw else []
        deps.discard(opid)
        for f in self._fence:
            if f < opid:
                deps.add(f)
        if self.PARANOID and opid > 0:
            deps.add(opid - 1)
        self.ops.append(dict(stream=stream, emit=emit, deps=deps))
        return opid

    DEBUG = False

    def dbg(self, name, ap, reads):
        if not self.DEBUG:
            return
        t = self.nc.dram_tensor("dbg_" + name, [ap.shape[0], ap.shape[1]],
                                ap.dtype, kind="ExternalOutput").ap()
        self.dma("qsp", t[:], ap, reads=tuple(reads), writes=(f"_dbg{name}",))

    def dma(self, queue, out, in_, reads=(), writes=(), **kw):
        return self.op(queue, lambda e: e.dma_start(out=out, in_=in_, **kw),
                       reads=reads, writes=writes)

    DMA_ROT = 8

    def finalize(self):
        assert not self._finalized
        self._finalized = True
        n = len(self.ops)
        import bisect
        dma_ord = {}
        counters = {s: 0 for s in self.DMA_STREAMS}
        for i, rec in enumerate(self.ops):
            s = rec["stream"]
            if s in self.DMA_STREAMS:
                dma_ord[i] = counters[s]
                counters[s] += 1
        marks = {s: [] for s in self.STREAMS}
        waits = [[] for _ in range(n)]
        pending = [sorted(rec["deps"]) for rec in self.ops]
        for i, rec in enumerate(self.ops):
            for d in pending[i]:
                ds = self.ops[d]["stream"]
                if ds in self.DMA_STREAMS:
                    waits[i].append(("dma", ds, d))
                    continue
                if ds == rec["stream"] == "pe":
                    continue
                m = marks[ds]
                j = bisect.bisect_left(m, d)
                if not (j < len(m) and m[j] < i):
                    bisect.insort(m, d)
        for i, rec in enumerate(self.ops):
            for d in pending[i]:
                ds = self.ops[d]["stream"]
                if ds in self.DMA_STREAMS or (ds == rec["stream"] == "pe"):
                    continue
                m = marks[ds]
                j = bisect.bisect_left(m, d)
                assert j < len(m) and m[j] >= d and m[j] < i
                waits[i].append(("cmp", ds, m[j]))
        for s in ("pe", "act", "dve", "pool"):
            sops = [i for i, r in enumerate(self.ops) if r["stream"] == s]
            if sops and sops[-1] not in marks[s]:
                marks[s].append(sops[-1])
        cval = {}
        for s in ("pe", "act", "dve", "pool"):
            for k, opid in enumerate(marks[s]):
                cval[(s, opid)] = k + 1
        markset = {(s, o) for s in ("pe", "act", "dve", "pool") for o in marks[s]}

        nc = self.nc
        R = self.DMA_ROT
        csem = {s: self.stack.enter_context(nc.semaphore("sem_" + s))
                for s in ("pe", "act", "dve", "pool")}
        dsem = {s: [self.stack.enter_context(nc.semaphore(f"sem_{s}{r}"))
                    for r in range(R)] for s in self.DMA_STREAMS}

        def dma_sem_val(opid):
            o = dma_ord[opid]
            s = self.ops[opid]["stream"]
            step = 1 if s == "qcc" else 16
            return dsem[s][o % R], step * (o // R + 1)

        block = self.stack.enter_context(nc.Block())
        bass_engines = dict(pe="tensor", act="scalar", dve="vector",
                            pool="gpsimd", qsp="sync", qg="gpsimd", qcc="gpsimd",
                            qact="scalar")
        groups = {}
        for i, rec in enumerate(self.ops):
            groups.setdefault(bass_engines[rec["stream"]], []).append(i)
        watermark = {}

        def emit_for(beng, eng):
            for i in groups.get(beng, ()):
                rec = self.ops[i]
                s = rec["stream"]
                if s in self.DMA_STREAMS and dma_ord[i] >= R:
                    o = dma_ord[i] - R
                    prev = [k for k, v in dma_ord.items()
                            if self.ops[k]["stream"] == s and v == o][0]
                    h, v = dma_sem_val(prev)
                    wkey = (beng, ("dma", s, o % R))
                    if watermark.get(wkey, 0) < v:
                        eng.wait_ge(h, v)
                        watermark[wkey] = v
                for w in waits[i]:
                    if w[0] == "dma":
                        _, ds, dop = w
                        h, v = dma_sem_val(dop)
                        wkey = (beng, ("dma", ds, dma_ord[dop] % R))
                    else:
                        _, ds, dop = w
                        h, v = csem[ds], cval[(ds, dop)]
                        wkey = (beng, ds)
                    if watermark.get(wkey, 0) >= v:
                        continue
                    eng.wait_ge(h, v)
                    watermark[wkey] = v
                ins = rec["emit"](eng)
                if s in self.DMA_STREAMS:
                    h, v = dma_sem_val(i)
                    ins.then_inc(h, 1 if s == "qcc" else 16)
                elif (s, i) in markset:
                    ins.then_inc(csem[s], 1)

        for beng_name, attr in (("tensor", "tensor"), ("scalar", "scalar"),
                                ("vector", "vector"), ("gpsimd", "gpsimd"),
                                ("sync", "sync")):
            deco = getattr(block, attr)
            deco(lambda eng, b=beng_name: emit_for(b, eng))

        @block.sync
        def _(sync):
            for s in ("pe", "act", "dve", "pool"):
                if marks[s]:
                    sync.wait_ge(csem[s], len(marks[s]))
            for s in self.DMA_STREAMS:
                cnt = counters[s]
                step = 1 if s == "qcc" else 16
                for r in range(R):
                    tot = (cnt - r + R - 1) // R
                    if tot > 0:
                        sync.wait_ge(dsem[s][r], step * tot)

        self.stack.close()
        return nc


def bcast_row(p, tag, row_ap, row_key, ncols, out_tile, out_key, nrows=128, dtype=F32):
    d = p.dram(f"bcd_{tag}", [1, ncols], dtype)
    p.dma("qsp", d[:], row_ap, reads=(row_key,), writes=(f"bcd_{tag}",))
    p.dma("qsp", out_tile, d[:].partition_broadcast(nrows),
          reads=(f"bcd_{tag}",), writes=(out_key,))


def np_split_rows(x, n):
    return [np.ascontiguousarray(v) for v in np.split(x, n, axis=0)]


def ln_slice(p, tag, xs, xs_key, W, xhat_w, xhat_key, sc, eps=1e-5, affine=None):
    assert W <= 512
    rows = sc["lnrows"]
    m, s2, rstd, mr = (rows[:, 0:W], rows[:, 512:512 + W],
                       rows[:, 1024:1024 + W], rows[:, 1536:1536 + W])
    nc8 = len(xs)
    for ec in range(nc8):
        p.op("pe", lambda e, ec=ec: e.matmul(
            sc["ps_stat"][:, 0:512], sc["ones_f32"][:], xs[ec][:, 0:W],
            start=(ec == 0), stop=(ec == nc8 - 1)),
            reads=(f"{xs_key}{ec}", "ones"), writes=("ps_stat",))
    p.op("dve", lambda e: e.tensor_scalar(
        out=m[0:1, :], in0=sc["ps_stat"][:, 0:512], scalar1=1.0 / (nc8 * 128),
        scalar2=None, op0=ALU.mult), reads=("ps_stat",), writes=("lnrows",))
    for ec in range(nc8):
        p.op("act", lambda e, ec=ec: e.activation(
            out=sc["sq"][:, 0:W], in_=xs[ec][:, 0:W], func=AF.Square),
            reads=(f"{xs_key}{ec}",), writes=("sq",))
        p.op("pe", lambda e, ec=ec: e.matmul(
            sc["ps_stat"][:, 0:W], sc["ones_f32"][:], sc["sq"][:, 0:W],
            start=(ec == 0), stop=(ec == nc8 - 1)),
            reads=("sq", "ones"), writes=("ps_stat",))
    p.op("dve", lambda e: e.tensor_scalar(
        out=s2[0:1, :], in0=sc["ps_stat"][:, 0:W], scalar1=1.0 / (nc8 * 128),
        scalar2=None, op0=ALU.mult), reads=("ps_stat",), writes=("lnrows",))
    p.op("dve", lambda e: e.tensor_tensor(out=rstd, in0=m, in1=m, op=ALU.mult),
         rmw=("lnrows",))
    p.op("dve", lambda e: e.tensor_tensor(out=rstd, in0=s2, in1=rstd, op=ALU.subtract),
         rmw=("lnrows",))
    p.op("act", lambda e: e.activation(out=rstd, in_=rstd, func=AF.Sqrt,
         bias=sc["eps"][0:1, :]), reads=("consts",), rmw=("lnrows",))
    p.op("dve", lambda e: e.reciprocal(rstd, rstd), rmw=("lnrows",))
    p.op("dve", lambda e: e.tensor_tensor(out=mr, in0=m, in1=rstd, op=ALU.mult),
         rmw=("lnrows",))
    bcast_row(p, f"{tag}r", rstd[0:1, :], "lnrows", W, sc["rb"][:, 0:W], "rb")
    bcast_row(p, f"{tag}m", mr[0:1, :], "lnrows", W, sc["mb"][:, 0:W], "mb")
    for ec in range(nc8):
        p.op("dve", lambda e, ec=ec: e.tensor_tensor(
            out=sc["sq"][:, 0:W], in0=xs[ec][:, 0:W], in1=sc["rb"][:, 0:W], op=ALU.mult),
            reads=(f"{xs_key}{ec}", "rb"), writes=("sq",))
        if affine is None:
            p.op("pool", lambda e, ec=ec: e.tensor_tensor(
                out=xhat_w(ec), in0=sc["sq"][:, 0:W], in1=sc["mb"][:, 0:W],
                op=ALU.subtract),
                reads=("sq", "mb"), writes=(f"{xhat_key}{ec}",))
        else:
            gcol, bcol, akey = affine
            p.op("pool", lambda e, ec=ec: e.tensor_tensor(
                out=sc["sq"][:, 0:W], in0=sc["sq"][:, 0:W], in1=sc["mb"][:, 0:W],
                op=ALU.subtract), reads=("mb",), rmw=("sq",))
            p.op("pool", lambda e, ec=ec, g=gcol(ec), b=bcol(ec): e.tensor_scalar(
                out=xhat_w(ec), in0=sc["sq"][:, 0:W], scalar1=g, scalar2=b,
                op0=ALU.mult, op1=ALU.add),
                reads=("sq", akey), writes=(f"{xhat_key}{ec}",))


def base_scratch(p):
    sc = {
        "ones_f32": p.sb("ones_f32", [128, 1], F32),
        "ones_bf": p.sb("ones_bf", [128, 1], BF16),
        "sq": p.sb("sq_s", [128, 512], F32),
        "lnrows": p.sb("lnrows", [1, 2048], F32),
        "rb": p.sb("rb_s", [128, 512], F32),
        "mb": p.sb("mb_s", [128, 512], F32),
        "ps_stat": p.ps("ps_stat", [1, 512], F32),
        "eps": None,
    }
    sc["eps"] = p.sb("eps_c", [128, 1], F32)
    sc["cm24"] = p.sb("cm24_c", [128, 1], F32)
    p.op("dve", lambda e: e.memset(sc["ones_f32"][:], 1.0), writes=("ones",))
    p.op("dve", lambda e: e.memset(sc["ones_bf"][:], 1.0), writes=("ones",))
    p.op("dve", lambda e: e.memset(sc["eps"][:], 1e-5), writes=("consts",))
    p.op("dve", lambda e: e.memset(sc["cm24"][:], -24.0), writes=("consts",))
    return sc


def bld_a1(p=None, ext=None):
    standalone = p is None
    if standalone:
        p = Prog()
        ext = dict(xqT=p.inp("xqT", [D, TC]), wqT=p.inp("wqT", [D, D]),
                   wkT=p.inp("wkT", [D, D]), wvT=p.inp("wvT", [D, D]),
                   bq=p.inp("bq", [D, 1]), bk=p.inp("bk", [D, 1]),
                   bvr=p.inp("bvr", [1, D]), q_o=p.out("q_o", [D, TC], BF16),
                   k_o=p.out("k_o", [D, TC], BF16), v_o=p.out("v_o", [TC, D], BF16),
                   sc=base_scratch(p))
    xqT, wqT, wkT, wvT = ext["xqT"], ext["wqT"], ext["wkT"], ext["wvT"]
    bq, bk, bvr = ext["bq"], ext["bk"], ext["bvr"]
    q_o, k_o, v_o = ext["q_o"], ext["k_o"], ext["v_o"]
    sc = ext["sc"]
    p.stage_begin("a1")
    xs = [p.sb(f"xs{i}", [128, 512], F32) for i in range(NCHUNK)]
    xh = [p.sb(f"xh{i}", [128, 512], BF16) for i in range(NCHUNK)]
    wq = [p.sb(f"wq{i}", [128, D], BF16) for i in range(NCHUNK)]
    wk = [p.sb(f"wk{i}", [128, D], BF16) for i in range(NCHUNK)]
    wv = [p.sb(f"wv{i}", [128, D], BF16) for i in range(NCHUNK)]
    bq_sb = p.sb("bq_sb", [128, NCHUNK], F32)
    bk_sb = p.sb("bk_sb", [128, NCHUNK], F32)
    bv_bc = p.sb("bv_bc", [128, D], F32)
    stg = [p.sb(f"stg{j}", [128, 512], BF16) for j in range(4)]
    pp = [p.ps(f"pp{j}", [128, 512], F32) for j in range(4)]
    for i in range(NCHUNK):
        p.dma("qg", wq[i][:], wqT[i * 128:(i + 1) * 128, :], writes=("wq",))
        p.dma("qg", wk[i][:], wkT[i * 128:(i + 1) * 128, :], writes=("wk",))
        p.dma("qg", wv[i][:], wvT[i * 128:(i + 1) * 128, :], writes=("wv",))
        p.dma("qsp", bq_sb[:, i:i + 1], bq[i * 128:(i + 1) * 128, :], writes=("bias",))
        p.dma("qsp", bk_sb[:, i:i + 1], bk[i * 128:(i + 1) * 128, :], writes=("bias",))
    p.dma("qsp", bv_bc[:], bvr[:].partition_broadcast(128), writes=("bias",))

    for si in range(TC // 512):
        sl = slice(si * 512, (si + 1) * 512)
        for ec in range(NCHUNK):
            p.dma("qsp", xs[ec][:], xqT[ec * 128:(ec + 1) * 128, sl], writes=(f"xs{ec}",))
        ln_slice(p, f"s{si}", xs, "xs", 512, lambda ec: xh[ec][:], "xh", sc)
        for (wt, wkey, bsb, dst) in ((wq, "wq", bq_sb, q_o), (wk, "wk", bk_sb, k_o)):
            for eo in range(NCHUNK):
                pj = pp[eo % 2]
                for ec in range(NCHUNK):
                    p.op("pe", lambda e, pj=pj, wt=wt, ec=ec, eo=eo: e.matmul(
                        pj[:], wt[ec][:, eo * 128:(eo + 1) * 128], xh[ec][:],
                        start=(ec == 0), stop=(ec == NCHUNK - 1)),
                        reads=(f"xh{ec}", wkey), writes=(f"pp{eo % 2}",))
                st = stg[eo % 2]
                p.op("dve", lambda e, pj=pj, st=st, bsb=bsb, eo=eo: e.tensor_scalar(
                    out=st[:], in0=pj[:], scalar1=bsb[:, eo:eo + 1], scalar2=None,
                    op0=ALU.add), reads=(f"pp{eo % 2}", "bias"), writes=(f"stg{eo % 2}",))
                p.dma("qsp", dst[eo * 128:(eo + 1) * 128, sl], st[:],
                      reads=(f"stg{eo % 2}",), writes=("out",))
        for tt in range(4):
            for vg in range(2):
                j = 2 + (tt * 2 + vg) % 2
                pj = pp[j]
                for ec in range(NCHUNK):
                    p.op("pe", lambda e, pj=pj, ec=ec, tt=tt, vg=vg: e.matmul(
                        pj[:], xh[ec][:, tt * 128:(tt + 1) * 128],
                        wv[ec][:, vg * 512:(vg + 1) * 512],
                        start=(ec == 0), stop=(ec == NCHUNK - 1)),
                        reads=(f"xh{ec}", "wv"), writes=(f"pp{j}",))
                st = stg[j]
                p.op("dve", lambda e, pj=pj, st=st, vg=vg: e.tensor_tensor(
                    out=st[:], in0=pj[:], in1=bv_bc[:, vg * 512:(vg + 1) * 512],
                    op=ALU.add), reads=(f"pp{j}", "bias"), writes=(f"stg{j}",))
                p.dma("qsp", v_o[si * 512 + tt * 128: si * 512 + (tt + 1) * 128,
                              vg * 512:(vg + 1) * 512], st[:],
                      reads=(f"stg{j}",), writes=("out",))
    p.stage_end()
    return p.finalize() if standalone else None


def bld_a2(p=None, ext=None):
    standalone = p is None
    if standalone:
        p = Prog()
        ext = dict(qT=p.inp("qT", [D, TC], BF16), kT=p.inp("kT", [D, S], BF16),
                   vF=p.inp("vF", [S, D], BF16), xtm=p.inp("xtm", [TC, D]),
                   idenb=p.inp("idenb", [128, 128], BF16),
                   xo=p.out("xo", [TC, D]))
    qT, kT, vF, xtm = ext["qT"], ext["kT"], ext["vF"], ext["xtm"]
    idenb, xo = ext["idenb"], ext["xo"]
    fm_out = ext.get("xo_fm") is not None
    kT_gath = ext.get("kT_gathered", False)
    cck = (ext["cck"],) if "cck" in ext else ()
    p.stage_begin("a2")
    q_fm = [p.sb(f"qf{i}", [128, TC], BF16) for i in range(NCHUNK)]
    iden = p.sb("iden", [128, 128], BF16)
    scores = p.sb("scores", [128, S], F32)
    prob = p.sb("prob", [128, S], BF16)
    pt = p.sb("pt", [128, S], BF16)
    kst = [p.sb(f"kst{j}", [128, NCHUNK * 512], BF16) for j in range(4)]
    vst = [p.sb(f"vst{j}", [128, D], BF16) for j in range(8)]
    mxc = p.sb("mxc", [128, 4], F32)
    out_tm = p.sb("out_tm", [128, D], F32)
    xr = p.sb("xr", [128, D], F32)
    ps_sc = [p.ps(f"ps_sc{j}", [128, 512], F32) for j in range(2)]
    ps_o = [p.ps(f"ps_o{j}", [128, 512], F32) for j in range(2)]
    ps_tr = [p.ps(f"ps_tr{j}", [128, 128], BF16) for j in range(2)]
    if fm_out:
        ps_trf = [p.ps("ps_trf0", [128, 128], F32)] * 2
        idenf_sb = p.sb("idenf_sb", [128, 128], F32)
        ofm = p.sb("ofm", [128, 128], F32)
        p.dma("qsp", idenf_sb[:], ext["idf"][:], writes=("idenf",))

    p.dma("qg", iden[:], idenb[:], writes=("iden",))
    for i in range(NCHUNK):
        p.dma("qg", q_fm[i][:], qT[i * 128:(i + 1) * 128, :], writes=("qf",))
    NKG = S // 512
    NKT = S // 128
    for qt in range(TC // 128):
        qsl = slice(qt * 128, (qt + 1) * 128)
        for kg in range(NKG):
            buf = kst[kg % 4]
            bk_ = f"kst{kg % 4}"
            if not kT_gath:
                p.dma("qsp", buf[:].rearrange("p (c w) -> p c w", w=512),
                      kT[:, kg * 512:(kg + 1) * 512].rearrange("(c p) w -> p c w", p=128),
                      reads=(), writes=(bk_,))
            else:
                jc, lc = kg // 4, (kg % 4) * 512
                p.dma("qsp", buf[:].rearrange("p (c w) -> p c w", w=512),
                      kT[jc * D:(jc + 1) * D, lc:lc + 512]
                      .rearrange("(c p) w -> p c w", p=128),
                      reads=cck, writes=(bk_,))
            pj = ps_sc[kg % 2]
            for ec in range(NCHUNK):
                p.op("pe", lambda e, pj=pj, ec=ec, buf=buf, qsl=qsl: e.matmul(
                    pj[:], q_fm[ec][:, qsl], buf[:, ec * 512:(ec + 1) * 512],
                    start=(ec == 0), stop=(ec == NCHUNK - 1)),
                    reads=("qf", bk_), writes=(f"ps_sc{kg % 2}",))
            p.op("dve", lambda e, pj=pj, kg=kg: e.tensor_copy(
                scores[:, kg * 512:(kg + 1) * 512], pj[:]),
                reads=(f"ps_sc{kg % 2}",), writes=(f"sc{kg}",))
        p.op("dve", lambda e: e.reduce_max(mxc[:, 0:1], scores[:], axis=AX.X),
             reads=tuple(f"sc{kg}" for kg in range(NKG)), writes=("mxc",))
        p.op("dve", lambda e: e.tensor_scalar(
            out=mxc[:, 1:2], in0=mxc[:, 0:1], scalar1=-1.0 / 32.0, scalar2=None,
            op0=ALU.mult), rmw=("mxc",))
        p.op("act", lambda e: e.activation(
            out=prob[:], in_=scores[:], func=AF.Exp, bias=mxc[:, 1:2],
            scale=1.0 / 32.0, accum_out=mxc[:, 2:3]),
            reads=tuple(f"sc{kg}" for kg in range(NKG)), writes=("prob",), rmw=("mxc",))
        p.op("dve", lambda e: e.reciprocal(mxc[:, 3:4], mxc[:, 2:3]), rmw=("mxc",))
        for kt in range(NKT):
            pj = ps_tr[kt % 2]
            p.op("pe", lambda e, pj=pj, kt=kt: e.transpose(
                pj[:], prob[:, kt * 128:(kt + 1) * 128], iden[:]),
                reads=("prob", "iden"), writes=(f"ps_tr{kt % 2}",))
            eng = "dve" if kt % 2 == 0 else "act"
            if eng == "dve":
                p.op("dve", lambda e, pj=pj, kt=kt: e.tensor_copy(
                    pt[:, kt * 128:(kt + 1) * 128], pj[:]),
                    reads=(f"ps_tr{kt % 2}",), writes=(f"pt{kt}",))
            else:
                p.op("act", lambda e, pj=pj, kt=kt: e.activation(
                    out=pt[:, kt * 128:(kt + 1) * 128], in_=pj[:], func=AF.Copy),
                    reads=(f"ps_tr{kt % 2}",), writes=(f"pt{kt}",))
        for kt in range(NKT):
            vb = vst[kt % 8]
            vk = f"vst{kt % 8}"
            p.dma("qg", vb[:], vF[kt * 128:(kt + 1) * 128, :], reads=cck, writes=(vk,))
            for og in range(2):
                p.op("pe", lambda e, vb=vb, kt=kt, og=og: e.matmul(
                    ps_o[og][:], pt[:, kt * 128:(kt + 1) * 128],
                    vb[:, og * 512:(og + 1) * 512],
                    start=(kt == 0), stop=(kt == NKT - 1)),
                    reads=(f"pt{kt}", vk), writes=(f"ps_o{og}",))
        p.dma("qsp", xr[:], xtm[qsl, :], writes=("xr",))
        for og in range(2):
            p.op("dve", lambda e, og=og: e.tensor_scalar(
                out=out_tm[:, og * 512:(og + 1) * 512], in0=ps_o[og][:],
                scalar1=mxc[:, 3:4], scalar2=None, op0=ALU.mult),
                reads=(f"ps_o{og}", "mxc"), writes=("out_tm",))
            p.op("pool", lambda e, og=og: e.tensor_tensor(
                out=out_tm[:, og * 512:(og + 1) * 512],
                in0=out_tm[:, og * 512:(og + 1) * 512],
                in1=xr[:, og * 512:(og + 1) * 512], op=ALU.add),
                reads=("xr",), rmw=("out_tm",))
        if not fm_out:
            p.dma("qsp", xo[qsl, :], out_tm[:], reads=("out_tm",), writes=("xout",))
        else:
            for oc in range(NCHUNK):
                pj = ps_trf[oc % 2]
                p.op("pe", lambda e, pj=pj, oc=oc: e.transpose(
                    pj[:], out_tm[:, oc * 128:(oc + 1) * 128], idenf_sb[:]),
                    reads=("out_tm", "idenf"), writes=("ps_trf0",))
                p.op("dve", lambda e, pj=pj, oc=oc: e.tensor_copy(ofm[:], pj[:]),
                     reads=("ps_trf0",), writes=("ofm",))
                p.dma("qsp", ext["xo_fm"][oc * 128:(oc + 1) * 128, qsl], ofm[:],
                      reads=("ofm",), writes=("xout",))
    p.stage_end()
    return p.finalize() if standalone else None


_EYE_BF = np.eye(128, dtype=np.float32).astype(BF)


def _cores():
    return [(c, c // 4, (c % 4) * TC) for c in range(NCORE)]


def host_attn_a(x, Wq, bq, Wk, bk, Wv, bv):
    wqT = np.ascontiguousarray(Wq.T)
    wkT = np.ascontiguousarray(Wk.T)
    wvT = np.ascontiguousarray(Wv.T)
    bqc = np.ascontiguousarray(bq.reshape(D, 1))
    bkc = np.ascontiguousarray(bk.reshape(D, 1))
    bvr = np.ascontiguousarray(bv.reshape(1, D))
    nc1 = bld_a1()
    in1 = []
    for c, b, o in _cores():
        chunk = x[b, o:o + TC]
        in1.append(dict(xqT=np.ascontiguousarray(chunk.T), wqT=wqT, wkT=wkT,
                        wvT=wvT, bq=bqc, bk=bkc, bvr=bvr))
    r1 = run_bass_kernel_spmd(nc1, in1, core_ids=list(range(NCORE))).results
    kT = {b: np.concatenate([r1[c]["k_o"] for c, bb, o in _cores() if bb == b], axis=1)
          for b in range(B)}
    vF = {b: np.concatenate([r1[c]["v_o"] for c, bb, o in _cores() if bb == b], axis=0)
          for b in range(B)}
    nc2 = bld_a2()
    in2 = []
    for c, b, o in _cores():
        in2.append(dict(qT=r1[c]["q_o"], kT=np.ascontiguousarray(kT[b]),
                        vF=np.ascontiguousarray(vF[b]),
                        xtm=np.ascontiguousarray(x[b, o:o + TC]), idenb=_EYE_BF))
    r2 = run_bass_kernel_spmd(nc2, in2, core_ids=list(range(NCORE))).results
    out = np.empty((B, S, D), np.float32)
    for c, b, o in _cores():
        out[b, o:o + TC] = r2[c]["xo"]
    return out


def bld_ffn(pre_add=False, pre_proj=False, p=None, ext=None, stage="f"):
    standalone = p is None
    if standalone:
        p = Prog()
        ext = dict(xT=p.inp("xT", [D, TC]), rw1=p.inp("rw1", [D, D]),
                   rw2=p.inp("rw2", [D, 8]), rb1=p.inp("rb1", [D, 1]),
                   rb2=p.inp("rb2", [8, 1]), we1=p.inp("we1", [8, D, 4096]),
                   we2=p.inp("we2", [8, 4096, D]), be1c=p.inp("be1c", [128, 256]),
                   be2r=p.inp("be2r", [8, D]), idf=p.inp("idf", [128, 128]),
                   xoT=p.out("xoT", [D, TC]), sc=base_scratch(p))
        if pre_add:
            ext["oT"] = p.inp("oT", [D, TC])
        if pre_proj:
            ext["oT"] = p.inp("oT", [D, TC], BF16)
            ext["wo"] = p.inp("wo", [D, D])
    xT, rw1, rw2, rb1, rb2 = ext["xT"], ext["rw1"], ext["rw2"], ext["rb1"], ext["rb2"]
    we1, we2, be1c, be2r, idf = ext["we1"], ext["we2"], ext["be1c"], ext["be2r"], ext["idf"]
    xoT, sc = ext["xoT"], ext["sc"]
    if pre_add or pre_proj:
        oT = ext["oT"]
    if pre_proj:
        wo = ext["wo"]
    ocol_dyn = ext.get("ocol_dyn")
    wbf = ext.get("wbf", False)
    cck = (ext["cck"],) if "cck" in ext else ()
    wq1 = "qsp" if wbf else "qg"
    wq2 = "qg"
    p.stage_begin(stage)
    TB = 1024
    x_tb = [p.sb(f"x{i}", [128, TB], F32) for i in range(NCHUNK)]
    xh = [p.sb(f"xh{i}", [128, TB], BF16) for i in range(NCHUNK)]
    r1sl = [p.sb(f"r1_{i}", [128, 512], BF16) for i in range(NCHUNK)]
    rw2_sb = [p.sb(f"rw2_{i}", [128, 8], BF16) for i in range(NCHUNK)]
    rb1_sb = p.sb("rb1_sb", [128, NCHUNK], F32)
    rb2_sb = p.sb("rb2_sb", [8, 1], F32)
    be1_sb = p.sb("be1_sb", [128, 256], F32)
    be2_sb = p.sb("be2_sb", [8, D], BF16)
    lgT = p.sb("lgT", [8, TB], F32)
    lg = [p.sb(f"lg{t}", [128, 8], F32) for t in range(8)]
    wtm = [p.sb(f"wtm{t}", [128, 8], F32) for t in range(8)]
    gt8 = [p.sb(f"gt{t}", [128, 8], F32) for t in range(8)]
    ex8 = [p.sb(f"ex{t}", [128, 8], F32) for t in range(8)]
    nm8 = [p.sb(f"nm{t}", [128, 2], F32) for t in range(8)]
    wT_bf = p.sb("wT_bf", [8, TB], BF16)
    tk = p.sb("tk", [128, 16], F32)
    iden_f = p.sb("iden_f", [128, 128], F32)
    osl = p.sb("osl", [128, 512], F32)
    if pre_proj:
        o_all = p.sb("o_all", [128, NCHUNK * TB], BF16)
        o_fm = [o_all[:, i * TB:(i + 1) * TB] for i in range(NCHUNK)]
    hbuf = [p.sb(f"hb{i}", [128, TB], BF16) for i in range(16)]
    w2buf = [p.sb(f"w2b{i}", [128, D], BF16) for i in range(16)]
    w1col = [p.sb(f"w1c{j}", [128, NCHUNK * 128], BF16) for j in range(4)]
    y_acc = [p.sb(f"ya{t}", [128, D], F32) for t in range(8)]
    ps_h = [p.ps(f"ps_h{j}", [128, 512], F32) for j in range(2)]
    ps_eo = [p.ps(f"ps_eo{j}", [128, 512], F32) for j in range(2)]
    ps_l = p.ps("ps_l", [8, 512], F32)
    ps_t = p.ps("ps_t", [128, 128], F32)

    p.dma("qsp", iden_f[:], idf[:], writes=("idenf",))
    for i in range(NCHUNK):
        p.dma("qg", rw2_sb[i][:], rw2[i * 128:(i + 1) * 128, :], writes=("rw2",))
        p.dma("qsp", rb1_sb[:, i:i + 1], rb1[i * 128:(i + 1) * 128, :], writes=("rbias",))
    p.dma("qsp", rb2_sb[:], rb2[:], writes=("rbias",))
    p.dma("qsp", be1_sb[:], be1c[:], writes=("rbias",))
    p.dma("qg", be2_sb[:], be2r[:], writes=("be2",))

    for tb in range(2):
        tb0 = tb * TB
        for ec in range(NCHUNK):
            p.dma("qsp", x_tb[ec][:], xT[ec * 128:(ec + 1) * 128, tb0:tb0 + TB],
                  writes=(f"x{ec}",))
        if pre_add:
            for ec in range(NCHUNK):
                for si in range(2):
                    sl = slice(si * 512, (si + 1) * 512)
                    p.dma("qsp", osl[:], oT[ec * 128:(ec + 1) * 128,
                                            tb0 + si * 512:tb0 + (si + 1) * 512],
                          writes=("osl",))
                    p.op("dve", lambda e, ec=ec, sl=sl: e.tensor_tensor(
                        out=x_tb[ec][:, sl], in0=x_tb[ec][:, sl], in1=osl[:],
                        op=ALU.add), reads=("osl",), rmw=(f"x{ec}",))
        if pre_proj:
            if ocol_dyn is None:
                for ec in range(NCHUNK):
                    p.dma("qg", o_fm[ec], oT[ec * 128:(ec + 1) * 128, tb0:tb0 + TB],
                          writes=(f"of{ec}",))
            else:
                def _emit(e, tb0=tb0):
                    with e.register() as r0:
                        e.reg_load(r0, ocol_dyn)
                        e.reg_add(r0, r0, tb0)
                        return e.dma_start(
                            out=o_all[:].rearrange("p (ec c) -> p ec c", c=TB),
                            in_=bass.AP(oT.tensor, r0,
                                        [[S, 128], [128 * S, NCHUNK], [1, TB]]))
                p.op("qsp", _emit, reads=("offs",) + cck,
                     writes=tuple(f"of{ec}" for ec in range(NCHUNK)))
            for oc in range(NCHUNK):
                wc = w1col[oc % 2]
                wck = f"w1c{oc % 2}"
                p.dma("qg", wc[:].rearrange("p (c w) -> p c w", w=128),
                      wo[:, oc * 128:(oc + 1) * 128].rearrange(
                          "(c p) w -> p c w", p=128), writes=(wck,))
                for qg in range(2):
                    pj = ps_h[qg]
                    pk = f"ps_h{qg}"
                    sl = slice(qg * 512, (qg + 1) * 512)
                    for ec in range(NCHUNK):
                        p.op("pe", lambda e, pj=pj, wc=wc, ec=ec, sl=sl: e.matmul(
                            pj[:], wc[:, ec * 128:(ec + 1) * 128], o_fm[ec][:, sl.start:sl.stop],
                            start=(ec == 0), stop=(ec == NCHUNK - 1)),
                            reads=(f"of{ec}", wck), writes=(pk,))
                    p.op("dve", lambda e, pj=pj, oc=oc, sl=sl: e.tensor_tensor(
                        out=x_tb[oc][:, sl], in0=x_tb[oc][:, sl], in1=pj[:],
                        op=ALU.add), reads=(pk,), rmw=(f"x{oc}",))
        for si in range(2):
            sl = slice(si * 512, (si + 1) * 512)
            xs = [x_tb[ec][:, sl] for ec in range(NCHUNK)]
            ln_slice(p, f"f{tb}{si}", xs, "x", 512,
                     lambda ec, sl=sl: xh[ec][:, sl], "xh", sc)
        for qg in range(2):
            sl = slice(qg * 512, (qg + 1) * 512)
            for eo in range(NCHUNK):
                pj = ps_h[eo % 2]
                pk = f"ps_h{eo % 2}"
                wc = w1col[eo % 2]
                wck = f"w1c{eo % 2}"
                p.dma("qg", wc[:].rearrange("p (c w) -> p c w", w=128),
                      rw1[:, eo * 128:(eo + 1) * 128].rearrange(
                          "(c p) w -> p c w", p=128), writes=(wck,))
                for ec in range(NCHUNK):
                    p.op("pe", lambda e, pj=pj, wc=wc, ec=ec, sl=sl: e.matmul(
                        pj[:], wc[:, ec * 128:(ec + 1) * 128], xh[ec][:, sl],
                        start=(ec == 0), stop=(ec == NCHUNK - 1)),
                        reads=(f"xh{ec}", wck), writes=(pk,))
                p.op("act", lambda e, pj=pj, eo=eo: e.activation(
                    out=r1sl[eo][:], in_=pj[:], func=AF.Silu,
                    bias=rb1_sb[:, eo:eo + 1]),
                    reads=(pk, "rbias"), writes=(f"r1_{eo}",))
            for ec in range(NCHUNK):
                p.op("pe", lambda e, ec=ec: e.matmul(
                    ps_l[:], rw2_sb[ec][:, :], r1sl[ec][:],
                    start=(ec == 0), stop=(ec == NCHUNK - 1)),
                    reads=(f"r1_{ec}", "rw2"), writes=("ps_l",))
            p.op("dve", lambda e, sl=sl: e.tensor_scalar(
                out=lgT[:, sl], in0=ps_l[:], scalar1=rb2_sb[:], scalar2=None,
                op0=ALU.add), reads=("ps_l", "rbias"), writes=("lgT",))
        for tt in range(8):
            p.op("pe", lambda e, tt=tt: e.transpose(
                ps_t[0:128, 0:8], lgT[0:8, tt * 128:(tt + 1) * 128], iden_f[0:8, 0:8]),
                reads=("lgT", "idenf"), writes=("ps_t",))
            p.op("dve", lambda e, tt=tt: e.tensor_copy(lg[tt][:], ps_t[0:128, 0:8]),
                 reads=("ps_t",), writes=(f"lg{tt}",))
        for tt in range(8):
            L, lk = lg[tt], f"lg{tt}"
            m1 = tk[:, tt:tt + 1]
            m2 = tk[:, 8 + tt:9 + tt]
            p.op("dve", lambda e, L=L, m1=m1: e.reduce_max(m1, L[:], axis=AX.X),
                 reads=(lk,), writes=(f"tk{tt}",))
            p.op("dve", lambda e, L=L, m1=m1, tt=tt: e.tensor_scalar(
                out=wtm[tt][:], in0=L[:], scalar1=m1, scalar2=-1e30,
                op0=ALU.is_ge, op1=ALU.mult), reads=(lk, f"tk{tt}"), writes=(f"wtm{tt}",))
            p.op("dve", lambda e, L=L, tt=tt: e.tensor_tensor(
                out=wtm[tt][:], in0=L[:], in1=wtm[tt][:], op=ALU.add),
                reads=(lk,), rmw=(f"wtm{tt}",))
            p.op("dve", lambda e, tt=tt, m2=m2: e.reduce_max(
                m2, wtm[tt][:], axis=AX.X),
                reads=(f"wtm{tt}",), writes=(f"tk2_{tt}",))
            gt, ex, nm = gt8[tt], ex8[tt], nm8[tt]
            p.op("dve", lambda e, L=L, m2=m2, gt=gt: e.tensor_single_scalar(
                out=gt[:], in_=L[:], scalar=m2, op=ALU.is_ge),
                reads=(lk, f"tk2_{tt}"), writes=(f"gt{tt}",))
            p.op("dve", lambda e, m1=m1, nm=nm: e.tensor_scalar(
                out=nm[:, 0:1], in0=m1, scalar1=-1.0, scalar2=None, op0=ALU.mult),
                reads=(f"tk{tt}",), writes=(f"nm{tt}",))
            p.op("act", lambda e, L=L, nm=nm, ex=ex: e.activation(
                out=ex[:], in_=L[:], func=AF.Exp, bias=nm[:, 0:1]),
                reads=(lk, f"nm{tt}"), writes=(f"ex{tt}",))
            p.op("dve", lambda e, m2=m2, nm=nm: e.tensor_tensor(
                out=nm[:, 1:2], in0=m2, in1=nm[:, 0:1], op=ALU.add),
                reads=(f"tk2_{tt}",), rmw=(f"nm{tt}",))
            p.op("act", lambda e, nm=nm: e.activation(
                out=nm[:, 1:2], in_=nm[:, 1:2], func=AF.Exp), rmw=(f"nm{tt}",))
            p.op("dve", lambda e, nm=nm: e.tensor_scalar(
                out=nm[:, 1:2], in0=nm[:, 1:2], scalar1=1.0, scalar2=None,
                op0=ALU.add), rmw=(f"nm{tt}",))
            p.op("dve", lambda e, nm=nm: e.reciprocal(nm[:, 1:2], nm[:, 1:2]),
                 rmw=(f"nm{tt}",))
            p.op("dve", lambda e, ex=ex, nm=nm, gt=gt, tt=tt: e.scalar_tensor_tensor(
                out=wtm[tt][:], in0=ex[:], scalar=nm[:, 1:2], in1=gt[:],
                op0=ALU.mult, op1=ALU.mult),
                reads=(f"ex{tt}", f"nm{tt}", f"gt{tt}"), writes=(f"wtm{tt}",))
            p.op("pe", lambda e, tt=tt: e.transpose(
                ps_t[0:8, :], wtm[tt][:], iden_f[:]),
                reads=(f"wtm{tt}", "idenf"), writes=("ps_t",))
            p.op("dve", lambda e, tt=tt: e.tensor_copy(
                wT_bf[0:8, tt * 128:(tt + 1) * 128], ps_t[0:8, 0:128]),
                reads=("ps_t",), writes=("wT",))
        p.dbg(f"lgT{tb}", lgT[:], ("lgT",))
        p.dbg(f"wT{tb}", wT_bf[:], ("wT",))
        p.dbg(f"xh0_{tb}", xh[0][:], ("xh0",))
        p.dbg(f"tk{tb}", tk[:], tuple(f"tk{t}" for t in range(8)) + tuple(f"tk2_{t}" for t in range(8)))
        p.dbg(f"gt0_{tb}", gt8[0][:], ("gt0",))
        p.dbg(f"ex0_{tb}", ex8[0][:], ("ex0",))
        p.dbg(f"nm0_{tb}", nm8[0][:], ("nm0",))
        p.dbg(f"wtm0_{tb}", wtm[0][:], ("wtm0",))
        for tl in range(8):
            for og in range(2):
                pj = ps_eo[og]
                p.op("pe", lambda e, pj=pj, tl=tl, og=og: e.matmul(
                    pj[:], wT_bf[0:8, tl * 128:(tl + 1) * 128],
                    be2_sb[0:8, og * 512:(og + 1) * 512], start=True, stop=True),
                    reads=("wT", "be2"), writes=(f"ps_eo{og}",))
                p.op("dve", lambda e, pj=pj, tl=tl, og=og: e.tensor_copy(
                    y_acc[tl][:, og * 512:(og + 1) * 512], pj[:]),
                    reads=(f"ps_eo{og}",), writes=(f"ya{tl}",))
        p.dbg(f"yinit{tb}", y_acc[0][:], ("ya0",))
        for exi in range(8):
            for hh in range(2):
                for hc in range(16):
                    g = hh * 16 + hc
                    p.dma(wq2, w2buf[hc][:],
                          we2[exi, g * 128:(g + 1) * 128, :], writes=(f"w2b{hc}",))
                for hc in range(16):
                    g = hh * 16 + hc
                    wc = w1col[hc % 4]
                    wck = f"w1c{hc % 4}"
                    p.dma(wq1, wc[:].rearrange("p (c w) -> p c w", w=128),
                          we1[exi, :, g * 128:(g + 1) * 128].rearrange(
                              "(c p) w -> p c w", p=128), writes=(wck,))
                    for qg in range(2):
                        pj = ps_h[qg]
                        pk = f"ps_h{qg}"
                        sl = slice(qg * 512, (qg + 1) * 512)
                        for ec in range(NCHUNK):
                            p.op("pe", lambda e, pj=pj, wc=wc, ec=ec, sl=sl: e.matmul(
                                pj[:], wc[:, ec * 128:(ec + 1) * 128], xh[ec][:, sl],
                                start=(ec == 0), stop=(ec == NCHUNK - 1)),
                                reads=(f"xh{ec}", wck), writes=(pk,))
                        p.op("act", lambda e, pj=pj, hc=hc, sl=sl, exi=exi, g=g:
                             e.activation(
                                 out=hbuf[hc][:, sl], in_=pj[:], func=AF.Silu,
                                 bias=be1_sb[:, exi * 32 + g:exi * 32 + g + 1]),
                             reads=(pk, "rbias"), writes=(f"hb{hc}",))
                for tl in range(8):
                    for og in range(2):
                        pj = ps_eo[og]
                        pk = f"ps_eo{og}"
                        for hc in range(16):
                            p.op("pe", lambda e, pj=pj, hc=hc, tl=tl, og=og: e.matmul(
                                pj[:], hbuf[hc][:, tl * 128:(tl + 1) * 128],
                                w2buf[hc][:, og * 512:(og + 1) * 512],
                                start=(hc == 0), stop=(hc == 15)),
                                reads=(f"hb{hc}", f"w2b{hc}"), writes=(pk,))
                        p.op("dve", lambda e, pj=pj, tl=tl, og=og, exi=exi:
                             e.scalar_tensor_tensor(
                                 out=y_acc[tl][:, og * 512:(og + 1) * 512], in0=pj[:],
                                 scalar=wtm[tl][:, exi:exi + 1],
                                 in1=y_acc[tl][:, og * 512:(og + 1) * 512],
                                 op0=ALU.mult, op1=ALU.add),
                             reads=(pk, f"wtm{tl}"), rmw=(f"ya{tl}",))
        p.dbg(f"yfin{tb}", y_acc[0][:], ("ya0",))
        p.dbg(f"hb0_{tb}", hbuf[0][:], ("hb0",))
        for tl in range(8):
            for oc in range(NCHUNK):
                p.op("pe", lambda e, tl=tl, oc=oc: e.transpose(
                    ps_t[:], y_acc[tl][:, oc * 128:(oc + 1) * 128], iden_f[:]),
                    reads=(f"ya{tl}", "idenf"), writes=("ps_t",))
                p.op("dve", lambda e, tl=tl, oc=oc: e.tensor_tensor(
                    out=x_tb[oc][:, tl * 128:(tl + 1) * 128],
                    in0=x_tb[oc][:, tl * 128:(tl + 1) * 128],
                    in1=ps_t[:], op=ALU.add),
                    reads=("ps_t",), rmw=(f"x{oc}",))
                p.op("pool", lambda e, oc=oc, tl=tl: e.tensor_tensor(
                    out=x_tb[oc][:, tl * 128:(tl + 1) * 128],
                    in0=x_tb[oc][:, tl * 128:(tl + 1) * 128],
                    in1=xh[oc][:, tl * 128:(tl + 1) * 128], op=ALU.add),
                    reads=(f"xh{oc}",), rmw=(f"x{oc}",))
        for ec in range(NCHUNK):
            p.dma("qsp", xoT[ec * 128:(ec + 1) * 128, tb0:tb0 + TB], x_tb[ec][:],
                  reads=(f"x{ec}",), writes=("xout",))
    p.stage_end()
    return p.finalize() if standalone else None


def host_ffn(x, rw1, rb1, rw2, rb2, we1, be1, we2, be2, o=None):
    nc = bld_ffn(pre_add=o is not None)
    be1c = np.ascontiguousarray(be1.reshape(8, 32, 128).transpose(2, 0, 1).reshape(128, 256))
    idf = np.eye(128, dtype=np.float32)
    ins = []
    for c, b, ofs in _cores():
        d = dict(xT=np.ascontiguousarray(x[b, ofs:ofs + TC].T),
                 rw1=rw1, rw2=rw2, rb1=np.ascontiguousarray(rb1.reshape(D, 1)),
                 rb2=np.ascontiguousarray(rb2.reshape(8, 1)),
                 we1=we1, we2=we2, be1c=be1c, be2r=be2, idf=idf)
        if o is not None:
            d["oT"] = np.ascontiguousarray(o[b, ofs:ofs + TC].T)
        ins.append(d)
    r = run_bass_kernel_spmd(nc, ins, core_ids=list(range(NCORE))).results
    out = np.empty((B, S, D), np.float32)
    for c, b, ofs in _cores():
        out[b, ofs:ofs + TC] = r[c]["xoT"].T
    return out


def bld_b1(p=None, ext=None):
    standalone = p is None
    if standalone:
        p = Prog()
        ext = dict(xT=p.inp("xT", [D, TC]), wqp=p.inp("wqp", [D, D]),
                   wkp=p.inp("wkp", [D, D]), wv=p.inp("wv", [D, D]),
                   cosd=p.inp("cosd", [TC, 512]), sind=p.inp("sind", [TC, 512]),
                   idb=p.inp("idb", [128, 128], BF16),
                   q_o=p.out("q_o", [D, TC], BF16), k_o=p.out("k_o", [D, TC], BF16),
                   v_o=p.out("v_o", [D, TC], BF16), sc=base_scratch(p))
    xT, wqp, wkp, wv = ext["xT"], ext["wqp"], ext["wkp"], ext["wv"]
    cosd, sind = ext["cosd"], ext["sind"]
    q_o, k_o, v_o, sc = ext["q_o"], ext["k_o"], ext["v_o"], ext["sc"]
    p.stage_begin("b1")
    idb_sb = p.sb("idb_sb", [128, 128], BF16)
    p.dma("qg", idb_sb[:], ext["idb"][:], writes=("idb",))
    ps_t2 = [p.ps(f"ps_t2{j}", [128, 128], BF16) for j in range(2)]
    tstg = p.sb("tstg", [128, 128], BF16)
    xs = [p.sb(f"xs{i}", [128, 512], F32) for i in range(NCHUNK)]
    xh = [p.sb(f"xh{i}", [128, 512], BF16) for i in range(NCHUNK)]
    wq_sb = [p.sb(f"wq{i}", [128, D], BF16) for i in range(NCHUNK)]
    wk_sb = [p.sb(f"wk{i}", [128, D], BF16) for i in range(NCHUNK)]
    wv_sb = [p.sb(f"wv{i}", [128, D], BF16) for i in range(NCHUNK)]
    qtm = p.sb("qtm", [128, D], F32)
    ktm = p.sb("ktm", [128, D], F32)
    rtm = p.sb("rtm", [128, D], F32)
    rtmb = p.sb("rtmb", [128, D], BF16)
    cs = p.sb("cs", [128, 512], F32)
    sn = p.sb("sn", [128, 512], F32)
    tmp = p.sb("tmp", [128, 512], F32)
    tmp2 = p.sb("tmp2", [128, 512], F32)
    stg = [p.sb(f"stg{j}", [128, 512], BF16) for j in range(2)]
    pp = [p.ps(f"pp{j}", [128, 512], F32) for j in range(4)]
    for i in range(NCHUNK):
        p.dma("qg", wq_sb[i][:], wqp[i * 128:(i + 1) * 128, :], writes=("wq",))
        p.dma("qg", wk_sb[i][:], wkp[i * 128:(i + 1) * 128, :], writes=("wk",))
        p.dma("qg", wv_sb[i][:], wv[i * 128:(i + 1) * 128, :], writes=("wv",))

    def rope(src_tile, src_key, tt):
        EV, OD = src_tile[:, 0:512], src_tile[:, 512:1024]
        RE, RO = rtm[:, 0:512], rtm[:, 512:1024]
        p.op("dve", lambda e: e.tensor_tensor(out=RE, in0=EV, in1=cs[:], op=ALU.mult),
             reads=(src_key, "cs"), writes=("rtm",))
        p.op("dve", lambda e: e.tensor_tensor(out=tmp[:], in0=OD, in1=sn[:], op=ALU.mult),
             reads=(src_key, "sn"), writes=("tmp",))
        p.op("dve", lambda e: e.tensor_tensor(out=RE, in0=RE, in1=tmp[:], op=ALU.subtract),
             reads=("tmp",), rmw=("rtm",))
        p.op("pool", lambda e: e.tensor_tensor(out=RO, in0=EV, in1=sn[:], op=ALU.mult),
             reads=(src_key, "sn"), writes=("rtm2",))
        p.op("pool", lambda e: e.tensor_tensor(out=tmp2[:], in0=OD, in1=cs[:], op=ALU.mult),
             reads=(src_key, "cs"), writes=("tmp2",))
        p.op("pool", lambda e: e.tensor_tensor(out=RO, in0=RO, in1=tmp2[:], op=ALU.add),
             reads=("tmp2",), rmw=("rtm2",))

    for si in range(4):
        sl = slice(si * 512, (si + 1) * 512)
        for ec in range(NCHUNK):
            p.dma("qsp", xs[ec][:], xT[ec * 128:(ec + 1) * 128, sl], writes=(f"xs{ec}",))
        ln_slice(p, f"s{si}", xs, "xs", 512, lambda ec: xh[ec][:], "xh", sc)
        for tt in range(4):
            trow = si * 512 + tt * 128
            for (wt, wkey, dst_t, dkey) in ((wq_sb, "wq", qtm, "qtm"),
                                            (wk_sb, "wk", ktm, "ktm")):
                for fg in range(2):
                    pj = pp[fg]
                    for ec in range(NCHUNK):
                        p.op("pe", lambda e, pj=pj, wt=wt, ec=ec, tt=tt, fg=fg: e.matmul(
                            pj[:], xh[ec][:, tt * 128:(tt + 1) * 128],
                            wt[ec][:, fg * 512:(fg + 1) * 512],
                            start=(ec == 0), stop=(ec == NCHUNK - 1)),
                            reads=(f"xh{ec}", wkey), writes=(f"pp{fg}",))
                    p.op("dve", lambda e, pj=pj, dst_t=dst_t, fg=fg: e.tensor_copy(
                        dst_t[:, fg * 512:(fg + 1) * 512], pj[:]),
                        reads=(f"pp{fg}",), writes=(dkey,))
            p.dma("qsp", cs[:], cosd[trow:trow + 128, :], writes=("cs",))
            p.dma("qsp", sn[:], sind[trow:trow + 128, :], writes=("sn",))
            for (t_t, tkey, tkey2, dst) in ((qtm, "qtm", "rtm", q_o),
                                            (ktm, "ktm", "rtm", k_o)):
                rope(t_t, tkey, tt)
                p.op("dve", lambda e: e.tensor_copy(rtmb[:, 0:512], rtm[:, 0:512]),
                     reads=("rtm",), writes=("rtmb",))
                p.op("pool", lambda e: e.tensor_copy(rtmb[:, 512:1024], rtm[:, 512:1024]),
                     reads=("rtm2",), rmw=("rtmb",))
                for f in range(8):
                    pj = ps_t2[f % 2]
                    p.op("pe", lambda e, pj=pj, f=f: e.transpose(
                        pj[:], rtmb[:, f * 128:(f + 1) * 128], idb_sb[:]),
                        reads=("rtmb", "idb"), writes=(f"ps_t2{f % 2}",))
                    p.op("dve", lambda e, pj=pj: e.tensor_copy(tstg[:], pj[:]),
                         reads=(f"ps_t2{f % 2}",), writes=("tstg",))
                    for hp in range(4):
                        head = (f % 4) * 4 + hp
                        roff = head * 64 + (0 if f < 4 else 32)
                        p.dma("qsp", dst[roff:roff + 32, trow:trow + 128],
                              tstg[hp * 32:(hp + 1) * 32, :],
                              reads=("tstg",), writes=("out",))
            for vg in range(2):
                pj = pp[2 + vg]
                for ec in range(NCHUNK):
                    p.op("pe", lambda e, pj=pj, ec=ec, tt=tt, vg=vg: e.matmul(
                        pj[:], xh[ec][:, tt * 128:(tt + 1) * 128],
                        wv_sb[ec][:, vg * 512:(vg + 1) * 512],
                        start=(ec == 0), stop=(ec == NCHUNK - 1)),
                        reads=(f"xh{ec}", "wv"), writes=(f"pp{2 + vg}",))
                p.op("dve", lambda e, pj=pj, vg=vg: e.tensor_copy(stg[vg][:], pj[:]),
                     reads=(f"pp{2 + vg}",), writes=(f"stg{vg}",))
                for fq in range(4):
                    f = vg * 4 + fq
                    pj2 = ps_t2[fq % 2]
                    p.op("pe", lambda e, pj2=pj2, vg=vg, fq=fq: e.transpose(
                        pj2[:], stg[vg][:, fq * 128:(fq + 1) * 128], idb_sb[:]),
                        reads=(f"stg{vg}", "idb"), writes=(f"ps_t2{fq % 2}",))
                    p.op("dve", lambda e, pj2=pj2: e.tensor_copy(tstg[:], pj2[:]),
                         reads=(f"ps_t2{fq % 2}",), writes=("tstg",))
                    p.dma("qsp", v_o[f * 128:(f + 1) * 128, trow:trow + 128], tstg[:],
                          reads=("tstg",), writes=("out",))
    p.stage_end()
    return p.finalize() if standalone else None


def bld_b2(p=None, ext=None):
    standalone = p is None
    if standalone:
        p = Prog()
        ext = dict(qb=p.inp("qb", [4, 64, S], BF16), kb=p.inp("kb", [4, 64, S], BF16),
                   vb=p.inp("vb", [4, 64, S], BF16), triA=p.inp("triA", [128, 256], BF16),
                   triB=p.inp("triB", [128, 256], BF16),
                   idb=p.inp("idb2", [128, 128], BF16),
                   ob=p.out("ob", [4, 64, S], BF16), sc=base_scratch(p))
    triA, triB, ob, sc = ext["triA"], ext["triB"], ext["ob"], ext["sc"]
    fused = "qg" in ext
    if not fused:
        qb, kb, vb = ext["qb"], ext["kb"], ext["vb"]
    else:
        qgth, kgth, vgth, ho = ext["qg"], ext["kg"], ext["vg"], ext["ho"]
    cck = (ext["cck"],) if "cck" in ext else ()
    p.stage_begin("b2")
    idb_sb = p.sb("idb_sb", [128, 128], BF16)
    p.dma("qg", idb_sb[:], ext["idb"][:], writes=("idb",))
    hosb = p.sb("hosb", [1, 4], mybir.dt.int32)
    if fused:
        p.dma("qsp", hosb[:], ho[:], writes=("offs",))
    qt = p.sb("qt", [64, S], BF16)
    kt = p.sb("kt", [64, S], BF16)
    vfm = p.sb("vfm", [64, S], BF16)
    ps_vt = [p.ps(f"ps_vt{j}", [128, 64], BF16) for j in range(2)]
    vt = p.sb("vt", [128, 64 * 65], BF16)
    tA = p.sb("tA", [128, 256], BF16)
    tB = p.sb("tB", [128, 256], BF16)
    pt = [p.sb(f"pt{j}", [128, 256], BF16) for j in range(3)]
    rd = p.sb("rd", [128, 256], F32)
    rdb = p.sb("rdb", [64, 256], F32)
    o_sb = [p.sb(f"osb{j}", [64, 256], BF16) for j in range(2)]
    ps_sc = [p.ps(f"ps_sc{j}", [128, 256], F32) for j in range(3)]
    ps_o = [p.ps(f"ps_o{j}", [65, 256], F32) for j in range(2)]
    p.dma("qsp", tA[:], triA[:], writes=("tri",))
    p.dma("qsp", tB[:], triB[:], writes=("tri",))
    NQG = S // 256
    for pair in range(4):
        p.op("dve", lambda e: e.memset(vt[:], 1.0), writes=("vt",))
        if not fused:
            p.dma("qg", qt[:], qb[pair], writes=("qt",))
            p.dma("qg", kt[:], kb[pair], writes=("kt",))
            p.dma("qg", vfm[:], vb[pair], writes=("vfm",))
        else:
            for (dst, srcg, dk) in ((qt, qgth, "qt"), (kt, kgth, "kt"),
                                    (vfm, vgth, "vfm")):
                def _emq(e, dst=dst, srcg=srcg, pair=pair):
                    with e.register() as r0:
                        e.reg_load(r0, hosb[0:1, pair:pair + 1])
                        return e.dma_start(
                            out=dst[:].rearrange("p (j c) -> p j c", c=TC),
                            in_=bass.AP(srcg.tensor, r0,
                                        [[TC, 64], [1024 * TC, 4], [1, TC]]))
                p.op("qact", _emq, reads=("offs",) + cck, writes=(dk,))
        for kti in range(64):
            pj = ps_vt[kti % 2]
            p.op("pe", lambda e, pj=pj, kti=kti: e.transpose(
                pj[:], vfm[0:64, kti * 128:(kti + 1) * 128], idb_sb[0:64, 0:64]),
                reads=("vfm", "idb"), writes=(f"ps_vt{kti % 2}",))
            p.op("dve", lambda e, pj=pj, kti=kti: e.tensor_copy(
                vt[:, kti * 65:kti * 65 + 64], pj[:]),
                reads=(f"ps_vt{kti % 2}",), rmw=("vt",))
        for qg in range(NQG):
            nkt = 2 * qg + 2
            po = ps_o[qg % 2]
            pok = f"ps_o{qg % 2}"
            for kti in range(nkt):
                j = kti % 3
                pj = ps_sc[j]
                pk = f"ps_sc{j}"
                ptj = pt[j]
                ptk = f"pt{j}"
                p.op("pe", lambda e, pj=pj, kti=kti, qg=qg: e.matmul(
                    pj[:], kt[:, kti * 128:(kti + 1) * 128],
                    qt[:, qg * 256:(qg + 1) * 256], start=True, stop=True),
                    reads=("kt", "qt"), writes=(pk,))
                p.op("act", lambda e, pj=pj, ptj=ptj: e.activation(
                    out=ptj[:], in_=pj[:], func=AF.Exp, bias=sc["cm24"][:, 0:1],
                    scale=0.125), reads=(pk, "consts"), writes=(ptk,))
                if kti == nkt - 2:
                    p.op("dve", lambda e, ptj=ptj: e.tensor_tensor(
                        out=ptj[:], in0=ptj[:], in1=tA[:], op=ALU.mult),
                        reads=("tri",), rmw=(ptk,))
                elif kti == nkt - 1:
                    p.op("dve", lambda e, ptj=ptj: e.tensor_tensor(
                        out=ptj[:], in0=ptj[:], in1=tB[:], op=ALU.mult),
                        reads=("tri",), rmw=(ptk,))
                p.op("pe", lambda e, po=po, kti=kti, ptj=ptj: e.matmul(
                    po[:], vt[:, kti * 65:(kti + 1) * 65], ptj[:],
                    start=(kti == 0), stop=(kti == nkt - 1)),
                    reads=("vt", ptk), writes=(pok,))
            p.op("dve", lambda e, po=po: e.reciprocal(rd[64:65, :], po[64:65, :]),
                 reads=(pok,), writes=("rd",))
            bcast_row(p, f"d{pair}_{qg}", rd[64:65, :], "rd", 256, rdb[0:64, :], "rdb", nrows=64)
            osb = o_sb[qg % 2]
            p.op("dve", lambda e, po=po, osb=osb: e.tensor_tensor(
                out=osb[:], in0=po[0:64, :], in1=rdb[0:64, :], op=ALU.mult),
                reads=(pok, "rdb"), writes=(f"osb{qg % 2}",))
            p.dma("qsp", ob[pair][:, qg * 256:(qg + 1) * 256], osb[:],
                  reads=(f"osb{qg % 2}",), writes=("out",))
    p.stage_end()
    return p.finalize() if standalone else None


_ROPE_PERM = None


def _rope_perm():
    global _ROPE_PERM
    if _ROPE_PERM is None:
        perm = []
        for h in range(16):
            for i in range(32):
                perm.append(h * 64 + 2 * i)
        for h in range(16):
            for i in range(32):
                perm.append(h * 64 + 2 * i + 1)
        _ROPE_PERM = np.array(perm)
    return _ROPE_PERM


def _rope_tables(offset, n, theta=10000.0):
    inv = 1.0 / theta ** (np.arange(0, 64, 2, dtype=np.float64) / 64)
    pos = np.arange(offset, offset + n, dtype=np.float64)
    ang = pos[:, None] * inv[None, :]
    cos = np.cos(ang).astype(np.float32)
    sin = np.sin(ang).astype(np.float32)
    return np.tile(cos, (1, 16)), np.tile(sin, (1, 16))


def host_attn_b(x, wq, wk, wv, wo, rw1, rb1, rw2, rb2, we1, be1, we2, be2):
    perm = _rope_perm()
    wqp = np.ascontiguousarray(wq[:, perm])
    wkp = np.ascontiguousarray(wk[:, perm])
    nc1 = bld_b1()
    in1 = []
    for c, b, o in _cores():
        cos, sin = _rope_tables(o, TC)
        in1.append(dict(xT=np.ascontiguousarray(x[b, o:o + TC].T), wqp=wqp, wkp=wkp,
                        wv=wv, cosd=cos, sind=sin, idb=_EYE_BF))
    r1 = run_bass_kernel_spmd(nc1, in1, core_ids=list(range(NCORE))).results
    qf = {b: np.concatenate([r1[c]["q_o"] for c, bb, o in _cores() if bb == b], axis=1)
          for b in range(B)}
    kf = {b: np.concatenate([r1[c]["k_o"] for c, bb, o in _cores() if bb == b], axis=1)
          for b in range(B)}
    vf = {b: np.concatenate([r1[c]["v_o"] for c, bb, o in _cores() if bb == b], axis=1)
          for b in range(B)}
    nc2 = bld_b2()
    r_ = np.arange(128)[:, None]
    c_ = np.arange(256)[None, :]
    triA = (c_ >= r_).astype(BF)
    triB = (c_ >= r_ + 128).astype(BF)
    in2 = []
    for c in range(NCORE):
        qb = np.empty((4, 64, S), BF)
        kb = np.empty((4, 64, S), BF)
        vb = np.empty((4, 64, S), BF)
        for j in range(4):
            pair = c * 4 + j
            b, h = pair // 16, pair % 16
            qb[j] = qf[b][h * 64:(h + 1) * 64, :]
            kb[j] = kf[b][h * 64:(h + 1) * 64, :]
            vb[j] = vf[b][h * 64:(h + 1) * 64, :]
        in2.append(dict(qb=qb, kb=kb, vb=vb, triA=triA, triB=triB, idb2=_EYE_BF))
    r2 = run_bass_kernel_spmd(nc2, in2, core_ids=list(range(NCORE))).results
    of = {b: np.empty((S, D), BF) for b in range(B)}
    for c in range(NCORE):
        for j in range(4):
            pair = c * 4 + j
            b, h = pair // 16, pair % 16
            of[b][:, h * 64:(h + 1) * 64] = r2[c]["ob"][j].T
    nc3 = bld_ffn(pre_proj=True)
    be1c = np.ascontiguousarray(
        be1.reshape(8, 32, 128).transpose(2, 0, 1).reshape(128, 256))
    idf = np.eye(128, dtype=np.float32)
    in3 = []
    for c, b, o in _cores():
        in3.append(dict(xT=np.ascontiguousarray(x[b, o:o + TC].T),
                        oT=np.ascontiguousarray(of[b][o:o + TC].T), wo=wo,
                        rw1=rw1, rw2=rw2, rb1=np.ascontiguousarray(rb1.reshape(D, 1)),
                        rb2=np.ascontiguousarray(rb2.reshape(8, 1)),
                        we1=we1, we2=we2, be1c=be1c, be2r=be2, idf=idf))
    r3 = run_bass_kernel_spmd(nc3, in3, core_ids=list(range(NCORE))).results
    out = np.empty((B, S, D), np.float32)
    for c, b, o in _cores():
        out[b, o:o + TC] = r3[c]["xoT"].T
    return out


def bld_cffn(p=None, ext=None, stage="c"):
    standalone = p is None
    if standalone:
        p = Prog()
        ext = dict(xT=p.inp("xT", [D, TC]), G1R=p.inp("G1R", [32, 128]),
                   G2R=p.inp("G2R", [64, 16]), G1I=p.inp("G1I", [32, 128]),
                   G2I=p.inp("G2I", [64, 16]), brc=p.inp("brc", [512, 1]),
                   bic=p.inp("bic", [512, 1]), idf=p.inp("idf", [128, 128]),
                   xoT=p.out("xoT", [D, TC]), sc=base_scratch(p))
    xT, G1R, G2R, G1I, G2I = ext["xT"], ext["G1R"], ext["G2R"], ext["G1I"], ext["G2I"]
    brc, bic, idf, xoT, sc = ext["brc"], ext["bic"], ext["idf"], ext["xoT"], ext["sc"]
    p.stage_begin(stage)
    iden_f = p.sb("iden_f", [128, 128], F32)
    g1 = p.sb("g1", [32, 128], F32)
    g2 = p.sb("g2", [64, 16], F32)
    g1t = p.sb("g1t", [128, 32], F32)
    g2t = p.sb("g2t", [16, 64], F32)
    urow = p.sb("urow", [1, 1024], F32)
    vrow = p.sb("vrow", [1, 256], F32)
    ot = [p.sb(f"ot{j}", [128, 256], F32) for j in range(8)]
    wr_sb = [p.sb(f"wr{j}", [128, 512], BF16) for j in range(4)]
    wi_sb = [p.sb(f"wi{j}", [128, 512], BF16) for j in range(4)]
    wni_sb = [p.sb(f"wni{j}", [128, 512], BF16) for j in range(4)]
    b_sb = p.sb("b_sb", [128, NCHUNK], F32)
    x_fm = [p.sb(f"x{i}", [128, TC], F32) for i in range(NCHUNK)]
    xh = [p.sb(f"xh{i}", [128, TC], BF16) for i in range(NCHUNK)]
    ps_t = p.ps("ps_t", [128, 128], F32)
    ps_w = [p.ps(f"ps_w{j}", [128, 256], F32) for j in range(4)]
    ps_o = [p.ps(f"ps_o{j}", [128, 512], F32) for j in range(2)]
    g1d = p.dram("g1d", [128, 32])
    g2d = p.dram("g2d", [16, 64])
    outer_d = p.dram("outer_d", [1024, 256])

    p.dma("qsp", iden_f[:], idf[:], writes=("idenf",))
    for i in range(4):
        p.dma("qsp", b_sb[:, i:i + 1], brc[i * 128:(i + 1) * 128, :], writes=("bias",))
        p.dma("qsp", b_sb[:, 4 + i:5 + i], bic[i * 128:(i + 1) * 128, :], writes=("bias",))
    for ec in range(NCHUNK):
        p.dma("qsp", x_fm[ec][:], xT[ec * 128:(ec + 1) * 128, :], writes=(f"x{ec}",))
    for si in range(4):
        sl = slice(si * 512, (si + 1) * 512)
        xs = [x_fm[ec][:, sl] for ec in range(NCHUNK)]
        ln_slice(p, f"c{si}", xs, "x", 512, lambda ec, sl=sl: xh[ec][:, sl], "xh", sc)

    def build_w(G1, G2, dst, dkey, negdst=None, negkey=None):
        p.dma("qsp", g1[:], G1[:], writes=("g1",))
        p.dma("qsp", g2[:], G2[:], writes=("g2",))
        p.op("pe", lambda e: e.transpose(ps_t[0:128, 0:32], g1[:], iden_f[0:32, 0:32]),
             reads=("g1", "idenf"), writes=("ps_t",))
        p.op("dve", lambda e: e.tensor_copy(g1t[:], ps_t[0:128, 0:32]),
             reads=("ps_t",), writes=("g1t",))
        p.op("pe", lambda e: e.transpose(ps_t[0:16, 0:64], g2[:], iden_f[0:64, 0:64]),
             reads=("g2", "idenf"), writes=("ps_t",))
        p.op("dve", lambda e: e.tensor_copy(g2t[:], ps_t[0:16, 0:64]),
             reads=("ps_t",), writes=("g2t",))
        p.dma("qsp", g1d[:], g1t[:], reads=("g1t",), writes=("g1d",))
        p.dma("qsp", g2d[:], g2t[:], reads=("g2t",), writes=("g2d",))
        for wave in range(2):
            for c in range(4):
                p.dma("qsp", urow[:],
                      bass.AP(g1d.tensor, c * 32 * 32, [[1, 1], [1, 1024]]),
                      reads=("g1d",), writes=("urow",))
                p.dma("qsp", vrow[:],
                      bass.AP(g2d.tensor, c * 16, [[1, 1], [64, 16], [1, 16]]),
                      reads=("g2d",), writes=("vrow",))
                p.dbg(f"urow{dkey}{c}", urow[:], ("urow",))
                p.dbg(f"vrow{dkey}{c}", vrow[:], ("vrow",))
                for j in range(4):
                    u8 = wave * 4 + j
                    pj = ps_w[j]
                    p.op("pe", lambda e, pj=pj, u8=u8, c=c: e.matmul(
                        pj[:], urow[0:1, u8 * 128:(u8 + 1) * 128], vrow[0:1, :],
                        start=(c == 0), stop=(c == 3)),
                        reads=("urow", "vrow"), writes=(f"ps_w{j}",))
                    if c == 3:
                        p.op("dve", lambda e, pj=pj, u8=u8, j=j: e.tensor_copy(
                            ot[u8][:], pj[:]),
                            reads=(f"ps_w{j}",), writes=(f"ot{u8}",))
        for u8 in range(8):
            p.dma("qsp", outer_d[u8 * 128:(u8 + 1) * 128, :], ot[u8][:],
                  reads=(f"ot{u8}",), writes=("outer_d",))
        for qc in range(4):
            for dd in range(8):
                row0 = qc * 128 + dd * 16
                d0 = row0 // 16
                p.dma("qg", dst[qc][dd * 16:(dd + 1) * 16, :].rearrange(
                    "p (b e) -> p b e", e=16),
                    bass.AP(outer_d.tensor, d0 * 32 * 256,
                            [[16, 16], [256, 32], [1, 16]]),
                    reads=("outer_d",), writes=(dkey,))
        if negdst is not None:
            for qc in range(4):
                p.op("dve", lambda e, qc=qc: e.tensor_scalar(
                    out=negdst[qc][:], in0=dst[qc][:], scalar1=-1.0, scalar2=None,
                    op0=ALU.mult), reads=(dkey,), writes=(negkey,))

    build_w(G1R, G2R, wr_sb, "wr")
    build_w(G1I, G2I, wi_sb, "wi", wni_sb, "wni")
    for qc in range(4):
        p.dbg(f"wr{qc}", wr_sb[qc][:], ("wr",))
        p.dbg(f"wi{qc}", wi_sb[qc][:], ("wi",))

    for half, (wa, wak, wb, wbk) in enumerate(
            ((wr_sb, "wr", wni_sb, "wni"), (wi_sb, "wi", wr_sb, "wr"))):
        for rc in range(4):
            oc = half * 4 + rc
            for qg in range(4):
                sl = slice(qg * 512, (qg + 1) * 512)
                pj = ps_o[qg % 2]
                pk = f"ps_o{qg % 2}"
                for qc in range(4):
                    p.op("pe", lambda e, pj=pj, wa=wa, qc=qc, rc=rc, sl=sl: e.matmul(
                        pj[:], wa[qc][:, rc * 128:(rc + 1) * 128], xh[qc][:, sl],
                        start=(qc == 0), stop=False),
                        reads=(f"xh{qc}", wak), writes=(pk,))
                for qc in range(4):
                    p.op("pe", lambda e, pj=pj, wb=wb, qc=qc, rc=rc, sl=sl: e.matmul(
                        pj[:], wb[qc][:, rc * 128:(rc + 1) * 128], xh[4 + qc][:, sl],
                        start=False, stop=(qc == 3)),
                        reads=(f"xh{4 + qc}", wbk), writes=(pk,))
                p.op("dve", lambda e, pj=pj, oc=oc, sl=sl: e.scalar_tensor_tensor(
                    out=x_fm[oc][:, sl], in0=pj[:], scalar=b_sb[:, oc:oc + 1],
                    in1=x_fm[oc][:, sl], op0=ALU.add, op1=ALU.add),
                    reads=(pk, "bias"), rmw=(f"x{oc}",))
    for ec in range(NCHUNK):
        p.dma("qsp", xoT[ec * 128:(ec + 1) * 128, :], x_fm[ec][:],
              reads=(f"x{ec}",), writes=("xout",))
    p.stage_end()
    return p.finalize() if standalone else None


def host_cffn(x, G1R, G2R, G1I, G2I, br, bi):
    nc = bld_cffn()
    idf = np.eye(128, dtype=np.float32)
    g1r = np.ascontiguousarray(G1R.reshape(32, 128))
    g2r = np.ascontiguousarray(G2R.reshape(64, 16))
    g1i = np.ascontiguousarray(G1I.reshape(32, 128))
    g2i = np.ascontiguousarray(G2I.reshape(64, 16))
    ins = []
    for c, b, o in _cores():
        ins.append(dict(xT=np.ascontiguousarray(x[b, o:o + TC].T),
                        G1R=g1r, G2R=g2r, G1I=g1i, G2I=g2i,
                        brc=np.ascontiguousarray(br.reshape(512, 1)),
                        bic=np.ascontiguousarray(bi.reshape(512, 1)), idf=idf))
    r = run_bass_kernel_spmd(nc, ins, core_ids=list(range(NCORE))).results
    out = np.empty((B, S, D), np.float32)
    for c, b, o in _cores():
        out[b, o:o + TC] = r[c]["xoT"].T
    return out


def bld_n1(p=None, ext=None):
    standalone = p is None
    if standalone:
        p = Prog()
        ext = dict(xT=p.inp("xT", [D, TC]), Wh=p.inp("Wh", [D, 4096]),
                   bhr=p.inp("bhr", [1, 4096]), Wqk=p.inp("Wqk", [D, 128]),
                   bqk=p.inp("bqk", [128, 1]), lng=p.inp("lng", [D, 1]),
                   lnb=p.inp("lnb", [D, 1]), gb=p.inp("gb", [128, 4]),
                   dnr=p.inp("dnr", [1, TC]), v_o=p.out("v_o", [TC, 2048], BF16),
                   g_o=p.out("g_o", [TC, 2048], BF16), q_o=p.out("q_o", [128, TC], BF16),
                   k_o=p.out("k_o", [128, TC], BF16),
                   xh1_o=p.out("xh1_o", [D, TC], BF16), sc=base_scratch(p))
    xT, Wh, bhr, Wqk, bqk = ext["xT"], ext["Wh"], ext["bhr"], ext["Wqk"], ext["bqk"]
    lng, lnb, gb, dnr = ext["lng"], ext["lnb"], ext["gb"], ext["dnr"]
    v_o, g_o, q_o, k_o, xh1_o, sc = (ext["v_o"], ext["g_o"], ext["q_o"],
                                     ext["k_o"], ext["xh1_o"], ext["sc"])
    p.stage_begin("n1")
    xs = [p.sb(f"xs{i}", [128, 512], F32) for i in range(NCHUNK)]
    xh1 = [p.sb(f"xh1_{i}", [128, 512], F32) for i in range(NCHUNK)]
    xh = [p.sb(f"xh{i}", [128, 512], BF16) for i in range(NCHUNK)]
    xh1b = [p.sb(f"xh1b{i}", [128, 512], BF16) for i in range(NCHUNK)]
    wh_sb = [p.sb(f"wh{i}", [128, 4096], BF16) for i in range(NCHUNK)]
    wqk_sb = [p.sb(f"wqk{i}", [128, 128], BF16) for i in range(NCHUNK)]
    bh_bc = p.sb("bh_bc", [128, 4096], F32)
    bqk_sb = p.sb("bqk_sb", [128, 1], F32)
    lng_sb = p.sb("lng_sb", [128, NCHUNK], F32)
    lnb_sb = p.sb("lnb_sb", [128, NCHUNK], F32)
    gb_sb = p.sb("gb_sb", [128, 4], F32)
    dn_bc = p.sb("dn_bc", [128, 512], F32)
    qkf = p.sb("qkf", [128, 512], F32)
    stg = [p.sb(f"stg{j}", [128, 512], BF16) for j in range(4)]
    pp = [p.ps(f"pp{j}", [128, 512], F32) for j in range(4)]
    for i in range(NCHUNK):
        p.dma("qg", wh_sb[i][:], Wh[i * 128:(i + 1) * 128, :], writes=("wh",))
        p.dma("qg", wqk_sb[i][:], Wqk[i * 128:(i + 1) * 128, :], writes=("wqk",))
        p.dma("qsp", lng_sb[:, i:i + 1], lng[i * 128:(i + 1) * 128, :], writes=("aff",))
        p.dma("qsp", lnb_sb[:, i:i + 1], lnb[i * 128:(i + 1) * 128, :], writes=("aff",))
    p.dma("qsp", bh_bc[:], bhr[:].partition_broadcast(128), writes=("bias",))
    p.dma("qsp", bqk_sb[:], bqk[:], writes=("bias",))
    p.dma("qsp", gb_sb[:], gb[:], writes=("bias",))
    for si in range(4):
        sl = slice(si * 512, (si + 1) * 512)
        for ec in range(NCHUNK):
            p.dma("qsp", xs[ec][:], xT[ec * 128:(ec + 1) * 128, sl], writes=(f"xs{ec}",))
        ln_slice(p, f"a{si}", xs, "xs", 512, lambda ec: xh1[ec][:], "xh1", sc)
        ln_slice(p, f"b{si}", xh1, "xh1", 512, lambda ec: xh[ec][:], "xh", sc,
                 affine=(lambda ec: lng_sb[:, ec:ec + 1],
                         lambda ec: lnb_sb[:, ec:ec + 1], "aff"))
        for ec in range(NCHUNK):
            p.op("dve", lambda e, ec=ec, sl=sl: e.tensor_copy(
                xh1b[ec][:], xh1[ec][:]), reads=(f"xh1{ec}",), writes=("x1b",))
            p.dma("qsp", xh1_o[ec * 128:(ec + 1) * 128, sl], xh1b[ec][:],
                  reads=("x1b",), writes=("out",))
        for tt in range(4):
            trow = si * 512 + tt * 128
            for vg in range(8):
                j = vg % 4
                pj = pp[j]
                pk = f"pp{j}"
                for ec in range(NCHUNK):
                    p.op("pe", lambda e, pj=pj, ec=ec, tt=tt, vg=vg: e.matmul(
                        pj[:], xh[ec][:, tt * 128:(tt + 1) * 128],
                        wh_sb[ec][:, vg * 512:(vg + 1) * 512],
                        start=(ec == 0), stop=(ec == NCHUNK - 1)),
                        reads=(f"xh{ec}", "wh"), writes=(pk,))
                p.op("dve", lambda e, pj=pj, j=j, vg=vg: e.tensor_tensor(
                    out=pj[:], in0=pj[:], in1=bh_bc[:, vg * 512:(vg + 1) * 512],
                    op=ALU.add), reads=("bias",), rmw=(pk,))
                p.op("act", lambda e, pj=pj, j=j: e.activation(
                    out=stg[j][:], in_=pj[:], func=AF.Silu),
                    reads=(pk,), writes=(f"stg{j}",))
                dst = v_o if vg < 4 else g_o
                col = (vg % 4) * 512
                p.dma("qsp", dst[trow:trow + 128, col:col + 512], stg[j][:],
                      reads=(f"stg{j}",), writes=("out",))
    for qg in range(4):
        sl = slice(qg * 512, (qg + 1) * 512)
        for ec in range(NCHUNK):
            p.dma("qsp", xs[ec][:], xT[ec * 128:(ec + 1) * 128, sl], writes=(f"xs{ec}",))
        ln_slice(p, f"c{qg}", xs, "xs", 512, lambda ec: xh1[ec][:], "xh1", sc)
        ln_slice(p, f"d{qg}", xh1, "xh1", 512, lambda ec: xh[ec][:], "xh", sc,
                 affine=(lambda ec: lng_sb[:, ec:ec + 1],
                         lambda ec: lnb_sb[:, ec:ec + 1], "aff"))
        pj = pp[0]
        for ec in range(NCHUNK):
            p.op("pe", lambda e, pj=pj, ec=ec, sl=sl: e.matmul(
                pj[:], wqk_sb[ec][:], xh[ec][:],
                start=(ec == 0), stop=(ec == NCHUNK - 1)),
                reads=(f"xh{ec}", "wqk"), writes=("pp0",))
        p.op("act", lambda e, pj=pj: e.activation(
            out=qkf[:], in_=pj[:], func=AF.Silu, bias=bqk_sb[:]),
            reads=("pp0", "bias"), writes=("qkf",))
        bcast_row(p, f"dn{qg}", dnr[0:1, sl], "_inp", 512, dn_bc[:], "dnbc")
        p.op("dve", lambda e: e.tensor_scalar(
            out=stg[0][:], in0=qkf[:], scalar1=gb_sb[:, 0:1], scalar2=gb_sb[:, 1:2],
            op0=ALU.mult, op1=ALU.add), reads=("qkf", "bias"), writes=("stg0",))
        p.op("dve", lambda e: e.tensor_tensor(
            out=stg[0][:], in0=stg[0][:], in1=dn_bc[:], op=ALU.mult),
            reads=("dnbc",), rmw=("stg0",))
        p.op("pool", lambda e: e.tensor_scalar(
            out=stg[1][:], in0=qkf[:], scalar1=gb_sb[:, 2:3], scalar2=gb_sb[:, 3:4],
            op0=ALU.mult, op1=ALU.add), reads=("qkf", "bias"), writes=("stg1",))
        p.dma("qsp", q_o[:, sl], stg[0][:], reads=("stg0",), writes=("out",))
        p.dma("qsp", k_o[:, sl], stg[1][:], reads=("stg1",), writes=("out",))
    p.stage_end()
    return p.finalize() if standalone else None


def bld_n2(p=None, ext=None):
    standalone = p is None
    if standalone:
        p = Prog()
        ext = dict(q=p.inp("q", [128, TC], BF16), kF=p.inp("kF", [128, S], BF16),
                   vF=p.inp("vF", [S, 2048], BF16), gate=p.inp("gate", [TC, 2048], BF16),
                   Mm=p.inp("Mm", [S, TC], BF16), xT=p.inp("xT", [D, TC]),
                   xh1T=p.inp("xh1T", [D, TC], BF16), Wo=p.inp("Wo", [2048, D]),
                   bo=p.inp("bo", [D, 1]), idb=p.inp("idb", [128, 128], BF16),
                   xoT=p.out("xoT", [D, TC]))
    q, kF, vF, gate = ext["q"], ext["kF"], ext["vF"], ext["gate"]
    Mm, xT, xh1T, Wo, bo, idb, xoT = (ext["Mm"], ext["xT"], ext["xh1T"], ext["Wo"],
                                      ext["bo"], ext["idb"], ext["xoT"])
    kgath = ext.get("kgath")
    cck = (ext["cck"],) if "cck" in ext else ()
    p.stage_begin("n2")
    og_d = p.dram("og_d", [2048, TC], BF16)
    pt_d = p.dram("pt_d", [16, 128, 64 * 128], BF16)
    q_sb = p.sb("q_sb", [128, TC], BF16)
    k_sb = p.sb("k_sb", [128, S], BF16)
    vv = p.sb("vv", [128, 64 * 512], BF16)
    ptb = p.sb("ptb", [128, 64 * 128], BF16)
    mcol = p.sb("mcol", [128, 64 * 128], BF16)
    gt = p.sb("gt", [128, 512], BF16)
    ogs = p.sb("ogs", [128, 512], BF16)
    iden = p.sb("iden", [128, 128], BF16)
    relu_t = p.sb("relu_t", [128, 128], F32)
    xsl = [p.sb(f"xsl{i}", [128, 512], F32) for i in range(NCHUNK)]
    ocol0 = p.sb("ocol0", [128, 512], BF16)
    oco = p.sb("oco", [128, 16 * 512], BF16)
    wcol = [p.sb(f"wcol{j}", [128, 16 * 128], BF16) for j in range(2)]
    bo_sb = p.sb("bo_sb", [128, NCHUNK], F32)
    x1sl = p.sb("x1sl", [128, 512], BF16)
    ps_s = [p.ps(f"ps_s{j}", [128, 128], F32) for j in range(2)]
    ps_pv = [p.ps(f"ps_pv{j}", [128, 512], F32) for j in range(2)]
    ps_tr = [p.ps(f"ps_tr{j}", [128, 128], BF16) for j in range(2)]
    ps_pj = [p.ps("ps_pj0", [128, 512], F32)] * 2
    p.dma("qg", iden[:], idb[:], writes=("iden",))
    p.dma("qg", q_sb[:], q[:], writes=("q",))
    if kgath is None:
        p.dma("qg", k_sb[:], kF[:], writes=("k",))
    else:
        for j in range(4):
            p.dma("qg", k_sb[:, j * TC:(j + 1) * TC], kgath[j * 128:(j + 1) * 128, :],
                  reads=cck, writes=("k",))
    for i in range(NCHUNK):
        p.dma("qsp", bo_sb[:, i:i + 1], bo[i * 128:(i + 1) * 128, :], writes=("bias",))
    for vq in range(4):
        for h in range(4):
            p.dma("qsp", vv[:].rearrange("p (k w) -> p k w", w=512)[:, h * 16:(h + 1) * 16, :],
                  vF[h * 2048:(h + 1) * 2048, vq * 512:(vq + 1) * 512].rearrange(
                      "(k p) w -> p k w", p=128), reads=cck, writes=("vv",))
        for qt in range(16):
            if vq == 0:
                p.dma("qsp", mcol[:].rearrange("p (k w) -> p k w", w=128),
                      Mm[:, qt * 128:(qt + 1) * 128].rearrange("(k p) w -> p k w", p=128),
                      writes=("mcol",))
                for kt in range(64):
                    j = kt % 2
                    pj = ps_s[j]
                    pk = f"ps_s{j}"
                    p.op("pe", lambda e, pj=pj, kt=kt, qt=qt: e.matmul(
                        pj[:], k_sb[:, kt * 128:(kt + 1) * 128],
                        q_sb[:, qt * 128:(qt + 1) * 128], start=True, stop=True),
                        reads=("k", "q"), writes=(pk,))
                    p.op("dve", lambda e, pj=pj, kt=kt: e.scalar_tensor_tensor(
                        out=relu_t[:], in0=pj[:], scalar=0.0,
                        in1=mcol[:, kt * 128:(kt + 1) * 128],
                        op0=ALU.max, op1=ALU.mult),
                        reads=(pk, "mcol"), writes=("relu_t",))
                    p.op("act", lambda e, kt=kt: e.activation(
                        out=ptb[:, kt * 128:(kt + 1) * 128], in_=relu_t[:],
                        func=AF.Square), reads=("relu_t",), writes=(f"ptb{kt}",))
                p.dma("qsp", pt_d[qt], ptb[:],
                      reads=tuple(f"ptb{kt}" for kt in range(64)), writes=("pt_d",))
                if qt == 3:
                    p.dbg("ptb3", ptb[:], tuple(f"ptb{kt}" for kt in range(64)))
            else:
                p.dma("qsp", ptb[:], pt_d[qt],
                      reads=("pt_d",), writes=tuple(f"ptb{kt}" for kt in range(64)))
            pj = ps_pv[qt % 2]
            pk = f"ps_pv{qt % 2}"
            for kt in range(64):
                p.op("pe", lambda e, pj=pj, kt=kt: e.matmul(
                    pj[:], ptb[:, kt * 128:(kt + 1) * 128],
                    vv[:, kt * 512:(kt + 1) * 512],
                    start=(kt == 0), stop=(kt == 63)),
                    reads=(f"ptb{kt}", "vv"), writes=(pk,))
            p.dma("qg", gt[:], gate[qt * 128:(qt + 1) * 128, vq * 512:(vq + 1) * 512],
                  writes=("gt",))
            p.op("dve", lambda e, pj=pj: e.tensor_tensor(
                out=ogs[:], in0=pj[:], in1=gt[:], op=ALU.mult),
                reads=(pk, "gt"), writes=("ogs",))
            for j4 in range(4):
                tj = ps_tr[j4 % 2]
                p.op("pe", lambda e, tj=tj, j4=j4: e.transpose(
                    tj[:], ogs[:, j4 * 128:(j4 + 1) * 128], iden[:]),
                    reads=("ogs", "iden"), writes=(f"ps_tr{j4 % 2}",))
                p.op("dve", lambda e, tj=tj, j4=j4: e.tensor_copy(
                    ocol0[:, j4 * 128:(j4 + 1) * 128], tj[:]),
                    reads=(f"ps_tr{j4 % 2}",), writes=("oc0",))
            p.dma("qsp", og_d[vq * 512:(vq + 1) * 512, qt * 128:(qt + 1) * 128]
                  .rearrange("(a p) w -> p a w", p=128),
                  ocol0[:].rearrange("p (a w) -> p a w", w=128),
                  reads=("oc0",), writes=("og_d",))
    p.dbg("ogd", og_d[:, 0:512], ("og_d",))
    for oc in range(NCHUNK):
        wc = wcol[oc % 2]
        wck = f"wcol{oc % 2}"
        p.dma("qg", wc[:].rearrange("p (c w) -> p c w", w=128),
              Wo[:, oc * 128:(oc + 1) * 128].rearrange("(c p) w -> p c w", p=128),
              writes=(wck,))
        for qg in range(4):
            sl = slice(qg * 512, (qg + 1) * 512)
            p.dma("qsp", oco[:].rearrange("p (c w) -> p c w", w=512),
                  og_d[:, sl].rearrange("(c p) w -> p c w", p=128),
                  reads=("og_d",), writes=("oc1",))
            p.dma("qsp", xsl[oc][:], xT[oc * 128:(oc + 1) * 128, sl],
                  writes=(f"xsl{oc}",))
            p.dma("qg", x1sl[:], xh1T[oc * 128:(oc + 1) * 128, sl], writes=("x1sl",))
            p.op("pool", lambda e, oc=oc: e.tensor_tensor(
                out=xsl[oc][:], in0=xsl[oc][:], in1=x1sl[:], op=ALU.add),
                reads=("x1sl",), rmw=(f"xsl{oc}",))
            pj = ps_pj[0]
            pk = "ps_pj0"
            for vc in range(16):
                p.op("pe", lambda e, pj=pj, wc=wc, vc=vc: e.matmul(
                    pj[:], wc[:, vc * 128:(vc + 1) * 128],
                    oco[:, vc * 512:(vc + 1) * 512],
                    start=(vc == 0), stop=(vc == 15)),
                    reads=("oc1", wck), writes=(pk,))
            p.op("dve", lambda e, pj=pj, oc=oc: e.scalar_tensor_tensor(
                out=xsl[oc][:], in0=pj[:], scalar=bo_sb[:, oc:oc + 1],
                in1=xsl[oc][:], op0=ALU.add, op1=ALU.add),
                reads=(pk, "bias"), rmw=(f"xsl{oc}",))
            p.dma("qsp", xoT[oc * 128:(oc + 1) * 128, sl], xsl[oc][:],
                  reads=(f"xsl{oc}",), writes=("xout",))
    p.stage_end()
    return p.finalize() if standalone else None


def host_nblock(x, ln_g, ln_b, W_h, b_h, W_qk, b_qk, osg, osb, W_out, b_out):
    nc1 = bld_n1()
    gb = np.stack([osg[0], osb[0], osg[1], osb[1]], axis=1).astype(np.float32)
    gb = np.ascontiguousarray(gb)
    in1 = []
    for c, b, o in _cores():
        dnr = (1.0 / (np.arange(o, o + TC, dtype=np.float64) + 1.0)).astype(np.float32)
        in1.append(dict(xT=np.ascontiguousarray(x[b, o:o + TC].T), Wh=W_h,
                        bhr=np.ascontiguousarray(b_h.reshape(1, 4096)), Wqk=W_qk,
                        bqk=np.ascontiguousarray(b_qk.reshape(128, 1)),
                        lng=np.ascontiguousarray(ln_g.reshape(D, 1)),
                        lnb=np.ascontiguousarray(ln_b.reshape(D, 1)),
                        gb=gb, dnr=dnr.reshape(1, TC)))
    r1 = run_bass_kernel_spmd(nc1, in1, core_ids=list(range(NCORE))).results
    vF = {b: np.concatenate([r1[c]["v_o"] for c, bb, o in _cores() if bb == b], axis=0)
          for b in range(B)}
    kB = {b: np.concatenate([r1[c]["k_o"] for c, bb, o in _cores() if bb == b], axis=1)
          for b in range(B)}
    nc2 = bld_n2()
    in2 = []
    for c, b, o in _cores():
        kidx = np.arange(S)[:, None]
        qidx = np.arange(o, o + TC)[None, :]
        Mm = (kidx <= qidx).astype(BF)
        in2.append(dict(q=r1[c]["q_o"], kF=np.ascontiguousarray(kB[b]),
                        vF=np.ascontiguousarray(vF[b]), gate=r1[c]["g_o"], Mm=Mm,
                        xT=np.ascontiguousarray(x[b, o:o + TC].T),
                        xh1T=r1[c]["xh1_o"], Wo=W_out,
                        bo=np.ascontiguousarray(b_out.reshape(D, 1)), idb=_EYE_BF))
    r2 = run_bass_kernel_spmd(nc2, in2, core_ids=list(range(NCORE))).results
    out = np.empty((B, S, D), np.float32)
    for c, b, o in _cores():
        out[b, o:o + TC] = r2[c]["xoT"].T
    return out


def bld_m1(p=None, ext=None):
    standalone = p is None
    TE = 2560
    if standalone:
        p = Prog()
        ext = dict(xTe=p.inp("xTe", [D, TE]), nw=p.inp("nw", [D, 1]),
                   ipT=p.inp("ipT", [D, 4096]), cw=p.inp("cw", [128, 64]),
                   cb=p.inp("cb", [128, 16]), xpT=p.inp("xpT", [2048, 96]),
                   dpT=p.inp("dpT", [64, 2048]), dtb=p.inp("dtb", [128, 16]),
                   uc_o=p.out("uc_o", [2048, TC], BF16),
                   dt_o=p.out("dt_o", [2048, TC], BF16), bc_o=p.out("bc_o", [32, TC]),
                   z_o=p.out("z_o", [2048, TC], BF16),
                   xh_o=p.out("xh_o", [D, TC], BF16), sc=base_scratch(p))
    nw, ipT, cw, cb, xpT, dpT, dtb = (ext["nw"], ext["ipT"], ext["cw"], ext["cb"],
                                      ext["xpT"], ext["dpT"], ext["dtb"])
    uc_o, dt_o, bc_o, z_o, xh_o, sc = (ext["uc_o"], ext["dt_o"], ext["bc_o"],
                                       ext["z_o"], ext["xh_o"], ext["sc"])
    fused = "xcur" in ext
    cck = (ext["cck"],) if "cck" in ext else ()
    if not fused:
        xTe = ext["xTe"]
    else:
        xcur, halog, hoffi, hmaski = (ext["xcur"], ext["halog"], ext["hoff"],
                                      ext["hmask"])
    p.stage_begin("m1")
    if fused:
        hoff_sb = p.sb("hoff_sb", [1, 2], mybir.dt.int32)
        hmask_sb = p.sb("hmask_sb", [128, 1], F32)
        hal_all = p.sb("hal_all", [128, NCHUNK * 3], F32)
        p.dma("qsp", hoff_sb[:], hoffi[:], writes=("offs",))
        p.dma("qsp", hmask_sb[:], hmaski[:], writes=("par",))
    epsm = p.sb("epsm", [128, 1], F32)
    c1 = p.sb("c1", [128, 1], F32)
    p.op("dve", lambda e: e.memset(epsm[:], 1.1920928955078125e-07), writes=("consts",))
    p.op("dve", lambda e: e.memset(c1[:], 1.0), writes=("consts",))
    xs = [p.sb(f"xs{i}", [128, 512], F32) for i in range(NCHUNK)]
    xh = [p.sb(f"xh{i}", [128, 512], BF16) for i in range(NCHUNK)]
    hh = [p.sb(f"hh{i}", [128, 512], BF16) for i in range(NCHUNK)]
    nw_sb = p.sb("nw_sb", [128, NCHUNK], F32)
    cw_sb = p.sb("cw_sb", [128, 64], F32)
    cb_sb = p.sb("cb_sb", [128, 16], F32)
    dtb_sb = p.sb("dtb_sb", [128, 16], F32)
    uext = [p.sb(f"ue{i}", [128, TE], BF16) for i in range(16)]
    ipcol = [p.sb(f"ipc{j}", [128, NCHUNK * 128], BF16) for j in range(2)]
    xp_sb = [p.sb(f"xp{i}", [128, 96], BF16) for i in range(16)]
    dp_sb = p.sb("dp_sb", [64, 2048], BF16)
    xdb = p.sb("xdb", [96, 2048], F32)
    dti = p.sb("dti", [64, 2048], BF16)
    cvb = [p.sb(f"cvb{j}", [128, 2048], F32) for j in range(2)]
    ucb = p.sb("ucb", [128, 2048], BF16)
    spb = [p.sb(f"spb{j}", [128, 512], F32) for j in range(3)]
    stg = [p.sb(f"stg{j}", [128, 512], BF16) for j in range(2)]
    pp = [p.ps(f"pp{j}", [128, 512], F32) for j in range(2)]
    ps_xd = p.ps("ps_xd", [96, 2048], F32)
    for i in range(NCHUNK):
        p.dma("qsp", nw_sb[:, i:i + 1], nw[i * 128:(i + 1) * 128, :], writes=("par",))
    p.dma("qsp", cw_sb[:], cw[:], writes=("par",))
    p.dma("qsp", cb_sb[:], cb[:], writes=("par",))
    p.dma("qsp", dtb_sb[:], dtb[:], writes=("par",))
    for i in range(16):
        p.dma("qg", xp_sb[i][:], xpT[i * 128:(i + 1) * 128, :], writes=("xp",))
    p.dma("qg", dp_sb[:], dpT[:], writes=("dp",))

    for si in range(5):
        sl = slice(si * 512, (si + 1) * 512)
        for ec in range(NCHUNK):
            if not fused:
                p.dma("qsp", xs[ec][:], xTe[ec * 128:(ec + 1) * 128, sl],
                      writes=(f"xs{ec}",))
            elif si == 0:
                p.op("pool", lambda e, ec=ec: e.memset(xs[ec][:], 0.0),
                     writes=(f"xs{ec}",))
                if ec == 0:
                    def _emh(e):
                        with e.register() as r0:
                            e.reg_load(r0, hoff_sb[0:1, 0:1])
                            return e.dma_start(
                                out=hal_all[:].rearrange("p (ec c) -> p ec c", c=3),
                                in_=bass.AP(halog.tensor, r0,
                                            [[3, 128], [128 * 3, NCHUNK], [1, 3]]))
                    p.op("qsp", _emh, reads=("offs",) + cck, writes=("hal_all",))
                p.op("dve", lambda e, ec=ec: e.tensor_scalar(
                    out=xs[ec][:, 509:512], in0=hal_all[:, ec * 3:(ec + 1) * 3],
                    scalar1=hmask_sb[:, 0:1], scalar2=None, op0=ALU.mult),
                    reads=("par", "hal_all"), rmw=(f"xs{ec}",))
            else:
                p.dma("qsp", xs[ec][:], xcur[ec * 128:(ec + 1) * 128,
                                             (si - 1) * 512: si * 512],
                      writes=(f"xs{ec}",))
        ln_slice(p, f"m{si}", xs, "xs", 512, lambda ec: xh[ec][:], "xh", sc)
        if si > 0:
            csl = slice((si - 1) * 512, si * 512)
            for ec in range(NCHUNK):
                p.op("dve", lambda e, ec=ec: e.tensor_copy(stg[0][:], xh[ec][:]),
                     reads=(f"xh{ec}",), writes=("stg0",))
                p.dma("qsp", xh_o[ec * 128:(ec + 1) * 128, csl], stg[0][:],
                      reads=("stg0",), writes=("out",))
        rows = sc["lnrows"]
        rr = rows[:, 1024:1536]
        for ec in range(NCHUNK):
            p.op("act", lambda e, ec=ec: e.activation(
                out=sc["sq"][:], in_=xh[ec][:], func=AF.Square),
                reads=(f"xh{ec}",), writes=("sq",))
            p.op("pe", lambda e, ec=ec: e.matmul(
                sc["ps_stat"][:, 0:512], sc["ones_f32"][:], sc["sq"][:],
                start=(ec == 0), stop=(ec == NCHUNK - 1)),
                reads=("sq", "ones"), writes=("ps_stat",))
        p.op("dve", lambda e: e.tensor_scalar(
            out=rr[0:1, :], in0=sc["ps_stat"][:, 0:512], scalar1=1.0 / D,
            scalar2=None, op0=ALU.mult), reads=("ps_stat",), writes=("lnrows",))
        p.op("act", lambda e: e.activation(out=rr[0:1, :], in_=rr[0:1, :],
             func=AF.Sqrt, bias=epsm[0:1, :]), reads=("consts",), rmw=("lnrows",))
        p.op("dve", lambda e: e.reciprocal(rr[0:1, :], rr[0:1, :]), rmw=("lnrows",))
        bcast_row(p, f"rm{si}", rr[0:1, :], "lnrows", 512, sc["rb"][:], "rb")
        for ec in range(NCHUNK):
            p.op("dve", lambda e, ec=ec: e.scalar_tensor_tensor(
                out=hh[ec][:], in0=xh[ec][:], scalar=nw_sb[:, ec:ec + 1],
                in1=sc["rb"][:], op0=ALU.mult, op1=ALU.mult),
                reads=(f"xh{ec}", "par", "rb"), writes=(f"hh{ec}",))
        for uc in range(32):
            wc = ipcol[uc % 2]
            wck = f"ipc{uc % 2}"
            p.dma("qg", wc[:].rearrange("p (c w) -> p c w", w=128),
                  ipT[:, uc * 128:(uc + 1) * 128].rearrange("(c p) w -> p c w", p=128),
                  writes=(wck,))
            if si == 0 and uc >= 16:
                continue
            pj = pp[uc % 2]
            pk = f"pp{uc % 2}"
            for ec in range(NCHUNK):
                p.op("pe", lambda e, pj=pj, wc=wc, ec=ec: e.matmul(
                    pj[:], wc[:, ec * 128:(ec + 1) * 128], hh[ec][:],
                    start=(ec == 0), stop=(ec == NCHUNK - 1)),
                    reads=(f"hh{ec}", wck), writes=(pk,))
            if uc < 16:
                p.op("dve", lambda e, pj=pj, uc=uc, sl=sl: e.tensor_copy(
                    uext[uc][:, sl], pj[:]), reads=(pk,), writes=(f"ue{uc}",))
            else:
                p.op("dve", lambda e, pj=pj, uc=uc: e.tensor_copy(
                    stg[uc % 2][:], pj[:]), reads=(pk,), writes=(f"stg{uc % 2}",))
                p.dma("qsp", z_o[(uc - 16) * 128:(uc - 15) * 128,
                               (si - 1) * 512: si * 512], stg[uc % 2][:],
                      reads=(f"stg{uc % 2}",), writes=("out",))
    for dc in range(16):
        p.op("dve", lambda e, dc=dc: e.tensor_scalar(
            out=cvb[0][:], in0=uext[dc][:, 509:509 + 2048],
            scalar1=cw_sb[:, dc * 4:dc * 4 + 1], scalar2=cb_sb[:, dc:dc + 1],
            op0=ALU.mult, op1=ALU.add),
            reads=(f"ue{dc}", "par"), writes=("cvb0",))
        for k in range(1, 4):
            src_b, dst_b = cvb[(k + 1) % 2], cvb[k % 2]
            p.op("dve", lambda e, dc=dc, k=k, src_b=src_b, dst_b=dst_b:
                 e.scalar_tensor_tensor(
                     out=dst_b[:], in0=uext[dc][:, 509 + k:509 + k + 2048],
                     scalar=cw_sb[:, dc * 4 + k:dc * 4 + k + 1], in1=src_b[:],
                     op0=ALU.mult, op1=ALU.add),
                 reads=(f"ue{dc}", "par", f"cvb{(k + 1) % 2}"), writes=(f"cvb{k % 2}",))
        p.op("act", lambda e: e.activation(out=ucb[:], in_=cvb[1][:], func=AF.Silu),
             reads=("cvb1",), writes=("ucb",))
        p.dma("qsp", uc_o[dc * 128:(dc + 1) * 128, :], ucb[:],
              reads=("ucb",), writes=("out",))
        for qg in range(4):
            p.op("pe", lambda e, dc=dc, qg=qg: e.matmul(
                ps_xd[:, qg * 512:(qg + 1) * 512], xp_sb[dc][:],
                ucb[:, qg * 512:(qg + 1) * 512],
                start=(dc == 0), stop=(dc == 15)),
                reads=("ucb", "xp"), writes=("ps_xd",))
    for qg in range(4):
        p.op("dve", lambda e, qg=qg: e.tensor_copy(
            xdb[:, qg * 512:(qg + 1) * 512], ps_xd[:, qg * 512:(qg + 1) * 512]),
            reads=("ps_xd",), writes=("xdb",))
    p.dma("qsp", bc_o[:], xdb[64:96, :], reads=("xdb",), writes=("out",))
    p.op("dve", lambda e: e.tensor_copy(dti[:], xdb[0:64, :]),
         reads=("xdb",), writes=("dti",))
    for dc in range(16):
        for qg in range(4):
            pj = pp[qg % 2]
            pk = f"pp{qg % 2}"
            p.op("pe", lambda e, pj=pj, dc=dc, qg=qg: e.matmul(
                pj[:], dp_sb[:, dc * 128:(dc + 1) * 128],
                dti[0:64, qg * 512:(qg + 1) * 512], start=True, stop=True),
                reads=("dti", "dp"), writes=(pk,))
            a, bq, cq = spb[0], spb[1], spb[2]
            p.op("dve", lambda e, pj=pj, dc=dc: e.tensor_scalar(
                out=a[:], in0=pj[:], scalar1=dtb_sb[:, dc:dc + 1], scalar2=None,
                op0=ALU.add), reads=(pk, "par"), writes=("spb0",))
            p.op("dve", lambda e: e.tensor_scalar(
                out=bq[:], in0=a[:], scalar1=-1.0, scalar2=None, op0=ALU.mult),
                reads=("spb0",), writes=("spb1",))
            p.op("dve", lambda e: e.tensor_tensor(
                out=bq[:], in0=a[:], in1=bq[:], op=ALU.max),
                reads=("spb0",), rmw=("spb1",))
            p.op("act", lambda e: e.activation(out=bq[:], in_=bq[:], func=AF.Exp,
                 scale=-1.0), rmw=("spb1",))
            p.op("act", lambda e: e.activation(out=bq[:], in_=bq[:], func=AF.Ln,
                 bias=c1[:, 0:1]), reads=("consts",), rmw=("spb1",))
            p.op("dve", lambda e: e.tensor_scalar(
                out=cq[:], in0=a[:], scalar1=0.0, scalar2=None, op0=ALU.max),
                reads=("spb0",), writes=("spb2",))
            p.op("pool", lambda e, qg=qg: e.tensor_tensor(
                out=stg[qg % 2][:], in0=bq[:], in1=cq[:], op=ALU.add),
                reads=("spb1", "spb2"), writes=(f"stg{qg % 2}",))
            p.dma("qsp", dt_o[dc * 128:(dc + 1) * 128, qg * 512:(qg + 1) * 512],
                  stg[qg % 2][:], reads=(f"stg{qg % 2}",), writes=("out",))
    p.stage_end()
    return p.finalize() if standalone else None


def bld_m2(p=None, ext=None):
    standalone = p is None
    if standalone:
        p = Prog()
        ext = dict(dts=p.inp("dts", [512, S], BF16), ucs=p.inp("ucs", [512, S], BF16),
                   bcf=p.inp("bcf", [32, S]), alog_i=p.inp("alog_i", [64, 128]),
                   dpar=p.inp("dpar", [128, 4]),
                   selc_i=p.inp("selc_i", [128, 2048], BF16),
                   sely_i=p.inp("sely_i", [128, 2048], BF16),
                   y_o=p.out("y_o", [512, S], BF16))
    alog_i, dpar, selc_i, sely_i, y_o = (ext["alog_i"], ext["dpar"], ext["selc_i"],
                                         ext["sely_i"], ext["y_o"])
    fused = "dtg" in ext
    if not fused:
        dts, ucs, bcf = ext["dts"], ext["ucs"], ext["bcf"]
    else:
        dtg, ucg, bcg, dsoffi = ext["dtg"], ext["ucg"], ext["bcg"], ext["dsoff"]
    cck = (ext["cck"],) if "cck" in ext else ()
    p.stage_begin("m2")
    if fused:
        dsoff_sb = p.sb("dsoff_sb", [1, 2], mybir.dt.int32)
        p.dma("qsp", dsoff_sb[:], dsoffi[:], writes=("offs",))
    FS = 1024
    NFS = S // FS
    dt_dc = p.sb("dt_dc", [128, S], BF16)
    uc_dc = p.sb("uc_dc", [128, S], BF16)
    dtu = p.sb("dtu", [128, S], BF16)
    brep = p.sb("brep", [128, S], F32)
    crep = p.sb("crep", [128, S], F32)
    alog_sb = p.sb("alog_sb", [64, 128], F32)
    selc = p.sb("selc", [128, 2048], BF16)
    sely = p.sb("sely", [128, 2048], BF16)
    selp = [p.sb(f"selp{g}", [128, 128], BF16) for g in range(16)]
    abc = p.sb("abc", [128, 128], F32)
    dA = [p.sb(f"dA{j}", [128, FS], F32) for j in range(2)]
    dBu = [p.sb(f"dBu{j}", [128, FS], F32) for j in range(2)]
    hbf = [p.sb(f"hbf{j}", [128, FS], F32) for j in range(2)]
    hc = [p.sb(f"hc{j}", [128, FS], BF16) for j in range(2)]
    ysb = [p.sb(f"ysb{j}", [128, FS], BF16) for j in range(2)]
    carr = p.sb("carr", [128, 16], F32)
    dp_sb = p.sb("dp_sb", [128, 4], F32)
    ps_a = p.ps("ps_a", [128, FS], F32)
    ps_b = p.ps("ps_b", [128, FS], F32)
    ps_a2, ps_b2 = ps_a, ps_b
    ps_y = p.ps("ps_y", [128, FS], F32)
    p.dma("qsp", selc[:], selc_i[:], writes=("selc",))
    p.dma("qsp", dp_sb[:], dpar[:], writes=("dp",))
    p.dma("qsp", alog_sb[:], alog_i[:], writes=("alog",))
    p.op("act", lambda e: e.activation(out=alog_sb[:], in_=alog_sb[:], func=AF.Exp),
         rmw=("alog",))
    p.op("dve", lambda e: e.tensor_scalar(
        out=alog_sb[:], in0=alog_sb[:], scalar1=-1.0, scalar2=None, op0=ALU.mult),
        rmw=("alog",))
    p.dma("qsp", sely[:], sely_i[:], writes=("sely",))
    for g in range(8):
        if not fused:
            p.dma("qsp", brep[g * 16:(g + 1) * 16, :], bcf[0:16, :], writes=("brep",))
            p.dma("qsp", crep[g * 16:(g + 1) * 16, :], bcf[16:32, :], writes=("crep",))
        else:
            for j in range(4):
                p.dma("qsp", brep[g * 16:(g + 1) * 16, j * TC:(j + 1) * TC],
                      bcg[j * 32:j * 32 + 16, :], reads=cck, writes=("brep",))
                p.dma("qsp", crep[g * 16:(g + 1) * 16, j * TC:(j + 1) * TC],
                      bcg[j * 32 + 16:j * 32 + 32, :], reads=cck, writes=("crep",))
    for dc in range(4):
        if not fused:
            p.dma("qg", dt_dc[:], dts[dc * 128:(dc + 1) * 128, :], writes=("dtdc",))
            p.dma("qg", uc_dc[:], ucs[dc * 128:(dc + 1) * 128, :], writes=("ucdc",))
        else:
            for (dstt, srcg, dk) in ((dt_dc, dtg, "dtdc"), (uc_dc, ucg, "ucdc")):
                def _emd(e, dstt=dstt, srcg=srcg, dc=dc):
                    with e.register() as r0:
                        e.reg_load(r0, dsoff_sb[0:1, 0:1])
                        e.reg_add(r0, r0, dc * 128 * TC)
                        return e.dma_start(
                            out=dstt[:].rearrange("p (j c) -> p j c", c=TC),
                            in_=bass.AP(srcg.tensor, r0,
                                        [[TC, 128], [2048 * TC, 4], [1, TC]]))
                p.op("qsp", _emd, reads=("offs",) + cck, writes=(dk,))
        for h in range(4):
            p.op("dve", lambda e, h=h: e.tensor_tensor(
                out=dtu[:, h * 2048:(h + 1) * 2048], in0=dt_dc[:, h * 2048:(h + 1) * 2048],
                in1=uc_dc[:, h * 2048:(h + 1) * 2048], op=ALU.mult),
                reads=("dtdc", "ucdc"), writes=(f"dtu{h}",))
        for g in range(16):
            bcast_row(p, f"a{dc}_{g}", alog_sb[dc * 16 + g:dc * 16 + g + 1, :],
                      "alog", 128, abc[:], "abc")
            p.op("dve", lambda e, g=g: e.tensor_tensor(
                out=selp[g][:], in0=selc[:, g * 128:(g + 1) * 128], in1=abc[:],
                op=ALU.mult), reads=("selc", "abc"), writes=(f"selp{g}",))
        p.op("dve", lambda e: e.memset(carr[:], 0.0),
             writes=tuple(f"carr{g}" for g in range(16)))
        for fs in range(NFS):
            fsl = slice(fs * FS, (fs + 1) * FS)
            for g in range(16):
                w = g % 2
                dAw, dBw, hbw, hcw = dA[w], dBu[w], hbf[w], hc[w]
                psaw, psbw = ps_a, ps_b
                pka, pkb = ("ps_a", "ps_b")
                for j in range(FS // 512):
                    p.op("pe", lambda e, g=g, j=j, fsl=fsl, psaw=psaw: e.matmul(
                        psaw[:, j * 512:(j + 1) * 512], selp[g][:],
                        dt_dc[:, fsl.start + j * 512: fsl.start + (j + 1) * 512],
                        start=True, stop=True),
                        reads=(f"selp{g}", "dtdc"), writes=(pka,))
                    p.op("pe", lambda e, g=g, j=j, fsl=fsl, psbw=psbw: e.matmul(
                        psbw[:, j * 512:(j + 1) * 512], selc[:, g * 128:(g + 1) * 128],
                        dtu[:, fsl.start + j * 512: fsl.start + (j + 1) * 512],
                        start=True, stop=True),
                        reads=("selc", f"dtu{fsl.start // 2048}"), writes=(pkb,))
                p.op("act", lambda e, dAw=dAw, psaw=psaw: e.activation(
                    out=dAw[:], in_=psaw[:], func=AF.Exp),
                    reads=(pka,), writes=(f"dA{w}",))
                p.op("dve", lambda e, fsl=fsl, dBw=dBw, psbw=psbw: e.tensor_tensor(
                    out=dBw[:], in0=psbw[:], in1=brep[:, fsl], op=ALU.mult),
                    reads=(pkb, "brep"), writes=(f"dBu{w}",))
                p.op("dve", lambda e, g=g, dAw=dAw, dBw=dBw, hbw=hbw: e.tensor_tensor_scan(
                    out=hbw[:], data0=dAw[:], data1=dBw[:], initial=carr[:, g:g + 1],
                    op0=ALU.mult, op1=ALU.add),
                    reads=(f"dA{w}", f"dBu{w}", f"carr{g}"), writes=(f"hbf{w}",))
                p.op("dve", lambda e, g=g, hbw=hbw: e.tensor_copy(
                    carr[:, g:g + 1], hbw[:, FS - 1:FS]),
                    reads=(f"hbf{w}",), writes=(f"carr{g}",))
                p.op("pool", lambda e, fsl=fsl, hbw=hbw, hcw=hcw: e.tensor_tensor(
                    out=hcw[:], in0=hbw[:], in1=crep[:, fsl], op=ALU.mult),
                    reads=(f"hbf{w}", "crep"), writes=(f"hc{w}",))
                for j in range(FS // 512):
                    p.op("pe", lambda e, g=g, j=j, hcw=hcw: e.matmul(
                        ps_y[:, j * 512:(j + 1) * 512], sely[:, g * 128:(g + 1) * 128],
                        hcw[:, j * 512:(j + 1) * 512],
                        start=(g == 0), stop=(g == 15)),
                        reads=("sely", f"hc{w}"), writes=("ps_y",))
            yw = ysb[fs % 2]
            for j in range(FS // 512):
                p.op("dve", lambda e, j=j, fsl=fsl, dc=dc, yw=yw: e.scalar_tensor_tensor(
                    out=yw[:, j * 512:(j + 1) * 512],
                    in0=uc_dc[:, fsl.start + j * 512: fsl.start + (j + 1) * 512],
                    scalar=dp_sb[:, dc:dc + 1],
                    in1=ps_y[:, j * 512:(j + 1) * 512], op0=ALU.mult, op1=ALU.add),
                    reads=("ucdc", "dp", "ps_y"), writes=(f"ysb{fs % 2}",))
            p.dma("qsp", y_o[dc * 128:(dc + 1) * 128, fsl], yw[:],
                  reads=(f"ysb{fs % 2}",), writes=("yout",))
    p.stage_end()
    return p.finalize() if standalone else None


def bld_m3(p=None, ext=None):
    standalone = p is None
    if standalone:
        p = Prog()
        ext = dict(yT=p.inp("yT", [2048, TC], BF16), zT=p.inp("zT", [2048, TC], BF16),
                   xhT=p.inp("xhT", [D, TC], BF16), xT=p.inp("xT", [D, TC]),
                   opT=p.inp("opT", [2048, D]), xoT=p.out("xoT", [D, TC]))
    yT, zT, xhT, xT, opT, xoT = (ext["yT"], ext["zT"], ext["xhT"], ext["xT"],
                                 ext["opT"], ext["xoT"])
    ycol_dyn = ext.get("ycol_dyn")
    cck = (ext["cck"],) if "cck" in ext else ()
    p.stage_begin("m3")
    op_sb = [p.sb(f"op{i}", [128, D], BF16) for i in range(16)]
    yall = p.sb("yall", [128, 16 * 512], BF16)
    ysl = [yall[:, i * 512:(i + 1) * 512] for i in range(16)]
    gy = [p.sb(f"gy{i}", [128, 512], BF16) for i in range(16)]
    zsl = p.sb("zsl", [128, 512], BF16)
    zss = p.sb("zss", [128, 512], BF16)
    xsl = p.sb("xsl", [128, 512], F32)
    x1l = p.sb("x1l", [128, 512], BF16)
    ost = p.sb("ost", [128, 512], F32)
    pp = [p.ps(f"pp{j}", [128, 512], F32) for j in range(2)]
    for i in range(16):
        p.dma("qg", op_sb[i][:], opT[i * 128:(i + 1) * 128, :], writes=("op",))
    for qg in range(4):
        sl = slice(qg * 512, (qg + 1) * 512)
        if ycol_dyn is None:
            for dc in range(16):
                p.dma("qg", ysl[dc], yT[dc * 128:(dc + 1) * 128, sl],
                      writes=(f"ysl{dc}",))
        else:
            def _emity(e, sl=sl):
                with e.register() as r0:
                    e.reg_load(r0, ycol_dyn)
                    e.reg_add(r0, r0, sl.start)
                    return e.dma_start(
                        out=yall[:].rearrange("p (dc c) -> p dc c", c=512),
                        in_=bass.AP(yT.tensor, r0, [[S, 128], [128 * S, 16], [1, 512]]))
            p.op("qsp", _emity, reads=("offs",) + cck,
                 writes=tuple(f"ysl{dc}" for dc in range(16)))
        for dc in range(16):
            p.dma("qg", zsl[:], zT[dc * 128:(dc + 1) * 128, sl], writes=("zsl",))
            p.op("act", lambda e: e.activation(out=zss[:], in_=zsl[:], func=AF.Silu),
                 reads=("zsl",), writes=("zss",))
            p.op("dve", lambda e, dc=dc: e.tensor_tensor(
                out=gy[dc][:], in0=ysl[dc], in1=zss[:], op=ALU.mult),
                reads=(f"ysl{dc}", "zss"), writes=(f"gy{dc}",))
        for oc in range(NCHUNK):
            pj = pp[oc % 2]
            pk = f"pp{oc % 2}"
            for dc in range(16):
                p.op("pe", lambda e, pj=pj, dc=dc, oc=oc: e.matmul(
                    pj[:], op_sb[dc][:, oc * 128:(oc + 1) * 128], gy[dc][:],
                    start=(dc == 0), stop=(dc == 15)),
                    reads=(f"gy{dc}", "op"), writes=(pk,))
            p.dma("qsp", xsl[:], xT[oc * 128:(oc + 1) * 128, sl], writes=("xsl",))
            p.dma("qg", x1l[:], xhT[oc * 128:(oc + 1) * 128, sl], writes=("x1l",))
            p.op("dve", lambda e, pj=pj: e.tensor_tensor(
                out=ost[:], in0=pj[:], in1=xsl[:], op=ALU.add),
                reads=(pk, "xsl"), writes=("ost",))
            p.op("pool", lambda e: e.tensor_tensor(
                out=ost[:], in0=ost[:], in1=x1l[:], op=ALU.add),
                reads=("x1l",), rmw=("ost",))
            p.dma("qsp", xoT[oc * 128:(oc + 1) * 128, sl], ost[:],
                  reads=("ost",), writes=("xout",))
    p.stage_end()
    return p.finalize() if standalone else None


def host_mamba(x, norm_w, in_proj_w, conv_w, conv_b, x_proj_w, dt_proj_w, dt_proj_b,
               A_log, Dp, out_proj_w):
    nc1 = bld_m1()
    cwm = np.ascontiguousarray(conv_w.reshape(16, 128, 4).transpose(1, 0, 2).reshape(128, 64))
    cbm = np.ascontiguousarray(conv_b.reshape(16, 128).T)
    dtbm = np.ascontiguousarray(dt_proj_b.reshape(16, 128).T)
    ipT = np.ascontiguousarray(in_proj_w.T)
    xpT = np.ascontiguousarray(x_proj_w.T)
    dpT = np.ascontiguousarray(dt_proj_w.T)
    in1 = []
    for c, b, o in _cores():
        xTe = np.zeros((D, 2560), np.float32)
        xTe[:, 512:] = x[b, o:o + TC].T
        if o > 0:
            xTe[:, 509:512] = x[b, o - 3:o].T
        in1.append(dict(xTe=xTe, nw=np.ascontiguousarray(norm_w.reshape(D, 1)),
                        ipT=ipT, cw=cwm, cb=cbm, xpT=xpT, dpT=dpT, dtb=dtbm))
    r1 = run_bass_kernel_spmd(nc1, in1, core_ids=list(range(NCORE))).results
    ucf = {b: np.concatenate([r1[c]["uc_o"] for c, bb, o in _cores() if bb == b], axis=1)
           for b in range(B)}
    dtf = {b: np.concatenate([r1[c]["dt_o"] for c, bb, o in _cores() if bb == b], axis=1)
           for b in range(B)}
    bcf = {b: np.concatenate([r1[c]["bc_o"] for c, bb, o in _cores() if bb == b], axis=1)
           for b in range(B)}
    selc = np.zeros((128, 16, 128), np.float32)
    for g in range(16):
        for pp_ in range(128):
            selc[g * 8 + pp_ // 16, g, pp_] = 1.0
    selc = selc.reshape(128, 2048).astype(BF)
    sely = np.zeros((128, 16, 128), np.float32)
    for g in range(16):
        for pp_ in range(128):
            sely[pp_, g, g * 8 + pp_ // 16] = 1.0
    sely = sely.reshape(128, 2048).astype(BF)
    nc2 = bld_m2()
    in2 = []
    for c in range(NCORE):
        b, ds = c // 4, (c % 4) * 512
        alog = np.ascontiguousarray(A_log[ds:ds + 512].reshape(64, 128))
        dpar = np.ascontiguousarray(Dp[ds:ds + 512].reshape(4, 128).T)
        in2.append(dict(dts=np.ascontiguousarray(dtf[b][ds:ds + 512]),
                        ucs=np.ascontiguousarray(ucf[b][ds:ds + 512]),
                        bcf=np.ascontiguousarray(bcf[b]), alog_i=alog, dpar=dpar,
                        selc_i=np.ascontiguousarray(selc),
                        sely_i=np.ascontiguousarray(sely)))
    r2 = run_bass_kernel_spmd(nc2, in2, core_ids=list(range(NCORE))).results
    yf = {b: np.concatenate([r2[b * 4 + j]["y_o"] for j in range(4)], axis=0)
          for b in range(B)}
    nc3 = bld_m3()
    opT = np.ascontiguousarray(out_proj_w.T)
    in3 = []
    for c, b, o in _cores():
        in3.append(dict(yT=np.ascontiguousarray(yf[b][:, o:o + TC]), zT=r1[c]["z_o"],
                        xhT=r1[c]["xh_o"], xT=np.ascontiguousarray(x[b, o:o + TC].T),
                        opT=opT))
    r3 = run_bass_kernel_spmd(nc3, in3, core_ids=list(range(NCORE))).results
    out = np.empty((B, S, D), np.float32)
    for c, b, o in _cores():
        out[b, o:o + TC] = r3[c]["xoT"].T
    return out


def kernel(**inp):
    return kernel_fused(**inp)


def kernel_unfused(**inp):
    g = lambda k: np.ascontiguousarray(np.asarray(inp[k], np.float32))
    x = g("L0_attn_a_strat")
    x = host_attn_a(x, g("L0_attn_a_Wq"), g("L0_attn_a_bq"), g("L0_attn_a_Wk"),
                    g("L0_attn_a_bk"), g("L0_attn_a_Wv"), g("L0_attn_a_bv"))
    x = host_ffn(x, g("L0_ffn_rw1"), g("L0_ffn_rb1"), g("L0_ffn_rw2"), g("L0_ffn_rb2"),
                 g("L0_ffn_we1"), g("L0_ffn_be1"), g("L0_ffn_we2"), g("L0_ffn_be2"))
    x = host_attn_b(x, g("L0_attn_b_wq"), g("L0_attn_b_wk"), g("L0_attn_b_wv"),
                    g("L0_attn_b_wo"), g("L0_ffn_rw1"), g("L0_ffn_rb1"),
                    g("L0_ffn_rw2"), g("L0_ffn_rb2"), g("L0_ffn_we1"),
                    g("L0_ffn_be1"), g("L0_ffn_we2"), g("L0_ffn_be2"))
    x = host_mamba(x, g("L1_attn_a_norm_w"), g("L1_attn_a_in_proj_w"),
                   g("L1_attn_a_conv_w"), g("L1_attn_a_conv_b"), g("L1_attn_a_x_proj_w"),
                   g("L1_attn_a_dt_proj_w"), g("L1_attn_a_dt_proj_b"),
                   g("L1_attn_a_A_log"), g("L1_attn_a_D"), g("L1_attn_a_out_proj_w"))
    x = host_cffn(x, g("L1_ffn_G1_R"), g("L1_ffn_G2_R"), g("L1_ffn_G1_I"),
                  g("L1_ffn_G2_I"), g("L1_ffn_bias_real"), g("L1_ffn_bias_imag"))
    x = host_nblock(x, g("L1_attn_b_ln_g"), g("L1_attn_b_ln_b"), g("L1_attn_b_W_hidden"),
                    g("L1_attn_b_b_hidden"), g("L1_attn_b_W_qk"), g("L1_attn_b_b_qk"),
                    g("L1_attn_b_os_gamma"), g("L1_attn_b_os_beta"),
                    g("L1_attn_b_W_out"), g("L1_attn_b_b_out"))
    x = host_cffn(x, g("L1_ffn_G1_R"), g("L1_ffn_G2_R"), g("L1_ffn_G1_I"),
                  g("L1_ffn_G2_I"), g("L1_ffn_bias_real"), g("L1_ffn_bias_imag"))
    return x


GROUPS = [[0, 1, 2, 3], [4, 5, 6, 7]]


def bld_fused(upto=99):
    p = Prog()
    I = p.inp
    ext_in = dict(
        xqT=I("xqT", [D, TC]), xtm=I("xtm", [TC, D]),
        wqT=I("wqT", [D, D]), wkT=I("wkT", [D, D]), wvT=I("wvT", [D, D]),
        bq=I("bq", [D, 1]), bk=I("bk", [D, 1]), bvr=I("bvr", [1, D]),
        idenb=I("idenb", [128, 128], BF16), idf=I("idf", [128, 128]),
        rw1=I("rw1", [D, D]), rw2=I("rw2", [D, 8]), rb1=I("rb1", [D, 1]),
        rb2=I("rb2", [8, 1]), we1=I("we1", [8, D, 4096]), we2=I("we2", [8, 4096, D]),
        be1c=I("be1c", [128, 256]), be2r=I("be2r", [8, D]),
        wqp=I("wqp", [D, D]), wkp=I("wkp", [D, D]), wv=I("wv", [D, D]),
        cosd=I("cosd", [TC, 512]), sind=I("sind", [TC, 512]),
        triA=I("triA", [128, 256], BF16), triB=I("triB", [128, 256], BF16),
        ho=I("ho", [1, 4], mybir.dt.int32), wo=I("wo", [D, D]),
        nw=I("nw", [D, 1]), ipT=I("ipT", [D, 4096]), cw=I("cw", [128, 64]),
        cb=I("cb", [128, 16]), xpT=I("xpT", [2048, 96]), dpT=I("dpT", [64, 2048]),
        dtb=I("dtb", [128, 16]), hoff=I("hoff", [1, 2], mybir.dt.int32),
        hmask=I("hmask", [128, 1]),
        alog_i=I("alog_i", [64, 128]), dpar=I("dpar", [128, 4]),
        selc_i=I("selc_i", [128, 2048], BF16), sely_i=I("sely_i", [128, 2048], BF16),
        dsoff=I("dsoff", [1, 2], mybir.dt.int32), opT=I("opT", [2048, D]),
        G1R=I("G1R", [32, 128]), G2R=I("G2R", [64, 16]),
        G1I=I("G1I", [32, 128]), G2I=I("G2I", [64, 16]),
        brc=I("brc", [512, 1]), bic=I("bic", [512, 1]),
        Wh=I("Wh", [D, 4096]), bhr=I("bhr", [1, 4096]), Wqk=I("Wqk", [D, 128]),
        bqk=I("bqk", [128, 1]), lng=I("lng", [D, 1]), lnb=I("lnb", [D, 1]),
        gb=I("gb", [128, 4]), dnr=I("dnr", [1, TC]), Mm=I("Mm", [S, TC], BF16),
        Wo=I("Wo", [2048, D]), bo=I("bo", [D, 1]),
        offs=I("offs", [1, 8], mybir.dt.int32),
    )
    xout = p.out("xout", [D, TC])
    sc = base_scratch(p)
    offs_sb = p.sb("offs_sb", [1, 8], mybir.dt.int32)
    p.dma("qsp", offs_sb[:], ext_in["offs"][:], writes=("offs",))
    we1b = p.dram("we1b", [8, D, 4096], BF16)
    we2b = p.dram("we2b", [8, 4096, D], BF16)
    for e8 in range(8):
        p.dma("qg", we1b[e8], ext_in["we1"][e8], writes=("we1b",))
        p.dma("qg", we2b[e8], ext_in["we2"][e8], writes=("we2b",))

    E = dict(ext_in)
    E["sc"] = sc
    dram = p.dram
    cur = None

    def tap(x_ap, stage_no):
        if upto == stage_no:
            for ec in range(NCHUNK):
                p.dma("qsp", xout[ec * 128:(ec + 1) * 128, :],
                      x_ap[ec * 128:(ec + 1) * 128, :], writes=("tapout",))
            return True
        return False

    q1 = dram("q1", [D, TC], BF16)
    k1c = dram("k1c", [D, TC], BF16)
    v1c = dram("v1c", [TC, D], BF16)
    bld_a1(p, dict(E, q_o=q1, k_o=k1c, v_o=v1c))
    kg1 = dram("kg1", [4 * D, TC], BF16)
    vg1 = dram("vg1", [S, D], BF16)
    p.collective([k1c, v1c], [kg1, vg1], GROUPS, key="cc1")
    xc1 = dram("xc1", [D, TC])
    bld_a2(p, dict(E, qT=q1, kT=kg1, vF=vg1, xo=None, xo_fm=xc1, kT_gathered=True,
                   cck="cc1"))
    if tap(xc1, 2):
        return p.finalize()
    xc2 = dram("xc2", [D, TC])
    bld_ffn(p=p, ext=dict(E, xT=xc1, xoT=xc2, we1=we1b, we2=we2b, wbf=True),
            stage="f1")
    if tap(xc2, 3):
        return p.finalize()
    q2c = dram("q2c", [D, TC], BF16)
    k2c = dram("k2c", [D, TC], BF16)
    v2c = dram("v2c", [D, TC], BF16)
    bld_b1(p, dict(E, xT=xc2, idb=E["idenb"], q_o=q2c, k_o=k2c, v_o=v2c))
    qg2 = dram("qg2", [4 * D, TC], BF16)
    kg2 = dram("kg2", [4 * D, TC], BF16)
    vg2 = dram("vg2", [4 * D, TC], BF16)
    p.collective([q2c, k2c, v2c], [qg2, kg2, vg2], GROUPS, key="cc2")
    ob2 = dram("ob2", [4, 64, S], BF16)
    bld_b2(p, dict(E, qg=qg2, kg=kg2, vg=vg2, idb=E["idenb"], ob=ob2, cck="cc2"))
    og2 = dram("og2", [D, S], BF16)
    p.collective([ob2.rearrange("a b c -> (a b) c")], [og2], GROUPS, key="cc3")
    xc3 = dram("xc3", [D, TC])
    bld_ffn(pre_proj=True, p=p,
            ext=dict(E, xT=xc2, xoT=xc3, oT=og2, ocol_dyn=offs_sb[0:1, 0:1],
                     we1=we1b, we2=we2b, wbf=True, cck="cc3"),
            stage="f2")
    if tap(xc3, 6):
        return p.finalize()
    h3c = dram("h3c", [D, 3])
    p.dma("qsp", h3c[:], xc3[:, TC - 3:TC], writes=("h3c",))
    hg3 = dram("hg3", [4 * D, 3])
    p.collective([h3c], [hg3], GROUPS, key="cch", reads=("h3c",))
    uc1 = dram("uc1", [2048, TC], BF16)
    dt1 = dram("dt1", [2048, TC], BF16)
    bc1 = dram("bc1", [32, TC])
    z1 = dram("z1", [2048, TC], BF16)
    xh1m = dram("xh1m", [D, TC], BF16)
    bld_m1(p, dict(E, xcur=xc3, halog=hg3, uc_o=uc1, dt_o=dt1, bc_o=bc1,
                   z_o=z1, xh_o=xh1m, cck="cch"))
    ucg = dram("ucg", [4 * 2048, TC], BF16)
    dtg = dram("dtg", [4 * 2048, TC], BF16)
    bcg = dram("bcg", [128, TC])
    p.collective([uc1, dt1, bc1], [ucg, dtg, bcg], GROUPS, key="cc4")
    yc = dram("yc", [512, S], BF16)
    bld_m2(p, dict(E, dtg=dtg, ucg=ucg, bcg=bcg, y_o=yc, cck="cc4"))
    yg = dram("yg", [2048, S], BF16)
    p.collective([yc], [yg], GROUPS, key="cc5")
    xc4 = dram("xc4", [D, TC])
    bld_m3(p, dict(E, yT=yg, ycol_dyn=offs_sb[0:1, 1:2], zT=z1, xhT=xh1m,
                   xT=xc3, xoT=xc4, cck="cc5"))
    if tap(xc4, 9):
        return p.finalize()
    xc5 = dram("xc5", [D, TC])
    bld_cffn(p=p, ext=dict(E, xT=xc4, xoT=xc5), stage="c1")
    if tap(xc5, 10):
        return p.finalize()
    v5c = dram("v5c", [TC, 2048], BF16)
    g5 = dram("g5", [TC, 2048], BF16)
    q5 = dram("q5", [128, TC], BF16)
    k5c = dram("k5c", [128, TC], BF16)
    xh15 = dram("xh15", [D, TC], BF16)
    bld_n1(p, dict(E, xT=xc5, v_o=v5c, g_o=g5, q_o=q5, k_o=k5c, xh1_o=xh15))
    kg5 = dram("kg5", [512, TC], BF16)
    vg5 = dram("vg5", [S, 2048], BF16)
    p.collective([k5c, v5c], [kg5, vg5], GROUPS, key="cc6")
    xc6 = dram("xc6", [D, TC])
    bld_n2(p, dict(E, q=q5, kF=None, kgath=kg5, vF=vg5, gate=g5, xT=xc5,
                   xh1T=xh15, idb=E["idenb"], xoT=xc6, cck="cc6"))
    if tap(xc6, 12):
        return p.finalize()
    bld_cffn(p=p, ext=dict(E, xT=xc6, xoT=xout), stage="c2")
    return p.finalize()


def _fused_inmaps(inp):
    g = lambda k: np.ascontiguousarray(np.asarray(inp[k], np.float32))
    x = g("L0_attn_a_strat")
    perm = _rope_perm()
    wqp = np.ascontiguousarray(g("L0_attn_b_wq")[:, perm])
    wkp = np.ascontiguousarray(g("L0_attn_b_wk")[:, perm])
    be1c = np.ascontiguousarray(
        g("L0_ffn_be1").reshape(8, 32, 128).transpose(2, 0, 1).reshape(128, 256))
    idf = np.eye(128, dtype=np.float32)
    r_ = np.arange(128)[:, None]
    c_ = np.arange(256)[None, :]
    triA = (c_ >= r_).astype(BF)
    triB = (c_ >= r_ + 128).astype(BF)
    cwm = np.ascontiguousarray(
        g("L1_attn_a_conv_w").reshape(16, 128, 4).transpose(1, 0, 2).reshape(128, 64))
    cbm = np.ascontiguousarray(g("L1_attn_a_conv_b").reshape(16, 128).T)
    dtbm = np.ascontiguousarray(g("L1_attn_a_dt_proj_b").reshape(16, 128).T)
    selc = np.zeros((128, 16, 128), np.float32)
    sely = np.zeros((128, 16, 128), np.float32)
    for gg in range(16):
        for pp_ in range(128):
            selc[gg * 8 + pp_ // 16, gg, pp_] = 1.0
            sely[pp_, gg, gg * 8 + pp_ // 16] = 1.0
    selc = np.ascontiguousarray(selc.reshape(128, 2048).astype(BF))
    sely = np.ascontiguousarray(sely.reshape(128, 2048).astype(BF))
    gb = np.ascontiguousarray(np.stack(
        [g("L1_attn_b_os_gamma")[0], g("L1_attn_b_os_beta")[0],
         g("L1_attn_b_os_gamma")[1], g("L1_attn_b_os_beta")[1]], axis=1).astype(np.float32))
    com = dict(
        wqT=np.ascontiguousarray(g("L0_attn_a_Wq").T),
        wkT=np.ascontiguousarray(g("L0_attn_a_Wk").T),
        wvT=np.ascontiguousarray(g("L0_attn_a_Wv").T),
        bq=np.ascontiguousarray(g("L0_attn_a_bq").reshape(D, 1)),
        bk=np.ascontiguousarray(g("L0_attn_a_bk").reshape(D, 1)),
        bvr=np.ascontiguousarray(g("L0_attn_a_bv").reshape(1, D)),
        idenb=_EYE_BF, idf=idf,
        rw1=g("L0_ffn_rw1"), rw2=g("L0_ffn_rw2"),
        rb1=np.ascontiguousarray(g("L0_ffn_rb1").reshape(D, 1)),
        rb2=np.ascontiguousarray(g("L0_ffn_rb2").reshape(8, 1)),
        we1=g("L0_ffn_we1"), we2=g("L0_ffn_we2"), be1c=be1c, be2r=g("L0_ffn_be2"),
        wqp=wqp, wkp=wkp, wv=g("L0_attn_b_wv"), triA=triA, triB=triB,
        wo=g("L0_attn_b_wo"),
        nw=np.ascontiguousarray(g("L1_attn_a_norm_w").reshape(D, 1)),
        ipT=np.ascontiguousarray(g("L1_attn_a_in_proj_w").T), cw=cwm, cb=cbm,
        xpT=np.ascontiguousarray(g("L1_attn_a_x_proj_w").T),
        dpT=np.ascontiguousarray(g("L1_attn_a_dt_proj_w").T), dtb=dtbm,
        dpar=None, selc_i=selc, sely_i=sely,
        opT=np.ascontiguousarray(g("L1_attn_a_out_proj_w").T),
        G1R=np.ascontiguousarray(g("L1_ffn_G1_R").reshape(32, 128)),
        G2R=np.ascontiguousarray(g("L1_ffn_G2_R").reshape(64, 16)),
        G1I=np.ascontiguousarray(g("L1_ffn_G1_I").reshape(32, 128)),
        G2I=np.ascontiguousarray(g("L1_ffn_G2_I").reshape(64, 16)),
        brc=np.ascontiguousarray(g("L1_ffn_bias_real").reshape(512, 1)),
        bic=np.ascontiguousarray(g("L1_ffn_bias_imag").reshape(512, 1)),
        Wh=g("L1_attn_b_W_hidden"),
        bhr=np.ascontiguousarray(g("L1_attn_b_b_hidden").reshape(1, 4096)),
        Wqk=g("L1_attn_b_W_qk"),
        bqk=np.ascontiguousarray(g("L1_attn_b_b_qk").reshape(128, 1)),
        lng=np.ascontiguousarray(g("L1_attn_b_ln_g").reshape(D, 1)),
        lnb=np.ascontiguousarray(g("L1_attn_b_ln_b").reshape(D, 1)),
        gb=gb, Wo=g("L1_attn_b_W_out"),
        bo=np.ascontiguousarray(g("L1_attn_b_b_out").reshape(D, 1)),
    )
    alog = g("L1_attn_a_A_log")
    Dp = g("L1_attn_a_D")
    ins = []
    for c, b, o in _cores():
        j = c % 4
        ds = j * 512
        cos, sin = _rope_tables(o, TC)
        kidx = np.arange(S)[:, None]
        qidx = np.arange(o, o + TC)[None, :]
        d = dict(com)
        d.update(
            xqT=np.ascontiguousarray(x[b, o:o + TC].T),
            xtm=np.ascontiguousarray(x[b, o:o + TC]),
            cosd=cos, sind=sin,
            ho=np.array([[((c * 4 + jj) % 16) * 64 * TC for jj in range(4)]], np.int32),
            hoff=np.array([[((j - 1) % 4) * 1024 * 3, 0]], np.int32),
            hmask=np.full((128, 1), 0.0 if j == 0 else 1.0, np.float32),
            alog_i=np.ascontiguousarray(alog[ds:ds + 512].reshape(64, 128)),
            dpar=np.ascontiguousarray(Dp[ds:ds + 512].reshape(4, 128).T),
            dsoff=np.array([[ds * TC, 0]], np.int32),
            dnr=(1.0 / (np.arange(o, o + TC, dtype=np.float64) + 1.0))
                .astype(np.float32).reshape(1, TC),
            Mm=(kidx <= qidx).astype(BF),
            offs=np.array([[o, o, 0, 0, 0, 0, 0, 0]], np.int32),
        )
        ins.append(d)
    return ins


def kernel_fused(upto=99, **inp):
    nc = bld_fused(upto=upto)
    ins = _fused_inmaps(inp)
    r = run_bass_kernel_spmd(nc, ins, core_ids=list(range(NCORE))).results
    out = np.empty((B, S, D), np.float32)
    for c, b, o in _cores():
        out[b, o:o + TC] = r[c]["xout"].T
    return out
